# version 08
# baseline (speedup 1.0000x reference)
from contextlib import ExitStack

import numpy as np
import ml_dtypes
import concourse.bass as bass
import concourse.mybir as mybir
from concourse.bass_utils import run_bass_kernel_spmd

F32 = mybir.dt.float32
BF16 = mybir.dt.bfloat16
FP16 = mybir.dt.float16
AF = mybir.ActivationFunctionType
ALU = mybir.AluOpType
AX = mybir.AxisListType

B, N, DMODEL = 4, 8192, 1024
T = 4096
HALO = 1152
EXT = HALO + T
W = 512
INNER_M = 2048
DSTATE = 64
HEADS, DH = 8, 64
INNER_A = 512
HID = 2730
NEG = -1.0e30
EPS = 1e-5
DH0 = 644
DACC_W = DH0 + T
HT = T // 2
HREM = HID - 21 * 128
ABF_W = 87040
AF_W = 6400
CONSTS_LIMIT = None
KH_W = 384 + EXT


def make_core_inputs(core, inputs):
    b, h = core // 2, core % 2
    s0 = h * T
    x = np.asarray(inputs["L0_attn_x"])
    lo = s0 - HALO
    xs = np.zeros((EXT, DMODEL), np.float32)
    src_lo = max(lo, 0)
    xs[src_lo - lo:] = np.asarray(x[b, src_lo:s0 + T], np.float32)

    pos = np.clip(np.arange(lo, s0 + T, dtype=np.float64), 0, None)
    half = np.arange(0, DH, 2, dtype=np.float64)
    inv_freq = 1.0 / 10000.0 ** (half / DH)
    fr = pos[None, :] * inv_freq[:, None]
    freqs = np.concatenate([fr, fr], 0)
    base = (half + 0.4 * DH) / (1.4 * DH)
    power = (pos - N // 2) / float(W // 2)
    scale = np.concatenate([base[:, None] ** power[None, :]] * 2, 0)
    cos, sin = np.cos(freqs), np.sin(freqs)
    sgn = np.concatenate([-np.ones(32), np.ones(32)])[:, None]

    r = np.arange(128)
    m_own = np.where(r[:, None] > r[None, :], NEG, 0.0)
    m_look = np.where(r[None, :] > r[:, None], NEG, 0.0)
    if h == 0:
        m0_d = np.full((128, 128), NEG, np.float32)
        m0_rect = np.full((128, W), NEG, np.float32)
    else:
        m0_d = m_look.astype(np.float32)
        m0_rect = np.zeros((128, W), np.float32)

    blkdiag = np.zeros((128, 128))
    blkdiag[:64, :64] = 1.0
    blkdiag[64:, 64:] = 1.0

    def f32(a):
        return np.ascontiguousarray(a, np.float32)

    d = {
        "xs": xs,
        "cs1q": f32(cos * scale), "cs2q": f32(sin * scale * sgn),
        "cs1k": f32(cos / scale), "cs2k": f32(sin / scale * sgn),
        "m_own": m_own.astype(ml_dtypes.bfloat16),
        "m_look": m_look.astype(ml_dtypes.bfloat16),
        "m0_d": m0_d.astype(ml_dtypes.bfloat16),
        "m0_rect": m0_rect.astype(ml_dtypes.bfloat16),
        "eye_bf": np.eye(128, dtype=ml_dtypes.bfloat16),
        "eye_f32": np.eye(128, dtype=np.float32),
        "ones_bf": np.ones((128, 128), ml_dtypes.bfloat16),
        "blkdiag_bf": blkdiag.astype(ml_dtypes.bfloat16),
        "evenm": np.full((128, 1), 1.0 if h == 0 else 0.0, np.float32),
        "oddm": np.full((128, 1), 1.0 if h == 1 else 0.0, np.float32),
        "haloflag": np.full((128, 1), 0.0 if h == 0 else 1.0, np.float32),
    }
    for k, v in inputs.items():
        if k != "L0_attn_x":
            d[k] = np.ascontiguousarray(np.asarray(v))
    cw = np.asarray(inputs["L0_attn_conv_w"], np.float32)
    for k in range(3):
        d[f"conv_w{k}"] = np.ascontiguousarray(cw[:, k])
    return d


def ap2(handle, offset, dims):
    return bass.AP(handle, offset, [list(x) for x in dims])


class Sem:
    def __init__(self, ctx, nc, name):
        self.h = ctx.enter_context(nc.semaphore(name))
        self.v = 0

    def inc(self, instr, n=1):
        instr.then_inc(self.h, n)
        self.v += n
        return self.v


class Tile:
    def __init__(self, ap, col):
        self.ap = ap
        self.col = col

    def __getitem__(self, idx):
        return self.ap[idx]


class Alloc:
    def __init__(self, arena, width):
        self.a, self.w, self.top = arena, width, 0

    def tile(self, n, parts=128, base=0):
        c0 = self.top
        self.top += n
        assert self.top <= self.w, f"arena overflow {self.top} > {self.w}"
        return Tile(self.a[base:base + parts, c0:c0 + n], c0)


class Drained:

    _PASS = ("wait_ge", "dma_start", "drain", "engine_nop", "nop",
             "collective_compute", "indirect_dma_start", "then_inc")

    def __init__(self, eng):
        self._e = eng

    def __getattr__(self, nm):
        f = getattr(self._e, nm)
        if nm in self._PASS or not callable(f):
            return f
        e = self._e

        def w(*a, **k):
            f(*a, **k)
            return e.drain()
        return w


class P:
    pass


def build_program(dev=False, upto=None):
    nc = bass.Bass("TRN2", target_bir_lowering=False)
    io = {}

    def din(name, shape, dtype=F32):
        io[name] = nc.dram_tensor(name, list(shape), dtype, kind="ExternalInput")

    def dscr(name, shape, dtype):
        kind = "ExternalOutput" if dev else "Internal"
        io[name] = nc.dram_tensor(name, list(shape), dtype, kind=kind)

    din("xs", [EXT, DMODEL])
    for nm in ["cs1q", "cs2q", "cs1k", "cs2k"]:
        din(nm, [DH, EXT])
    for nm in ["m_own", "m_look", "m0_d"]:
        din(nm, [128, 128], BF16)
    din("m0_rect", [128, W], BF16)
    din("eye_bf", [128, 128], BF16)
    din("eye_f32", [128, 128], F32)
    din("ones_bf", [128, 128], BF16)
    din("blkdiag_bf", [128, 128], BF16)
    din("evenm", [128, 1])
    din("oddm", [128, 1])
    din("haloflag", [128, 1])
    din("L0_attn_W_in", [DMODEL, 2 * INNER_M])
    din("L0_attn_b_in", [2 * INNER_M])
    din("L0_attn_conv_w", [INNER_M, 3])
    for k in range(3):
        din(f"conv_w{k}", [INNER_M])
    din("L0_attn_conv_b", [INNER_M])
    din("L0_attn_W_dt", [INNER_M, DSTATE])
    din("L0_attn_b_dt", [DSTATE])
    din("L0_attn_W_B", [INNER_M, DSTATE])
    din("L0_attn_b_B", [DSTATE])
    din("L0_attn_W_C", [INNER_M, DSTATE])
    din("L0_attn_b_C", [DSTATE])
    din("L0_attn_W_si", [DSTATE, INNER_M])
    din("L0_attn_b_si", [INNER_M])
    din("L0_attn_D", [INNER_M])
    din("L0_attn_W_out", [INNER_M, DMODEL])
    din("L0_attn_b_out", [DMODEL])
    din("L0_ffn_wqkv", [2, DMODEL, 3 * INNER_A])
    din("L0_ffn_q_scale", [2, DH])
    din("L0_ffn_k_scale", [2, DH])
    din("L0_ffn_wo", [2, INNER_A, DMODEL])
    din("L0_ffn_ln_g", [2, DMODEL])
    din("L0_ffn_ln_b", [2, DMODEL])
    din("L0_ffn_w1", [2, DMODEL, 2 * HID])
    din("L0_ffn_b1", [2, 2 * HID])
    din("L0_ffn_w2", [2, HID, DMODEL])
    din("L0_ffn_b2", [2, DMODEL])
    io["out"] = nc.dram_tensor("out", [T, DMODEL], F32, kind="ExternalOutput")
    dscr("u_scr", [T, DMODEL], F32)
    dscr("xnfm", [DMODEL, 1 + EXT], BF16)
    dscr("xmain", [INNER_M, T], BF16)
    dscr("sgate", [INNER_M, T], FP16)
    dscr("dm", [DMODEL, T], F32)
    dscr("st_scr", [128, HT], F32)
    dscr("pp_scr", [128, HT], BF16)
    dscr("c_scr", [128, HT], BF16)
    dscr("qh", [INNER_A, EXT], BF16)
    dscr("kh", [INNER_A, KH_W], BF16)
    dscr("vstat", [44 * 128, 768], BF16)
    dscr("dacc_a", [DMODEL, DACC_W], F32)
    dscr("dacc_b", [DMODEL, DACC_W], F32)
    dscr("dbg", [128, 2048], BF16)
    io["cc_in"] = nc.dram_tensor("cc_in", [DSTATE, 1], F32)
    io["cc_out"] = nc.dram_tensor("cc_out", [DSTATE, 1], F32)

    with ExitStack() as ctx:
        p = P()
        p.nc, p.ctx, p.io, p.dev = nc, ctx, io, dev
        p.abf = ctx.enter_context(nc.sbuf_tensor("abf", [128, ABF_W], BF16))
        p.af = ctx.enter_context(nc.sbuf_tensor("af", [128, AF_W], F32))
        p.af16 = ctx.enter_context(nc.sbuf_tensor("af16", [128, 1024], FP16))
        p.af16b = ctx.enter_context(nc.sbuf_tensor("af16b", [128, 1024], FP16))
        p.eye_bf = ctx.enter_context(nc.sbuf_tensor("sb_eye_bf", [128, 128], BF16))
        p.eye_f32 = ctx.enter_context(nc.sbuf_tensor("sb_eye_f32", [128, 128], F32))
        p.ones_bf = ctx.enter_context(nc.sbuf_tensor("sb_ones_bf", [128, 128], BF16))
        p.blkdiag = ctx.enter_context(nc.sbuf_tensor("sb_blkdiag", [128, 128], BF16))
        p.masks = ctx.enter_context(nc.sbuf_tensor("masks", [128, 384 + W], BF16))
        p.bias = ctx.enter_context(nc.sbuf_tensor("bias", [128, 288], F32))
        p.flags = ctx.enter_context(nc.sbuf_tensor("flags", [128, 8], F32))
        p.small = ctx.enter_context(nc.sbuf_tensor("small", [128, 80], F32))
        p.pf = [ctx.enter_context(nc.psum_tensor(f"pf{i}", [128, 512], F32))
                for i in range(6)]
        p.pb = [ctx.enter_context(nc.psum_tensor(f"pb{i}", [128, 1024], BF16))
                for i in range(2)]
        for nm in ["ld", "wl", "mm", "ev", "sc", "dd", "st", "cc", "g0", "g1",
                   "g2", "g3", "a0", "a1", "a2", "a3", "a4", "a5", "a6", "a7",
                   "d0", "d1", "d2", "d3", "d4", "d5", "d6", "d7", "d8", "d9"]:
            setattr(p, "s_" + nm, Sem(ctx, nc, nm))
        p.BOFF = {}
        steps = [lambda: load_consts(p),
                 lambda: phase_p0(p),
                 lambda: phase_p1a(p),
                 lambda: phase_qkv(p, 0),
                 lambda: phase_p1b(p),
                 lambda: phase_attn(p, 0),
                 lambda: phase_mlp(p, 0),
                 lambda: phase_qkv(p, 1),
                 lambda: phase_attn(p, 1),
                 lambda: phase_mlp(p, 1),
                 lambda: phase_fin(p)]
        if upto is None:
            upto = len(steps)
        for f in steps[:upto]:
            f()
    return nc, io


def load_consts(p):
    nc, io = p.nc, p.io
    s_ld, s_mm, s_ev = p.s_ld, p.s_mm, p.s_ev
    vtmp = p.af[:, 0:128]
    vps = p.pf[0][:, 0:128]
    cs = [0]

    with nc.Block("consts") as blk:
        @blk.gpsimd
        def _(g):
            g = Drained(g)
            for dst, src in [(p.eye_bf, "eye_bf"), (p.eye_f32, "eye_f32"),
                             (p.ones_bf, "ones_bf"), (p.blkdiag, "blkdiag_bf")]:
                s_ld.inc(g.dma_start(dst[:, :], io[src][:, :]), 16)
            s_ld.inc(g.dma_start(p.masks[:, 0:128], io["m_own"][:, :]), 16)
            s_ld.inc(g.dma_start(p.masks[:, 128:256], io["m_look"][:, :]), 16)
            s_ld.inc(g.dma_start(p.masks[:, 256:384], io["m0_d"][:, :]), 16)
            s_ld.inc(g.dma_start(p.masks[:, 384:384 + W], io["m0_rect"][:, :]), 16)
            s_ld.inc(g.dma_start(p.flags[:, 0:1], io["evenm"][:, :]), 16)
            s_ld.inc(g.dma_start(p.flags[:, 1:2], io["oddm"][:, :]), 16)
            s_ld.inc(g.dma_start(p.flags[:, 2:3], io["haloflag"][:, :]), 16)
            for l in range(2):
                s_ld.inc(g.dma_start(
                    p.af[0:1, 512 + 128 * l:576 + 128 * l],
                    ap2(io["L0_ffn_q_scale"], l * DH, [[64, 1], [1, 64]])), 16)
                s_ld.inc(g.dma_start(
                    p.af[0:1, 576 + 128 * l:640 + 128 * l],
                    ap2(io["L0_ffn_k_scale"], l * DH, [[64, 1], [1, 64]])), 16)
            g.memset(p.abf[:, 0:768], 0.0)
            g.memset(p.flags[:, 3:4], EPS)
            g.memset(p.af[0:1, 1280:1408], 1.0)
            p.s_a5.inc(g.memset(p.flags[:, 4:5], 0.0))
            g.wait_ge(s_ld.h, s_ld.v)

        @blk.sync
        def _(sy):
            if CONSTS_LIMIT == -4:
                return
            sy.wait_ge(p.s_a5.h, p.s_a5.v)
            p.s_st.inc(sy.dma_start(
                ap2(io["xnfm"], 0, [[1 + EXT, 128], [128 * (1 + EXT), 8], [1, 1]]),
                ap2(p.abf, 0, [[ABF_W, 128], [1, 8], [1, 1]]),
                allow_slow_non_contiguous=True), 16)
            for c in range(4):
                p.s_st.inc(sy.dma_start(
                    ap2(io["kh"], c * 128 * KH_W, [[KH_W, 128], [1, 384]]),
                    ap2(p.abf, 0, [[ABF_W, 128], [1, 384]])), 16)
            for blki in range(3):
                p.s_st.inc(sy.dma_start(
                    ap2(io["vstat"], blki * 128 * 768, [[768, 128], [1, 768]]),
                    ap2(p.abf, 0, [[ABF_W, 128], [1, 768]])), 16)
            sy.wait_ge(p.s_st.h, p.s_st.v)

    def vec(name, dram, length, offset=0):
        nchunk = (length + 127) // 128
        col = cs[0]
        cs[0] += nchunk
        p.BOFF[name] = col
        with nc.Block("v_" + name) as blk:
            @blk.gpsimd
            def _(g):
                g = Drained(g)
                p.s_a5.inc(g.memset(vtmp[:, :], 0.0))
                g.wait_ge(p.s_a5.h, p.s_a5.v)
                full, rem = divmod(length, 128)
                if full:
                    s_ld.inc(g.dma_start(
                        vtmp[0:full, 0:128],
                        ap2(dram, offset, [[128, full], [1, 128]])), 16)
                if rem:
                    s_ld.inc(g.dma_start(
                        vtmp[full:full + 1, 0:rem],
                        ap2(dram, offset + full * 128, [[rem, 1], [1, rem]])), 16)

            @blk.tensor
            def _(t):
                t.wait_ge(s_ld.h, s_ld.v)
                s_mm.inc(t.transpose(vps, vtmp[:, :], p.eye_f32[:, :]))

            @blk.vector
            def _(v):
                v = Drained(v)
                v.wait_ge(s_mm.h, s_mm.v)
                s_ev.inc(v.tensor_copy(p.bias[:, col:col + nchunk],
                                       vps[:, 0:nchunk]))

    veclist = [(f"conv_w{k}", io[f"conv_w{k}"], INNER_M, 0) for k in range(3)]
    veclist += [("b_in", io["L0_attn_b_in"], 2 * INNER_M, 0),
               ("conv_b", io["L0_attn_conv_b"], INNER_M, 0),
               ("b_si", io["L0_attn_b_si"], INNER_M, 0),
               ("D", io["L0_attn_D"], INNER_M, 0),
               ("b_out", io["L0_attn_b_out"], DMODEL, 0)]
    for l in range(2):
        veclist += [(f"ln_g{l}", io["L0_ffn_ln_g"], DMODEL, l * DMODEL),
                    (f"ln_b{l}", io["L0_ffn_ln_b"], DMODEL, l * DMODEL),
                    (f"b1a{l}", io["L0_ffn_b1"], HID, l * 2 * HID),
                    (f"b1g{l}", io["L0_ffn_b1"], HID, l * 2 * HID + HID),
                    (f"b2{l}", io["L0_ffn_b2"], DMODEL, l * DMODEL)]
    if CONSTS_LIMIT is not None:
        veclist = veclist[:max(0, CONSTS_LIMIT)]
    for nm_, dram_, ln_, off_ in veclist:
        vec(nm_, dram_, ln_, off_)
    if CONSTS_LIMIT == -4:
        pass
    if CONSTS_LIMIT is not None and CONSTS_LIMIT == -1:
        return
    SKIP_TAIL = CONSTS_LIMIT == -2
    SKIP_BOUND = CONSTS_LIMIT == -3
    if CONSTS_LIMIT in (-4, -5, -6):
        SKIP_TAIL = SKIP_BOUND = False

    if SKIP_TAIL:
        with nc.Block("bound") as blk:
            @blk.gpsimd
            def _(g):
                g = Drained(g)
                for l in range(2):
                    s_ld.inc(g.dma_start(
                        p.af[0:1, 0 + 128 * l:64 + 128 * l],
                        ap2(io["L0_ffn_q_scale"], l * DH, [[64, 1], [1, 64]])), 16)
                    s_ld.inc(g.dma_start(
                        p.af[0:1, 64 + 128 * l:128 + 128 * l],
                        ap2(io["L0_ffn_k_scale"], l * DH, [[64, 1], [1, 64]])), 16)
            @blk.vector
            def _(v):
                v = Drained(v)
                v.wait_ge(s_ld.h, s_ld.v)
                s_ev.inc(v.tensor_copy(p.af[0:1, 192:256], p.af[0:1, 0:64]))
        return
    with nc.Block("v_tail") as blk:
        @blk.gpsimd
        def _(g):
            g = Drained(g)
            for nm, dram, base in [("b_dt", io["L0_attn_b_dt"], 0),
                                   ("b_B", io["L0_attn_b_B"], 64),
                                   ("b_C", io["L0_attn_b_C"], 0)]:
                col = cs[0]
                cs[0] += 1
                p.BOFF[nm] = col
                s_ld.inc(g.dma_start(p.bias[base:base + 64, col:col + 1],
                                     ap2(dram, 0, [[1, 64], [1, 1]]),
                                     allow_slow_non_contiguous=True), 16)
            for l in range(2):
                for j, nm in [(0, "L0_ffn_q_scale"), (1, "L0_ffn_k_scale")]:
                    cc = l * 2 + j
                    for half in range(2):
                        s_ld.inc(g.dma_start(
                            p.small[64 * half:64 * half + 64, cc:cc + 1],
                            ap2(io[nm], l * DH, [[1, 64], [1, 1]]),
                            allow_slow_non_contiguous=True), 16)
            g.wait_ge(s_ld.h, s_ld.v)
    assert cs[0] <= 288, cs[0]

    if SKIP_BOUND:
        return
    with nc.Block("bound") as blk:
        @blk.vector
        def _(v):
            v = Drained(v)
            v.wait_ge(s_ld.h, s_ld.v)
            for l in range(2):
                v.tensor_tensor(p.af[0:1, 1024 + 64 * l:1024 + 64 * l + 64],
                                p.af[0:1, 512 + 128 * l:512 + 128 * l + 64],
                                p.af[0:1, 512 + 128 * l + 64:512 + 128 * l + 128],
                                ALU.mult)
                v.reduce_max(p.small[0:1, 56 + l:57 + l],
                             p.af[0:1, 1024 + 64 * l:1024 + 64 * l + 64],
                             axis=AX.X, apply_absolute_value=True)
                s_ev.inc(v.tensor_scalar(p.small[0:1, 56 + l:57 + l],
                                         p.small[0:1, 56 + l:57 + l],
                                         -8.0, None, ALU.mult))

        @blk.tensor
        def _(t):
            t.wait_ge(s_ev.h, s_ev.v)
            s_mm.inc(t.matmul(p.pf[1][:, 0:2], p.af[0:1, 1280:1408],
                              p.small[0:1, 56:58], start=True, stop=True))

        @blk.scalar
        def _(s):
            s = Drained(s)
            s.wait_ge(s_mm.h, s_mm.v)
            s_ev.inc(s.copy(p.small[:, 6:8], p.pf[1][:, 0:2]))


def phase_p0(p):
    nc, io = p.nc, p.io
    s_a, s_b, s_t, s_e2, s_e3 = (p.s_a0, p.s_a1, p.s_a2, p.s_a3, p.s_a4)
    s_ldp = [p.s_d0, p.s_d1]
    s_fmp = [p.s_d2, p.s_d3]
    s_up = [p.s_d4, p.s_d5]
    ab = Alloc(p.abf, ABF_W)
    af = Alloc(p.af, AF_W)
    xnbf = [ab.tile(1024) for _ in range(2)]
    sqj = ab.tile(1024)
    fmst = [ab.tile(1024) for _ in range(2)]
    xt = [af.tile(1024) for _ in range(2)]
    xn32 = af.tile(1024)
    ut = [af.tile(1024) for _ in range(2)]
    stt = af.tile(16)
    NT = EXT // 128
    OWN0 = HALO // 128
    e20, e30, a0, b0, t0 = s_e2.v, s_e3.v, s_a.v, s_b.v, s_t.v
    ld_mk, fm_mk, u_mk = {}, {}, {}
    for i in range(NT):
        ld_mk[i] = s_ldp[i % 2].v + 16 * (i // 2 + 1)
        fm_mk[i] = s_fmp[i % 2].v + 16 * (i // 2 + 1)
    ucnt = [0, 0]
    for i in range(OWN0, NT):
        ucnt[i % 2] += 1
        u_mk[i] = s_up[i % 2].v + 16 * ucnt[i % 2]

    with nc.Block("p0") as blk:
        @blk.sync
        def _(s):
            for i in range(NT):
                if i >= 2:
                    s.wait_ge(s_e3.h, e30 + i - 1)
                assert s_ldp[i % 2].inc(
                    s.dma_start(xt[i % 2][:, :],
                                io["xs"][i * 128:(i + 1) * 128, :]),
                    16) == ld_mk[i]
                s.wait_ge(s_e2.h, e20 + 8 * (i + 1))
                assert s_fmp[i % 2].inc(s.dma_start(
                    ap2(io["xnfm"], 1 + i * 128,
                        [[1 + EXT, 128], [128 * (1 + EXT), 8], [1, 128]]),
                    ap2(p.abf, fmst[i % 2].col,
                        [[ABF_W, 128], [128, 8], [1, 128]])), 16) == fm_mk[i]
                if i >= OWN0:
                    s.wait_ge(s_e3.h, e30 + i + 1)
                    assert s_up[i % 2].inc(s.dma_start(
                        io["u_scr"][(i - OWN0) * 128:(i - OWN0 + 1) * 128, :],
                        ut[i % 2][:, :]), 16) == u_mk[i]
            for sp in (s_fmp[0], s_fmp[1], s_up[0], s_up[1]):
                s.wait_ge(sp.h, sp.v)

        @blk.vector
        def _(v):
            v = Drained(v)
            for i in range(NT):
                x = xt[i % 2]
                v.wait_ge(s_ldp[i % 2].h, ld_mk[i])
                if i >= 1:
                    v.wait_ge(s_b.h, b0 + i)
                v.scalar_tensor_tensor(sqj[:, :], x[:, :], 1.0, x[:, :],
                                       ALU.mult, ALU.mult, accum_out=stt[:, 1:2])
                v.reduce_sum(stt[:, 0:1], x[:, :], axis=AX.X)
                v.tensor_scalar(stt[:, 2:3], stt[:, 0:1], 1.0 / DMODEL, None, ALU.mult)
                v.tensor_scalar(stt[:, 3:4], stt[:, 1:2], 1.0 / DMODEL, None, ALU.mult)
                v.tensor_tensor(stt[:, 4:5], stt[:, 2:3], stt[:, 2:3], ALU.mult)
                s_a.inc(v.tensor_tensor(stt[:, 3:4], stt[:, 3:4], stt[:, 4:5],
                                        ALU.subtract))
                for j in range(8):
                    v.wait_ge(s_t.h, t0 + i * 8 + j + 1)
                    s_e2.inc(v.tensor_copy(fmst[i % 2][:, j * 128:(j + 1) * 128],
                                           p.pb[j % 2][:, 0:128]))
                if i >= OWN0:
                    v.wait_ge(s_b.h, b0 + i + 1)
                    if i - 2 in u_mk:
                        v.wait_ge(s_up[i % 2].h, u_mk[i - 2])
                    s_e3.inc(v.tensor_add(ut[i % 2][:, :], x[:, :],
                                          xn32[:, :]))
                else:
                    s_e3.inc(v.engine_nop())

        @blk.scalar
        def _(s):
            s = Drained(s)
            for i in range(NT):
                x = xt[i % 2]
                s.wait_ge(s_a.h, a0 + i + 1)
                if i >= 1:
                    s.wait_ge(s_e3.h, e30 + i)
                s.activation(stt[:, 5:6], stt[:, 3:4], AF.Ln, bias=p.flags[:, 3:4], scale=1.0)
                s.activation(stt[:, 6:7], stt[:, 5:6], AF.Exp, bias=0.0, scale=-0.5)
                s.mul(stt[:, 7:8], stt[:, 2:3], stt[:, 6:7])
                s.mul(stt[:, 7:8], stt[:, 7:8], -1.0)
                last = s.activation(xnbf[i % 2][:, :], x[:, :], AF.Identity,
                                    bias=stt[:, 7:8], scale=stt[:, 6:7])
                if i >= OWN0:
                    last = s.activation(xn32[:, :], x[:, :], AF.Identity,
                                        bias=stt[:, 7:8], scale=stt[:, 6:7])
                s_b.inc(last)

        @blk.tensor
        def _(t):
            for i in range(NT):
                t.wait_ge(s_b.h, b0 + i + 1)
                if i >= 2:
                    t.wait_ge(s_fmp[i % 2].h, fm_mk[i - 2])
                for j in range(8):
                    k = i * 8 + j
                    if k >= 2:
                        t.wait_ge(s_e2.h, e20 + k - 1)
                    s_t.inc(t.transpose(p.pb[j % 2][:, 0:128],
                                        xnbf[i % 2][:, j * 128:(j + 1) * 128],
                                        p.eye_bf[:, :]))


def phase_p1a(p):
    nc, io = p.nc, p.io
    BO = p.BOFF
    ab = Alloc(p.abf, ABF_W)
    af = Alloc(p.af, AF_W)
    w_in = ab.tile(8 * 4096)
    wdtB = ab.tile(16 * 128)
    wC = ab.tile(16 * 64)
    xslab = [ab.tile(8 * 512) for _ in range(2)]
    work = [ab.tile(514) for _ in range(16)]
    xmn = [ab.tile(512) for _ in range(16)]
    C_bf = ab.tile(HT)
    pp_bf = ab.tile(HT)
    zz_bf = ab.tile(HT)
    xm2 = [ab.tile(512) for _ in range(2)]
    gt4 = [Tile(p.af16[:, 0:512], 0), Tile(p.af16[:, 512:1024], 512),
           Tile(p.af16b[:, 0:512], 0), Tile(p.af16b[:, 512:1024], 512)]
    cva = af.tile(512)
    cvb = [af.tile(512) for _ in range(2)]
    tmp_ev = af.tile(512)
    dt_s = af.tile(HT)
    B_s = af.tile(HT)
    st_s = Tile(p.af[:, 0:HT], 0)
    NTT = 9
    s_wl, s_mm, s_sc, s_cv, s_xm, s_w = (p.s_wl, p.s_mm, p.s_sc,
                                         p.s_ev, p.s_a0, p.s_st)
    s_d2, s_d3 = p.s_a1, p.s_a2
    s_ldp = [p.s_d0, p.s_d1]
    s_xmp = [p.s_d2b, p.s_d3b] if False else [p.s_d6, p.s_d7]
    s_gtp = [p.s_d8, p.s_d9]
    s_ld2 = [p.s_d4, p.s_d5]

    with nc.Block("p1a_w") as blk:
        @blk.gpsimd
        def _(g):
            g = Drained(g)
            for c in range(8):
                s_wl.inc(g.dma_start(
                    ap2(p.abf, w_in.col + c * 4096, [[ABF_W, 128], [1, 4096]]),
                    io["L0_attn_W_in"][c * 128:(c + 1) * 128, :]), 16)
            for c in range(16):
                s_wl.inc(g.dma_start(
                    ap2(p.abf, wdtB.col + c * 128, [[ABF_W, 128], [1, 64]]),
                    io["L0_attn_W_dt"][c * 128:(c + 1) * 128, :]), 16)
                s_wl.inc(g.dma_start(
                    ap2(p.abf, wdtB.col + c * 128 + 64, [[ABF_W, 128], [1, 64]]),
                    io["L0_attn_W_B"][c * 128:(c + 1) * 128, :]), 16)
                s_wl.inc(g.dma_start(
                    ap2(p.abf, wC.col + c * 64, [[ABF_W, 128], [1, 64]]),
                    io["L0_attn_W_C"][c * 128:(c + 1) * 128, :]), 16)
            g.memset(zz_bf[:, :], 0.0)
            for fc in range(16):
                g.memset(work[fc][:, 0:2], 0.0)
            g.wait_ge(s_wl.h, s_wl.v)

    mm0, sc0, cv0, xm0 = s_mm.v, s_sc.v, s_cv.v, s_xm.v
    ld_mk = {t: s_ldp[t % 2].v + 16 * (t // 2 + 1) for t in range(NTT)}
    xm_mk = {}
    gt_sp = {}
    gb = [s_gtp[0].v, s_gtp[1].v]
    for tt in range(1, NTT):
        xm_mk[tt] = s_xmp[tt % 2].v + 16 * 16 * ((tt - 1) // 2 + 1)
        for fc in range(16):
            gi = (tt - 1) * 16 + fc
            gt_sp[gi] = gb[gi % 2] + 16 * (gi // 2 + 1)

    with nc.Block("p1a1") as blk:
        @blk.sync
        def _(s):
            for t in range(NTT + 1):
                if t >= 2:
                    tt = t - 1
                    for fc in range(16):
                        s.wait_ge(s_xm.h, xm0 + tt * 16 + fc + 1)
                        mk = s_xmp[tt % 2].inc(s.dma_start(
                            io["xmain"][fc * 128:(fc + 1) * 128,
                                        (tt - 1) * 512:tt * 512],
                            xmn[fc][:, :]), 16)
                        if fc == 15:
                            assert mk == xm_mk[tt]
                    for fc in range(16):
                        gi = (tt - 1) * 16 + fc
                        s.wait_ge(s_sc.h, sc0 + tt * 32 + 16 + fc + 1)
                        if gi >= 2:
                            s.wait_ge(s_gtp[gi % 2].h, gt_sp[gi - 2])
                        assert s_gtp[gi % 2].inc(s.dma_start(
                            io["sgate"][fc * 128:(fc + 1) * 128,
                                        (tt - 1) * 512:tt * 512],
                            gt4[fc % 4][:, :]), 16) == gt_sp[gi]
                if t < NTT:
                    if t >= 2:
                        s.wait_ge(s_mm.h, mm0 + (t - 1) * 32)
                    assert s_ldp[t % 2].inc(s.dma_start(
                        ap2(p.abf, xslab[t % 2].col,
                            [[ABF_W, 128], [512, 8], [1, 512]]),
                        ap2(io["xnfm"], 1 + HALO - 512 + t * 512,
                            [[1 + EXT, 128], [128 * (1 + EXT), 8], [1, 512]])),
                        16) == ld_mk[t]

        @blk.tensor
        def _(t_):
            for t in range(NTT):
                t_.wait_ge(s_ldp[t % 2].h, ld_mk[t])
                for fc in range(32):
                    k = t * 32 + fc
                    if k >= 2:
                        t_.wait_ge(s_sc.h, sc0 + k - 1)
                    ps = p.pf[fc % 2]
                    for c in range(8):
                        mi = t_.matmul(ps[:, :],
                                       ap2(p.abf, w_in.col + c * 4096 + fc * 128,
                                           [[ABF_W, 128], [1, 128]]),
                                       ap2(p.abf, xslab[t % 2].col + c * 512,
                                           [[ABF_W, 128], [1, 512]]),
                                       start=(c == 0), stop=(c == 7))
                    s_mm.inc(mi)

        @blk.scalar
        def _(s):
            s = Drained(s)
            for t in range(NTT):
                for fc in range(16):
                    s.wait_ge(s_mm.h, mm0 + t * 32 + fc + 1)
                    if t >= 1:
                        s.wait_ge(s_cv.h, cv0 + ((t - 1) * 16 + fc) * 2 + 2)
                    s_sc.inc(s.activation(
                        work[fc][:, 2:514], p.pf[fc % 2][:, :], AF.Identity,
                        bias=p.bias[:, BO["b_in"] + fc:BO["b_in"] + fc + 1],
                        scale=1.0))
                    if fc >= 1:
                        fp = fc - 1
                        s.wait_ge(s_cv.h, cv0 + (t * 16 + fp) * 2 + 1)
                        if fp == 0 and (t - 1) in xm_mk:
                            s.wait_ge(s_xmp[(t - 1) % 2].h, xm_mk[t - 1])
                        s_xm.inc(s.activation(
                            xmn[fp][:, :], cvb[fp % 2][:, :], AF.Silu,
                            bias=p.bias[:, BO["conv_b"] + fp:BO["conv_b"] + fp + 1],
                            scale=1.0))
                fp = 15
                s.wait_ge(s_cv.h, cv0 + (t * 16 + fp) * 2 + 1)
                s_xm.inc(s.activation(
                    xmn[fp][:, :], cvb[fp % 2][:, :], AF.Silu,
                    bias=p.bias[:, BO["conv_b"] + fp:BO["conv_b"] + fp + 1],
                    scale=1.0))
                for fc in range(16):
                    s.wait_ge(s_mm.h, mm0 + t * 32 + 16 + fc + 1)
                    gi = ((t - 1) * 16 + fc - 4) if fc >= 4 \
                        else ((t - 2) * 16 + fc + 12)
                    if gi in gt_sp:
                        s.wait_ge(s_gtp[gi % 2].h, gt_sp[gi])
                    s_sc.inc(s.activation(
                        gt4[fc % 4][:, :], p.pf[fc % 2][:, :], AF.Sigmoid,
                        bias=p.bias[:, BO["b_in"] + 16 + fc:BO["b_in"] + 17 + fc],
                        scale=1.0))

        @blk.vector
        def _(v):
            v = Drained(v)
            for t in range(NTT):
                for fc in range(16):
                    v.wait_ge(s_sc.h, sc0 + t * 32 + fc + 1)
                    wk = work[fc]
                    c0 = BO["conv_w0"] + fc
                    c1 = BO["conv_w1"] + fc
                    c2 = BO["conv_w2"] + fc
                    v.tensor_scalar(cva[:, :], wk[:, 1:513],
                                    p.bias[:, c1:c1 + 1], None, ALU.mult)
                    v.scalar_tensor_tensor(cvb[fc % 2][:, :], wk[:, 0:512],
                                           p.bias[:, c0:c0 + 1], cva[:, :],
                                           ALU.mult, ALU.add)
                    s_cv.inc(v.scalar_tensor_tensor(
                        cvb[fc % 2][:, :], wk[:, 2:514],
                        p.bias[:, c2:c2 + 1], cvb[fc % 2][:, :],
                        ALU.mult, ALU.add))
                    s_cv.inc(v.tensor_copy(wk[:, 0:2], wk[:, 512:514]))

    mm1, sc1, d20, d30 = s_mm.v, s_sc.v, s_d2.v, s_d3.v
    ld2_mk = {}
    for t in range(8):
        for fc in range(16):
            k = t * 16 + fc
            ld2_mk[k] = s_ld2[k % 2].v + 16 * (k // 2 + 1)
    with nc.Block("p1a2") as blk:
        @blk.sync
        def _(s):
            for sp in (s_xmp[0], s_xmp[1], s_gtp[0], s_gtp[1]):
                s.wait_ge(sp.h, sp.v)
            for t in range(8):
                for fc in range(16):
                    k = t * 16 + fc
                    if k >= 2:
                        s.wait_ge(s_mm.h, mm1 + k - 1)
                    assert s_ld2[k % 2].inc(s.dma_start(
                        xm2[fc % 2][:, :],
                        io["xmain"][fc * 128:(fc + 1) * 128,
                                    t * 512:(t + 1) * 512]), 16) == ld2_mk[k]

        @blk.tensor
        def _(t_):
            for t in range(8):
                for fc in range(16):
                    k = t * 16 + fc
                    t_.wait_ge(s_ld2[k % 2].h, ld2_mk[k])
                    if fc == 0 and t >= 1:
                        t_.wait_ge(s_sc.h, sc1 + t)
                    t_.matmul(p.pf[2][:, :],
                              ap2(p.abf, wdtB.col + fc * 128,
                                  [[ABF_W, 128], [1, 128]]),
                              xm2[fc % 2][:, :],
                              start=(fc == 0), stop=(fc == 15))
                    mi = t_.matmul(p.pf[3][0:64, :],
                                   ap2(p.abf, wC.col + fc * 64,
                                       [[ABF_W, 128], [1, 64]]),
                                   xm2[fc % 2][:, :],
                                   start=(fc == 0), stop=(fc == 15))
                    s_mm.inc(mi)

        @blk.scalar
        def _(s):
            s = Drained(s)
            for t in range(8):
                s.wait_ge(s_mm.h, mm1 + (t + 1) * 16)
                if t >= 1:
                    s.wait_ge(s_d3.h, d30 + t * 3)
                s.activation(tmp_ev[0:64, :], p.pf[2][0:64, :], AF.Sigmoid,
                             bias=p.bias[0:64, BO["b_dt"]:BO["b_dt"] + 1],
                             scale=1.0)
                s.activation(tmp_ev[64:128, :], p.pf[2][64:128, :], AF.Identity,
                             bias=p.bias[64:128, BO["b_B"]:BO["b_B"] + 1],
                             scale=1.0)
                s_sc.inc(s.activation(cva[0:64, :], p.pf[3][0:64, :], AF.Identity,
                                      bias=p.bias[0:64, BO["b_C"]:BO["b_C"] + 1],
                                      scale=1.0))

        @blk.vector
        def _(v):
            v = Drained(v)
            for t in range(8):
                v.wait_ge(s_sc.h, sc1 + t + 1)
                half, c0 = t // 4, (t * 512) % HT
                b = 64 * half
                s_d3.inc(v.tensor_copy(dt_s[b:b + 64, c0:c0 + 512],
                                       tmp_ev[0:64, :]))
                s_d3.inc(v.tensor_copy(B_s[b:b + 64, c0:c0 + 512],
                                       tmp_ev[64:128, :]))
                s_d3.inc(v.tensor_copy(C_bf[b:b + 64, c0:c0 + 512],
                                       cva[0:64, :]))

    with nc.Block("p1a3") as blk:
        @blk.vector
        def _(v):
            v = Drained(v)
            v.tensor_tensor(B_s[:, :], dt_s[:, :], B_s[:, :], ALU.mult)
            v.tensor_scalar(dt_s[:, :], dt_s[:, :], -1.0, 1.0, ALU.mult, ALU.add)
            v.tensor_tensor_scan(st_s[:, :], dt_s[:, :], B_s[:, :], 0.0,
                                 ALU.mult, ALU.add)
            v.tensor_tensor_scan(pp_bf[:, :], dt_s[:, :], zz_bf[:, :], 1.0,
                                 ALU.mult, ALU.add)
            v.tensor_copy(p.small[64:128, 61:62], st_s[0:64, HT - 1:HT])
            v.tensor_copy(p.small[64:128, 62:63], pp_bf[0:64, HT - 1:HT])
            v.scalar_tensor_tensor(st_s[64:128, :], pp_bf[64:128, :],
                                   p.small[64:128, 61:62], st_s[64:128, :],
                                   ALU.mult, ALU.add)
            v.tensor_scalar(pp_bf[64:128, :], pp_bf[64:128, :],
                            p.small[64:128, 62:63], None, ALU.mult)
            s_cv.inc(v.tensor_tensor(p.small[64:128, 63:64],
                                     st_s[64:128, HT - 1:HT],
                                     p.flags[64:128, 0:1], ALU.mult))

        @blk.sync
        def _(s):
            s.wait_ge(s_cv.h, s_cv.v)
            s_w.inc(s.dma_start(io["st_scr"][:, :], st_s[:, :]), 16)
            s_w.inc(s.dma_start(io["pp_scr"][:, :], pp_bf[:, :]), 16)
            s_w.inc(s.dma_start(io["c_scr"][:, :], C_bf[:, :]), 16)
            s_w.inc(s.dma_start(ap2(io["cc_in"], 0, [[1, 64], [1, 1]]),
                                p.small[64:128, 63:64]), 16)
            s.wait_ge(s_w.h, s_w.v)

        @blk.gpsimd
        def _(g):
            g = Drained(g)
            g.wait_ge(s_w.h, s_w.v)
            s_cc_i = p.s_cc
            s_cc_i.inc(g.collective_compute(
                "AllReduce", ALU.add,
                replica_groups=[[0, 1], [2, 3], [4, 5], [6, 7]],
                ins=[io["cc_in"].ap().opt()],
                outs=[io["cc_out"].ap().opt()]))




def phase_qkv(p, l):
    nc, io = p.nc, p.io
    ab = Alloc(p.abf, ABF_W)
    af = Alloc(p.af, AF_W)
    wq = ab.tile(8 * 1536)
    xnsl = [ab.tile(8 * 513) for _ in range(2)]
    dsl = [ab.tile(8 * 513) for _ in range(2)] if l == 1 else None
    sqb = ab.tile(512)
    qhb2 = [ab.tile(512) for _ in range(2)]
    vcf = ab.tile(512)
    vsb = [ab.tile(768) for _ in range(4)]
    tabs = [[af.tile(512) for _ in range(4)] for _ in range(2)]
    nn = af.tile(512)
    rstd = af.tile(512)
    qn = af.tile(512)
    qrot = af.tile(512)
    (s_mm, s_sq, s_m2, s_fr2, s_rs, s_fr, s_sp, s_t, s_vev) = (
        p.s_mm, p.s_sc, p.s_a4, p.s_a5, p.s_a6, p.s_a0, p.s_a1,
        p.s_a2, p.s_a3)
    s_hs = p.s_ev
    s_ldp = [p.s_d0, p.s_d1]
    s_gp = [p.s_g0, p.s_g2]
    s_tabp = [p.s_d6, p.s_d7]
    s_qsp = [p.s_d8, p.s_d9]
    s_vwp = [p.s_d2, p.s_d3]

    if l == 0:
        tlist = [(-HALO + 512 * t, 512) for t in range(10)] + [(3968, 128)]
    else:
        tlist = [(-640, 128)] + [(-512 + 512 * t, 512) for t in range(9)]
    TABN = ["cs1q", "cs2q", "cs1k", "cs2k"]
    NTI = len(tlist)

    with nc.Block(f"qkv{l}_w") as blk:
        @blk.gpsimd
        def _(g):
            g = Drained(g)
            for c in range(8):
                p.s_wl.inc(g.dma_start(
                    ap2(p.abf, wq.col + c * 1536, [[ABF_W, 128], [1, 1536]]),
                    ap2(io["L0_ffn_wqkv"], l * DMODEL * 1536 + c * 128 * 1536,
                        [[1536, 128], [1, 1536]])), 16)
            for j in range(4):
                for hp in range(4):
                    g.memset(vsb[j][:, 192 * hp + 64:192 * hp + 128], 1.0)
            g.wait_ge(p.s_wl.h, p.s_wl.v)

    mm0, sq0, m20, fr20, rs0, fr0, sp0 = (s_mm.v, s_sq.v, s_m2.v,
                                          s_fr2.v, s_rs.v, s_fr.v, s_sp.v)
    t0v, vev0, hs0 = s_t.v, s_vev.v, s_hs.v
    mk_mm, mk_sq, mk_m2, mk_fr2, mk_rs, mk_fr, mk_sp = {}, {}, {}, {}, {}, {}, {}
    mk_t, mk_vev, mk_vw, mk_spw, mk_hs, mk_ld, mk_tab = {}, {}, {}, {}, {}, {}, {}
    tp_order = []
    mk_g0 = {}
    mk_qs = {}
    for ti, (tau0, n) in enumerate(tlist):
        nj = n // 128
        mk_ld[ti] = s_ldp[ti % 2].v + 16 * (ti // 2 + 1)
        mk_g0[ti] = s_gp[ti % 2].v + 32 * (ti // 2 + 1)
        mk_hs[ti] = hs0 + 8 * (ti + 1)
        mk_tab[ti] = s_tabp[ti % 2].v + 128 * (ti // 2 + 1)
        for fc in range(8):
            kq = ti * 8 + fc
            mk_qs[(ti, fc)] = s_qsp[kq % 2].v + 16 * (kq // 2 + 1)
        for fc in range(12):
            mk_mm[(ti, fc)] = mm0 + ti * 12 + fc + 1
            mk_fr[(ti, fc)] = fr0 + ti * 12 + fc + 1
        for fc in range(8):
            mk_sq[(ti, fc)] = sq0 + ti * 8 + fc + 1
            mk_m2[(ti, fc)] = m20 + ti * 8 + fc + 1
            mk_fr2[(ti, fc)] = fr20 + ti * 8 + fc + 1
            mk_rs[(ti, fc)] = rs0 + ti * 8 + fc + 1
            mk_sp[(ti, fc)] = sp0 + ti * 8 + fc + 1
        for hp in range(4):
            for j in range(nj):
                tp_order.append((ti, hp, j))
    for k, key in enumerate(tp_order):
        mk_t[key] = t0v + k + 1
        mk_vev[key] = vev0 + k + 1
    vw_par = [s_vwp[0].v, s_vwp[1].v]
    for ti, (tau0, n) in enumerate(tlist):
        vw_par[ti % 2] += 16 * (n // 128)
        mk_vw[ti] = vw_par[ti % 2]

    with nc.Block(f"qkv{l}") as blk:
        @blk.vector
        def _(v):
            v = Drained(v)
            for ti, (tau0, n) in enumerate(tlist):
                nj = n // 128
                if l == 1:
                    v.wait_ge(s_ldp[ti % 2].h, mk_ld[ti])
                    v.wait_ge(s_gp[ti % 2].h, mk_g0[ti])
                    for c in range(8):
                        ins = v.tensor_tensor(
                            xnsl[ti % 2][:, c * 513:c * 513 + n + 1],
                            xnsl[ti % 2][:, c * 513:c * 513 + n + 1],
                            dsl[ti % 2][:, c * 513:c * 513 + n + 1], ALU.add)
                    assert s_hs.inc(ins, 8) == mk_hs[ti]
                for fc in range(8):
                    cc = l * 2 + (0 if fc < 4 else 1)
                    v.wait_ge(s_m2.h, mk_m2[(ti, fc)])
                    assert s_fr2.inc(
                        v.tensor_scalar(nn[:, 0:n], p.pf[3 + fc % 2][:, 0:n],
                                        1e-24, None, ALU.max)) == mk_fr2[(ti, fc)]
                    v.wait_ge(s_rs.h, mk_rs[(ti, fc)])
                    v.tensor_scalar(rstd[:, 0:n], rstd[:, 0:n],
                                    p.small[:, cc:cc + 1], None, ALU.mult)
                    assert s_fr.inc(
                        v.tensor_tensor(qn[:, 0:n], p.pf[fc % 3][:, 0:n],
                                        rstd[:, 0:n], ALU.mult)) == mk_fr[(ti, fc)]
                    if p.dev and l == 0 and ti == 2 and fc == 0:
                        v.tensor_copy(p.abf[:, 85024:85536], rstd[:, 0:512])
                        v.tensor_copy(p.abf[:, 84512:85024], qn[:, 0:512])
                        v.tensor_copy(p.abf[:, 85536:86048], nn[:, 0:512])
                    for bb in range(4):
                        src = [32, 0, 96, 64][bb]
                        v.tensor_copy(qrot[32 * bb:32 * bb + 32, 0:n],
                                      qn[src:src + 32, 0:n])
                    tb1 = tabs[ti % 2][0 if fc < 4 else 2].ap
                    tb2 = tabs[ti % 2][1 if fc < 4 else 3].ap
                    v.wait_ge(s_tabp[ti % 2].h, mk_tab[ti])
                    v.tensor_tensor(qn[:, 0:n], qn[:, 0:n], tb1[:, 0:n], ALU.mult)
                    v.tensor_tensor(qrot[:, 0:n], qrot[:, 0:n], tb2[:, 0:n],
                                    ALU.mult)
                    kq = ti * 8 + fc
                    prev = (ti, fc - 2) if fc >= 2 else (ti - 1, fc + 6)
                    if prev in mk_qs:
                        v.wait_ge(s_qsp[kq % 2].h, mk_qs[prev])
                    qhb = qhb2[kq % 2]
                    if p.dev and l == 0 and ti == 4 and fc == 0:
                        v.tensor_copy(p.abf[:, 80000:80512], rstd[:, 0:n])
                        v.tensor_copy(p.abf[:, 80512:81024], qn[:, 0:n])
                        v.tensor_copy(p.abf[:, 81024:81536], qrot[:, 0:n])
                        v.tensor_copy(p.abf[:, 81536:82048], nn[:, 0:n])
                    assert s_sp.inc(
                        v.tensor_tensor(qhb[:, 0:n], qn[:, 0:n], qrot[:, 0:n],
                                        ALU.add)) == mk_sp[(ti, fc)]
                for hp in range(4):
                    fc = 8 + hp
                    v.wait_ge(s_mm.h, mk_mm[(ti, fc)])
                    if hp >= 1:
                        v.wait_ge(s_t.h, mk_t[(ti, hp - 1, nj - 1)])
                    assert s_fr.inc(
                        v.tensor_copy(vcf[:, 0:n], p.pf[fc % 3][:, 0:n])
                    ) == mk_fr[(ti, fc)]
                    if p.dev and l == 0 and ti == 2 and hp == 0:
                        v.tensor_copy(p.abf[:, 84000:84512], vcf[:, 0:512])
                    for j in range(nj):
                        v.wait_ge(s_t.h, mk_t[(ti, hp, j)])
                        if hp == 0 and j == 0 and (ti - 1) in mk_vw:
                            v.wait_ge(s_vwp[(ti - 1) % 2].h, mk_vw[ti - 1])
                        v.tensor_copy(vsb[j][:, 192 * hp:192 * hp + 64],
                                      p.pb[j % 2][:, 0:64])
                        assert s_vev.inc(v.tensor_copy(
                            vsb[j][:, 192 * hp + 128:192 * hp + 192],
                            p.pb[j % 2][:, 64:128])) == mk_vev[(ti, hp, j)]

        @blk.scalar
        def _(s):
            s = Drained(s)
            for ti, (tau0, n) in enumerate(tlist):
                for fc in range(8):
                    s.wait_ge(s_mm.h, mk_mm[(ti, fc)])
                    assert s_sq.inc(
                        s.activation(sqb[:, 0:n], p.pf[fc % 3][:, 0:n],
                                     AF.Square, bias=0.0, scale=1.0)
                    ) == mk_sq[(ti, fc)]
                    s.wait_ge(s_fr2.h, mk_fr2[(ti, fc)])
                    s.activation(nn[:, 0:n], nn[:, 0:n], AF.Ln,
                                 bias=0.0, scale=1.0)
                    assert s_rs.inc(
                        s.activation(rstd[:, 0:n], nn[:, 0:n], AF.Exp,
                                     bias=0.0, scale=-0.5)) == mk_rs[(ti, fc)]

        @blk.tensor
        def _(t_):
            tcnt = 0
            for ti, (tau0, n) in enumerate(tlist):
                nj = n // 128
                if l == 1:
                    t_.wait_ge(s_hs.h, mk_hs[ti])
                else:
                    t_.wait_ge(s_ldp[ti % 2].h, mk_ld[ti])
                for fc in range(12):
                    prev = (ti, fc - 3) if fc >= 3 else (ti - 1, fc + 9)
                    if prev in mk_fr:
                        t_.wait_ge(s_fr.h, mk_fr[prev])
                    ps = p.pf[fc % 3]
                    for c in range(8):
                        off = 1 if c < 4 else 0
                        mi = t_.matmul(
                            ps[:, 0:n],
                            ap2(p.abf, wq.col + c * 1536 + fc * 128,
                                [[ABF_W, 128], [1, 128]]),
                            ap2(p.abf, xnsl[ti % 2].col + c * 513 + off,
                                [[ABF_W, 128], [1, n]]),
                            start=(c == 0), stop=(c == 7))
                    assert s_mm.inc(mi) == mk_mm[(ti, fc)]
                    if fc < 8:
                        t_.wait_ge(s_sq.h, mk_sq[(ti, fc)])
                        prev2 = (ti, fc - 2) if fc >= 2 else (ti - 1, fc + 6)
                        if prev2 in mk_fr2:
                            t_.wait_ge(s_fr2.h, mk_fr2[prev2])
                        assert s_m2.inc(
                            t_.matmul(p.pf[3 + fc % 2][:, 0:n],
                                      p.blkdiag[:, :], sqb[:, 0:n],
                                      start=True, stop=True)) == mk_m2[(ti, fc)]
                    else:
                        hp = fc - 8
                        t_.wait_ge(s_fr.h, mk_fr[(ti, fc)])
                        for j in range(nj):
                            if tcnt >= 2:
                                t_.wait_ge(s_vev.h,
                                           mk_vev[tp_order[tcnt - 2]])
                            assert s_t.inc(
                                t_.transpose(p.pb[j % 2][:, 0:128],
                                             vcf[:, j * 128:(j + 1) * 128],
                                             p.eye_bf[:, :])
                            ) == mk_t[(ti, hp, j)]
                            tcnt += 1

        @blk.sync
        def _(s):
            for ti, (tau0, n) in enumerate(tlist):
                nj = n // 128
                if ti >= 2:
                    s.wait_ge(s_mm.h, mk_mm[(ti - 2, 11)])
                assert s_ldp[ti % 2].inc(s.dma_start(
                    ap2(p.abf, xnsl[ti % 2].col,
                        [[ABF_W, 128], [513, 8], [1, n + 1]]),
                    ap2(io["xnfm"], tau0 + HALO,
                        [[1 + EXT, 128], [128 * (1 + EXT), 8], [1, n + 1]])),
                    16) == mk_ld[ti]
                if ti >= 2:
                    s.wait_ge(s_sp.h, mk_sp[(ti - 2, 7)])
                for tb in range(4):
                    for half in range(2):
                        s_tabp[ti % 2].inc(s.dma_start(
                            tabs[ti % 2][tb].ap[64 * half:64 * half + 64, 0:n],
                            ap2(io[TABN[tb]], tau0 + HALO, [[EXT, 64], [1, n]])),
                            16)
                assert s_tabp[ti % 2].v == mk_tab[ti]
                for fc in range(8):
                    kq = ti * 8 + fc
                    s.wait_ge(s_sp.h, mk_sp[(ti, fc)])
                    if fc < 4:
                        dst = ap2(io["qh"], fc * 128 * EXT + tau0 + HALO,
                                  [[EXT, 128], [1, n]])
                    else:
                        dst = ap2(io["kh"], (fc - 4) * 128 * KH_W + tau0 + 1536,
                                  [[KH_W, 128], [1, n]])
                    assert s_qsp[kq % 2].inc(
                        s.dma_start(dst, qhb2[kq % 2][:, 0:n]), 16) \
                        == mk_qs[(ti, fc)]
                for j in range(nj):
                    s.wait_ge(s_vev.h, mk_vev[(ti, 3, j)])
                    bidx = (tau0 + 128 * j + 1536) // 128
                    mkv = s_vwp[ti % 2].inc(s.dma_start(
                        ap2(io["vstat"], bidx * 128 * 768, [[768, 128], [1, 768]]),
                        vsb[j][:, :]), 16)
                    if j == nj - 1:
                        assert mkv == mk_vw[ti]
                if p.dev and l == 0 and ti == 2:
                    p.s_ev.inc(s.dma_start(
                        ap2(io["dbg"], 0, [[2048, 128], [1, 2048]]),
                        p.abf[:, 84000:86048]), 16)
            for sp in (s_qsp[0], s_qsp[1], s_vwp[0], s_vwp[1]):
                s.wait_ge(sp.h, sp.v)

        if l == 1:
            @blk.gpsimd
            def _(g):
                g = Drained(g)
                for ti, (tau0, n) in enumerate(tlist):
                    if ti >= 2:
                        g.wait_ge(s_hs.h, mk_hs[ti - 2])
                    for hf in range(2):
                        mk = s_gp[ti % 2].inc(g.dma_start(
                            ap2(p.abf, dsl[ti % 2].col + hf * 4 * 513,
                                [[ABF_W, 128], [513, 4], [1, n + 1]]),
                            ap2(io["dacc_b"],
                                tau0 - 1 + DH0 + hf * 4 * 128 * DACC_W,
                                [[DACC_W, 128], [128 * DACC_W, 4], [1, n + 1]])),
                            16)
                    assert mk == mk_g0[ti]


def phase_p1b(p):
    nc, io = p.nc, p.io
    BO = p.BOFF
    ab = Alloc(p.abf, ABF_W)
    af = Alloc(p.af, AF_W)
    wsi = ab.tile(2048)
    wout = ab.tile(16 * 1024)
    C_bf = ab.tile(HT)
    pp_bf = ab.tile(HT)
    ybf = ab.tile(HT)
    y2b = [ab.tile(512) for _ in range(2)]
    dxb = ab.tile(512)
    ysig = [ab.tile(512) for _ in range(16)]
    xm2 = [ab.tile(512) for _ in range(2)]
    sg2 = [Tile(p.af16[:, 0:512], 0), Tile(p.af16[:, 512:1024], 512)]
    st = af.tile(HT)
    muT = af.tile(512)
    msT = af.tile(512)
    tmpT = af.tile(512)
    ST = af.tile(512)
    e32 = af.tile(512)
    dmt = [af.tile(512) for _ in range(2)]
    (s_mm, s_sc, s_ev, s_fr, s_mm2, s_dev) = (
        p.s_mm, p.s_sc, p.s_ev, p.s_a0, p.s_a1, p.s_a2)
    s_ldp = [p.s_d0, p.s_d1]
    s_stp = [p.s_d2, p.s_d3]

    with nc.Block("p1b_w") as blk:
        @blk.gpsimd
        def _(g):
            g = Drained(g)
            for half in range(2):
                p.s_wl.inc(g.dma_start(
                    p.abf[64 * half:64 * half + 64, wsi.col:wsi.col + 2048],
                    io["L0_attn_W_si"][0:64, :]), 16)
            for c in range(16):
                p.s_wl.inc(g.dma_start(
                    ap2(p.abf, wout.col + c * 1024, [[ABF_W, 128], [1, 1024]]),
                    io["L0_attn_W_out"][c * 128:(c + 1) * 128, :]), 16)
            p.s_g0.inc(g.dma_start(st[:, :], io["st_scr"][:, :]), 16)
            p.s_g0.inc(g.dma_start(pp_bf[:, :], io["pp_scr"][:, :]), 16)
            p.s_g0.inc(g.dma_start(C_bf[:, :], io["c_scr"][:, :]), 16)
            g.wait_ge(p.s_cc.h, 1)
            for half in range(2):
                p.s_g0.inc(g.dma_start(
                    p.small[64 * half:64 * half + 64, 60:61],
                    ap2(io["cc_out"], 0, [[1, 64], [1, 1]]),
                    allow_slow_non_contiguous=True), 16)
            g.wait_ge(p.s_wl.h, p.s_wl.v)
            g.wait_ge(p.s_g0.h, p.s_g0.v)

    mm0, sc0, ev0, fr0, mm20, dev0 = (s_mm.v, s_sc.v, s_ev.v, s_fr.v,
                                      s_mm2.v, s_dev.v)
    stats_mm = {g: mm0 + 2 * (g + 1) for g in range(8)}
    sq_mk = {g: sc0 + 2 * g + 1 for g in range(8)}
    rstd_mk = {g: sc0 + 2 * g + 2 for g in range(8)}
    var_mk = {g: ev0 + 1 + 2 * g + 1 for g in range(8)}
    apply_mk = {g: ev0 + 1 + 2 * g + 2 for g in range(8)}
    ysi_mm = {}
    fr_mk = {}
    v = mm0 + 16
    for tt in range(8):
        for fc in range(16):
            ysi_mm[(tt, fc)] = mm0 + 16 + tt * 16 + fc + 1
            fr_mk[(tt, fc)] = fr0 + tt * 16 + fc + 1
    wout_mm = {(tt, fo): mm20 + tt * 8 + fo + 1 for tt in range(8)
               for fo in range(8)}
    dev_mk = {(tt, fo): dev0 + tt * 8 + fo + 1 for tt in range(8)
              for fo in range(8)}
    st_mk = {(tt, fo): s_stp[(tt * 8 + fo) % 2].v
             + 16 * ((tt * 8 + fo) // 2 + 1)
             for tt in range(8) for fo in range(8)}
    ld_mk = {(tt, fc): s_ldp[(tt * 16 + fc) % 2].v
             + 32 * ((tt * 16 + fc) // 2 + 1)
             for tt in range(8) for fc in range(16)}

    with nc.Block("p1b") as blk:
        @blk.vector
        def _(v_):
            v_ = Drained(v_)
            v_.tensor_tensor(p.small[:, 60:61], p.small[:, 60:61],
                             p.flags[:, 1:2], ALU.mult)
            v_.scalar_tensor_tensor(st[:, :], pp_bf[:, :], p.small[:, 60:61],
                                    st[:, :], ALU.mult, ALU.add)
            v_.tensor_tensor(st[:, :], C_bf[:, :], st[:, :], ALU.mult)
            s_ev.inc(v_.tensor_copy(ybf[:, :], st[:, :]))
            for g in range(8):
                half, ct = g // 4, g % 4
                b = 64 * half
                cols = slice(ct * 512 + (0 if half == 0 else 0), 0)
                c0 = ct * 512
                v_.wait_ge(s_mm.h, stats_mm[g])
                if g >= 1:
                    v_.wait_ge(s_sc.h, rstd_mk[g - 1])
                v_.tensor_scalar(muT[b:b + 64, :], p.pf[4][0:64, :],
                                 1.0 / DSTATE, None, ALU.mult)
                v_.tensor_scalar(msT[b:b + 64, :], p.pf[5][0:64, :],
                                 1.0 / DSTATE, None, ALU.mult)
                v_.tensor_tensor(tmpT[b:b + 64, :], muT[b:b + 64, :],
                                 muT[b:b + 64, :], ALU.mult)
                assert s_ev.inc(v_.tensor_tensor(
                    msT[b:b + 64, :], msT[b:b + 64, :], tmpT[b:b + 64, :],
                    ALU.subtract)) == var_mk[g]
                v_.wait_ge(s_sc.h, rstd_mk[g])
                v_.tensor_tensor(tmpT[b:b + 64, :], muT[b:b + 64, :],
                                 ST[b:b + 64, :], ALU.mult)
                v_.tensor_tensor(st[b:b + 64, c0:c0 + 512],
                                 st[b:b + 64, c0:c0 + 512],
                                 ST[b:b + 64, :], ALU.mult)
                assert s_ev.inc(v_.tensor_tensor(
                    ybf[b:b + 64, c0:c0 + 512], st[b:b + 64, c0:c0 + 512],
                    tmpT[b:b + 64, :], ALU.subtract)) == apply_mk[g]
            for tt in range(8):
                for fc in range(16):
                    v_.wait_ge(s_mm.h, ysi_mm[(tt, fc)])
                    v_.wait_ge(s_ldp[(tt * 16 + fc) % 2].h, ld_mk[(tt, fc)])
                    if tt >= 1:
                        v_.wait_ge(s_mm2.h, wout_mm[(tt - 1, 7)])
                    v_.tensor_scalar(dxb[:, :], xm2[fc % 2][:, :],
                                     p.bias[:, BO["D"] + fc:BO["D"] + fc + 1],
                                     None, ALU.mult)
                    v_.scalar_tensor_tensor(
                        e32[:, :], p.pf[fc % 2][:, :],
                        p.bias[:, BO["b_si"] + fc:BO["b_si"] + fc + 1],
                        dxb[:, :], ALU.add, ALU.add)
                    assert s_fr.inc(v_.tensor_tensor(
                        ysig[fc][:, :], e32[:, :], sg2[fc % 2][:, :],
                        ALU.mult)) == fr_mk[(tt, fc)]
                for fo in range(8):
                    v_.wait_ge(s_mm2.h, wout_mm[(tt, fo)])
                    if (tt * 8 + fo) >= 2:
                        pk = (tt * 8 + fo - 2)
                        v_.wait_ge(s_stp[pk % 2].h, st_mk[(pk // 8, pk % 8)])
                    assert s_dev.inc(v_.tensor_scalar(
                        dmt[fo % 2][:, :], p.pf[2 + fo % 2][:, :],
                        p.bias[:, BO["b_out"] + fo:BO["b_out"] + fo + 1],
                        None, ALU.add)) == dev_mk[(tt, fo)]

        @blk.scalar
        def _(s):
            s = Drained(s)
            for g in range(8):
                half, ct = g // 4, g % 4
                b = 64 * half
                s.wait_ge(s_ev.h, ev0 + 1)
                if g >= 2:
                    s.wait_ge(s_mm.h, stats_mm[g - 2])
                assert s_sc.inc(s.activation(
                    y2b[g % 2][b:b + 64, :],
                    st[b:b + 64, ct * 512:(ct + 1) * 512], AF.Square,
                    bias=0.0, scale=1.0)) == sq_mk[g]
                s.wait_ge(s_ev.h, var_mk[g])
                s.activation(tmpT[b:b + 64, :], msT[b:b + 64, :], AF.Ln,
                             bias=p.flags[b:b + 64, 3:4], scale=1.0)
                assert s_sc.inc(s.activation(
                    ST[b:b + 64, :], tmpT[b:b + 64, :], AF.Exp,
                    bias=0.0, scale=-0.5)) == rstd_mk[g]

        @blk.tensor
        def _(t_):
            for g in range(8):
                half, ct = g // 4, g % 4
                b = 64 * half
                t_.wait_ge(s_sc.h, sq_mk[g])
                if g >= 1:
                    t_.wait_ge(s_ev.h, var_mk[g - 1])
                t_.matmul(p.pf[4][0:64, :], p.ones_bf[b:b + 64, 0:64],
                          ybf[b:b + 64, ct * 512:(ct + 1) * 512],
                          start=True, stop=True)
                assert s_mm.inc(t_.matmul(
                    p.pf[5][0:64, :], p.ones_bf[b:b + 64, 0:64],
                    y2b[g % 2][b:b + 64, :], start=True, stop=True), 2) \
                    == stats_mm[g]
            for tt in range(8):
                half, ct = tt // 4, tt % 4
                b = 64 * half
                t_.wait_ge(s_ev.h, apply_mk[tt // 4 * 4 + tt % 4])
                for fc in range(16):
                    if (tt * 16 + fc) >= 2:
                        pk = tt * 16 + fc - 2
                        t_.wait_ge(s_fr.h, fr_mk[(pk // 16, pk % 16)])
                    assert s_mm.inc(t_.matmul(
                        p.pf[fc % 2][:, :],
                        p.abf[b:b + 64, wsi.col + fc * 128:wsi.col + fc * 128 + 128],
                        ybf[b:b + 64, ct * 512:(ct + 1) * 512],
                        start=True, stop=True)) == ysi_mm[(tt, fc)]
                t_.wait_ge(s_fr.h, fr_mk[(tt, 15)])
                for fo in range(8):
                    if (tt * 8 + fo) >= 2:
                        pk = tt * 8 + fo - 2
                        t_.wait_ge(s_dev.h, dev_mk[(pk // 8, pk % 8)])
                    for fc in range(16):
                        mi = t_.matmul(
                            p.pf[2 + fo % 2][:, :],
                            ap2(p.abf, wout.col + fc * 1024 + fo * 128,
                                [[ABF_W, 128], [1, 128]]),
                            ysig[fc][:, :], start=(fc == 0), stop=(fc == 15))
                    assert s_mm2.inc(mi) == wout_mm[(tt, fo)]

        @blk.sync
        def _(s):
            for tt in range(8):
                for fc in range(16):
                    if (tt * 16 + fc) >= 2:
                        pk = tt * 16 + fc - 2
                        s.wait_ge(s_fr.h, fr_mk[(pk // 16, pk % 16)])
                    sl = s_ldp[(tt * 16 + fc) % 2]
                    sl.inc(s.dma_start(
                        xm2[fc % 2][:, :],
                        io["xmain"][fc * 128:(fc + 1) * 128,
                                    tt * 512:(tt + 1) * 512]), 16)
                    sl.inc(s.dma_start(
                        sg2[fc % 2].ap,
                        io["sgate"][fc * 128:(fc + 1) * 128,
                                    tt * 512:(tt + 1) * 512]), 16)
                    assert sl.v == ld_mk[(tt, fc)]
                for fo in range(8):
                    s.wait_ge(s_dev.h, dev_mk[(tt, fo)])
                    assert s_stp[(tt * 8 + fo) % 2].inc(s.dma_start(
                        ap2(io["dm"], fo * 128 * T + tt * 512,
                            [[T, 128], [1, 512]]),
                        dmt[fo % 2][:, :]), 16) == st_mk[(tt, fo)]
            for sp in (s_stp[0], s_stp[1]):
                s.wait_ge(sp.h, sp.v)


def phase_attn(p, l):
    nc, io = p.nc, p.io
    ab = Alloc(p.abf, ABF_W)
    af = Alloc(p.af, AF_W)
    wo_sb = ab.tile(4 * 1024)
    qsl = [ab.tile(4 * 512) for _ in range(2)]
    ksl = [ab.tile(4 * 512) for _ in range(2)]
    vt = [ab.tile(768) for _ in range(8)]
    attE = [ab.tile(512) for _ in range(2)]
    attO = [ab.tile(512) for _ in range(2)]
    o_all = ab.tile(4 * 512)
    recT = af.tile(512)
    dwt = [af.tile(512) for _ in range(2)]
    dbt = [af.tile(512) for _ in range(2)]
    (s_mm, s_msk, s_exp, s_oev, s_dev) = (
        p.s_mm, p.s_ev, p.s_sc, p.s_a0, p.s_a1)
    s_ldp = [p.s_d0, p.s_d1]
    s_stp = [p.s_d2, p.s_d3]
    s_dldp = [p.s_g1, p.s_g3]

    wlist = list(range(-2, 8)) if l == 0 else list(range(-1, 8))
    NW = len(wlist)
    dacc_rd = "dacc_b"
    M_OWN = p.masks[:, 0:128]
    M_LOOK = p.masks[:, 128:256]
    M0D = p.masks[:, 256:384]
    M0R = p.masks[:, 384:384 + W]

    with nc.Block(f"attn{l}_w") as blk:
        @blk.gpsimd
        def _(g):
            g = Drained(g)
            for c in range(4):
                p.s_wl.inc(g.dma_start(
                    ap2(p.abf, wo_sb.col + c * 1024, [[ABF_W, 128], [1, 1024]]),
                    ap2(io["L0_ffn_wo"], l * INNER_A * DMODEL + c * 128 * DMODEL,
                        [[DMODEL, 128], [1, 1024]])), 16)
            g.wait_ge(p.s_wl.h, p.s_wl.v)

    mm0, msk0, exp0, oev0, dev0 = (s_mm.v, s_msk.v, s_exp.v,
                                   s_oev.v, s_dev.v)
    sides = ["own", "look"]
    sim_mk, msk_mk, exp_mk, pv_mk = {}, {}, {}, {}
    oev_mk, wo_mk, dev_mk, stw_mk, ld_mk, dld_mk = {}, {}, {}, {}, {}, {}
    mmv = mm0
    for wi in range(NW):
        for hp in range(4):
            for si in range(2):
                for kc in range(4):
                    mmv += 1
                    sim_mk[(wi, hp, si, kc)] = mmv
                    mmv += 1
                    pv_mk[(wi, hp, si, kc)] = mmv
        for fo in range(8):
            mmv += 1
            wo_mk[(wi, fo)] = mmv
    k = 0
    for wi in range(NW):
        for hp in range(4):
            for si in range(2):
                for kc in range(4):
                    k += 1
                    msk_mk[(wi, hp, si, kc)] = msk0 + k
                    exp_mk[(wi, hp, si, kc)] = exp0 + k
    for wi in range(NW):
        for hp in range(4):
            oev_mk[(wi, hp)] = oev0 + wi * 4 + hp + 1
        for fo in range(8):
            dev_mk[(wi, fo)] = dev0 + wi * 8 + fo + 1
    for wi in range(NW):
        for fo in range(8):
            k2 = wi * 8 + fo
            stw_mk[(wi, fo)] = s_stp[k2 % 2].v + 16 * (k2 // 2 + 1)
            dld_mk[(wi, fo)] = s_dldp[k2 % 2].v + 16 * (k2 // 2 + 1)
    lvp = [s_ldp[0].v, s_ldp[1].v]
    for wi in range(NW):
        lvp[wi % 2] += 32
        if wi == 0:
            lvp[wi % 2] += 16
        lvp[wi % 2] += 16 * (8 if wi == 0 else 4)
        ld_mk[wi] = lvp[wi % 2]

    def vslot(wi, side, kc):
        w = wlist[wi]
        bidx = 4 * w + (12 if side == 0 else 8) + kc
        return bidx % 8, bidx

    with nc.Block(f"attn{l}") as blk:
        @blk.sync
        def _(s):
            for wi in range(NW):
                w = wlist[wi]
                tau0 = w * 512
                if wi >= 2:
                    s.wait_ge(s_mm.h, wo_mk[(wi - 2, 7)])
                s_ldp[wi % 2].inc(s.dma_start(
                    ap2(p.abf, qsl[wi % 2].col, [[ABF_W, 128], [512, 4], [1, 512]]),
                    ap2(io["qh"], tau0 + HALO,
                        [[EXT, 128], [128 * EXT, 4], [1, 512]])), 16)
                if wi >= 2:
                    s.wait_ge(s_mm.h, wo_mk[(wi - 1, 7)])
                s_ldp[wi % 2].inc(s.dma_start(
                    ap2(p.abf, ksl[wi % 2].col, [[ABF_W, 128], [512, 4], [1, 512]]),
                    ap2(io["kh"], tau0 + 1536,
                        [[KH_W, 128], [128 * KH_W, 4], [1, 512]])), 16)
                if wi == 0:
                    s_ldp[wi % 2].inc(s.dma_start(
                        ap2(p.abf, ksl[1].col, [[ABF_W, 128], [512, 4], [1, 512]]),
                        ap2(io["kh"], tau0 + 1024,
                            [[KH_W, 128], [128 * KH_W, 4], [1, 512]])), 16)
                rng = range(8) if wi == 0 else range(4)
                for j in rng:
                    bidx = 4 * w + 8 + j if wi == 0 else 4 * w + 12 + j
                    slot = bidx % 8
                    if wi >= 2:
                        s.wait_ge(s_mm.h, wo_mk[(wi - 1, 7)])
                    s_ldp[wi % 2].inc(s.dma_start(
                        vt[slot][:, :],
                        ap2(io["vstat"], bidx * 128 * 768,
                            [[768, 128], [1, 768]])), 16)
                assert s_ldp[wi % 2].v == ld_mk[wi]
                for fo in range(8):
                    k2 = wi * 8 + fo
                    s.wait_ge(s_dev.h, dev_mk[(wi, fo)])
                    w_lo = max(0, -tau0 - DH0)
                    assert s_stp[k2 % 2].inc(s.dma_start(
                        ap2(io["dacc_a"], fo * 128 * DACC_W + tau0 + DH0 + w_lo,
                            [[DACC_W, 128], [1, 512 - w_lo]]),
                        dwt[fo % 2][:, w_lo:512]), 16) == stw_mk[(wi, fo)]
                if p.dev and l == 2 and w == 3:
                    s.wait_ge(s_dev.h, dev_mk[(wi, 7)])
                    p.s_a7.inc(s.dma_start(
                        ap2(io["dbg"], 0, [[2048, 128], [1, 2048]]),
                        p.abf[:, 84000:86048]), 16)
                    s.wait_ge(p.s_a7.h, p.s_a7.v)
            for sp in (s_stp[0], s_stp[1]):
                s.wait_ge(sp.h, sp.v)

        if l == 1:
            @blk.gpsimd
            def _(g):
                g = Drained(g)
                for wi in range(NW):
                    tau0 = wlist[wi] * 512
                    for fo in range(8):
                        if (wi * 8 + fo) >= 2:
                            pk = wi * 8 + fo - 2
                            g.wait_ge(s_dev.h, dev_mk[(pk // 8, pk % 8)])
                        w_lo = max(0, -tau0 - DH0)
                        assert s_dldp[(wi * 8 + fo) % 2].inc(g.dma_start(
                            dbt[fo % 2][:, w_lo:512],
                            ap2(io["dacc_b"],
                                fo * 128 * DACC_W + tau0 + DH0 + w_lo,
                                [[DACC_W, 128], [1, 512 - w_lo]])),
                            16) == dld_mk[(wi, fo)]

        @blk.tensor
        def _(t_):
            for wi in range(NW):
                w = wlist[wi]
                t_.wait_ge(s_ldp[wi % 2].h, ld_mk[wi])
                for hp in range(4):
                    if hp >= 1:
                        t_.wait_ge(s_oev.h, oev_mk[(wi, hp - 1)])
                    elif wi >= 1:
                        t_.wait_ge(s_oev.h, oev_mk[(wi - 1, 3)])
                    for si in range(2):
                        kslab = ksl[wi % 2] if si == 0 else \
                            (ksl[(wi - 1) % 2] if wi >= 1 else ksl[1])
                        for kc in range(4):
                            if si == 0:
                                u0, u1 = kc * 128, 512
                            else:
                                u0, u1 = 0, kc * 128 + 128
                            pkey = (wi, hp, si, kc - 1) if kc >= 1 else \
                                ((wi, hp, si - 1, 3) if si == 1 else
                                 ((wi, hp - 1, 1, 3) if hp >= 1 else
                                  ((wi - 1, 3, 1, 3) if wi >= 1 else None)))
                            if pkey is not None and pkey in exp_mk:
                                t_.wait_ge(s_exp.h, exp_mk[pkey])
                            for half, pspsum in ((0, p.pf[0]), (1, p.pf[1])):
                                b = 64 * half
                                mi = t_.matmul(
                                    pspsum[:, u0:u1],
                                    p.abf[b:b + 64,
                                          kslab.col + hp * 512 + kc * 128:
                                          kslab.col + hp * 512 + kc * 128 + 128],
                                    p.abf[b:b + 64,
                                          qsl[wi % 2].col + hp * 512 + u0:
                                          qsl[wi % 2].col + hp * 512 + u1],
                                    start=True, stop=True)
                            assert s_mm.inc(mi) == sim_mk[(wi, hp, si, kc)]
                            t_.wait_ge(s_exp.h, exp_mk[(wi, hp, si, kc)])
                            slot, _b = vslot(wi, si, kc)
                            first = (si == 0 and kc == 0)
                            last = (si == 1 and kc == 3)
                            t_.matmul(p.pf[2][:, u0:u1],
                                      vt[slot][:, 192 * hp:192 * hp + 128],
                                      attE[kc % 2][:, u0:u1],
                                      start=first, stop=last)
                            mi = t_.matmul(p.pf[3][:, u0:u1],
                                           vt[slot][:, 192 * hp + 64:
                                                    192 * hp + 192],
                                           attO[kc % 2][:, u0:u1],
                                           start=first, stop=last)
                            assert s_mm.inc(mi) == pv_mk[(wi, hp, si, kc)]
                t_.wait_ge(s_oev.h, oev_mk[(wi, 3)])
                for fo in range(8):
                    if (wi * 8 + fo) >= 2:
                        pk = wi * 8 + fo - 2
                        t_.wait_ge(s_dev.h, dev_mk[(pk // 8, pk % 8)])
                    for c in range(4):
                        mi = t_.matmul(
                            p.pf[4 + fo % 2][:, :],
                            ap2(p.abf, wo_sb.col + c * 1024 + fo * 128,
                                [[ABF_W, 128], [1, 128]]),
                            o_all[:, c * 512:(c + 1) * 512],
                            start=(c == 0), stop=(c == 3))
                    assert s_mm.inc(mi) == wo_mk[(wi, fo)]

        @blk.vector
        def _(v_):
            v_ = Drained(v_)
            for wi in range(NW):
                w = wlist[wi]
                for hp in range(4):
                    for si in range(2):
                        for kc in range(4):
                            if si == 0:
                                u0, u1 = kc * 128, 512
                                dmask = M_OWN
                            else:
                                u0, u1 = 0, kc * 128 + 128
                                dmask = M0D if w == 0 else M_LOOK
                            d0 = kc * 128
                            v_.wait_ge(s_mm.h, sim_mk[(wi, hp, si, kc)])
                            v_.tensor_tensor(p.pf[0][:, d0:d0 + 128],
                                             p.pf[0][:, d0:d0 + 128],
                                             dmask, ALU.add)
                            ins = v_.tensor_tensor(p.pf[1][:, d0:d0 + 128],
                                                   p.pf[1][:, d0:d0 + 128],
                                                   dmask, ALU.add)
                            if si == 1 and w == 0 and kc >= 1:
                                v_.tensor_tensor(p.pf[0][:, 0:d0],
                                                 p.pf[0][:, 0:d0],
                                                 M0R[:, 0:d0], ALU.add)
                                ins = v_.tensor_tensor(p.pf[1][:, 0:d0],
                                                       p.pf[1][:, 0:d0],
                                                       M0R[:, 0:d0], ALU.add)
                            assert s_msk.inc(ins) == msk_mk[(wi, hp, si, kc)]
                    v_.wait_ge(s_mm.h, pv_mk[(wi, hp, 1, 3)])
                    if wi >= 1:
                        v_.wait_ge(s_mm.h, wo_mk[(wi - 1, 7)])
                    v_.reciprocal(recT[0:64, :], p.pf[2][64:128, :])
                    v_.tensor_tensor(o_all[0:64, hp * 512:hp * 512 + 512],
                                     p.pf[2][0:64, :], recT[0:64, :], ALU.mult)
                    v_.reciprocal(recT[64:128, :], p.pf[3][0:64, :])
                    assert s_oev.inc(v_.tensor_tensor(
                        o_all[64:128, hp * 512:hp * 512 + 512],
                        p.pf[3][64:128, :], recT[64:128, :],
                        ALU.mult)) == oev_mk[(wi, hp)]
                if p.dev and l == 2 and w == 3:
                    v_.tensor_copy(p.abf[:, 84000:86048], o_all[:, :])
                tau0 = w * 512
                w_lo = max(0, -tau0 - DH0)
                for fo in range(8):
                    v_.wait_ge(s_mm.h, wo_mk[(wi, fo)])
                    if (wi * 8 + fo) >= 2:
                        pk = wi * 8 + fo - 2
                        v_.wait_ge(s_stp[pk % 2].h, stw_mk[(pk // 8, pk % 8)])
                    if l == 0:
                        ins = v_.tensor_copy(dwt[fo % 2][:, w_lo:512],
                                             p.pf[4 + fo % 2][:, w_lo:512])
                    else:
                        v_.wait_ge(s_dldp[(wi * 8 + fo) % 2].h,
                                   dld_mk[(wi, fo)])
                        ins = v_.tensor_tensor(dwt[fo % 2][:, w_lo:512],
                                               p.pf[4 + fo % 2][:, w_lo:512],
                                               dbt[fo % 2][:, w_lo:512], ALU.add)
                    assert s_dev.inc(ins) == dev_mk[(wi, fo)]

        @blk.scalar
        def _(s):
            s = Drained(s)
            negb = p.small[:, 6 + l:7 + l]
            for wi in range(NW):
                for hp in range(4):
                    for si in range(2):
                        for kc in range(4):
                            if si == 0:
                                u0, u1 = kc * 128, 512
                            else:
                                u0, u1 = 0, kc * 128 + 128
                            s.wait_ge(s_msk.h, msk_mk[(wi, hp, si, kc)])
                            kk = hp * 8 + si * 4 + kc
                            if kk >= 2:
                                pk2 = kk - 2
                                key2 = (wi, pk2 // 8, (pk2 % 8) // 4, pk2 % 4)
                                s.wait_ge(s_mm.h, pv_mk[key2])
                            elif wi >= 1:
                                key2 = (wi - 1, 3, 1, 2 + kk)
                                s.wait_ge(s_mm.h, pv_mk[key2])
                            s.activation(attE[kc % 2][:, u0:u1],
                                         p.pf[0][:, u0:u1], AF.Exp,
                                         bias=negb, scale=8.0)
                            ins = s.activation(attO[kc % 2][:, u0:u1],
                                               p.pf[1][:, u0:u1], AF.Exp,
                                               bias=negb, scale=8.0)
                            assert s_exp.inc(ins) == exp_mk[(wi, hp, si, kc)]


def phase_mlp(p, l):
    nc, io = p.nc, p.io
    BO = p.BOFF
    ab = Alloc(p.abf, ABF_W)
    af = Alloc(p.af, AF_W)
    w1sb = ab.tile(8 * 5460)
    w2sb = ab.tile(22 * 1024)
    hslab = ab.tile(8 * 513)
    xln = ab.tile(8 * 512)
    h2b = [ab.tile(512) for _ in range(2)]
    tqb = h2b[0]
    h1act = [ab.tile(512) for _ in range(22)]
    dslab = af.tile(8 * 513)
    muT = af.tile(512)
    msT = af.tile(512)
    tmpT = af.tile(512)
    ST = af.tile(512)
    gg = muT
    dnt = [msT, tmpT]
    (s_ld, s_mm, s_sc, s_ev, s_fr, s_mm2, s_dev) = (
        p.s_dd, p.s_mm, p.s_sc, p.s_ev, p.s_a0, p.s_a1, p.s_a2)
    s_stp = [p.s_d2, p.s_d3]

    if l == 0:
        tlist = [(-640, 128)] + [(-512 + 512 * t, 512) for t in range(9)]
    else:
        tlist = [(512 * t, 512) for t in range(8)]
    NTI = len(tlist)
    JW = [128] * 21 + [HREM]

    with nc.Block(f"mlp{l}_w") as blk:
        @blk.gpsimd
        def _(g):
            g = Drained(g)
            for c in range(8):
                p.s_wl.inc(g.dma_start(
                    ap2(p.abf, w1sb.col + c * 5460, [[ABF_W, 128], [1, 5460]]),
                    ap2(io["L0_ffn_w1"], l * DMODEL * 2 * HID + c * 128 * 2 * HID,
                        [[2 * HID, 128], [1, 2 * HID]])), 16)
            for j in range(22):
                wj = JW[j]
                p.s_wl.inc(g.dma_start(
                    ap2(p.abf, w2sb.col + j * 1024, [[ABF_W, wj], [1, 1024]]),
                    ap2(io["L0_ffn_w2"], l * HID * DMODEL + j * 128 * DMODEL,
                        [[DMODEL, wj], [1, 1024]])), 16)
            g.wait_ge(p.s_wl.h, p.s_wl.v)

    mm0, sc0, ev0, fr0, mm20, dev0, ld0 = (s_mm.v, s_sc.v, s_ev.v, s_fr.v,
                                           s_mm2.v, s_dev.v, s_ld.v)
    hs_mk = {ti: ev0 + 9 * ti + 8 for ti in range(NTI)}
    var_mk = {ti: ev0 + 9 * ti + 9 for ti in range(NTI)}
    sq_mk = {(ti, c): sc0 + 10 * ti + c + 1 for ti in range(NTI) for c in range(8)}
    rstd_mk = {ti: sc0 + 10 * ti + 9 for ti in range(NTI)}
    stat_mm = {(ti, c): mm0 + (22 * 8 * 2 + 16) * ti + 2 * (c + 1)
               for ti in range(NTI) for c in range(8)}
    apl_mk = {}
    gel_mk = {}
    act_mk = {}
    w1_mm = {}
    w2_mm = {}
    dev_mk = {}
    stw_mk = {}
    ld_mk = {ti: ld0 + 32 * (ti + 1) for ti in range(NTI)}
    for ti in range(NTI):
        base = mm0 + (22 * 8 * 2 + 16) * ti + 16
        for j in range(22):
            w1_mm[(ti, j)] = base + 16 * (j + 1)
    mv = mm20
    for ti in range(NTI):
        for fo in range(8):
            mv += 1
            w2_mm[(ti, fo)] = mv
    for ti in range(NTI):
        for fo in range(8):
            k2 = ti * 8 + fo
            dev_mk[(ti, fo)] = dev0 + k2 + 1
            stw_mk[(ti, fo)] = s_stp[k2 % 2].v + 16 * (k2 // 2 + 1)
        for j in range(22):
            gel_mk[(ti, j)] = sc0 + 10 * NTI + ti * 22 + j + 1
            act_mk[(ti, j)] = fr0 + ti * 22 + j + 1
    sqv = sc0
    sq_mk, rstd_mk, gel_mk = {}, {}, {}
    for ti in range(NTI):
        for c in range(8):
            sqv += 1
            sq_mk[(ti, c)] = sqv
        sqv += 1
        rstd_mk[ti] = sqv
        for j in range(22):
            sqv += 1
            gel_mk[(ti, j)] = sqv
    evv, frv, devv = ev0, fr0, dev0
    hs_mk, var_mk, apl_mk = {}, {}, {}
    for ti in range(NTI):
        evv += 8
        hs_mk[ti] = evv
        evv += 1
        var_mk[ti] = evv
        evv += 1
        apl_mk[ti] = evv

    with nc.Block(f"mlp{l}") as blk:
        @blk.sync
        def _(s):
            for ti, (tau0, n) in enumerate(tlist):
                if ti >= 1:
                    s.wait_ge(s_mm.h, w1_mm[(ti - 1, 21)])
                if ti >= 1:
                    s.wait_ge(s_dev.h, dev_mk[(ti - 1, 7)])
                s_ld.inc(s.dma_start(
                    ap2(p.abf, hslab.col, [[ABF_W, 128], [513, 8], [1, n + 1]]),
                    ap2(io["xnfm"], tau0 + HALO,
                        [[1 + EXT, 128], [128 * (1 + EXT), 8], [1, n + 1]])), 16)
                s_ld.inc(s.dma_start(
                    ap2(p.af, dslab.col, [[AF_W, 128], [513, 8], [1, n + 1]]),
                    ap2(io["dacc_a"], tau0 - 1 + DH0,
                        [[DACC_W, 128], [128 * DACC_W, 8], [1, n + 1]])), 16)
                assert s_ld.v == ld_mk[ti]
                for fo in range(8):
                    s.wait_ge(s_dev.h, dev_mk[(ti, fo)])
                    assert s_stp[(ti * 8 + fo) % 2].inc(s.dma_start(
                        ap2(io["dacc_b"], fo * 128 * DACC_W + tau0 + DH0,
                            [[DACC_W, 128], [1, n]]),
                        dnt[fo % 2][:, 0:n]), 16) == stw_mk[(ti, fo)]
                if p.dev and l == 0 and ti == 2:
                    p.s_a7.inc(s.dma_start(
                        ap2(io["dbg"], 0, [[2048, 128], [1, 512]]),
                        xln[:, 0:512]), 16)
                    p.s_a7.inc(s.dma_start(
                        ap2(io["dbg"], 512, [[2048, 128], [1, 512]]),
                        xln[:, 4 * 512:5 * 512]), 16)
                    p.s_a7.inc(s.dma_start(
                        ap2(io["dbg"], 1024, [[2048, 128], [1, 512]]),
                        hslab[:, 4 * 513:4 * 513 + 512]), 16)
                    s.wait_ge(p.s_a7.h, p.s_a7.v)
                    s.wait_ge(p.s_g1.h, p.s_g1.v)
            for sp in (s_stp[0], s_stp[1]):
                s.wait_ge(sp.h, sp.v)

        if p.dev and l == 0:
            @blk.gpsimd
            def _(g):
                g.wait_ge(s_dev.h, dev_mk[(2, 7)])
                p.s_g1.inc(g.dma_start(
                    ap2(io["dbg"], 1536, [[2048, 128], [1, 512]]),
                    ST[:, 0:512]), 16)

        @blk.vector
        def _(v_):
            v_ = Drained(v_)
            for ti, (tau0, n) in enumerate(tlist):
                v_.wait_ge(s_ld.h, ld_mk[ti])
                for c in range(8):
                    ins = v_.tensor_tensor(
                        hslab[:, c * 513:c * 513 + n + 1],
                        p.af[:, dslab.col + c * 513:dslab.col + c * 513 + n + 1],
                        p.abf[:, hslab.col + c * 513:hslab.col + c * 513 + n + 1],
                        ALU.add)
                    s_ev.inc(ins)
                assert s_ev.v == hs_mk[ti]
                v_.wait_ge(s_mm.h, stat_mm[(ti, 7)])
                if ti >= 1:
                    v_.wait_ge(s_stp[0].h, stw_mk[(ti - 1, 6)])
                    v_.wait_ge(s_stp[1].h, stw_mk[(ti - 1, 7)])
                v_.tensor_scalar(muT[:, 0:n], p.pf[0][:, 0:n], 1.0 / DMODEL,
                                 None, ALU.mult)
                v_.tensor_scalar(msT[:, 0:n], p.pf[1][:, 0:n], 1.0 / DMODEL,
                                 None, ALU.mult)
                v_.tensor_tensor(tmpT[:, 0:n], muT[:, 0:n], muT[:, 0:n],
                                 ALU.mult)
                assert s_ev.inc(v_.tensor_tensor(
                    msT[:, 0:n], msT[:, 0:n], tmpT[:, 0:n],
                    ALU.subtract)) == var_mk[ti]
                v_.wait_ge(s_sc.h, rstd_mk[ti])
                ins = None
                for c in range(8):
                    off = 1 if c < 4 else 0
                    hsv = hslab[:, c * 513 + off:c * 513 + off + n]
                    v_.tensor_tensor(tqb[:, 0:n], hsv, muT[:, 0:n],
                                     ALU.subtract)
                    v_.tensor_tensor(tqb[:, 0:n], tqb[:, 0:n], ST[:, 0:n],
                                     ALU.mult)
                    ins = v_.tensor_scalar(
                        xln[:, c * 512:c * 512 + n], tqb[:, 0:n],
                        p.bias[:, BO[f"ln_g{l}"] + c:BO[f"ln_g{l}"] + c + 1],
                        p.bias[:, BO[f"ln_b{l}"] + c:BO[f"ln_b{l}"] + c + 1],
                        ALU.mult, ALU.add)
                assert s_ev.inc(ins) == apl_mk[ti]

                for j in range(22):
                    wj = JW[j]
                    v_.wait_ge(s_sc.h, gel_mk[(ti, j)])
                    if ti >= 1 or j >= 1:
                        pass
                    if ti >= 1:
                        v_.wait_ge(s_mm2.h, w2_mm[(ti - 1, 7)])
                    ins = v_.scalar_tensor_tensor(
                        h1act[j][0:wj, 0:n],
                        p.pf[2 + 2 * (j % 2)][0:wj, 0:n],
                        p.bias[0:wj, BO[f"b1a{l}"] + j:BO[f"b1a{l}"] + j + 1],
                        gg[0:wj, 0:n], ALU.add, ALU.mult)
                    assert s_fr.inc(ins) == act_mk[(ti, j)]
                for fo in range(8):
                    v_.wait_ge(s_mm2.h, w2_mm[(ti, fo)])
                    if (ti * 8 + fo) >= 2:
                        pk = ti * 8 + fo - 2
                        v_.wait_ge(s_stp[pk % 2].h, stw_mk[(pk // 8, pk % 8)])
                    ins = v_.scalar_tensor_tensor(
                        dnt[fo % 2][:, 0:n], p.pf[fo % 2][:, 0:n],
                        p.bias[:, BO[f"b2{l}"] + fo:BO[f"b2{l}"] + fo + 1],
                        p.af[:, dslab.col + fo * 513 + 1:
                              dslab.col + fo * 513 + 1 + n], ALU.add, ALU.add)
                    if l == 0 and tau0 < 0:
                        nh = min(n, -tau0)
                        ins = v_.tensor_scalar(dnt[fo % 2][:, 0:nh],
                                               dnt[fo % 2][:, 0:nh],
                                               p.flags[:, 2:3], None, ALU.mult)
                    assert s_dev.inc(ins) == dev_mk[(ti, fo)]

        @blk.scalar
        def _(s):
            s = Drained(s)
            for ti, (tau0, n) in enumerate(tlist):
                for c in range(8):
                    off = 1 if c < 4 else 0
                    s.wait_ge(s_ev.h, hs_mk[ti])
                    if c == 0 and ti >= 1:
                        s.wait_ge(s_ev.h, apl_mk[ti - 1])
                    if c >= 2:
                        s.wait_ge(s_mm.h, stat_mm[(ti, c - 2)])
                    elif ti >= 1:
                        s.wait_ge(s_mm.h, stat_mm[(ti - 1, 6 + c)])
                    hsv = hslab[:, c * 513 + off:c * 513 + off + n]
                    assert s_sc.inc(s.activation(
                        h2b[c % 2][:, 0:n], hsv, AF.Square,
                        bias=0.0, scale=1.0)) == sq_mk[(ti, c)]
                s.wait_ge(s_ev.h, var_mk[ti])
                s.activation(tmpT[:, 0:n], msT[:, 0:n], AF.Ln, bias=p.flags[:, 3:4],
                             scale=1.0)
                assert s_sc.inc(s.activation(ST[:, 0:n], tmpT[:, 0:n], AF.Exp,
                                             bias=0.0, scale=-0.5)) == rstd_mk[ti]
                for j in range(22):
                    wj = JW[j]
                    s.wait_ge(s_mm.h, w1_mm[(ti, j)])
                    if j >= 1:
                        s.wait_ge(s_fr.h, act_mk[(ti, j - 1)])
                    elif ti >= 1:
                        s.wait_ge(s_fr.h, act_mk[(ti - 1, 21)])
                    ins = s.activation(
                        gg[0:wj, 0:n], p.pf[3 + 2 * (j % 2)][0:wj, 0:n],
                        AF.Gelu,
                        bias=p.bias[0:wj, BO[f"b1g{l}"] + j:BO[f"b1g{l}"] + j + 1],
                        scale=1.0)
                    assert s_sc.inc(ins) == gel_mk[(ti, j)]

        @blk.tensor
        def _(t_):
            for ti, (tau0, n) in enumerate(tlist):
                for c in range(8):
                    off = 1 if c < 4 else 0
                    hsv = p.abf[:, hslab.col + c * 513 + off:
                                hslab.col + c * 513 + off + n]
                    if c == 0:
                        t_.wait_ge(s_ev.h, hs_mk[ti])
                        if ti >= 1:
                            t_.wait_ge(s_dev.h, dev_mk[(ti - 1, 7)])
                    t_.matmul(p.pf[0][:, 0:n], p.ones_bf[:, :], hsv,
                              start=(c == 0), stop=(c == 7))
                    t_.wait_ge(s_sc.h, sq_mk[(ti, c)])
                    mi = t_.matmul(p.pf[1][:, 0:n], p.ones_bf[:, :],
                                   h2b[c % 2][:, 0:n],
                                   start=(c == 0), stop=(c == 7))
                    assert s_mm.inc(mi, 2) == stat_mm[(ti, c)]
                t_.wait_ge(s_ev.h, apl_mk[ti])
                for j in range(22):
                    wj = JW[j]
                    if j >= 2:
                        t_.wait_ge(s_fr.h, act_mk[(ti, j - 2)])
                    elif ti >= 1:
                        t_.wait_ge(s_fr.h, act_mk[(ti - 1, 20 + j)])
                    if j >= 1:
                        t_.wait_ge(s_sc.h, gel_mk[(ti, j - 1)])
                    elif ti >= 1:
                        t_.wait_ge(s_sc.h, gel_mk[(ti - 1, 21)])
                    for c in range(8):
                        t_.matmul(p.pf[2 + 2 * (j % 2)][0:wj, 0:n],
                                  ap2(p.abf, w1sb.col + c * 5460 + j * 128,
                                      [[ABF_W, 128], [1, wj]]),
                                  p.abf[:, xln.col + c * 512:
                                        xln.col + c * 512 + n],
                                  start=(c == 0), stop=(c == 7))
                    for c in range(8):
                        mi = t_.matmul(p.pf[3 + 2 * (j % 2)][0:wj, 0:n],
                                       ap2(p.abf,
                                           w1sb.col + c * 5460 + HID + j * 128,
                                           [[ABF_W, 128], [1, wj]]),
                                       p.abf[:, xln.col + c * 512:
                                             xln.col + c * 512 + n],
                                       start=(c == 0), stop=(c == 7))
                    assert s_mm.inc(mi, 16) == w1_mm[(ti, j)]
                for fo in range(8):
                    if (ti * 8 + fo) >= 2:
                        pk = ti * 8 + fo - 2
                        t_.wait_ge(s_dev.h, dev_mk[(pk // 8, pk % 8)])
                    for j in range(22):
                        wj = JW[j]
                        if fo == 0:
                            t_.wait_ge(s_fr.h, act_mk[(ti, j)])
                        mi = t_.matmul(
                            p.pf[fo % 2][:, 0:n],
                            ap2(p.abf, w2sb.col + j * 1024 + fo * 128,
                                [[ABF_W, wj], [1, 128]]),
                            h1act[j][0:wj, 0:n],
                            start=(j == 0), stop=(j == 21))
                    assert s_mm2.inc(mi) == w2_mm[(ti, fo)]


def phase_fin(p):
    nc, io = p.nc, p.io
    ab = Alloc(p.abf, ABF_W)
    af = Alloc(p.af, AF_W)
    ut = [af.tile(1024) for _ in range(2)]
    ot = [af.tile(1024) for _ in range(2)]
    dmsl = af.tile(1024)
    dasl = af.tile(1024)
    s8 = dmsl
    (s_ld, s_mm, s_ev) = (p.s_dd, p.s_mm, p.s_ev)
    s_op = [p.s_d4, p.s_d5]
    ld0, mm0, ev0 = s_ld.v, s_mm.v, s_ev.v
    NG = 32
    ow_mk = {i: s_op[i % 2].v + 16 * (i // 2 + 1) for i in range(NG)}

    with nc.Block("fin") as blk:
        @blk.sync
        def _(s):
            for i in range(NG):
                if i >= 1:
                    s.wait_ge(s_ev.h, ev0 + i * 9 - 8)
                    s.wait_ge(s_mm.h, mm0 + 8 * i)
                s_ld.inc(s.dma_start(
                    ap2(p.af, dmsl.col, [[AF_W, 128], [128, 8], [1, 128]]),
                    ap2(io["dm"], i * 128,
                        [[T, 128], [128 * T, 8], [1, 128]])), 16)
                s_ld.inc(s.dma_start(
                    ap2(p.af, dasl.col, [[AF_W, 128], [128, 8], [1, 128]]),
                    ap2(io["dacc_b"], DH0 + i * 128,
                        [[DACC_W, 128], [128 * DACC_W, 8], [1, 128]])), 16)
                if i >= 2:
                    s.wait_ge(s_ev.h, ev0 + (i - 1) * 9 + 9)
                s_ld.inc(s.dma_start(ut[i % 2][:, :],
                                     io["u_scr"][i * 128:(i + 1) * 128, :]), 16)
                s.wait_ge(s_ev.h, ev0 + i * 9 + 9)
                assert s_op[i % 2].inc(
                    s.dma_start(io["out"][i * 128:(i + 1) * 128, :],
                                ot[i % 2][:, :]), 16) == ow_mk[i]
            for sp in (s_op[0], s_op[1]):
                s.wait_ge(sp.h, sp.v)

        @blk.vector
        def _(v_):
            v_ = Drained(v_)
            for i in range(NG):
                v_.wait_ge(s_ld.h, ld0 + 48 * (i + 1))
                if i >= 1:
                    v_.wait_ge(s_mm.h, mm0 + 8 * i)
                s_ev.inc(v_.tensor_add(s8[:, :], dmsl.ap, dasl.ap))
                for c in range(8):
                    v_.wait_ge(s_mm.h, mm0 + i * 8 + c + 1)
                    if c == 0 and i >= 2:
                        v_.wait_ge(s_op[i % 2].h, ow_mk[i - 2])
                    ins = v_.tensor_tensor(ot[i % 2][:, c * 128:(c + 1) * 128],
                                           ut[i % 2][:, c * 128:(c + 1) * 128],
                                           p.pf[4 + c % 2][:, 0:128], ALU.add)
                    s_ev.inc(ins)

        @blk.tensor
        def _(t_):
            for i in range(NG):
                t_.wait_ge(s_ev.h, ev0 + i * 9 + 1)
                for c in range(8):
                    k = i * 8 + c
                    if k >= 2:
                        t_.wait_ge(s_ev.h, ev0 + (k - 2) // 8 * 9 + (k - 2) % 8 + 2)
                    mi = t_.transpose(p.pf[4 + c % 2][:, 0:128],
                                      p.af[:, s8.col + c * 128:
                                           s8.col + c * 128 + 128],
                                      p.eye_f32[:, :])
                    s_mm.inc(mi)


_CACHE = {}


def kernel(**inputs):
    if "nc" not in _CACHE:
        _CACHE["nc"], _CACHE["io"] = build_program()
    nc = _CACHE["nc"]
    in_maps = [make_core_inputs(c, inputs) for c in range(8)]
    res = run_bass_kernel_spmd(nc, in_maps, core_ids=list(range(8)))
    out = np.zeros((B, N, DMODEL), np.float32)
    for c in range(8):
        b, h = c // 2, c % 2
        out[b, h * T:(h + 1) * T] = np.asarray(res.results[c]["out"])
    return out


# version 09
# speedup vs baseline: 1.0200x; 1.0200x over previous
from contextlib import ExitStack

import numpy as np
import ml_dtypes
import concourse.bass as bass
import concourse.mybir as mybir
from concourse.bass_utils import run_bass_kernel_spmd

F32 = mybir.dt.float32
BF16 = mybir.dt.bfloat16
FP16 = mybir.dt.float16
AF = mybir.ActivationFunctionType
ALU = mybir.AluOpType
AX = mybir.AxisListType

B, N, DMODEL = 4, 8192, 1024
T = 4096
HALO = 1152
EXT = HALO + T
W = 512
INNER_M = 2048
DSTATE = 64
HEADS, DH = 8, 64
INNER_A = 512
HID = 2730
NEG = -1.0e30
EPS = 1e-5
DH0 = 644
DACC_W = DH0 + T
HT = T // 2
HREM = HID - 21 * 128
ABF_W = 87040
AF_W = 6400
CONSTS_LIMIT = None
KH_W = 384 + EXT


def make_core_inputs(core, inputs):
    b, h = core // 2, core % 2
    s0 = h * T
    x = np.asarray(inputs["L0_attn_x"])
    lo = s0 - HALO
    xs = np.zeros((EXT, DMODEL), np.float32)
    src_lo = max(lo, 0)
    xs[src_lo - lo:] = np.asarray(x[b, src_lo:s0 + T], np.float32)

    pos = np.clip(np.arange(lo, s0 + T, dtype=np.float64), 0, None)
    half = np.arange(0, DH, 2, dtype=np.float64)
    inv_freq = 1.0 / 10000.0 ** (half / DH)
    fr = pos[None, :] * inv_freq[:, None]
    freqs = np.concatenate([fr, fr], 0)
    base = (half + 0.4 * DH) / (1.4 * DH)
    power = (pos - N // 2) / float(W // 2)
    scale = np.concatenate([base[:, None] ** power[None, :]] * 2, 0)
    cos, sin = np.cos(freqs), np.sin(freqs)
    sgn = np.concatenate([-np.ones(32), np.ones(32)])[:, None]

    r = np.arange(128)
    m_own = np.where(r[:, None] > r[None, :], NEG, 0.0)
    m_look = np.where(r[None, :] > r[:, None], NEG, 0.0)
    if h == 0:
        m0_d = np.full((128, 128), NEG, np.float32)
        m0_rect = np.full((128, W), NEG, np.float32)
    else:
        m0_d = m_look.astype(np.float32)
        m0_rect = np.zeros((128, W), np.float32)

    blkdiag = np.zeros((128, 128))
    blkdiag[:64, :64] = 1.0
    blkdiag[64:, 64:] = 1.0

    def f32(a):
        return np.ascontiguousarray(a, np.float32)

    d = {
        "xs": xs,
        "cs1q": f32(cos * scale), "cs2q": f32(sin * scale * sgn),
        "cs1k": f32(cos / scale), "cs2k": f32(sin / scale * sgn),
        "m_own": m_own.astype(ml_dtypes.bfloat16),
        "m_look": m_look.astype(ml_dtypes.bfloat16),
        "m0_d": m0_d.astype(ml_dtypes.bfloat16),
        "m0_rect": m0_rect.astype(ml_dtypes.bfloat16),
        "eye_bf": np.eye(128, dtype=ml_dtypes.bfloat16),
        "eye_f32": np.eye(128, dtype=np.float32),
        "ones_bf": np.ones((128, 128), ml_dtypes.bfloat16),
        "blkdiag_bf": blkdiag.astype(ml_dtypes.bfloat16),
        "evenm": np.full((128, 1), 1.0 if h == 0 else 0.0, np.float32),
        "oddm": np.full((128, 1), 1.0 if h == 1 else 0.0, np.float32),
        "haloflag": np.full((128, 1), 0.0 if h == 0 else 1.0, np.float32),
    }
    for k, v in inputs.items():
        if k != "L0_attn_x":
            d[k] = np.ascontiguousarray(np.asarray(v))
    cw = np.asarray(inputs["L0_attn_conv_w"], np.float32)
    for k in range(3):
        d[f"conv_w{k}"] = np.ascontiguousarray(cw[:, k])
    return d


def ap2(handle, offset, dims):
    return bass.AP(handle, offset, [list(x) for x in dims])


class Sem:
    def __init__(self, ctx, nc, name):
        self.h = ctx.enter_context(nc.semaphore(name))
        self.v = 0

    def inc(self, instr, n=1):
        instr.then_inc(self.h, n)
        self.v += n
        return self.v


class Tile:
    def __init__(self, ap, col):
        self.ap = ap
        self.col = col

    def __getitem__(self, idx):
        return self.ap[idx]


class Alloc:
    def __init__(self, arena, width):
        self.a, self.w, self.top = arena, width, 0

    def tile(self, n, parts=128, base=0):
        c0 = self.top
        self.top += n
        assert self.top <= self.w, f"arena overflow {self.top} > {self.w}"
        return Tile(self.a[base:base + parts, c0:c0 + n], c0)


class Drained:

    _PASS = ("wait_ge", "dma_start", "drain", "engine_nop", "nop",
             "collective_compute", "indirect_dma_start", "then_inc")

    def __init__(self, eng):
        self._e = eng

    def __getattr__(self, nm):
        f = getattr(self._e, nm)
        if nm in self._PASS or not callable(f):
            return f
        e = self._e

        def w(*a, **k):
            f(*a, **k)
            return e.drain()
        return w


class P:
    pass


def build_program(dev=False, upto=None):
    nc = bass.Bass("TRN2", target_bir_lowering=False)
    io = {}

    def din(name, shape, dtype=F32):
        io[name] = nc.dram_tensor(name, list(shape), dtype, kind="ExternalInput")

    def dscr(name, shape, dtype):
        kind = "ExternalOutput" if dev else "Internal"
        io[name] = nc.dram_tensor(name, list(shape), dtype, kind=kind)

    din("xs", [EXT, DMODEL])
    for nm in ["cs1q", "cs2q", "cs1k", "cs2k"]:
        din(nm, [DH, EXT])
    for nm in ["m_own", "m_look", "m0_d"]:
        din(nm, [128, 128], BF16)
    din("m0_rect", [128, W], BF16)
    din("eye_bf", [128, 128], BF16)
    din("eye_f32", [128, 128], F32)
    din("ones_bf", [128, 128], BF16)
    din("blkdiag_bf", [128, 128], BF16)
    din("evenm", [128, 1])
    din("oddm", [128, 1])
    din("haloflag", [128, 1])
    din("L0_attn_W_in", [DMODEL, 2 * INNER_M])
    din("L0_attn_b_in", [2 * INNER_M])
    din("L0_attn_conv_w", [INNER_M, 3])
    for k in range(3):
        din(f"conv_w{k}", [INNER_M])
    din("L0_attn_conv_b", [INNER_M])
    din("L0_attn_W_dt", [INNER_M, DSTATE])
    din("L0_attn_b_dt", [DSTATE])
    din("L0_attn_W_B", [INNER_M, DSTATE])
    din("L0_attn_b_B", [DSTATE])
    din("L0_attn_W_C", [INNER_M, DSTATE])
    din("L0_attn_b_C", [DSTATE])
    din("L0_attn_W_si", [DSTATE, INNER_M])
    din("L0_attn_b_si", [INNER_M])
    din("L0_attn_D", [INNER_M])
    din("L0_attn_W_out", [INNER_M, DMODEL])
    din("L0_attn_b_out", [DMODEL])
    din("L0_ffn_wqkv", [2, DMODEL, 3 * INNER_A])
    din("L0_ffn_q_scale", [2, DH])
    din("L0_ffn_k_scale", [2, DH])
    din("L0_ffn_wo", [2, INNER_A, DMODEL])
    din("L0_ffn_ln_g", [2, DMODEL])
    din("L0_ffn_ln_b", [2, DMODEL])
    din("L0_ffn_w1", [2, DMODEL, 2 * HID])
    din("L0_ffn_b1", [2, 2 * HID])
    din("L0_ffn_w2", [2, HID, DMODEL])
    din("L0_ffn_b2", [2, DMODEL])
    io["out"] = nc.dram_tensor("out", [T, DMODEL], F32, kind="ExternalOutput")
    dscr("u_scr", [T, DMODEL], F32)
    dscr("xnfm", [DMODEL, 1 + EXT], BF16)
    dscr("xmain", [INNER_M, T], BF16)
    dscr("sgate", [INNER_M, T], FP16)
    dscr("dm", [DMODEL, T], F32)
    dscr("st_scr", [128, HT], F32)
    dscr("pp_scr", [128, HT], BF16)
    dscr("c_scr", [128, HT], BF16)
    dscr("qh", [INNER_A, EXT], BF16)
    dscr("kh", [INNER_A, KH_W], BF16)
    dscr("vstat", [44 * 128, 768], BF16)
    dscr("dacc_a", [DMODEL, DACC_W], F32)
    dscr("dacc_b", [DMODEL, DACC_W], F32)
    dscr("dbg", [128, 2048], BF16)
    io["cc_in"] = nc.dram_tensor("cc_in", [DSTATE, 1], F32)
    io["cc_out"] = nc.dram_tensor("cc_out", [DSTATE, 1], F32)

    with ExitStack() as ctx:
        p = P()
        p.nc, p.ctx, p.io, p.dev = nc, ctx, io, dev
        p.abf = ctx.enter_context(nc.sbuf_tensor("abf", [128, ABF_W], BF16))
        p.af = ctx.enter_context(nc.sbuf_tensor("af", [128, AF_W], F32))
        p.af16 = ctx.enter_context(nc.sbuf_tensor("af16", [128, 1024], FP16))
        p.af16b = ctx.enter_context(nc.sbuf_tensor("af16b", [128, 1024], FP16))
        p.eye_bf = ctx.enter_context(nc.sbuf_tensor("sb_eye_bf", [128, 128], BF16))
        p.eye_f32 = ctx.enter_context(nc.sbuf_tensor("sb_eye_f32", [128, 128], F32))
        p.ones_bf = ctx.enter_context(nc.sbuf_tensor("sb_ones_bf", [128, 128], BF16))
        p.blkdiag = ctx.enter_context(nc.sbuf_tensor("sb_blkdiag", [128, 128], BF16))
        p.masks = ctx.enter_context(nc.sbuf_tensor("masks", [128, 384 + W], BF16))
        p.bias = ctx.enter_context(nc.sbuf_tensor("bias", [128, 288], F32))
        p.flags = ctx.enter_context(nc.sbuf_tensor("flags", [128, 8], F32))
        p.small = ctx.enter_context(nc.sbuf_tensor("small", [128, 80], F32))
        p.pf = [ctx.enter_context(nc.psum_tensor(f"pf{i}", [128, 512], F32))
                for i in range(6)]
        p.pb = [ctx.enter_context(nc.psum_tensor(f"pb{i}", [128, 1024], BF16))
                for i in range(2)]
        for nm in ["ld", "wl", "mm", "ev", "sc", "dd", "st", "cc", "g0", "g1",
                   "g2", "g3", "a0", "a1", "a2", "a3", "a4", "a5", "a6", "a7",
                   "d0", "d1", "d2", "d3", "d4", "d5", "d6", "d7", "d8", "d9"]:
            setattr(p, "s_" + nm, Sem(ctx, nc, nm))
        p.BOFF = {}
        steps = [lambda: load_consts(p),
                 lambda: phase_p0(p),
                 lambda: phase_p1a(p),
                 lambda: phase_qkv(p, 0),
                 lambda: phase_p1b(p),
                 lambda: phase_attn(p, 0),
                 lambda: phase_mlp(p, 0),
                 lambda: phase_qkv(p, 1),
                 lambda: phase_attn(p, 1),
                 lambda: phase_mlp(p, 1),
                 lambda: phase_fin(p)]
        if upto is None:
            upto = len(steps)
        for f in steps[:upto]:
            f()
    return nc, io


def load_consts(p):
    nc, io = p.nc, p.io
    s_ld, s_mm, s_ev = p.s_ld, p.s_mm, p.s_ev
    vtmp = p.af[:, 0:128]
    vps = p.pf[0][:, 0:128]
    cs = [0]

    with nc.Block("consts") as blk:
        @blk.gpsimd
        def _(g):
            g = Drained(g)
            for dst, src in [(p.eye_bf, "eye_bf"), (p.eye_f32, "eye_f32"),
                             (p.ones_bf, "ones_bf"), (p.blkdiag, "blkdiag_bf")]:
                s_ld.inc(g.dma_start(dst[:, :], io[src][:, :]), 16)
            s_ld.inc(g.dma_start(p.masks[:, 0:128], io["m_own"][:, :]), 16)
            s_ld.inc(g.dma_start(p.masks[:, 128:256], io["m_look"][:, :]), 16)
            s_ld.inc(g.dma_start(p.masks[:, 256:384], io["m0_d"][:, :]), 16)
            s_ld.inc(g.dma_start(p.masks[:, 384:384 + W], io["m0_rect"][:, :]), 16)
            s_ld.inc(g.dma_start(p.flags[:, 0:1], io["evenm"][:, :]), 16)
            s_ld.inc(g.dma_start(p.flags[:, 1:2], io["oddm"][:, :]), 16)
            s_ld.inc(g.dma_start(p.flags[:, 2:3], io["haloflag"][:, :]), 16)
            for l in range(2):
                s_ld.inc(g.dma_start(
                    p.af[0:1, 512 + 128 * l:576 + 128 * l],
                    ap2(io["L0_ffn_q_scale"], l * DH, [[64, 1], [1, 64]])), 16)
                s_ld.inc(g.dma_start(
                    p.af[0:1, 576 + 128 * l:640 + 128 * l],
                    ap2(io["L0_ffn_k_scale"], l * DH, [[64, 1], [1, 64]])), 16)
            g.memset(p.abf[:, 0:768], 0.0)
            g.memset(p.af[:, 1408:1440], 0.0)
            g.memset(p.flags[:, 3:4], EPS)
            g.memset(p.af[0:1, 1280:1408], 1.0)
            p.s_a5.inc(g.memset(p.flags[:, 4:5], 0.0))
            g.wait_ge(s_ld.h, s_ld.v)

        @blk.sync
        def _(sy):
            if CONSTS_LIMIT == -4:
                return
            sy.wait_ge(p.s_a5.h, p.s_a5.v)
            p.s_st.inc(sy.dma_start(
                ap2(io["dacc_b"], 0,
                    [[DACC_W, 128], [128 * DACC_W, 8], [1, 4]]),
                ap2(p.af, 1408, [[AF_W, 128], [4, 8], [1, 4]])), 16)
            p.s_st.inc(sy.dma_start(
                ap2(io["xnfm"], 0, [[1 + EXT, 128], [128 * (1 + EXT), 8], [1, 1]]),
                ap2(p.abf, 0, [[ABF_W, 128], [1, 8], [1, 1]]),
                allow_slow_non_contiguous=True), 16)
            for c in range(4):
                p.s_st.inc(sy.dma_start(
                    ap2(io["kh"], c * 128 * KH_W, [[KH_W, 128], [1, 384]]),
                    ap2(p.abf, 0, [[ABF_W, 128], [1, 384]])), 16)
            for blki in range(3):
                p.s_st.inc(sy.dma_start(
                    ap2(io["vstat"], blki * 128 * 768, [[768, 128], [1, 768]]),
                    ap2(p.abf, 0, [[ABF_W, 128], [1, 768]])), 16)
            sy.wait_ge(p.s_st.h, p.s_st.v)

    def vec(name, dram, length, offset=0):
        nchunk = (length + 127) // 128
        col = cs[0]
        cs[0] += nchunk
        p.BOFF[name] = col
        with nc.Block("v_" + name) as blk:
            @blk.gpsimd
            def _(g):
                g = Drained(g)
                p.s_a5.inc(g.memset(vtmp[:, :], 0.0))
                g.wait_ge(p.s_a5.h, p.s_a5.v)
                full, rem = divmod(length, 128)
                if full:
                    s_ld.inc(g.dma_start(
                        vtmp[0:full, 0:128],
                        ap2(dram, offset, [[128, full], [1, 128]])), 16)
                if rem:
                    s_ld.inc(g.dma_start(
                        vtmp[full:full + 1, 0:rem],
                        ap2(dram, offset + full * 128, [[rem, 1], [1, rem]])), 16)

            @blk.tensor
            def _(t):
                t.wait_ge(s_ld.h, s_ld.v)
                s_mm.inc(t.transpose(vps, vtmp[:, :], p.eye_f32[:, :]))

            @blk.vector
            def _(v):
                v = Drained(v)
                v.wait_ge(s_mm.h, s_mm.v)
                s_ev.inc(v.tensor_copy(p.bias[:, col:col + nchunk],
                                       vps[:, 0:nchunk]))

    veclist = [(f"conv_w{k}", io[f"conv_w{k}"], INNER_M, 0) for k in range(3)]
    veclist += [("b_in", io["L0_attn_b_in"], 2 * INNER_M, 0),
               ("conv_b", io["L0_attn_conv_b"], INNER_M, 0),
               ("b_si", io["L0_attn_b_si"], INNER_M, 0),
               ("D", io["L0_attn_D"], INNER_M, 0),
               ("b_out", io["L0_attn_b_out"], DMODEL, 0)]
    for l in range(2):
        veclist += [(f"ln_g{l}", io["L0_ffn_ln_g"], DMODEL, l * DMODEL),
                    (f"ln_b{l}", io["L0_ffn_ln_b"], DMODEL, l * DMODEL),
                    (f"b1a{l}", io["L0_ffn_b1"], HID, l * 2 * HID),
                    (f"b1g{l}", io["L0_ffn_b1"], HID, l * 2 * HID + HID),
                    (f"b2{l}", io["L0_ffn_b2"], DMODEL, l * DMODEL)]
    if CONSTS_LIMIT is not None:
        veclist = veclist[:max(0, CONSTS_LIMIT)]
    for nm_, dram_, ln_, off_ in veclist:
        vec(nm_, dram_, ln_, off_)
    if CONSTS_LIMIT == -4:
        pass
    if CONSTS_LIMIT is not None and CONSTS_LIMIT == -1:
        return
    SKIP_TAIL = CONSTS_LIMIT == -2
    SKIP_BOUND = CONSTS_LIMIT == -3
    if CONSTS_LIMIT in (-4, -5, -6):
        SKIP_TAIL = SKIP_BOUND = False

    if SKIP_TAIL:
        with nc.Block("bound") as blk:
            @blk.gpsimd
            def _(g):
                g = Drained(g)
                for l in range(2):
                    s_ld.inc(g.dma_start(
                        p.af[0:1, 0 + 128 * l:64 + 128 * l],
                        ap2(io["L0_ffn_q_scale"], l * DH, [[64, 1], [1, 64]])), 16)
                    s_ld.inc(g.dma_start(
                        p.af[0:1, 64 + 128 * l:128 + 128 * l],
                        ap2(io["L0_ffn_k_scale"], l * DH, [[64, 1], [1, 64]])), 16)
            @blk.vector
            def _(v):
                v = Drained(v)
                v.wait_ge(s_ld.h, s_ld.v)
                s_ev.inc(v.tensor_copy(p.af[0:1, 192:256], p.af[0:1, 0:64]))
        return
    with nc.Block("v_tail") as blk:
        @blk.gpsimd
        def _(g):
            g = Drained(g)
            for nm, dram, base in [("b_dt", io["L0_attn_b_dt"], 0),
                                   ("b_B", io["L0_attn_b_B"], 64),
                                   ("b_C", io["L0_attn_b_C"], 0)]:
                col = cs[0]
                cs[0] += 1
                p.BOFF[nm] = col
                s_ld.inc(g.dma_start(p.bias[base:base + 64, col:col + 1],
                                     ap2(dram, 0, [[1, 64], [1, 1]]),
                                     allow_slow_non_contiguous=True), 16)
            for l in range(2):
                for j, nm in [(0, "L0_ffn_q_scale"), (1, "L0_ffn_k_scale")]:
                    cc = l * 2 + j
                    for half in range(2):
                        s_ld.inc(g.dma_start(
                            p.small[64 * half:64 * half + 64, cc:cc + 1],
                            ap2(io[nm], l * DH, [[1, 64], [1, 1]]),
                            allow_slow_non_contiguous=True), 16)
            g.wait_ge(s_ld.h, s_ld.v)
    assert cs[0] <= 288, cs[0]

    if SKIP_BOUND:
        return
    with nc.Block("bound") as blk:
        @blk.vector
        def _(v):
            v = Drained(v)
            v.wait_ge(s_ld.h, s_ld.v)
            for l in range(2):
                v.tensor_tensor(p.af[0:1, 1024 + 64 * l:1024 + 64 * l + 64],
                                p.af[0:1, 512 + 128 * l:512 + 128 * l + 64],
                                p.af[0:1, 512 + 128 * l + 64:512 + 128 * l + 128],
                                ALU.mult)
                v.reduce_max(p.small[0:1, 56 + l:57 + l],
                             p.af[0:1, 1024 + 64 * l:1024 + 64 * l + 64],
                             axis=AX.X, apply_absolute_value=True)
                s_ev.inc(v.tensor_scalar(p.small[0:1, 56 + l:57 + l],
                                         p.small[0:1, 56 + l:57 + l],
                                         -8.0, None, ALU.mult))

        @blk.tensor
        def _(t):
            t.wait_ge(s_ev.h, s_ev.v)
            s_mm.inc(t.matmul(p.pf[1][:, 0:2], p.af[0:1, 1280:1408],
                              p.small[0:1, 56:58], start=True, stop=True))

        @blk.scalar
        def _(s):
            s = Drained(s)
            s.wait_ge(s_mm.h, s_mm.v)
            s_ev.inc(s.copy(p.small[:, 6:8], p.pf[1][:, 0:2]))


def phase_p0(p):
    nc, io = p.nc, p.io
    s_a, s_b, s_t, s_e2, s_e3 = (p.s_a0, p.s_a1, p.s_a2, p.s_a3, p.s_a4)
    s_ldp = [p.s_d0, p.s_d1]
    s_fmp = [p.s_d2, p.s_d3]
    s_up = [p.s_d4, p.s_d5]
    ab = Alloc(p.abf, ABF_W)
    af = Alloc(p.af, AF_W)
    xnbf = [ab.tile(1024) for _ in range(2)]
    sqj = ab.tile(1024)
    fmst = [ab.tile(1024) for _ in range(2)]
    xt = [af.tile(1024) for _ in range(2)]
    xn32 = af.tile(1024)
    ut = [af.tile(1024) for _ in range(2)]
    stt = af.tile(16)
    NT = EXT // 128
    OWN0 = HALO // 128
    e20, e30, a0, b0, t0 = s_e2.v, s_e3.v, s_a.v, s_b.v, s_t.v
    ld_mk, fm_mk, u_mk = {}, {}, {}
    for i in range(NT):
        ld_mk[i] = s_ldp[i % 2].v + 16 * (i // 2 + 1)
        fm_mk[i] = s_fmp[i % 2].v + 16 * (i // 2 + 1)
    ucnt = [0, 0]
    for i in range(OWN0, NT):
        ucnt[i % 2] += 1
        u_mk[i] = s_up[i % 2].v + 16 * ucnt[i % 2]

    with nc.Block("p0") as blk:
        @blk.sync
        def _(s):
            for i in range(NT):
                if i >= 2:
                    s.wait_ge(s_e3.h, e30 + i - 1)
                assert s_ldp[i % 2].inc(
                    s.dma_start(xt[i % 2][:, :],
                                io["xs"][i * 128:(i + 1) * 128, :]),
                    16) == ld_mk[i]
                s.wait_ge(s_e2.h, e20 + 8 * (i + 1))
                assert s_fmp[i % 2].inc(s.dma_start(
                    ap2(io["xnfm"], 1 + i * 128,
                        [[1 + EXT, 128], [128 * (1 + EXT), 8], [1, 128]]),
                    ap2(p.abf, fmst[i % 2].col,
                        [[ABF_W, 128], [128, 8], [1, 128]])), 16) == fm_mk[i]
                if i >= OWN0:
                    s.wait_ge(s_e3.h, e30 + i + 1)
                    assert s_up[i % 2].inc(s.dma_start(
                        io["u_scr"][(i - OWN0) * 128:(i - OWN0 + 1) * 128, :],
                        ut[i % 2][:, :]), 16) == u_mk[i]
            for sp in (s_fmp[0], s_fmp[1], s_up[0], s_up[1]):
                s.wait_ge(sp.h, sp.v)

        @blk.vector
        def _(v):
            v = Drained(v)
            for i in range(NT):
                x = xt[i % 2]
                v.wait_ge(s_ldp[i % 2].h, ld_mk[i])
                if i >= 1:
                    v.wait_ge(s_b.h, b0 + i)
                v.scalar_tensor_tensor(sqj[:, :], x[:, :], 1.0, x[:, :],
                                       ALU.mult, ALU.mult, accum_out=stt[:, 1:2])
                v.reduce_sum(stt[:, 0:1], x[:, :], axis=AX.X)
                v.tensor_scalar(stt[:, 2:3], stt[:, 0:1], 1.0 / DMODEL, None, ALU.mult)
                v.tensor_scalar(stt[:, 3:4], stt[:, 1:2], 1.0 / DMODEL, None, ALU.mult)
                v.tensor_tensor(stt[:, 4:5], stt[:, 2:3], stt[:, 2:3], ALU.mult)
                s_a.inc(v.tensor_tensor(stt[:, 3:4], stt[:, 3:4], stt[:, 4:5],
                                        ALU.subtract))
                for j in range(8):
                    v.wait_ge(s_t.h, t0 + i * 8 + j + 1)
                    s_e2.inc(v.tensor_copy(fmst[i % 2][:, j * 128:(j + 1) * 128],
                                           p.pb[j % 2][:, 0:128]))
                if i >= OWN0:
                    v.wait_ge(s_b.h, b0 + i + 1)
                    if i - 2 in u_mk:
                        v.wait_ge(s_up[i % 2].h, u_mk[i - 2])
                    s_e3.inc(v.tensor_add(ut[i % 2][:, :], x[:, :],
                                          xn32[:, :]))
                else:
                    s_e3.inc(v.engine_nop())

        @blk.scalar
        def _(s):
            s = Drained(s)
            for i in range(NT):
                x = xt[i % 2]
                s.wait_ge(s_a.h, a0 + i + 1)
                if i >= 1:
                    s.wait_ge(s_e3.h, e30 + i)
                s.activation(stt[:, 5:6], stt[:, 3:4], AF.Ln, bias=p.flags[:, 3:4], scale=1.0)
                s.activation(stt[:, 6:7], stt[:, 5:6], AF.Exp, bias=0.0, scale=-0.5)
                s.mul(stt[:, 7:8], stt[:, 2:3], stt[:, 6:7])
                s.mul(stt[:, 7:8], stt[:, 7:8], -1.0)
                last = s.activation(xnbf[i % 2][:, :], x[:, :], AF.Identity,
                                    bias=stt[:, 7:8], scale=stt[:, 6:7])
                if i >= OWN0:
                    last = s.activation(xn32[:, :], x[:, :], AF.Identity,
                                        bias=stt[:, 7:8], scale=stt[:, 6:7])
                s_b.inc(last)

        @blk.tensor
        def _(t):
            for i in range(NT):
                t.wait_ge(s_b.h, b0 + i + 1)
                if i >= 2:
                    t.wait_ge(s_fmp[i % 2].h, fm_mk[i - 2])
                for j in range(8):
                    k = i * 8 + j
                    if k >= 2:
                        t.wait_ge(s_e2.h, e20 + k - 1)
                    s_t.inc(t.transpose(p.pb[j % 2][:, 0:128],
                                        xnbf[i % 2][:, j * 128:(j + 1) * 128],
                                        p.eye_bf[:, :]))


def phase_p1a(p):
    nc, io = p.nc, p.io
    BO = p.BOFF
    ab = Alloc(p.abf, ABF_W)
    af = Alloc(p.af, AF_W)
    w_in = ab.tile(8 * 4096)
    wdtB = ab.tile(16 * 128)
    wC = ab.tile(16 * 64)
    xslab = [ab.tile(8 * 512) for _ in range(2)]
    work = [ab.tile(514) for _ in range(16)]
    xmn = [ab.tile(512) for _ in range(16)]
    C_bf = ab.tile(HT)
    pp_bf = ab.tile(HT)
    zz_bf = ab.tile(HT)
    xm2 = [ab.tile(512) for _ in range(2)]
    gt4 = [Tile(p.af16[:, 0:512], 0), Tile(p.af16[:, 512:1024], 512),
           Tile(p.af16b[:, 0:512], 0), Tile(p.af16b[:, 512:1024], 512)]
    cva = af.tile(512)
    cvb = [af.tile(512) for _ in range(2)]
    tmp_ev = af.tile(512)
    dt_s = af.tile(HT)
    B_s = af.tile(HT)
    st_s = Tile(p.af[:, 0:HT], 0)
    NTT = 9
    s_wl, s_mm, s_sc, s_cv, s_xm, s_w = (p.s_wl, p.s_mm, p.s_sc,
                                         p.s_ev, p.s_a0, p.s_st)
    s_d2, s_d3 = p.s_a1, p.s_a2
    s_ldp = [p.s_d0, p.s_d1]
    s_xmp = [p.s_d2b, p.s_d3b] if False else [p.s_d6, p.s_d7]
    s_gtp = [p.s_d8, p.s_d9]
    s_ld2 = [p.s_d4, p.s_d5]

    with nc.Block("p1a_w") as blk:
        @blk.gpsimd
        def _(g):
            g = Drained(g)
            for c in range(8):
                s_wl.inc(g.dma_start(
                    ap2(p.abf, w_in.col + c * 4096, [[ABF_W, 128], [1, 4096]]),
                    io["L0_attn_W_in"][c * 128:(c + 1) * 128, :]), 16)
            for c in range(16):
                s_wl.inc(g.dma_start(
                    ap2(p.abf, wdtB.col + c * 128, [[ABF_W, 128], [1, 64]]),
                    io["L0_attn_W_dt"][c * 128:(c + 1) * 128, :]), 16)
                s_wl.inc(g.dma_start(
                    ap2(p.abf, wdtB.col + c * 128 + 64, [[ABF_W, 128], [1, 64]]),
                    io["L0_attn_W_B"][c * 128:(c + 1) * 128, :]), 16)
                s_wl.inc(g.dma_start(
                    ap2(p.abf, wC.col + c * 64, [[ABF_W, 128], [1, 64]]),
                    io["L0_attn_W_C"][c * 128:(c + 1) * 128, :]), 16)
            g.memset(zz_bf[:, :], 0.0)
            for fc in range(16):
                g.memset(work[fc][:, 0:2], 0.0)
            g.wait_ge(s_wl.h, s_wl.v)

    mm0, sc0, cv0, xm0 = s_mm.v, s_sc.v, s_cv.v, s_xm.v
    ld_mk = {t: s_ldp[t % 2].v + 16 * (t // 2 + 1) for t in range(NTT)}
    xm_mk = {}
    gt_sp = {}
    gb = [s_gtp[0].v, s_gtp[1].v]
    for tt in range(1, NTT):
        xm_mk[tt] = s_xmp[tt % 2].v + 16 * 16 * ((tt - 1) // 2 + 1)
        for fc in range(16):
            gi = (tt - 1) * 16 + fc
            gt_sp[gi] = gb[gi % 2] + 16 * (gi // 2 + 1)

    with nc.Block("p1a1") as blk:
        @blk.sync
        def _(s):
            for t in range(NTT + 1):
                if t >= 2:
                    tt = t - 1
                    for fc in range(16):
                        s.wait_ge(s_xm.h, xm0 + tt * 16 + fc + 1)
                        mk = s_xmp[tt % 2].inc(s.dma_start(
                            io["xmain"][fc * 128:(fc + 1) * 128,
                                        (tt - 1) * 512:tt * 512],
                            xmn[fc][:, :]), 16)
                        if fc == 15:
                            assert mk == xm_mk[tt]
                    for fc in range(16):
                        gi = (tt - 1) * 16 + fc
                        s.wait_ge(s_sc.h, sc0 + tt * 32 + 16 + fc + 1)
                        if gi >= 2:
                            s.wait_ge(s_gtp[gi % 2].h, gt_sp[gi - 2])
                        assert s_gtp[gi % 2].inc(s.dma_start(
                            io["sgate"][fc * 128:(fc + 1) * 128,
                                        (tt - 1) * 512:tt * 512],
                            gt4[fc % 4][:, :]), 16) == gt_sp[gi]
                if t < NTT:
                    if t >= 2:
                        s.wait_ge(s_mm.h, mm0 + (t - 1) * 32)
                    assert s_ldp[t % 2].inc(s.dma_start(
                        ap2(p.abf, xslab[t % 2].col,
                            [[ABF_W, 128], [512, 8], [1, 512]]),
                        ap2(io["xnfm"], 1 + HALO - 512 + t * 512,
                            [[1 + EXT, 128], [128 * (1 + EXT), 8], [1, 512]])),
                        16) == ld_mk[t]

        @blk.tensor
        def _(t_):
            for t in range(NTT):
                t_.wait_ge(s_ldp[t % 2].h, ld_mk[t])
                for fc in range(32):
                    k = t * 32 + fc
                    if k >= 2:
                        t_.wait_ge(s_sc.h, sc0 + k - 1)
                    ps = p.pf[fc % 2]
                    for c in range(8):
                        mi = t_.matmul(ps[:, :],
                                       ap2(p.abf, w_in.col + c * 4096 + fc * 128,
                                           [[ABF_W, 128], [1, 128]]),
                                       ap2(p.abf, xslab[t % 2].col + c * 512,
                                           [[ABF_W, 128], [1, 512]]),
                                       start=(c == 0), stop=(c == 7))
                    s_mm.inc(mi)

        @blk.scalar
        def _(s):
            s = Drained(s)
            for t in range(NTT):
                for fc in range(16):
                    s.wait_ge(s_mm.h, mm0 + t * 32 + fc + 1)
                    if t >= 1:
                        s.wait_ge(s_cv.h, cv0 + ((t - 1) * 16 + fc) * 2 + 2)
                    s_sc.inc(s.activation(
                        work[fc][:, 2:514], p.pf[fc % 2][:, :], AF.Identity,
                        bias=p.bias[:, BO["b_in"] + fc:BO["b_in"] + fc + 1],
                        scale=1.0))
                    if fc >= 1:
                        fp = fc - 1
                        s.wait_ge(s_cv.h, cv0 + (t * 16 + fp) * 2 + 1)
                        if fp == 0 and (t - 1) in xm_mk:
                            s.wait_ge(s_xmp[(t - 1) % 2].h, xm_mk[t - 1])
                        s_xm.inc(s.activation(
                            xmn[fp][:, :], cvb[fp % 2][:, :], AF.Silu,
                            bias=p.bias[:, BO["conv_b"] + fp:BO["conv_b"] + fp + 1],
                            scale=1.0))
                fp = 15
                s.wait_ge(s_cv.h, cv0 + (t * 16 + fp) * 2 + 1)
                s_xm.inc(s.activation(
                    xmn[fp][:, :], cvb[fp % 2][:, :], AF.Silu,
                    bias=p.bias[:, BO["conv_b"] + fp:BO["conv_b"] + fp + 1],
                    scale=1.0))
                for fc in range(16):
                    s.wait_ge(s_mm.h, mm0 + t * 32 + 16 + fc + 1)
                    gi = ((t - 1) * 16 + fc - 4) if fc >= 4 \
                        else ((t - 2) * 16 + fc + 12)
                    if gi in gt_sp:
                        s.wait_ge(s_gtp[gi % 2].h, gt_sp[gi])
                    s_sc.inc(s.activation(
                        gt4[fc % 4][:, :], p.pf[fc % 2][:, :], AF.Sigmoid,
                        bias=p.bias[:, BO["b_in"] + 16 + fc:BO["b_in"] + 17 + fc],
                        scale=1.0))

        @blk.vector
        def _(v):
            v = Drained(v)
            for t in range(NTT):
                for fc in range(16):
                    v.wait_ge(s_sc.h, sc0 + t * 32 + fc + 1)
                    wk = work[fc]
                    c0 = BO["conv_w0"] + fc
                    c1 = BO["conv_w1"] + fc
                    c2 = BO["conv_w2"] + fc
                    v.tensor_scalar(cva[:, :], wk[:, 1:513],
                                    p.bias[:, c1:c1 + 1], None, ALU.mult)
                    v.scalar_tensor_tensor(cvb[fc % 2][:, :], wk[:, 0:512],
                                           p.bias[:, c0:c0 + 1], cva[:, :],
                                           ALU.mult, ALU.add)
                    s_cv.inc(v.scalar_tensor_tensor(
                        cvb[fc % 2][:, :], wk[:, 2:514],
                        p.bias[:, c2:c2 + 1], cvb[fc % 2][:, :],
                        ALU.mult, ALU.add))
                    s_cv.inc(v.tensor_copy(wk[:, 0:2], wk[:, 512:514]))

    mm1, sc1, d20, d30 = s_mm.v, s_sc.v, s_d2.v, s_d3.v
    ld2_mk = {}
    for t in range(8):
        for fc in range(16):
            k = t * 16 + fc
            ld2_mk[k] = s_ld2[k % 2].v + 16 * (k // 2 + 1)
    with nc.Block("p1a2") as blk:
        @blk.sync
        def _(s):
            for sp in (s_xmp[0], s_xmp[1], s_gtp[0], s_gtp[1]):
                s.wait_ge(sp.h, sp.v)
            for t in range(8):
                for fc in range(16):
                    k = t * 16 + fc
                    if k >= 2:
                        s.wait_ge(s_mm.h, mm1 + k - 1)
                    assert s_ld2[k % 2].inc(s.dma_start(
                        xm2[fc % 2][:, :],
                        io["xmain"][fc * 128:(fc + 1) * 128,
                                    t * 512:(t + 1) * 512]), 16) == ld2_mk[k]

        @blk.tensor
        def _(t_):
            for t in range(8):
                for fc in range(16):
                    k = t * 16 + fc
                    t_.wait_ge(s_ld2[k % 2].h, ld2_mk[k])
                    if fc == 0 and t >= 1:
                        t_.wait_ge(s_sc.h, sc1 + t)
                    t_.matmul(p.pf[2][:, :],
                              ap2(p.abf, wdtB.col + fc * 128,
                                  [[ABF_W, 128], [1, 128]]),
                              xm2[fc % 2][:, :],
                              start=(fc == 0), stop=(fc == 15))
                    mi = t_.matmul(p.pf[3][0:64, :],
                                   ap2(p.abf, wC.col + fc * 64,
                                       [[ABF_W, 128], [1, 64]]),
                                   xm2[fc % 2][:, :],
                                   start=(fc == 0), stop=(fc == 15))
                    s_mm.inc(mi)

        @blk.scalar
        def _(s):
            s = Drained(s)
            for t in range(8):
                s.wait_ge(s_mm.h, mm1 + (t + 1) * 16)
                if t >= 1:
                    s.wait_ge(s_d3.h, d30 + t * 3)
                s.activation(tmp_ev[0:64, :], p.pf[2][0:64, :], AF.Sigmoid,
                             bias=p.bias[0:64, BO["b_dt"]:BO["b_dt"] + 1],
                             scale=1.0)
                s.activation(tmp_ev[64:128, :], p.pf[2][64:128, :], AF.Identity,
                             bias=p.bias[64:128, BO["b_B"]:BO["b_B"] + 1],
                             scale=1.0)
                s_sc.inc(s.activation(cva[0:64, :], p.pf[3][0:64, :], AF.Identity,
                                      bias=p.bias[0:64, BO["b_C"]:BO["b_C"] + 1],
                                      scale=1.0))

        @blk.vector
        def _(v):
            v = Drained(v)
            for t in range(8):
                v.wait_ge(s_sc.h, sc1 + t + 1)
                half, c0 = t // 4, (t * 512) % HT
                b = 64 * half
                s_d3.inc(v.tensor_copy(dt_s[b:b + 64, c0:c0 + 512],
                                       tmp_ev[0:64, :]))
                s_d3.inc(v.tensor_copy(B_s[b:b + 64, c0:c0 + 512],
                                       tmp_ev[64:128, :]))
                s_d3.inc(v.tensor_copy(C_bf[b:b + 64, c0:c0 + 512],
                                       cva[0:64, :]))

    with nc.Block("p1a3") as blk:
        @blk.vector
        def _(v):
            v = Drained(v)
            v.tensor_tensor(B_s[:, :], dt_s[:, :], B_s[:, :], ALU.mult)
            v.tensor_scalar(dt_s[:, :], dt_s[:, :], -1.0, 1.0, ALU.mult, ALU.add)
            v.tensor_tensor_scan(st_s[:, :], dt_s[:, :], B_s[:, :], 0.0,
                                 ALU.mult, ALU.add)
            v.tensor_tensor_scan(pp_bf[:, :], dt_s[:, :], zz_bf[:, :], 1.0,
                                 ALU.mult, ALU.add)
            v.tensor_copy(p.small[64:128, 61:62], st_s[0:64, HT - 1:HT])
            v.tensor_copy(p.small[64:128, 62:63], pp_bf[0:64, HT - 1:HT])
            v.scalar_tensor_tensor(st_s[64:128, :], pp_bf[64:128, :],
                                   p.small[64:128, 61:62], st_s[64:128, :],
                                   ALU.mult, ALU.add)
            v.tensor_scalar(pp_bf[64:128, :], pp_bf[64:128, :],
                            p.small[64:128, 62:63], None, ALU.mult)
            s_cv.inc(v.tensor_tensor(p.small[64:128, 63:64],
                                     st_s[64:128, HT - 1:HT],
                                     p.flags[64:128, 0:1], ALU.mult))

        @blk.sync
        def _(s):
            s.wait_ge(s_cv.h, s_cv.v)
            s_w.inc(s.dma_start(io["st_scr"][:, :], st_s[:, :]), 16)
            s_w.inc(s.dma_start(io["pp_scr"][:, :], pp_bf[:, :]), 16)
            s_w.inc(s.dma_start(io["c_scr"][:, :], C_bf[:, :]), 16)
            s_w.inc(s.dma_start(ap2(io["cc_in"], 0, [[1, 64], [1, 1]]),
                                p.small[64:128, 63:64]), 16)
            s.wait_ge(s_w.h, s_w.v)

        @blk.gpsimd
        def _(g):
            g = Drained(g)
            g.wait_ge(s_w.h, s_w.v)
            s_cc_i = p.s_cc
            s_cc_i.inc(g.collective_compute(
                "AllReduce", ALU.add,
                replica_groups=[[0, 1], [2, 3], [4, 5], [6, 7]],
                ins=[io["cc_in"].ap().opt()],
                outs=[io["cc_out"].ap().opt()]))




def phase_qkv(p, l):
    nc, io = p.nc, p.io
    ab = Alloc(p.abf, ABF_W)
    af = Alloc(p.af, AF_W)
    wq = ab.tile(8 * 1536)
    xnsl = [ab.tile(8 * 513) for _ in range(2)]
    dsl = [ab.tile(8 * 513) for _ in range(2)] if l == 1 else None
    sqb = ab.tile(512)
    qhb2 = [ab.tile(512) for _ in range(2)]
    vcf = ab.tile(512)
    vsb = [ab.tile(768) for _ in range(4)]
    tabs = [[af.tile(512) for _ in range(4)] for _ in range(2)]
    nn = af.tile(512)
    rstd = af.tile(512)
    qn = af.tile(512)
    qrot = af.tile(512)
    (s_mm, s_sq, s_m2, s_fr2, s_rs, s_fr, s_sp, s_t, s_vev) = (
        p.s_mm, p.s_sc, p.s_a4, p.s_a5, p.s_a6, p.s_a0, p.s_a1,
        p.s_a2, p.s_a3)
    s_hs = p.s_ev
    s_ldp = [p.s_d0, p.s_d1]
    s_gp = [p.s_g0, p.s_g2]
    s_tabp = [p.s_d6, p.s_d7]
    s_qsp = [p.s_d8, p.s_d9]
    s_vwp = [p.s_d2, p.s_d3]

    if l == 0:
        tlist = [(-HALO + 512 * t, 512) for t in range(10)] + [(3968, 128)]
    else:
        tlist = [(-640, 128)] + [(-512 + 512 * t, 512) for t in range(9)]
    TABN = ["cs1q", "cs2q", "cs1k", "cs2k"]
    NTI = len(tlist)

    with nc.Block(f"qkv{l}_w") as blk:
        @blk.gpsimd
        def _(g):
            g = Drained(g)
            for c in range(8):
                p.s_wl.inc(g.dma_start(
                    ap2(p.abf, wq.col + c * 1536, [[ABF_W, 128], [1, 1536]]),
                    ap2(io["L0_ffn_wqkv"], l * DMODEL * 1536 + c * 128 * 1536,
                        [[1536, 128], [1, 1536]])), 16)
            for j in range(4):
                for hp in range(4):
                    g.memset(vsb[j][:, 192 * hp + 64:192 * hp + 128], 1.0)
            g.wait_ge(p.s_wl.h, p.s_wl.v)

    mm0, sq0, m20, fr20, rs0, fr0, sp0 = (s_mm.v, s_sq.v, s_m2.v,
                                          s_fr2.v, s_rs.v, s_fr.v, s_sp.v)
    t0v, vev0, hs0 = s_t.v, s_vev.v, s_hs.v
    mk_mm, mk_sq, mk_m2, mk_fr2, mk_rs, mk_fr, mk_sp = {}, {}, {}, {}, {}, {}, {}
    mk_t, mk_vev, mk_vw, mk_spw, mk_hs, mk_ld, mk_tab = {}, {}, {}, {}, {}, {}, {}
    tp_order = []
    mk_g0 = {}
    mk_qs = {}
    for ti, (tau0, n) in enumerate(tlist):
        nj = n // 128
        mk_ld[ti] = s_ldp[ti % 2].v + 16 * (ti // 2 + 1)
        mk_g0[ti] = s_gp[ti % 2].v + 32 * (ti // 2 + 1)
        mk_hs[ti] = hs0 + 8 * (ti + 1)
        mk_tab[ti] = s_tabp[ti % 2].v + 128 * (ti // 2 + 1)
        for fc in range(8):
            kq = ti * 8 + fc
            mk_qs[(ti, fc)] = s_qsp[kq % 2].v + 16 * (kq // 2 + 1)
        for fc in range(12):
            mk_mm[(ti, fc)] = mm0 + ti * 12 + fc + 1
            mk_fr[(ti, fc)] = fr0 + ti * 12 + fc + 1
        for fc in range(8):
            mk_sq[(ti, fc)] = sq0 + ti * 8 + fc + 1
            mk_m2[(ti, fc)] = m20 + ti * 8 + fc + 1
            mk_fr2[(ti, fc)] = fr20 + ti * 8 + fc + 1
            mk_rs[(ti, fc)] = rs0 + ti * 8 + fc + 1
            mk_sp[(ti, fc)] = sp0 + ti * 8 + fc + 1
        for hp in range(4):
            for j in range(nj):
                tp_order.append((ti, hp, j))
    for k, key in enumerate(tp_order):
        mk_t[key] = t0v + k + 1
        mk_vev[key] = vev0 + k + 1
    vw_par = [s_vwp[0].v, s_vwp[1].v]
    for ti, (tau0, n) in enumerate(tlist):
        vw_par[ti % 2] += 16 * (n // 128)
        mk_vw[ti] = vw_par[ti % 2]

    with nc.Block(f"qkv{l}") as blk:
        @blk.vector
        def _(v):
            v = Drained(v)
            for ti, (tau0, n) in enumerate(tlist):
                nj = n // 128
                if l == 1:
                    v.wait_ge(s_ldp[ti % 2].h, mk_ld[ti])
                    v.wait_ge(s_gp[ti % 2].h, mk_g0[ti])
                    for c in range(8):
                        ins = v.tensor_tensor(
                            xnsl[ti % 2][:, c * 513:c * 513 + n + 1],
                            xnsl[ti % 2][:, c * 513:c * 513 + n + 1],
                            dsl[ti % 2][:, c * 513:c * 513 + n + 1], ALU.add)
                    assert s_hs.inc(ins, 8) == mk_hs[ti]
                for fc in range(8):
                    cc = l * 2 + (0 if fc < 4 else 1)
                    v.wait_ge(s_m2.h, mk_m2[(ti, fc)])
                    assert s_fr2.inc(
                        v.tensor_scalar(nn[:, 0:n], p.pf[3 + fc % 2][:, 0:n],
                                        1e-24, None, ALU.max)) == mk_fr2[(ti, fc)]
                    v.wait_ge(s_rs.h, mk_rs[(ti, fc)])
                    v.tensor_scalar(rstd[:, 0:n], rstd[:, 0:n],
                                    p.small[:, cc:cc + 1], None, ALU.mult)
                    assert s_fr.inc(
                        v.tensor_tensor(qn[:, 0:n], p.pf[fc % 3][:, 0:n],
                                        rstd[:, 0:n], ALU.mult)) == mk_fr[(ti, fc)]
                    if p.dev and l == 0 and ti == 2 and fc == 0:
                        v.tensor_copy(p.abf[:, 85024:85536], rstd[:, 0:512])
                        v.tensor_copy(p.abf[:, 84512:85024], qn[:, 0:512])
                        v.tensor_copy(p.abf[:, 85536:86048], nn[:, 0:512])
                    for bb in range(4):
                        src = [32, 0, 96, 64][bb]
                        v.tensor_copy(qrot[32 * bb:32 * bb + 32, 0:n],
                                      qn[src:src + 32, 0:n])
                    tb1 = tabs[ti % 2][0 if fc < 4 else 2].ap
                    tb2 = tabs[ti % 2][1 if fc < 4 else 3].ap
                    v.wait_ge(s_tabp[ti % 2].h, mk_tab[ti])
                    v.tensor_tensor(qn[:, 0:n], qn[:, 0:n], tb1[:, 0:n], ALU.mult)
                    v.tensor_tensor(qrot[:, 0:n], qrot[:, 0:n], tb2[:, 0:n],
                                    ALU.mult)
                    kq = ti * 8 + fc
                    prev = (ti, fc - 2) if fc >= 2 else (ti - 1, fc + 6)
                    if prev in mk_qs:
                        v.wait_ge(s_qsp[kq % 2].h, mk_qs[prev])
                    qhb = qhb2[kq % 2]
                    if p.dev and l == 0 and ti == 4 and fc == 0:
                        v.tensor_copy(p.abf[:, 80000:80512], rstd[:, 0:n])
                        v.tensor_copy(p.abf[:, 80512:81024], qn[:, 0:n])
                        v.tensor_copy(p.abf[:, 81024:81536], qrot[:, 0:n])
                        v.tensor_copy(p.abf[:, 81536:82048], nn[:, 0:n])
                    assert s_sp.inc(
                        v.tensor_tensor(qhb[:, 0:n], qn[:, 0:n], qrot[:, 0:n],
                                        ALU.add)) == mk_sp[(ti, fc)]
                for hp in range(4):
                    fc = 8 + hp
                    v.wait_ge(s_mm.h, mk_mm[(ti, fc)])
                    if hp >= 1:
                        v.wait_ge(s_t.h, mk_t[(ti, hp - 1, nj - 1)])
                    assert s_fr.inc(
                        v.tensor_copy(vcf[:, 0:n], p.pf[fc % 3][:, 0:n])
                    ) == mk_fr[(ti, fc)]
                    if p.dev and l == 0 and ti == 2 and hp == 0:
                        v.tensor_copy(p.abf[:, 84000:84512], vcf[:, 0:512])
                    for j in range(nj):
                        v.wait_ge(s_t.h, mk_t[(ti, hp, j)])
                        if hp == 0 and j == 0 and (ti - 1) in mk_vw:
                            v.wait_ge(s_vwp[(ti - 1) % 2].h, mk_vw[ti - 1])
                        v.tensor_copy(vsb[j][:, 192 * hp:192 * hp + 64],
                                      p.pb[j % 2][:, 0:64])
                        assert s_vev.inc(v.tensor_copy(
                            vsb[j][:, 192 * hp + 128:192 * hp + 192],
                            p.pb[j % 2][:, 64:128])) == mk_vev[(ti, hp, j)]

        @blk.scalar
        def _(s):
            s = Drained(s)
            for ti, (tau0, n) in enumerate(tlist):
                for fc in range(8):
                    s.wait_ge(s_mm.h, mk_mm[(ti, fc)])
                    assert s_sq.inc(
                        s.activation(sqb[:, 0:n], p.pf[fc % 3][:, 0:n],
                                     AF.Square, bias=0.0, scale=1.0)
                    ) == mk_sq[(ti, fc)]
                    s.wait_ge(s_fr2.h, mk_fr2[(ti, fc)])
                    s.activation(nn[:, 0:n], nn[:, 0:n], AF.Ln,
                                 bias=0.0, scale=1.0)
                    assert s_rs.inc(
                        s.activation(rstd[:, 0:n], nn[:, 0:n], AF.Exp,
                                     bias=0.0, scale=-0.5)) == mk_rs[(ti, fc)]

        @blk.tensor
        def _(t_):
            tcnt = 0
            for ti, (tau0, n) in enumerate(tlist):
                nj = n // 128
                if l == 1:
                    t_.wait_ge(s_hs.h, mk_hs[ti])
                else:
                    t_.wait_ge(s_ldp[ti % 2].h, mk_ld[ti])
                for fc in range(12):
                    prev = (ti, fc - 3) if fc >= 3 else (ti - 1, fc + 9)
                    if prev in mk_fr:
                        t_.wait_ge(s_fr.h, mk_fr[prev])
                    ps = p.pf[fc % 3]
                    for c in range(8):
                        off = 1 if c < 4 else 0
                        mi = t_.matmul(
                            ps[:, 0:n],
                            ap2(p.abf, wq.col + c * 1536 + fc * 128,
                                [[ABF_W, 128], [1, 128]]),
                            ap2(p.abf, xnsl[ti % 2].col + c * 513 + off,
                                [[ABF_W, 128], [1, n]]),
                            start=(c == 0), stop=(c == 7))
                    assert s_mm.inc(mi) == mk_mm[(ti, fc)]
                    if fc < 8:
                        t_.wait_ge(s_sq.h, mk_sq[(ti, fc)])
                        prev2 = (ti, fc - 2) if fc >= 2 else (ti - 1, fc + 6)
                        if prev2 in mk_fr2:
                            t_.wait_ge(s_fr2.h, mk_fr2[prev2])
                        assert s_m2.inc(
                            t_.matmul(p.pf[3 + fc % 2][:, 0:n],
                                      p.blkdiag[:, :], sqb[:, 0:n],
                                      start=True, stop=True)) == mk_m2[(ti, fc)]
                    else:
                        hp = fc - 8
                        t_.wait_ge(s_fr.h, mk_fr[(ti, fc)])
                        for j in range(nj):
                            if tcnt >= 2:
                                t_.wait_ge(s_vev.h,
                                           mk_vev[tp_order[tcnt - 2]])
                            assert s_t.inc(
                                t_.transpose(p.pb[j % 2][:, 0:128],
                                             vcf[:, j * 128:(j + 1) * 128],
                                             p.eye_bf[:, :])
                            ) == mk_t[(ti, hp, j)]
                            tcnt += 1

        @blk.sync
        def _(s):
            for ti, (tau0, n) in enumerate(tlist):
                nj = n // 128
                if ti >= 2:
                    s.wait_ge(s_mm.h, mk_mm[(ti - 2, 11)])
                assert s_ldp[ti % 2].inc(s.dma_start(
                    ap2(p.abf, xnsl[ti % 2].col,
                        [[ABF_W, 128], [513, 8], [1, n + 1]]),
                    ap2(io["xnfm"], tau0 + HALO,
                        [[1 + EXT, 128], [128 * (1 + EXT), 8], [1, n + 1]])),
                    16) == mk_ld[ti]
                if ti >= 2:
                    s.wait_ge(s_sp.h, mk_sp[(ti - 2, 7)])
                for tb in range(4):
                    for half in range(2):
                        s_tabp[ti % 2].inc(s.dma_start(
                            tabs[ti % 2][tb].ap[64 * half:64 * half + 64, 0:n],
                            ap2(io[TABN[tb]], tau0 + HALO, [[EXT, 64], [1, n]])),
                            16)
                assert s_tabp[ti % 2].v == mk_tab[ti]
                for fc in range(8):
                    kq = ti * 8 + fc
                    s.wait_ge(s_sp.h, mk_sp[(ti, fc)])
                    if fc < 4:
                        dst = ap2(io["qh"], fc * 128 * EXT + tau0 + HALO,
                                  [[EXT, 128], [1, n]])
                    else:
                        dst = ap2(io["kh"], (fc - 4) * 128 * KH_W + tau0 + 1536,
                                  [[KH_W, 128], [1, n]])
                    assert s_qsp[kq % 2].inc(
                        s.dma_start(dst, qhb2[kq % 2][:, 0:n]), 16) \
                        == mk_qs[(ti, fc)]
                for j in range(nj):
                    s.wait_ge(s_vev.h, mk_vev[(ti, 3, j)])
                    bidx = (tau0 + 128 * j + 1536) // 128
                    mkv = s_vwp[ti % 2].inc(s.dma_start(
                        ap2(io["vstat"], bidx * 128 * 768, [[768, 128], [1, 768]]),
                        vsb[j][:, :]), 16)
                    if j == nj - 1:
                        assert mkv == mk_vw[ti]
                if p.dev and l == 0 and ti == 2:
                    p.s_ev.inc(s.dma_start(
                        ap2(io["dbg"], 0, [[2048, 128], [1, 2048]]),
                        p.abf[:, 84000:86048]), 16)
            for sp in (s_qsp[0], s_qsp[1], s_vwp[0], s_vwp[1]):
                s.wait_ge(sp.h, sp.v)

        if l == 1:
            @blk.gpsimd
            def _(g):
                g = Drained(g)
                for ti, (tau0, n) in enumerate(tlist):
                    if ti >= 2:
                        g.wait_ge(s_hs.h, mk_hs[ti - 2])
                    for hf in range(2):
                        mk = s_gp[ti % 2].inc(g.dma_start(
                            ap2(p.abf, dsl[ti % 2].col + hf * 4 * 513,
                                [[ABF_W, 128], [513, 4], [1, n + 1]]),
                            ap2(io["dacc_b"],
                                tau0 - 1 + DH0 + hf * 4 * 128 * DACC_W,
                                [[DACC_W, 128], [128 * DACC_W, 4], [1, n + 1]])),
                            16)
                    assert mk == mk_g0[ti]


def phase_p1b(p):
    nc, io = p.nc, p.io
    BO = p.BOFF
    ab = Alloc(p.abf, ABF_W)
    af = Alloc(p.af, AF_W)
    wsi = ab.tile(2048)
    wout = ab.tile(16 * 1024)
    C_bf = ab.tile(HT)
    pp_bf = ab.tile(HT)
    ybf = ab.tile(HT)
    y2b = [ab.tile(512) for _ in range(2)]
    dxb = ab.tile(512)
    ysig = [ab.tile(512) for _ in range(16)]
    xm2 = [ab.tile(512) for _ in range(2)]
    sg2 = [Tile(p.af16[:, 0:512], 0), Tile(p.af16[:, 512:1024], 512)]
    st = af.tile(HT)
    muT = af.tile(512)
    msT = af.tile(512)
    tmpT = af.tile(512)
    ST = af.tile(512)
    e32 = af.tile(512)
    dmt = [af.tile(512) for _ in range(2)]
    (s_mm, s_sc, s_ev, s_fr, s_mm2, s_dev) = (
        p.s_mm, p.s_sc, p.s_ev, p.s_a0, p.s_a1, p.s_a2)
    s_ldp = [p.s_d0, p.s_d1]
    s_stp = [p.s_d2, p.s_d3]

    with nc.Block("p1b_w") as blk:
        @blk.gpsimd
        def _(g):
            g = Drained(g)
            for half in range(2):
                p.s_wl.inc(g.dma_start(
                    p.abf[64 * half:64 * half + 64, wsi.col:wsi.col + 2048],
                    io["L0_attn_W_si"][0:64, :]), 16)
            for c in range(16):
                p.s_wl.inc(g.dma_start(
                    ap2(p.abf, wout.col + c * 1024, [[ABF_W, 128], [1, 1024]]),
                    io["L0_attn_W_out"][c * 128:(c + 1) * 128, :]), 16)
            p.s_g0.inc(g.dma_start(st[:, :], io["st_scr"][:, :]), 16)
            p.s_g0.inc(g.dma_start(pp_bf[:, :], io["pp_scr"][:, :]), 16)
            p.s_g0.inc(g.dma_start(C_bf[:, :], io["c_scr"][:, :]), 16)
            g.wait_ge(p.s_cc.h, 1)
            for half in range(2):
                p.s_g0.inc(g.dma_start(
                    p.small[64 * half:64 * half + 64, 60:61],
                    ap2(io["cc_out"], 0, [[1, 64], [1, 1]]),
                    allow_slow_non_contiguous=True), 16)
            g.wait_ge(p.s_wl.h, p.s_wl.v)
            g.wait_ge(p.s_g0.h, p.s_g0.v)

    mm0, sc0, ev0, fr0, mm20, dev0 = (s_mm.v, s_sc.v, s_ev.v, s_fr.v,
                                      s_mm2.v, s_dev.v)
    stats_mm = {g: mm0 + 2 * (g + 1) for g in range(8)}
    sq_mk = {g: sc0 + 2 * g + 1 for g in range(8)}
    rstd_mk = {g: sc0 + 2 * g + 2 for g in range(8)}
    var_mk = {g: ev0 + 1 + 2 * g + 1 for g in range(8)}
    apply_mk = {g: ev0 + 1 + 2 * g + 2 for g in range(8)}
    ysi_mm = {}
    fr_mk = {}
    v = mm0 + 16
    for tt in range(8):
        for fc in range(16):
            ysi_mm[(tt, fc)] = mm0 + 16 + tt * 16 + fc + 1
            fr_mk[(tt, fc)] = fr0 + tt * 16 + fc + 1
    wout_mm = {(tt, fo): mm20 + tt * 8 + fo + 1 for tt in range(8)
               for fo in range(8)}
    dev_mk = {(tt, fo): dev0 + tt * 8 + fo + 1 for tt in range(8)
              for fo in range(8)}
    st_mk = {(tt, fo): s_stp[(tt * 8 + fo) % 2].v
             + 16 * ((tt * 8 + fo) // 2 + 1)
             for tt in range(8) for fo in range(8)}
    ld_mk = {(tt, fc): s_ldp[(tt * 16 + fc) % 2].v
             + 32 * ((tt * 16 + fc) // 2 + 1)
             for tt in range(8) for fc in range(16)}

    with nc.Block("p1b") as blk:
        @blk.vector
        def _(v_):
            v_ = Drained(v_)
            v_.tensor_tensor(p.small[:, 60:61], p.small[:, 60:61],
                             p.flags[:, 1:2], ALU.mult)
            v_.scalar_tensor_tensor(st[:, :], pp_bf[:, :], p.small[:, 60:61],
                                    st[:, :], ALU.mult, ALU.add)
            v_.tensor_tensor(st[:, :], C_bf[:, :], st[:, :], ALU.mult)
            s_ev.inc(v_.tensor_copy(ybf[:, :], st[:, :]))
            for g in range(8):
                half, ct = g // 4, g % 4
                b = 64 * half
                cols = slice(ct * 512 + (0 if half == 0 else 0), 0)
                c0 = ct * 512
                v_.wait_ge(s_mm.h, stats_mm[g])
                if g >= 1:
                    v_.wait_ge(s_sc.h, rstd_mk[g - 1])
                v_.tensor_scalar(muT[b:b + 64, :], p.pf[4][0:64, :],
                                 1.0 / DSTATE, None, ALU.mult)
                v_.tensor_scalar(msT[b:b + 64, :], p.pf[5][0:64, :],
                                 1.0 / DSTATE, None, ALU.mult)
                v_.tensor_tensor(tmpT[b:b + 64, :], muT[b:b + 64, :],
                                 muT[b:b + 64, :], ALU.mult)
                assert s_ev.inc(v_.tensor_tensor(
                    msT[b:b + 64, :], msT[b:b + 64, :], tmpT[b:b + 64, :],
                    ALU.subtract)) == var_mk[g]
                v_.wait_ge(s_sc.h, rstd_mk[g])
                v_.tensor_tensor(tmpT[b:b + 64, :], muT[b:b + 64, :],
                                 ST[b:b + 64, :], ALU.mult)
                v_.tensor_tensor(st[b:b + 64, c0:c0 + 512],
                                 st[b:b + 64, c0:c0 + 512],
                                 ST[b:b + 64, :], ALU.mult)
                assert s_ev.inc(v_.tensor_tensor(
                    ybf[b:b + 64, c0:c0 + 512], st[b:b + 64, c0:c0 + 512],
                    tmpT[b:b + 64, :], ALU.subtract)) == apply_mk[g]
            for tt in range(8):
                for fc in range(16):
                    v_.wait_ge(s_mm.h, ysi_mm[(tt, fc)])
                    v_.wait_ge(s_ldp[(tt * 16 + fc) % 2].h, ld_mk[(tt, fc)])
                    if tt >= 1:
                        v_.wait_ge(s_mm2.h, wout_mm[(tt - 1, 7)])
                    v_.tensor_scalar(dxb[:, :], xm2[fc % 2][:, :],
                                     p.bias[:, BO["D"] + fc:BO["D"] + fc + 1],
                                     None, ALU.mult)
                    v_.scalar_tensor_tensor(
                        e32[:, :], p.pf[fc % 2][:, :],
                        p.bias[:, BO["b_si"] + fc:BO["b_si"] + fc + 1],
                        dxb[:, :], ALU.add, ALU.add)
                    assert s_fr.inc(v_.tensor_tensor(
                        ysig[fc][:, :], e32[:, :], sg2[fc % 2][:, :],
                        ALU.mult)) == fr_mk[(tt, fc)]
                for fo in range(8):
                    v_.wait_ge(s_mm2.h, wout_mm[(tt, fo)])
                    if (tt * 8 + fo) >= 2:
                        pk = (tt * 8 + fo - 2)
                        v_.wait_ge(s_stp[pk % 2].h, st_mk[(pk // 8, pk % 8)])
                    assert s_dev.inc(v_.tensor_scalar(
                        dmt[fo % 2][:, :], p.pf[2 + fo % 2][:, :],
                        p.bias[:, BO["b_out"] + fo:BO["b_out"] + fo + 1],
                        None, ALU.add)) == dev_mk[(tt, fo)]

        @blk.scalar
        def _(s):
            s = Drained(s)
            for g in range(8):
                half, ct = g // 4, g % 4
                b = 64 * half
                s.wait_ge(s_ev.h, ev0 + 1)
                if g >= 2:
                    s.wait_ge(s_mm.h, stats_mm[g - 2])
                assert s_sc.inc(s.activation(
                    y2b[g % 2][b:b + 64, :],
                    st[b:b + 64, ct * 512:(ct + 1) * 512], AF.Square,
                    bias=0.0, scale=1.0)) == sq_mk[g]
                s.wait_ge(s_ev.h, var_mk[g])
                s.activation(tmpT[b:b + 64, :], msT[b:b + 64, :], AF.Ln,
                             bias=p.flags[b:b + 64, 3:4], scale=1.0)
                assert s_sc.inc(s.activation(
                    ST[b:b + 64, :], tmpT[b:b + 64, :], AF.Exp,
                    bias=0.0, scale=-0.5)) == rstd_mk[g]

        @blk.tensor
        def _(t_):
            for g in range(8):
                half, ct = g // 4, g % 4
                b = 64 * half
                t_.wait_ge(s_sc.h, sq_mk[g])
                if g >= 1:
                    t_.wait_ge(s_ev.h, var_mk[g - 1])
                t_.matmul(p.pf[4][0:64, :], p.ones_bf[b:b + 64, 0:64],
                          ybf[b:b + 64, ct * 512:(ct + 1) * 512],
                          start=True, stop=True)
                assert s_mm.inc(t_.matmul(
                    p.pf[5][0:64, :], p.ones_bf[b:b + 64, 0:64],
                    y2b[g % 2][b:b + 64, :], start=True, stop=True), 2) \
                    == stats_mm[g]
            for tt in range(8):
                half, ct = tt // 4, tt % 4
                b = 64 * half
                t_.wait_ge(s_ev.h, apply_mk[tt // 4 * 4 + tt % 4])
                for fc in range(16):
                    if (tt * 16 + fc) >= 2:
                        pk = tt * 16 + fc - 2
                        t_.wait_ge(s_fr.h, fr_mk[(pk // 16, pk % 16)])
                    assert s_mm.inc(t_.matmul(
                        p.pf[fc % 2][:, :],
                        p.abf[b:b + 64, wsi.col + fc * 128:wsi.col + fc * 128 + 128],
                        ybf[b:b + 64, ct * 512:(ct + 1) * 512],
                        start=True, stop=True)) == ysi_mm[(tt, fc)]
                t_.wait_ge(s_fr.h, fr_mk[(tt, 15)])
                for fo in range(8):
                    if (tt * 8 + fo) >= 2:
                        pk = tt * 8 + fo - 2
                        t_.wait_ge(s_dev.h, dev_mk[(pk // 8, pk % 8)])
                    for fc in range(16):
                        mi = t_.matmul(
                            p.pf[2 + fo % 2][:, :],
                            ap2(p.abf, wout.col + fc * 1024 + fo * 128,
                                [[ABF_W, 128], [1, 128]]),
                            ysig[fc][:, :], start=(fc == 0), stop=(fc == 15))
                    assert s_mm2.inc(mi) == wout_mm[(tt, fo)]

        @blk.sync
        def _(s):
            for tt in range(8):
                for fc in range(16):
                    if (tt * 16 + fc) >= 2:
                        pk = tt * 16 + fc - 2
                        s.wait_ge(s_fr.h, fr_mk[(pk // 16, pk % 16)])
                    sl = s_ldp[(tt * 16 + fc) % 2]
                    sl.inc(s.dma_start(
                        xm2[fc % 2][:, :],
                        io["xmain"][fc * 128:(fc + 1) * 128,
                                    tt * 512:(tt + 1) * 512]), 16)
                    sl.inc(s.dma_start(
                        sg2[fc % 2].ap,
                        io["sgate"][fc * 128:(fc + 1) * 128,
                                    tt * 512:(tt + 1) * 512]), 16)
                    assert sl.v == ld_mk[(tt, fc)]
                for fo in range(8):
                    s.wait_ge(s_dev.h, dev_mk[(tt, fo)])
                    assert s_stp[(tt * 8 + fo) % 2].inc(s.dma_start(
                        ap2(io["dm"], fo * 128 * T + tt * 512,
                            [[T, 128], [1, 512]]),
                        dmt[fo % 2][:, :]), 16) == st_mk[(tt, fo)]
            for sp in (s_stp[0], s_stp[1]):
                s.wait_ge(sp.h, sp.v)


def phase_attn(p, l):
    nc, io = p.nc, p.io
    ab = Alloc(p.abf, ABF_W)
    af = Alloc(p.af, AF_W)
    wo_sb = ab.tile(4 * 1024)
    qsl = [ab.tile(4 * 512) for _ in range(2)]
    ksl = [ab.tile(4 * 512) for _ in range(2)]
    vt = [ab.tile(768) for _ in range(8)]
    attE = [ab.tile(512) for _ in range(2)]
    attO = [ab.tile(512) for _ in range(2)]
    o_all = ab.tile(4 * 512)
    recT = af.tile(512)
    dwt = [af.tile(512) for _ in range(2)]
    dbt = [af.tile(512) for _ in range(2)]
    (s_mm, s_msk, s_exp, s_oev, s_dev) = (
        p.s_mm, p.s_ev, p.s_sc, p.s_a0, p.s_a1)
    s_ldp = [p.s_d0, p.s_d1]
    s_stp = [p.s_d2, p.s_d3]
    s_dldp = [p.s_g1, p.s_g3]

    wlist = list(range(-2, 8)) if l == 0 else list(range(-1, 8))
    NW = len(wlist)
    dacc_rd = "dacc_b"
    M_OWN = p.masks[:, 0:128]
    M_LOOK = p.masks[:, 128:256]
    M0D = p.masks[:, 256:384]
    M0R = p.masks[:, 384:384 + W]

    with nc.Block(f"attn{l}_w") as blk:
        @blk.gpsimd
        def _(g):
            g = Drained(g)
            for c in range(4):
                p.s_wl.inc(g.dma_start(
                    ap2(p.abf, wo_sb.col + c * 1024, [[ABF_W, 128], [1, 1024]]),
                    ap2(io["L0_ffn_wo"], l * INNER_A * DMODEL + c * 128 * DMODEL,
                        [[DMODEL, 128], [1, 1024]])), 16)
            g.wait_ge(p.s_wl.h, p.s_wl.v)

    mm0, msk0, exp0, oev0, dev0 = (s_mm.v, s_msk.v, s_exp.v,
                                   s_oev.v, s_dev.v)
    sides = ["own", "look"]
    sim_mk, msk_mk, exp_mk, pv_mk = {}, {}, {}, {}
    oev_mk, wo_mk, dev_mk, stw_mk, ld_mk, dld_mk = {}, {}, {}, {}, {}, {}
    mmv = mm0
    for wi in range(NW):
        for hp in range(4):
            for si in range(2):
                for kc in range(4):
                    mmv += 1
                    sim_mk[(wi, hp, si, kc)] = mmv
                    mmv += 1
                    pv_mk[(wi, hp, si, kc)] = mmv
        for fo in range(8):
            mmv += 1
            wo_mk[(wi, fo)] = mmv
    k = 0
    for wi in range(NW):
        for hp in range(4):
            for si in range(2):
                for kc in range(4):
                    k += 1
                    msk_mk[(wi, hp, si, kc)] = msk0 + k
                    exp_mk[(wi, hp, si, kc)] = exp0 + k
    for wi in range(NW):
        for hp in range(4):
            oev_mk[(wi, hp)] = oev0 + wi * 4 + hp + 1
        for fo in range(8):
            dev_mk[(wi, fo)] = dev0 + wi * 8 + fo + 1
    for wi in range(NW):
        for fo in range(8):
            k2 = wi * 8 + fo
            stw_mk[(wi, fo)] = s_stp[k2 % 2].v + 16 * (k2 // 2 + 1)
            dld_mk[(wi, fo)] = s_dldp[k2 % 2].v + 16 * (k2 // 2 + 1)
    lvp = [s_ldp[0].v, s_ldp[1].v]
    for wi in range(NW):
        lvp[wi % 2] += 32
        if wi == 0:
            lvp[wi % 2] += 16
        lvp[wi % 2] += 16 * (8 if wi == 0 else 4)
        ld_mk[wi] = lvp[wi % 2]

    def vslot(wi, side, kc):
        w = wlist[wi]
        bidx = 4 * w + (12 if side == 0 else 8) + kc
        return bidx % 8, bidx

    with nc.Block(f"attn{l}") as blk:
        @blk.sync
        def _(s):
            for wi in range(NW):
                w = wlist[wi]
                tau0 = w * 512
                if wi >= 2:
                    s.wait_ge(s_mm.h, wo_mk[(wi - 2, 7)])
                s_ldp[wi % 2].inc(s.dma_start(
                    ap2(p.abf, qsl[wi % 2].col, [[ABF_W, 128], [512, 4], [1, 512]]),
                    ap2(io["qh"], tau0 + HALO,
                        [[EXT, 128], [128 * EXT, 4], [1, 512]])), 16)
                if wi >= 2:
                    s.wait_ge(s_mm.h, wo_mk[(wi - 1, 7)])
                s_ldp[wi % 2].inc(s.dma_start(
                    ap2(p.abf, ksl[wi % 2].col, [[ABF_W, 128], [512, 4], [1, 512]]),
                    ap2(io["kh"], tau0 + 1536,
                        [[KH_W, 128], [128 * KH_W, 4], [1, 512]])), 16)
                if wi == 0:
                    s_ldp[wi % 2].inc(s.dma_start(
                        ap2(p.abf, ksl[1].col, [[ABF_W, 128], [512, 4], [1, 512]]),
                        ap2(io["kh"], tau0 + 1024,
                            [[KH_W, 128], [128 * KH_W, 4], [1, 512]])), 16)
                rng = range(8) if wi == 0 else range(4)
                for j in rng:
                    bidx = 4 * w + 8 + j if wi == 0 else 4 * w + 12 + j
                    slot = bidx % 8
                    if wi >= 2:
                        s.wait_ge(s_mm.h, wo_mk[(wi - 1, 7)])
                    s_ldp[wi % 2].inc(s.dma_start(
                        vt[slot][:, :],
                        ap2(io["vstat"], bidx * 128 * 768,
                            [[768, 128], [1, 768]])), 16)
                assert s_ldp[wi % 2].v == ld_mk[wi]
                for fo in range(8):
                    k2 = wi * 8 + fo
                    s.wait_ge(s_dev.h, dev_mk[(wi, fo)])
                    w_lo = max(0, -tau0 - DH0)
                    assert s_stp[k2 % 2].inc(s.dma_start(
                        ap2(io["dacc_a"], fo * 128 * DACC_W + tau0 + DH0 + w_lo,
                            [[DACC_W, 128], [1, 512 - w_lo]]),
                        dwt[fo % 2][:, w_lo:512]), 16) == stw_mk[(wi, fo)]
                if p.dev and l == 2 and w == 3:
                    s.wait_ge(s_dev.h, dev_mk[(wi, 7)])
                    p.s_a7.inc(s.dma_start(
                        ap2(io["dbg"], 0, [[2048, 128], [1, 2048]]),
                        p.abf[:, 84000:86048]), 16)
                    s.wait_ge(p.s_a7.h, p.s_a7.v)
            for sp in (s_stp[0], s_stp[1]):
                s.wait_ge(sp.h, sp.v)

        if l == 1:
            @blk.gpsimd
            def _(g):
                g = Drained(g)
                for wi in range(NW):
                    tau0 = wlist[wi] * 512
                    for fo in range(8):
                        if (wi * 8 + fo) >= 2:
                            pk = wi * 8 + fo - 2
                            g.wait_ge(s_dev.h, dev_mk[(pk // 8, pk % 8)])
                        w_lo = max(0, -tau0 - DH0)
                        assert s_dldp[(wi * 8 + fo) % 2].inc(g.dma_start(
                            dbt[fo % 2][:, w_lo:512],
                            ap2(io["dacc_b"],
                                fo * 128 * DACC_W + tau0 + DH0 + w_lo,
                                [[DACC_W, 128], [1, 512 - w_lo]])),
                            16) == dld_mk[(wi, fo)]

        @blk.tensor
        def _(t_):
            for wi in range(NW):
                w = wlist[wi]
                t_.wait_ge(s_ldp[wi % 2].h, ld_mk[wi])
                for hp in range(4):
                    if hp >= 1:
                        t_.wait_ge(s_oev.h, oev_mk[(wi, hp - 1)])
                    elif wi >= 1:
                        t_.wait_ge(s_oev.h, oev_mk[(wi - 1, 3)])
                    for si in range(2):
                        kslab = ksl[wi % 2] if si == 0 else \
                            (ksl[(wi - 1) % 2] if wi >= 1 else ksl[1])
                        for kc in range(4):
                            if si == 0:
                                u0, u1 = kc * 128, 512
                            else:
                                u0, u1 = 0, kc * 128 + 128
                            pkey = (wi, hp, si, kc - 1) if kc >= 1 else \
                                ((wi, hp, si - 1, 3) if si == 1 else
                                 ((wi, hp - 1, 1, 3) if hp >= 1 else
                                  ((wi - 1, 3, 1, 3) if wi >= 1 else None)))
                            if pkey is not None and pkey in exp_mk:
                                t_.wait_ge(s_exp.h, exp_mk[pkey])
                            for half, pspsum in ((0, p.pf[0]), (1, p.pf[1])):
                                b = 64 * half
                                mi = t_.matmul(
                                    pspsum[:, u0:u1],
                                    p.abf[b:b + 64,
                                          kslab.col + hp * 512 + kc * 128:
                                          kslab.col + hp * 512 + kc * 128 + 128],
                                    p.abf[b:b + 64,
                                          qsl[wi % 2].col + hp * 512 + u0:
                                          qsl[wi % 2].col + hp * 512 + u1],
                                    start=True, stop=True)
                            assert s_mm.inc(mi) == sim_mk[(wi, hp, si, kc)]
                            t_.wait_ge(s_exp.h, exp_mk[(wi, hp, si, kc)])
                            slot, _b = vslot(wi, si, kc)
                            first = (si == 0 and kc == 0)
                            last = (si == 1 and kc == 3)
                            t_.matmul(p.pf[2][:, u0:u1],
                                      vt[slot][:, 192 * hp:192 * hp + 128],
                                      attE[kc % 2][:, u0:u1],
                                      start=first, stop=last)
                            mi = t_.matmul(p.pf[3][:, u0:u1],
                                           vt[slot][:, 192 * hp + 64:
                                                    192 * hp + 192],
                                           attO[kc % 2][:, u0:u1],
                                           start=first, stop=last)
                            assert s_mm.inc(mi) == pv_mk[(wi, hp, si, kc)]
                t_.wait_ge(s_oev.h, oev_mk[(wi, 3)])
                for fo in range(8):
                    if (wi * 8 + fo) >= 2:
                        pk = wi * 8 + fo - 2
                        t_.wait_ge(s_dev.h, dev_mk[(pk // 8, pk % 8)])
                    for c in range(4):
                        mi = t_.matmul(
                            p.pf[4 + fo % 2][:, :],
                            ap2(p.abf, wo_sb.col + c * 1024 + fo * 128,
                                [[ABF_W, 128], [1, 128]]),
                            o_all[:, c * 512:(c + 1) * 512],
                            start=(c == 0), stop=(c == 3))
                    assert s_mm.inc(mi) == wo_mk[(wi, fo)]

        @blk.vector
        def _(v_):
            v_ = Drained(v_)
            for wi in range(NW):
                w = wlist[wi]
                for hp in range(4):
                    for si in range(2):
                        for kc in range(4):
                            if si == 0:
                                u0, u1 = kc * 128, 512
                                dmask = M_OWN
                            else:
                                u0, u1 = 0, kc * 128 + 128
                                dmask = M0D if w == 0 else M_LOOK
                            d0 = kc * 128
                            v_.wait_ge(s_mm.h, sim_mk[(wi, hp, si, kc)])
                            v_.tensor_tensor(p.pf[0][:, d0:d0 + 128],
                                             p.pf[0][:, d0:d0 + 128],
                                             dmask, ALU.add)
                            ins = v_.tensor_tensor(p.pf[1][:, d0:d0 + 128],
                                                   p.pf[1][:, d0:d0 + 128],
                                                   dmask, ALU.add)
                            if si == 1 and w == 0 and kc >= 1:
                                v_.tensor_tensor(p.pf[0][:, 0:d0],
                                                 p.pf[0][:, 0:d0],
                                                 M0R[:, 0:d0], ALU.add)
                                ins = v_.tensor_tensor(p.pf[1][:, 0:d0],
                                                       p.pf[1][:, 0:d0],
                                                       M0R[:, 0:d0], ALU.add)
                            assert s_msk.inc(ins) == msk_mk[(wi, hp, si, kc)]
                    v_.wait_ge(s_mm.h, pv_mk[(wi, hp, 1, 3)])
                    if wi >= 1:
                        v_.wait_ge(s_mm.h, wo_mk[(wi - 1, 7)])
                    v_.reciprocal(recT[0:64, :], p.pf[2][64:128, :])
                    v_.tensor_tensor(o_all[0:64, hp * 512:hp * 512 + 512],
                                     p.pf[2][0:64, :], recT[0:64, :], ALU.mult)
                    v_.reciprocal(recT[64:128, :], p.pf[3][0:64, :])
                    assert s_oev.inc(v_.tensor_tensor(
                        o_all[64:128, hp * 512:hp * 512 + 512],
                        p.pf[3][64:128, :], recT[64:128, :],
                        ALU.mult)) == oev_mk[(wi, hp)]
                if p.dev and l == 2 and w == 3:
                    v_.tensor_copy(p.abf[:, 84000:86048], o_all[:, :])
                tau0 = w * 512
                w_lo = max(0, -tau0 - DH0)
                for fo in range(8):
                    v_.wait_ge(s_mm.h, wo_mk[(wi, fo)])
                    if (wi * 8 + fo) >= 2:
                        pk = wi * 8 + fo - 2
                        v_.wait_ge(s_stp[pk % 2].h, stw_mk[(pk // 8, pk % 8)])
                    if l == 0:
                        ins = v_.tensor_copy(dwt[fo % 2][:, w_lo:512],
                                             p.pf[4 + fo % 2][:, w_lo:512])
                    else:
                        v_.wait_ge(s_dldp[(wi * 8 + fo) % 2].h,
                                   dld_mk[(wi, fo)])
                        ins = v_.tensor_tensor(dwt[fo % 2][:, w_lo:512],
                                               p.pf[4 + fo % 2][:, w_lo:512],
                                               dbt[fo % 2][:, w_lo:512], ALU.add)
                    assert s_dev.inc(ins) == dev_mk[(wi, fo)]

        @blk.scalar
        def _(s):
            s = Drained(s)
            negb = p.small[:, 6 + l:7 + l]
            for wi in range(NW):
                for hp in range(4):
                    for si in range(2):
                        for kc in range(4):
                            if si == 0:
                                u0, u1 = kc * 128, 512
                            else:
                                u0, u1 = 0, kc * 128 + 128
                            s.wait_ge(s_msk.h, msk_mk[(wi, hp, si, kc)])
                            kk = hp * 8 + si * 4 + kc
                            if kk >= 2:
                                pk2 = kk - 2
                                key2 = (wi, pk2 // 8, (pk2 % 8) // 4, pk2 % 4)
                                s.wait_ge(s_mm.h, pv_mk[key2])
                            elif wi >= 1:
                                key2 = (wi - 1, 3, 1, 2 + kk)
                                s.wait_ge(s_mm.h, pv_mk[key2])
                            s.activation(attE[kc % 2][:, u0:u1],
                                         p.pf[0][:, u0:u1], AF.Exp,
                                         bias=negb, scale=8.0)
                            ins = s.activation(attO[kc % 2][:, u0:u1],
                                               p.pf[1][:, u0:u1], AF.Exp,
                                               bias=negb, scale=8.0)
                            assert s_exp.inc(ins) == exp_mk[(wi, hp, si, kc)]


def phase_mlp(p, l):
    nc, io = p.nc, p.io
    BO = p.BOFF
    ab = Alloc(p.abf, ABF_W)
    af = Alloc(p.af, AF_W)
    w1sb = ab.tile(8 * 5460)
    w2sb = ab.tile(22 * 1024)
    hslab = ab.tile(8 * 513)
    xln = ab.tile(8 * 512)
    h2b = [ab.tile(512) for _ in range(2)]
    tqb = h2b[0]
    h1act = [ab.tile(512) for _ in range(22)]
    dslab = af.tile(8 * 513)
    muT = af.tile(512)
    msT = af.tile(512)
    tmpT = af.tile(512)
    ST = af.tile(512)
    gg = muT
    dnt = [msT, tmpT]
    (s_ld, s_mm, s_sc, s_ev, s_fr, s_mm2, s_dev) = (
        p.s_dd, p.s_mm, p.s_sc, p.s_ev, p.s_a0, p.s_a1, p.s_a2)
    s_stp = [p.s_d2, p.s_d3]

    if l == 0:
        tlist = [(-640, 128)] + [(-512 + 512 * t, 512) for t in range(9)]
    else:
        tlist = [(512 * t, 512) for t in range(8)]
    NTI = len(tlist)
    JW = [128] * 21 + [HREM]

    with nc.Block(f"mlp{l}_w") as blk:
        @blk.gpsimd
        def _(g):
            g = Drained(g)
            for c in range(8):
                p.s_wl.inc(g.dma_start(
                    ap2(p.abf, w1sb.col + c * 5460, [[ABF_W, 128], [1, 5460]]),
                    ap2(io["L0_ffn_w1"], l * DMODEL * 2 * HID + c * 128 * 2 * HID,
                        [[2 * HID, 128], [1, 2 * HID]])), 16)
            for j in range(22):
                wj = JW[j]
                p.s_wl.inc(g.dma_start(
                    ap2(p.abf, w2sb.col + j * 1024, [[ABF_W, wj], [1, 1024]]),
                    ap2(io["L0_ffn_w2"], l * HID * DMODEL + j * 128 * DMODEL,
                        [[DMODEL, wj], [1, 1024]])), 16)
            g.wait_ge(p.s_wl.h, p.s_wl.v)

    mm0, sc0, ev0, fr0, mm20, dev0, ld0 = (s_mm.v, s_sc.v, s_ev.v, s_fr.v,
                                           s_mm2.v, s_dev.v, s_ld.v)
    hs_mk = {ti: ev0 + 9 * ti + 8 for ti in range(NTI)}
    var_mk = {ti: ev0 + 9 * ti + 9 for ti in range(NTI)}
    sq_mk = {(ti, c): sc0 + 10 * ti + c + 1 for ti in range(NTI) for c in range(8)}
    rstd_mk = {ti: sc0 + 10 * ti + 9 for ti in range(NTI)}
    stat_mm = {(ti, c): mm0 + (22 * 8 * 2 + 16) * ti + 2 * (c + 1)
               for ti in range(NTI) for c in range(8)}
    apl_mk = {}
    gel_mk = {}
    act_mk = {}
    w1_mm = {}
    w2_mm = {}
    dev_mk = {}
    stw_mk = {}
    ld_mk = {ti: ld0 + 32 * (ti + 1) for ti in range(NTI)}
    for ti in range(NTI):
        base = mm0 + (22 * 8 * 2 + 16) * ti + 16
        for j in range(22):
            w1_mm[(ti, j)] = base + 16 * (j + 1)
    mv = mm20
    for ti in range(NTI):
        for fo in range(8):
            mv += 1
            w2_mm[(ti, fo)] = mv
    for ti in range(NTI):
        for fo in range(8):
            k2 = ti * 8 + fo
            dev_mk[(ti, fo)] = dev0 + k2 + 1
            stw_mk[(ti, fo)] = s_stp[k2 % 2].v + 16 * (k2 // 2 + 1)
        for j in range(22):
            gel_mk[(ti, j)] = sc0 + 10 * NTI + ti * 22 + j + 1
            act_mk[(ti, j)] = fr0 + ti * 22 + j + 1
    sqv = sc0
    sq_mk, rstd_mk, gel_mk = {}, {}, {}
    for ti in range(NTI):
        for c in range(8):
            sqv += 1
            sq_mk[(ti, c)] = sqv
        sqv += 1
        rstd_mk[ti] = sqv
        for j in range(22):
            sqv += 1
            gel_mk[(ti, j)] = sqv
    evv, frv, devv = ev0, fr0, dev0
    hs_mk, var_mk, apl_mk = {}, {}, {}
    for ti in range(NTI):
        evv += 8
        hs_mk[ti] = evv
        evv += 1
        var_mk[ti] = evv
        evv += 1
        apl_mk[ti] = evv

    with nc.Block(f"mlp{l}") as blk:
        @blk.sync
        def _(s):
            for ti, (tau0, n) in enumerate(tlist):
                if ti >= 1:
                    s.wait_ge(s_mm.h, w1_mm[(ti - 1, 21)])
                if ti >= 1:
                    s.wait_ge(s_dev.h, dev_mk[(ti - 1, 7)])
                s_ld.inc(s.dma_start(
                    ap2(p.abf, hslab.col, [[ABF_W, 128], [513, 8], [1, n + 1]]),
                    ap2(io["xnfm"], tau0 + HALO,
                        [[1 + EXT, 128], [128 * (1 + EXT), 8], [1, n + 1]])), 16)
                s_ld.inc(s.dma_start(
                    ap2(p.af, dslab.col, [[AF_W, 128], [513, 8], [1, n + 1]]),
                    ap2(io["dacc_a"], tau0 - 1 + DH0,
                        [[DACC_W, 128], [128 * DACC_W, 8], [1, n + 1]])), 16)
                assert s_ld.v == ld_mk[ti]
                for fo in range(8):
                    s.wait_ge(s_dev.h, dev_mk[(ti, fo)])
                    assert s_stp[(ti * 8 + fo) % 2].inc(s.dma_start(
                        ap2(io["dacc_b"], fo * 128 * DACC_W + tau0 + DH0,
                            [[DACC_W, 128], [1, n]]),
                        dnt[fo % 2][:, 0:n]), 16) == stw_mk[(ti, fo)]
                if p.dev and l == 0 and ti == 2:
                    p.s_a7.inc(s.dma_start(
                        ap2(io["dbg"], 0, [[2048, 128], [1, 512]]),
                        xln[:, 0:512]), 16)
                    p.s_a7.inc(s.dma_start(
                        ap2(io["dbg"], 512, [[2048, 128], [1, 512]]),
                        xln[:, 4 * 512:5 * 512]), 16)
                    p.s_a7.inc(s.dma_start(
                        ap2(io["dbg"], 1024, [[2048, 128], [1, 512]]),
                        hslab[:, 4 * 513:4 * 513 + 512]), 16)
                    s.wait_ge(p.s_a7.h, p.s_a7.v)
                    s.wait_ge(p.s_g1.h, p.s_g1.v)
            for sp in (s_stp[0], s_stp[1]):
                s.wait_ge(sp.h, sp.v)

        if p.dev and l == 0:
            @blk.gpsimd
            def _(g):
                g.wait_ge(s_dev.h, dev_mk[(2, 7)])
                p.s_g1.inc(g.dma_start(
                    ap2(io["dbg"], 1536, [[2048, 128], [1, 512]]),
                    ST[:, 0:512]), 16)

        @blk.vector
        def _(v_):
            v_ = Drained(v_)
            for ti, (tau0, n) in enumerate(tlist):
                v_.wait_ge(s_ld.h, ld_mk[ti])
                for c in range(8):
                    ins = v_.tensor_tensor(
                        hslab[:, c * 513:c * 513 + n + 1],
                        p.af[:, dslab.col + c * 513:dslab.col + c * 513 + n + 1],
                        p.abf[:, hslab.col + c * 513:hslab.col + c * 513 + n + 1],
                        ALU.add)
                    s_ev.inc(ins)
                assert s_ev.v == hs_mk[ti]
                v_.wait_ge(s_mm.h, stat_mm[(ti, 7)])
                if ti >= 1:
                    v_.wait_ge(s_stp[0].h, stw_mk[(ti - 1, 6)])
                    v_.wait_ge(s_stp[1].h, stw_mk[(ti - 1, 7)])
                v_.tensor_scalar(muT[:, 0:n], p.pf[0][:, 0:n], 1.0 / DMODEL,
                                 None, ALU.mult)
                v_.tensor_scalar(msT[:, 0:n], p.pf[1][:, 0:n], 1.0 / DMODEL,
                                 None, ALU.mult)
                v_.tensor_tensor(tmpT[:, 0:n], muT[:, 0:n], muT[:, 0:n],
                                 ALU.mult)
                assert s_ev.inc(v_.tensor_tensor(
                    msT[:, 0:n], msT[:, 0:n], tmpT[:, 0:n],
                    ALU.subtract)) == var_mk[ti]
                v_.wait_ge(s_sc.h, rstd_mk[ti])
                ins = None
                for c in range(8):
                    off = 1 if c < 4 else 0
                    hsv = hslab[:, c * 513 + off:c * 513 + off + n]
                    v_.tensor_tensor(tqb[:, 0:n], hsv, muT[:, 0:n],
                                     ALU.subtract)
                    v_.tensor_tensor(tqb[:, 0:n], tqb[:, 0:n], ST[:, 0:n],
                                     ALU.mult)
                    ins = v_.tensor_scalar(
                        xln[:, c * 512:c * 512 + n], tqb[:, 0:n],
                        p.bias[:, BO[f"ln_g{l}"] + c:BO[f"ln_g{l}"] + c + 1],
                        p.bias[:, BO[f"ln_b{l}"] + c:BO[f"ln_b{l}"] + c + 1],
                        ALU.mult, ALU.add)
                assert s_ev.inc(ins) == apl_mk[ti]

                for j in range(22):
                    wj = JW[j]
                    v_.wait_ge(s_sc.h, gel_mk[(ti, j)])
                    if ti >= 1 or j >= 1:
                        pass
                    if ti >= 1:
                        v_.wait_ge(s_mm2.h, w2_mm[(ti - 1, 7)])
                    ins = v_.scalar_tensor_tensor(
                        h1act[j][0:wj, 0:n],
                        p.pf[2 + 2 * (j % 2)][0:wj, 0:n],
                        p.bias[0:wj, BO[f"b1a{l}"] + j:BO[f"b1a{l}"] + j + 1],
                        gg[0:wj, 0:n], ALU.add, ALU.mult)
                    assert s_fr.inc(ins) == act_mk[(ti, j)]
                for fo in range(8):
                    v_.wait_ge(s_mm2.h, w2_mm[(ti, fo)])
                    if (ti * 8 + fo) >= 2:
                        pk = ti * 8 + fo - 2
                        v_.wait_ge(s_stp[pk % 2].h, stw_mk[(pk // 8, pk % 8)])
                    ins = v_.scalar_tensor_tensor(
                        dnt[fo % 2][:, 0:n], p.pf[fo % 2][:, 0:n],
                        p.bias[:, BO[f"b2{l}"] + fo:BO[f"b2{l}"] + fo + 1],
                        p.af[:, dslab.col + fo * 513 + 1:
                              dslab.col + fo * 513 + 1 + n], ALU.add, ALU.add)
                    if l == 0 and tau0 < 0:
                        nh = min(n, -tau0)
                        ins = v_.tensor_scalar(dnt[fo % 2][:, 0:nh],
                                               dnt[fo % 2][:, 0:nh],
                                               p.flags[:, 2:3], None, ALU.mult)
                    assert s_dev.inc(ins) == dev_mk[(ti, fo)]

        @blk.scalar
        def _(s):
            s = Drained(s)
            for ti, (tau0, n) in enumerate(tlist):
                for c in range(8):
                    off = 1 if c < 4 else 0
                    s.wait_ge(s_ev.h, hs_mk[ti])
                    if c == 0 and ti >= 1:
                        s.wait_ge(s_ev.h, apl_mk[ti - 1])
                    if c >= 2:
                        s.wait_ge(s_mm.h, stat_mm[(ti, c - 2)])
                    elif ti >= 1:
                        s.wait_ge(s_mm.h, stat_mm[(ti - 1, 6 + c)])
                    hsv = hslab[:, c * 513 + off:c * 513 + off + n]
                    assert s_sc.inc(s.activation(
                        h2b[c % 2][:, 0:n], hsv, AF.Square,
                        bias=0.0, scale=1.0)) == sq_mk[(ti, c)]
                s.wait_ge(s_ev.h, var_mk[ti])
                s.activation(tmpT[:, 0:n], msT[:, 0:n], AF.Ln, bias=p.flags[:, 3:4],
                             scale=1.0)
                assert s_sc.inc(s.activation(ST[:, 0:n], tmpT[:, 0:n], AF.Exp,
                                             bias=0.0, scale=-0.5)) == rstd_mk[ti]
                for j in range(22):
                    wj = JW[j]
                    s.wait_ge(s_mm.h, w1_mm[(ti, j)])
                    if j >= 1:
                        s.wait_ge(s_fr.h, act_mk[(ti, j - 1)])
                    elif ti >= 1:
                        s.wait_ge(s_fr.h, act_mk[(ti - 1, 21)])
                    ins = s.activation(
                        gg[0:wj, 0:n], p.pf[3 + 2 * (j % 2)][0:wj, 0:n],
                        AF.Gelu,
                        bias=p.bias[0:wj, BO[f"b1g{l}"] + j:BO[f"b1g{l}"] + j + 1],
                        scale=1.0)
                    assert s_sc.inc(ins) == gel_mk[(ti, j)]

        @blk.tensor
        def _(t_):
            for ti, (tau0, n) in enumerate(tlist):
                for c in range(8):
                    off = 1 if c < 4 else 0
                    hsv = p.abf[:, hslab.col + c * 513 + off:
                                hslab.col + c * 513 + off + n]
                    if c == 0:
                        t_.wait_ge(s_ev.h, hs_mk[ti])
                        if ti >= 1:
                            t_.wait_ge(s_dev.h, dev_mk[(ti - 1, 7)])
                    t_.matmul(p.pf[0][:, 0:n], p.ones_bf[:, :], hsv,
                              start=(c == 0), stop=(c == 7))
                    t_.wait_ge(s_sc.h, sq_mk[(ti, c)])
                    mi = t_.matmul(p.pf[1][:, 0:n], p.ones_bf[:, :],
                                   h2b[c % 2][:, 0:n],
                                   start=(c == 0), stop=(c == 7))
                    assert s_mm.inc(mi, 2) == stat_mm[(ti, c)]
                t_.wait_ge(s_ev.h, apl_mk[ti])
                for j in range(22):
                    wj = JW[j]
                    if j >= 2:
                        t_.wait_ge(s_fr.h, act_mk[(ti, j - 2)])
                    elif ti >= 1:
                        t_.wait_ge(s_fr.h, act_mk[(ti - 1, 20 + j)])
                    if j >= 1:
                        t_.wait_ge(s_sc.h, gel_mk[(ti, j - 1)])
                    elif ti >= 1:
                        t_.wait_ge(s_sc.h, gel_mk[(ti - 1, 21)])
                    for c in range(8):
                        t_.matmul(p.pf[2 + 2 * (j % 2)][0:wj, 0:n],
                                  ap2(p.abf, w1sb.col + c * 5460 + j * 128,
                                      [[ABF_W, 128], [1, wj]]),
                                  p.abf[:, xln.col + c * 512:
                                        xln.col + c * 512 + n],
                                  start=(c == 0), stop=(c == 7))
                    for c in range(8):
                        mi = t_.matmul(p.pf[3 + 2 * (j % 2)][0:wj, 0:n],
                                       ap2(p.abf,
                                           w1sb.col + c * 5460 + HID + j * 128,
                                           [[ABF_W, 128], [1, wj]]),
                                       p.abf[:, xln.col + c * 512:
                                             xln.col + c * 512 + n],
                                       start=(c == 0), stop=(c == 7))
                    assert s_mm.inc(mi, 16) == w1_mm[(ti, j)]
                for fo in range(8):
                    if (ti * 8 + fo) >= 2:
                        pk = ti * 8 + fo - 2
                        t_.wait_ge(s_dev.h, dev_mk[(pk // 8, pk % 8)])
                    for j in range(22):
                        wj = JW[j]
                        if fo == 0:
                            t_.wait_ge(s_fr.h, act_mk[(ti, j)])
                        mi = t_.matmul(
                            p.pf[fo % 2][:, 0:n],
                            ap2(p.abf, w2sb.col + j * 1024 + fo * 128,
                                [[ABF_W, wj], [1, 128]]),
                            h1act[j][0:wj, 0:n],
                            start=(j == 0), stop=(j == 21))
                    assert s_mm2.inc(mi) == w2_mm[(ti, fo)]


def phase_fin(p):
    nc, io = p.nc, p.io
    ab = Alloc(p.abf, ABF_W)
    af = Alloc(p.af, AF_W)
    ut = [af.tile(1024) for _ in range(2)]
    ot = [af.tile(1024) for _ in range(2)]
    dmsl = af.tile(1024)
    dasl = af.tile(1024)
    s8 = dmsl
    (s_ld, s_mm, s_ev) = (p.s_dd, p.s_mm, p.s_ev)
    s_op = [p.s_d4, p.s_d5]
    ld0, mm0, ev0 = s_ld.v, s_mm.v, s_ev.v
    NG = 32
    ow_mk = {i: s_op[i % 2].v + 16 * (i // 2 + 1) for i in range(NG)}

    with nc.Block("fin") as blk:
        @blk.sync
        def _(s):
            for i in range(NG):
                if i >= 1:
                    s.wait_ge(s_ev.h, ev0 + i * 9 - 8)
                    s.wait_ge(s_mm.h, mm0 + 8 * i)
                s_ld.inc(s.dma_start(
                    ap2(p.af, dmsl.col, [[AF_W, 128], [128, 8], [1, 128]]),
                    ap2(io["dm"], i * 128,
                        [[T, 128], [128 * T, 8], [1, 128]])), 16)
                s_ld.inc(s.dma_start(
                    ap2(p.af, dasl.col, [[AF_W, 128], [128, 8], [1, 128]]),
                    ap2(io["dacc_b"], DH0 + i * 128,
                        [[DACC_W, 128], [128 * DACC_W, 8], [1, 128]])), 16)
                if i >= 2:
                    s.wait_ge(s_ev.h, ev0 + (i - 1) * 9 + 9)
                s_ld.inc(s.dma_start(ut[i % 2][:, :],
                                     io["u_scr"][i * 128:(i + 1) * 128, :]), 16)
                s.wait_ge(s_ev.h, ev0 + i * 9 + 9)
                assert s_op[i % 2].inc(
                    s.dma_start(io["out"][i * 128:(i + 1) * 128, :],
                                ot[i % 2][:, :]), 16) == ow_mk[i]
            for sp in (s_op[0], s_op[1]):
                s.wait_ge(sp.h, sp.v)

        @blk.vector
        def _(v_):
            v_ = Drained(v_)
            for i in range(NG):
                v_.wait_ge(s_ld.h, ld0 + 48 * (i + 1))
                if i >= 1:
                    v_.wait_ge(s_mm.h, mm0 + 8 * i)
                s_ev.inc(v_.tensor_add(s8[:, :], dmsl.ap, dasl.ap))
                for c in range(8):
                    v_.wait_ge(s_mm.h, mm0 + i * 8 + c + 1)
                    if c == 0 and i >= 2:
                        v_.wait_ge(s_op[i % 2].h, ow_mk[i - 2])
                    ins = v_.tensor_tensor(ot[i % 2][:, c * 128:(c + 1) * 128],
                                           ut[i % 2][:, c * 128:(c + 1) * 128],
                                           p.pf[4 + c % 2][:, 0:128], ALU.add)
                    s_ev.inc(ins)

        @blk.tensor
        def _(t_):
            for i in range(NG):
                t_.wait_ge(s_ev.h, ev0 + i * 9 + 1)
                for c in range(8):
                    k = i * 8 + c
                    if k >= 2:
                        t_.wait_ge(s_ev.h, ev0 + (k - 2) // 8 * 9 + (k - 2) % 8 + 2)
                    mi = t_.transpose(p.pf[4 + c % 2][:, 0:128],
                                      p.af[:, s8.col + c * 128:
                                           s8.col + c * 128 + 128],
                                      p.eye_f32[:, :])
                    s_mm.inc(mi)


_CACHE = {}


def kernel(**inputs):
    if "nc" not in _CACHE:
        _CACHE["nc"], _CACHE["io"] = build_program()
    nc = _CACHE["nc"]
    in_maps = [make_core_inputs(c, inputs) for c in range(8)]
    res = run_bass_kernel_spmd(nc, in_maps, core_ids=list(range(8)))
    out = np.zeros((B, N, DMODEL), np.float32)
    for c in range(8):
        b, h = c // 2, c % 2
        out[b, h * T:(h + 1) * T] = np.asarray(res.results[c]["out"])
    return out


# version 10 (final)
# speedup vs baseline: 1.0639x; 1.0431x over previous
from contextlib import ExitStack

import numpy as np
import ml_dtypes
import concourse.bass as bass
import concourse.mybir as mybir
from concourse.bass_utils import run_bass_kernel_spmd

F32 = mybir.dt.float32
BF16 = mybir.dt.bfloat16
FP16 = mybir.dt.float16
AF = mybir.ActivationFunctionType
ALU = mybir.AluOpType
AX = mybir.AxisListType

B, N, DMODEL = 4, 8192, 1024
T = 4096
HALO = 1152
EXT = HALO + T
W = 512
INNER_M = 2048
DSTATE = 64
HEADS, DH = 8, 64
INNER_A = 512
HID = 2730
NEG = -1.0e30
EPS = 1e-5
DH0 = 644
DACC_W = DH0 + T
HT = T // 2
HREM = HID - 21 * 128
ABF_W = 87040
AF_W = 6400
CONSTS_LIMIT = None
KH_W = 384 + EXT


def make_core_inputs(core, inputs):
    b, h = core // 2, core % 2
    s0 = h * T
    x = np.asarray(inputs["L0_attn_x"])
    lo = s0 - HALO
    xs = np.zeros((EXT, DMODEL), np.float32)
    src_lo = max(lo, 0)
    xs[src_lo - lo:] = np.asarray(x[b, src_lo:s0 + T], np.float32)

    pos = np.clip(np.arange(lo, s0 + T, dtype=np.float64), 0, None)
    half = np.arange(0, DH, 2, dtype=np.float64)
    inv_freq = 1.0 / 10000.0 ** (half / DH)
    fr = pos[None, :] * inv_freq[:, None]
    freqs = np.concatenate([fr, fr], 0)
    base = (half + 0.4 * DH) / (1.4 * DH)
    power = (pos - N // 2) / float(W // 2)
    scale = np.concatenate([base[:, None] ** power[None, :]] * 2, 0)
    cos, sin = np.cos(freqs), np.sin(freqs)
    sgn = np.concatenate([-np.ones(32), np.ones(32)])[:, None]

    r = np.arange(128)
    m_own = np.where(r[:, None] > r[None, :], NEG, 0.0)
    m_look = np.where(r[None, :] > r[:, None], NEG, 0.0)
    if h == 0:
        m0_d = np.full((128, 128), NEG, np.float32)
        m0_rect = np.full((128, W), NEG, np.float32)
    else:
        m0_d = m_look.astype(np.float32)
        m0_rect = np.zeros((128, W), np.float32)

    blkdiag = np.zeros((128, 128))
    blkdiag[:64, :64] = 1.0
    blkdiag[64:, 64:] = 1.0

    def f32(a):
        return np.ascontiguousarray(a, np.float32)

    d = {
        "xs": xs,
        "cs1q": f32(cos * scale), "cs2q": f32(sin * scale * sgn),
        "cs1k": f32(cos / scale), "cs2k": f32(sin / scale * sgn),
        "m_own": m_own.astype(ml_dtypes.bfloat16),
        "m_look": m_look.astype(ml_dtypes.bfloat16),
        "m0_d": m0_d.astype(ml_dtypes.bfloat16),
        "m0_rect": m0_rect.astype(ml_dtypes.bfloat16),
        "eye_bf": np.eye(128, dtype=ml_dtypes.bfloat16),
        "eye_f32": np.eye(128, dtype=np.float32),
        "ones_bf": np.ones((128, 128), ml_dtypes.bfloat16),
        "blkdiag_bf": blkdiag.astype(ml_dtypes.bfloat16),
        "evenm": np.full((128, 1), 1.0 if h == 0 else 0.0, np.float32),
        "oddm": np.full((128, 1), 1.0 if h == 1 else 0.0, np.float32),
        "haloflag": np.full((128, 1), 0.0 if h == 0 else 1.0, np.float32),
    }
    for k, v in inputs.items():
        if k != "L0_attn_x":
            d[k] = np.ascontiguousarray(np.asarray(v))
    cw = np.asarray(inputs["L0_attn_conv_w"], np.float32)
    for k in range(3):
        d[f"conv_w{k}"] = np.ascontiguousarray(cw[:, k])
    return d


def ap2(handle, offset, dims):
    return bass.AP(handle, offset, [list(x) for x in dims])


class Sem:
    def __init__(self, ctx, nc, name):
        self.h = ctx.enter_context(nc.semaphore(name))
        self.v = 0

    def inc(self, instr, n=1):
        instr.then_inc(self.h, n)
        self.v += n
        return self.v


class Tile:
    def __init__(self, ap, col):
        self.ap = ap
        self.col = col

    def __getitem__(self, idx):
        return self.ap[idx]


class Alloc:
    def __init__(self, arena, width):
        self.a, self.w, self.top = arena, width, 0

    def tile(self, n, parts=128, base=0):
        c0 = self.top
        self.top += n
        assert self.top <= self.w, f"arena overflow {self.top} > {self.w}"
        return Tile(self.a[base:base + parts, c0:c0 + n], c0)


class Drained:

    _PASS = ("wait_ge", "dma_start", "drain", "engine_nop", "nop",
             "collective_compute", "indirect_dma_start", "then_inc")

    def __init__(self, eng):
        self._e = eng

    def __getattr__(self, nm):
        f = getattr(self._e, nm)
        if nm in self._PASS or not callable(f):
            return f
        e = self._e

        def w(*a, **k):
            f(*a, **k)
            return e.drain()
        return w


class P:
    pass


def build_program(dev=False, upto=None):
    nc = bass.Bass("TRN2", target_bir_lowering=False)
    io = {}

    def din(name, shape, dtype=F32):
        io[name] = nc.dram_tensor(name, list(shape), dtype, kind="ExternalInput")

    def dscr(name, shape, dtype):
        kind = "ExternalOutput" if dev else "Internal"
        io[name] = nc.dram_tensor(name, list(shape), dtype, kind=kind)

    din("xs", [EXT, DMODEL])
    for nm in ["cs1q", "cs2q", "cs1k", "cs2k"]:
        din(nm, [DH, EXT])
    for nm in ["m_own", "m_look", "m0_d"]:
        din(nm, [128, 128], BF16)
    din("m0_rect", [128, W], BF16)
    din("eye_bf", [128, 128], BF16)
    din("eye_f32", [128, 128], F32)
    din("ones_bf", [128, 128], BF16)
    din("blkdiag_bf", [128, 128], BF16)
    din("evenm", [128, 1])
    din("oddm", [128, 1])
    din("haloflag", [128, 1])
    din("L0_attn_W_in", [DMODEL, 2 * INNER_M])
    din("L0_attn_b_in", [2 * INNER_M])
    din("L0_attn_conv_w", [INNER_M, 3])
    for k in range(3):
        din(f"conv_w{k}", [INNER_M])
    din("L0_attn_conv_b", [INNER_M])
    din("L0_attn_W_dt", [INNER_M, DSTATE])
    din("L0_attn_b_dt", [DSTATE])
    din("L0_attn_W_B", [INNER_M, DSTATE])
    din("L0_attn_b_B", [DSTATE])
    din("L0_attn_W_C", [INNER_M, DSTATE])
    din("L0_attn_b_C", [DSTATE])
    din("L0_attn_W_si", [DSTATE, INNER_M])
    din("L0_attn_b_si", [INNER_M])
    din("L0_attn_D", [INNER_M])
    din("L0_attn_W_out", [INNER_M, DMODEL])
    din("L0_attn_b_out", [DMODEL])
    din("L0_ffn_wqkv", [2, DMODEL, 3 * INNER_A])
    din("L0_ffn_q_scale", [2, DH])
    din("L0_ffn_k_scale", [2, DH])
    din("L0_ffn_wo", [2, INNER_A, DMODEL])
    din("L0_ffn_ln_g", [2, DMODEL])
    din("L0_ffn_ln_b", [2, DMODEL])
    din("L0_ffn_w1", [2, DMODEL, 2 * HID])
    din("L0_ffn_b1", [2, 2 * HID])
    din("L0_ffn_w2", [2, HID, DMODEL])
    din("L0_ffn_b2", [2, DMODEL])
    io["out"] = nc.dram_tensor("out", [T, DMODEL], F32, kind="ExternalOutput")
    dscr("u_scr", [T, DMODEL], F32)
    dscr("xnfm", [DMODEL, 1 + EXT], BF16)
    dscr("xmain", [INNER_M, T], BF16)
    dscr("sgate", [INNER_M, T], FP16)
    dscr("dm", [DMODEL, T], F32)
    dscr("st_scr", [128, HT], F32)
    dscr("pp_scr", [128, HT], BF16)
    dscr("c_scr", [128, HT], BF16)
    dscr("qh", [INNER_A, EXT], BF16)
    dscr("kh", [INNER_A, KH_W], BF16)
    dscr("vstat", [44 * 128, 768], BF16)
    dscr("dacc_a", [DMODEL, DACC_W], F32)
    dscr("dacc_b", [DMODEL, DACC_W], F32)
    dscr("dbg", [128, 2048], BF16)
    io["cc_in"] = nc.dram_tensor("cc_in", [DSTATE, 1], F32)
    io["cc_out"] = nc.dram_tensor("cc_out", [DSTATE, 1], F32)

    with ExitStack() as ctx:
        p = P()
        p.nc, p.ctx, p.io, p.dev = nc, ctx, io, dev
        p.abf = ctx.enter_context(nc.sbuf_tensor("abf", [128, ABF_W], BF16))
        p.af = ctx.enter_context(nc.sbuf_tensor("af", [128, AF_W], F32))
        p.af16 = ctx.enter_context(nc.sbuf_tensor("af16", [128, 1024], FP16))
        p.af16b = ctx.enter_context(nc.sbuf_tensor("af16b", [128, 1024], FP16))
        p.eye_bf = ctx.enter_context(nc.sbuf_tensor("sb_eye_bf", [128, 128], BF16))
        p.eye_f32 = ctx.enter_context(nc.sbuf_tensor("sb_eye_f32", [128, 128], F32))
        p.ones_bf = ctx.enter_context(nc.sbuf_tensor("sb_ones_bf", [128, 128], BF16))
        p.blkdiag = ctx.enter_context(nc.sbuf_tensor("sb_blkdiag", [128, 128], BF16))
        p.masks = ctx.enter_context(nc.sbuf_tensor("masks", [128, 384 + W], BF16))
        p.bias = ctx.enter_context(nc.sbuf_tensor("bias", [128, 288], F32))
        p.flags = ctx.enter_context(nc.sbuf_tensor("flags", [128, 8], F32))
        p.small = ctx.enter_context(nc.sbuf_tensor("small", [128, 80], F32))
        p.pf = [ctx.enter_context(nc.psum_tensor(f"pf{i}", [128, 512], F32))
                for i in range(6)]
        p.pb = [ctx.enter_context(nc.psum_tensor(f"pb{i}", [128, 1024], BF16))
                for i in range(2)]
        for nm in ["ld", "wl", "mm", "ev", "sc", "dd", "st", "cc", "g0", "g1",
                   "g2", "g3", "a0", "a1", "a2", "a3", "a4", "a5", "a6", "a7",
                   "d0", "d1", "d2", "d3", "d4", "d5", "d6", "d7", "d8", "d9"]:
            setattr(p, "s_" + nm, Sem(ctx, nc, nm))
        p.BOFF = {}
        steps = [lambda: load_consts(p),
                 lambda: phase_p0(p),
                 lambda: phase_p1a(p),
                 lambda: phase_qkv(p, 0),
                 lambda: phase_p1b(p),
                 lambda: phase_attn(p, 0),
                 lambda: phase_mlp(p, 0),
                 lambda: phase_qkv(p, 1),
                 lambda: phase_attn(p, 1),
                 lambda: phase_mlp(p, 1),
                 lambda: phase_fin(p)]
        if upto is None:
            upto = len(steps)
        for f in steps[:upto]:
            f()
    return nc, io


def load_consts(p):
    nc, io = p.nc, p.io
    s_ld, s_mm, s_ev = p.s_ld, p.s_mm, p.s_ev
    vtmp = p.af[:, 0:128]
    vps = p.pf[0][:, 0:128]
    cs = [0]

    with nc.Block("consts") as blk:
        @blk.gpsimd
        def _(g):
            g = Drained(g)
            for dst, src in [(p.eye_bf, "eye_bf"), (p.eye_f32, "eye_f32"),
                             (p.ones_bf, "ones_bf"), (p.blkdiag, "blkdiag_bf")]:
                s_ld.inc(g.dma_start(dst[:, :], io[src][:, :]), 16)
            s_ld.inc(g.dma_start(p.masks[:, 0:128], io["m_own"][:, :]), 16)
            s_ld.inc(g.dma_start(p.masks[:, 128:256], io["m_look"][:, :]), 16)
            s_ld.inc(g.dma_start(p.masks[:, 256:384], io["m0_d"][:, :]), 16)
            s_ld.inc(g.dma_start(p.masks[:, 384:384 + W], io["m0_rect"][:, :]), 16)
            s_ld.inc(g.dma_start(p.flags[:, 0:1], io["evenm"][:, :]), 16)
            s_ld.inc(g.dma_start(p.flags[:, 1:2], io["oddm"][:, :]), 16)
            s_ld.inc(g.dma_start(p.flags[:, 2:3], io["haloflag"][:, :]), 16)
            for l in range(2):
                s_ld.inc(g.dma_start(
                    p.af[0:1, 512 + 128 * l:576 + 128 * l],
                    ap2(io["L0_ffn_q_scale"], l * DH, [[64, 1], [1, 64]])), 16)
                s_ld.inc(g.dma_start(
                    p.af[0:1, 576 + 128 * l:640 + 128 * l],
                    ap2(io["L0_ffn_k_scale"], l * DH, [[64, 1], [1, 64]])), 16)
            g.memset(p.abf[:, 0:768], 0.0)
            g.memset(p.af[:, 1408:1440], 0.0)
            g.memset(p.flags[:, 3:4], EPS)
            g.memset(p.af[0:1, 1280:1408], 1.0)
            p.s_a5.inc(g.memset(p.flags[:, 4:5], 0.0))
            g.wait_ge(s_ld.h, s_ld.v)

        @blk.sync
        def _(sy):
            if CONSTS_LIMIT == -4:
                return
            sy.wait_ge(p.s_a5.h, p.s_a5.v)
            p.s_st.inc(sy.dma_start(
                ap2(io["dacc_b"], 0,
                    [[DACC_W, 128], [128 * DACC_W, 8], [1, 4]]),
                ap2(p.af, 1408, [[AF_W, 128], [4, 8], [1, 4]])), 16)
            p.s_st.inc(sy.dma_start(
                ap2(io["xnfm"], 0, [[1 + EXT, 128], [128 * (1 + EXT), 8], [1, 1]]),
                ap2(p.abf, 0, [[ABF_W, 128], [1, 8], [1, 1]]),
                allow_slow_non_contiguous=True), 16)
            for c in range(4):
                p.s_st.inc(sy.dma_start(
                    ap2(io["kh"], c * 128 * KH_W, [[KH_W, 128], [1, 384]]),
                    ap2(p.abf, 0, [[ABF_W, 128], [1, 384]])), 16)
            for blki in range(3):
                p.s_st.inc(sy.dma_start(
                    ap2(io["vstat"], blki * 128 * 768, [[768, 128], [1, 768]]),
                    ap2(p.abf, 0, [[ABF_W, 128], [1, 768]])), 16)
            sy.wait_ge(p.s_st.h, p.s_st.v)

    def vec(name, dram, length, offset=0):
        nchunk = (length + 127) // 128
        col = cs[0]
        cs[0] += nchunk
        p.BOFF[name] = col
        with nc.Block("v_" + name) as blk:
            @blk.gpsimd
            def _(g):
                g = Drained(g)
                p.s_a5.inc(g.memset(vtmp[:, :], 0.0))
                g.wait_ge(p.s_a5.h, p.s_a5.v)
                full, rem = divmod(length, 128)
                if full:
                    s_ld.inc(g.dma_start(
                        vtmp[0:full, 0:128],
                        ap2(dram, offset, [[128, full], [1, 128]])), 16)
                if rem:
                    s_ld.inc(g.dma_start(
                        vtmp[full:full + 1, 0:rem],
                        ap2(dram, offset + full * 128, [[rem, 1], [1, rem]])), 16)

            @blk.tensor
            def _(t):
                t.wait_ge(s_ld.h, s_ld.v)
                s_mm.inc(t.transpose(vps, vtmp[:, :], p.eye_f32[:, :]))

            @blk.vector
            def _(v):
                v = Drained(v)
                v.wait_ge(s_mm.h, s_mm.v)
                s_ev.inc(v.tensor_copy(p.bias[:, col:col + nchunk],
                                       vps[:, 0:nchunk]))

    veclist = [(f"conv_w{k}", io[f"conv_w{k}"], INNER_M, 0) for k in range(3)]
    veclist += [("b_in", io["L0_attn_b_in"], 2 * INNER_M, 0),
               ("conv_b", io["L0_attn_conv_b"], INNER_M, 0),
               ("b_si", io["L0_attn_b_si"], INNER_M, 0),
               ("D", io["L0_attn_D"], INNER_M, 0),
               ("b_out", io["L0_attn_b_out"], DMODEL, 0)]
    for l in range(2):
        veclist += [(f"ln_g{l}", io["L0_ffn_ln_g"], DMODEL, l * DMODEL),
                    (f"ln_b{l}", io["L0_ffn_ln_b"], DMODEL, l * DMODEL),
                    (f"b1a{l}", io["L0_ffn_b1"], HID, l * 2 * HID),
                    (f"b1g{l}", io["L0_ffn_b1"], HID, l * 2 * HID + HID),
                    (f"b2{l}", io["L0_ffn_b2"], DMODEL, l * DMODEL)]
    if CONSTS_LIMIT is not None:
        veclist = veclist[:max(0, CONSTS_LIMIT)]
    for nm_, dram_, ln_, off_ in veclist:
        vec(nm_, dram_, ln_, off_)
    if CONSTS_LIMIT == -4:
        pass
    if CONSTS_LIMIT is not None and CONSTS_LIMIT == -1:
        return
    SKIP_TAIL = CONSTS_LIMIT == -2
    SKIP_BOUND = CONSTS_LIMIT == -3
    if CONSTS_LIMIT in (-4, -5, -6):
        SKIP_TAIL = SKIP_BOUND = False

    if SKIP_TAIL:
        with nc.Block("bound") as blk:
            @blk.gpsimd
            def _(g):
                g = Drained(g)
                for l in range(2):
                    s_ld.inc(g.dma_start(
                        p.af[0:1, 0 + 128 * l:64 + 128 * l],
                        ap2(io["L0_ffn_q_scale"], l * DH, [[64, 1], [1, 64]])), 16)
                    s_ld.inc(g.dma_start(
                        p.af[0:1, 64 + 128 * l:128 + 128 * l],
                        ap2(io["L0_ffn_k_scale"], l * DH, [[64, 1], [1, 64]])), 16)
            @blk.vector
            def _(v):
                v = Drained(v)
                v.wait_ge(s_ld.h, s_ld.v)
                s_ev.inc(v.tensor_copy(p.af[0:1, 192:256], p.af[0:1, 0:64]))
        return
    with nc.Block("v_tail") as blk:
        @blk.gpsimd
        def _(g):
            g = Drained(g)
            for nm, dram, base in [("b_dt", io["L0_attn_b_dt"], 0),
                                   ("b_B", io["L0_attn_b_B"], 64),
                                   ("b_C", io["L0_attn_b_C"], 0)]:
                col = cs[0]
                cs[0] += 1
                p.BOFF[nm] = col
                s_ld.inc(g.dma_start(p.bias[base:base + 64, col:col + 1],
                                     ap2(dram, 0, [[1, 64], [1, 1]]),
                                     allow_slow_non_contiguous=True), 16)
            for l in range(2):
                for j, nm in [(0, "L0_ffn_q_scale"), (1, "L0_ffn_k_scale")]:
                    cc = l * 2 + j
                    for half in range(2):
                        s_ld.inc(g.dma_start(
                            p.small[64 * half:64 * half + 64, cc:cc + 1],
                            ap2(io[nm], l * DH, [[1, 64], [1, 1]]),
                            allow_slow_non_contiguous=True), 16)
            g.wait_ge(s_ld.h, s_ld.v)
    assert cs[0] <= 288, cs[0]

    if SKIP_BOUND:
        return
    with nc.Block("bound") as blk:
        @blk.vector
        def _(v):
            v = Drained(v)
            v.wait_ge(s_ld.h, s_ld.v)
            for l in range(2):
                v.tensor_tensor(p.af[0:1, 1024 + 64 * l:1024 + 64 * l + 64],
                                p.af[0:1, 512 + 128 * l:512 + 128 * l + 64],
                                p.af[0:1, 512 + 128 * l + 64:512 + 128 * l + 128],
                                ALU.mult)
                v.reduce_max(p.small[0:1, 56 + l:57 + l],
                             p.af[0:1, 1024 + 64 * l:1024 + 64 * l + 64],
                             axis=AX.X, apply_absolute_value=True)
                s_ev.inc(v.tensor_scalar(p.small[0:1, 56 + l:57 + l],
                                         p.small[0:1, 56 + l:57 + l],
                                         -8.0, None, ALU.mult))

        @blk.tensor
        def _(t):
            t.wait_ge(s_ev.h, s_ev.v)
            s_mm.inc(t.matmul(p.pf[1][:, 0:2], p.af[0:1, 1280:1408],
                              p.small[0:1, 56:58], start=True, stop=True))

        @blk.scalar
        def _(s):
            s = Drained(s)
            s.wait_ge(s_mm.h, s_mm.v)
            s_ev.inc(s.copy(p.small[:, 6:8], p.pf[1][:, 0:2]))


def phase_p0(p):
    nc, io = p.nc, p.io
    s_a, s_b, s_t, s_e2, s_e3 = (p.s_a0, p.s_a1, p.s_a2, p.s_a3, p.s_a4)
    s_ldp = [p.s_d0, p.s_d1]
    s_fmp = [p.s_d2, p.s_d3]
    s_up = [p.s_d4, p.s_d5]
    ab = Alloc(p.abf, ABF_W)
    af = Alloc(p.af, AF_W)
    xnbf = [ab.tile(1024) for _ in range(2)]
    sqj = ab.tile(1024)
    fmst = [ab.tile(1024) for _ in range(2)]
    xt = [af.tile(1024) for _ in range(2)]
    xn32 = af.tile(1024)
    ut = [af.tile(1024) for _ in range(2)]
    stt = af.tile(16)
    NT = EXT // 128
    OWN0 = HALO // 128
    e20, e30, a0, b0, t0 = s_e2.v, s_e3.v, s_a.v, s_b.v, s_t.v
    ld_mk, fm_mk, u_mk = {}, {}, {}
    for i in range(NT):
        ld_mk[i] = s_ldp[i % 2].v + 16 * (i // 2 + 1)
        fm_mk[i] = s_fmp[i % 2].v + 16 * (i // 2 + 1)
    ucnt = [0, 0]
    for i in range(OWN0, NT):
        ucnt[i % 2] += 1
        u_mk[i] = s_up[i % 2].v + 16 * ucnt[i % 2]

    with nc.Block("p0") as blk:
        @blk.sync
        def _(s):
            for i in range(NT):
                if i >= 2:
                    s.wait_ge(s_e3.h, e30 + i - 1)
                assert s_ldp[i % 2].inc(
                    s.dma_start(xt[i % 2][:, :],
                                io["xs"][i * 128:(i + 1) * 128, :]),
                    16) == ld_mk[i]
                s.wait_ge(s_e2.h, e20 + 8 * (i + 1))
                assert s_fmp[i % 2].inc(s.dma_start(
                    ap2(io["xnfm"], 1 + i * 128,
                        [[1 + EXT, 128], [128 * (1 + EXT), 8], [1, 128]]),
                    ap2(p.abf, fmst[i % 2].col,
                        [[ABF_W, 128], [128, 8], [1, 128]])), 16) == fm_mk[i]
                if i >= OWN0:
                    s.wait_ge(s_e3.h, e30 + i + 1)
                    assert s_up[i % 2].inc(s.dma_start(
                        io["u_scr"][(i - OWN0) * 128:(i - OWN0 + 1) * 128, :],
                        ut[i % 2][:, :]), 16) == u_mk[i]
            for sp in (s_fmp[0], s_fmp[1], s_up[0], s_up[1]):
                s.wait_ge(sp.h, sp.v)

        @blk.vector
        def _(v):
            v = Drained(v)
            for i in range(NT):
                x = xt[i % 2]
                v.wait_ge(s_ldp[i % 2].h, ld_mk[i])
                if i >= 1:
                    v.wait_ge(s_b.h, b0 + i)
                v.scalar_tensor_tensor(sqj[:, :], x[:, :], 1.0, x[:, :],
                                       ALU.mult, ALU.mult, accum_out=stt[:, 1:2])
                v.reduce_sum(stt[:, 0:1], x[:, :], axis=AX.X)
                v.tensor_scalar(stt[:, 2:3], stt[:, 0:1], 1.0 / DMODEL, None, ALU.mult)
                v.tensor_scalar(stt[:, 3:4], stt[:, 1:2], 1.0 / DMODEL, None, ALU.mult)
                v.tensor_tensor(stt[:, 4:5], stt[:, 2:3], stt[:, 2:3], ALU.mult)
                s_a.inc(v.tensor_tensor(stt[:, 3:4], stt[:, 3:4], stt[:, 4:5],
                                        ALU.subtract))
                for j in range(8):
                    v.wait_ge(s_t.h, t0 + i * 8 + j + 1)
                    s_e2.inc(v.tensor_copy(fmst[i % 2][:, j * 128:(j + 1) * 128],
                                           p.pb[j % 2][:, 0:128]))
                if i >= OWN0:
                    v.wait_ge(s_b.h, b0 + i + 1)
                    if i - 2 in u_mk:
                        v.wait_ge(s_up[i % 2].h, u_mk[i - 2])
                    s_e3.inc(v.tensor_add(ut[i % 2][:, :], x[:, :],
                                          xn32[:, :]))
                else:
                    s_e3.inc(v.engine_nop())

        @blk.scalar
        def _(s):
            s = Drained(s)
            for i in range(NT):
                x = xt[i % 2]
                s.wait_ge(s_a.h, a0 + i + 1)
                if i >= 1:
                    s.wait_ge(s_e3.h, e30 + i)
                s.activation(stt[:, 5:6], stt[:, 3:4], AF.Ln, bias=p.flags[:, 3:4], scale=1.0)
                s.activation(stt[:, 6:7], stt[:, 5:6], AF.Exp, bias=0.0, scale=-0.5)
                s.mul(stt[:, 7:8], stt[:, 2:3], stt[:, 6:7])
                s.mul(stt[:, 7:8], stt[:, 7:8], -1.0)
                last = s.activation(xnbf[i % 2][:, :], x[:, :], AF.Identity,
                                    bias=stt[:, 7:8], scale=stt[:, 6:7])
                if i >= OWN0:
                    last = s.activation(xn32[:, :], x[:, :], AF.Identity,
                                        bias=stt[:, 7:8], scale=stt[:, 6:7])
                s_b.inc(last)

        @blk.tensor
        def _(t):
            for i in range(NT):
                t.wait_ge(s_b.h, b0 + i + 1)
                if i >= 2:
                    t.wait_ge(s_fmp[i % 2].h, fm_mk[i - 2])
                for j in range(8):
                    k = i * 8 + j
                    if k >= 2:
                        t.wait_ge(s_e2.h, e20 + k - 1)
                    s_t.inc(t.transpose(p.pb[j % 2][:, 0:128],
                                        xnbf[i % 2][:, j * 128:(j + 1) * 128],
                                        p.eye_bf[:, :]))


def phase_p1a(p):
    nc, io = p.nc, p.io
    BO = p.BOFF
    ab = Alloc(p.abf, ABF_W)
    af = Alloc(p.af, AF_W)
    w_in = ab.tile(8 * 4096)
    wdtB = ab.tile(16 * 128)
    wC = ab.tile(16 * 64)
    xslab = [ab.tile(8 * 512) for _ in range(2)]
    work = [ab.tile(514) for _ in range(16)]
    xmn = [ab.tile(512) for _ in range(16)]
    C_bf = ab.tile(HT)
    pp_bf = ab.tile(HT)
    zz_bf = ab.tile(HT)
    xm2 = [ab.tile(512) for _ in range(2)]
    gt4 = [Tile(p.af16[:, 0:512], 0), Tile(p.af16[:, 512:1024], 512),
           Tile(p.af16b[:, 0:512], 0), Tile(p.af16b[:, 512:1024], 512)]
    cva = af.tile(512)
    cvb = [af.tile(512) for _ in range(2)]
    tmp_ev = af.tile(512)
    dt_s = af.tile(HT)
    B_s = af.tile(HT)
    st_s = Tile(p.af[:, 0:HT], 0)
    NTT = 9
    s_wl, s_mm, s_sc, s_cv, s_xm, s_w = (p.s_wl, p.s_mm, p.s_sc,
                                         p.s_ev, p.s_a0, p.s_st)
    s_d2, s_d3 = p.s_a1, p.s_a2
    s_ldp = [p.s_d0, p.s_d1]
    s_xmp = [p.s_d2b, p.s_d3b] if False else [p.s_d6, p.s_d7]
    s_gtp = [p.s_d8, p.s_d9]
    s_ld2 = [p.s_d4, p.s_d5]

    with nc.Block("p1a_w") as blk:
        @blk.gpsimd
        def _(g):
            g = Drained(g)
            for c in range(8):
                s_wl.inc(g.dma_start(
                    ap2(p.abf, w_in.col + c * 4096, [[ABF_W, 128], [1, 4096]]),
                    io["L0_attn_W_in"][c * 128:(c + 1) * 128, :]), 16)
            for c in range(16):
                s_wl.inc(g.dma_start(
                    ap2(p.abf, wdtB.col + c * 128, [[ABF_W, 128], [1, 64]]),
                    io["L0_attn_W_dt"][c * 128:(c + 1) * 128, :]), 16)
                s_wl.inc(g.dma_start(
                    ap2(p.abf, wdtB.col + c * 128 + 64, [[ABF_W, 128], [1, 64]]),
                    io["L0_attn_W_B"][c * 128:(c + 1) * 128, :]), 16)
                s_wl.inc(g.dma_start(
                    ap2(p.abf, wC.col + c * 64, [[ABF_W, 128], [1, 64]]),
                    io["L0_attn_W_C"][c * 128:(c + 1) * 128, :]), 16)
            g.memset(zz_bf[:, :], 0.0)
            for fc in range(16):
                g.memset(work[fc][:, 0:2], 0.0)
            g.wait_ge(s_wl.h, s_wl.v)

    mm0, sc0, cv0, xm0 = s_mm.v, s_sc.v, s_cv.v, s_xm.v
    ld_mk = {t: s_ldp[t % 2].v + 16 * (t // 2 + 1) for t in range(NTT)}
    xm_mk = {}
    gt_sp = {}
    gb = [s_gtp[0].v, s_gtp[1].v]
    for tt in range(1, NTT):
        xm_mk[tt] = s_xmp[tt % 2].v + 16 * 16 * ((tt - 1) // 2 + 1)
        for fc in range(16):
            gi = (tt - 1) * 16 + fc
            gt_sp[gi] = gb[gi % 2] + 16 * (gi // 2 + 1)

    with nc.Block("p1a1") as blk:
        @blk.sync
        def _(s):
            for t in range(NTT + 1):
                if t >= 2:
                    tt = t - 1
                    for fc in range(16):
                        s.wait_ge(s_xm.h, xm0 + tt * 16 + fc + 1)
                        mk = s_xmp[tt % 2].inc(s.dma_start(
                            io["xmain"][fc * 128:(fc + 1) * 128,
                                        (tt - 1) * 512:tt * 512],
                            xmn[fc][:, :]), 16)
                        if fc == 15:
                            assert mk == xm_mk[tt]
                    for fc in range(16):
                        gi = (tt - 1) * 16 + fc
                        s.wait_ge(s_sc.h, sc0 + tt * 32 + 16 + fc + 1)
                        if gi >= 2:
                            s.wait_ge(s_gtp[gi % 2].h, gt_sp[gi - 2])
                        assert s_gtp[gi % 2].inc(s.dma_start(
                            io["sgate"][fc * 128:(fc + 1) * 128,
                                        (tt - 1) * 512:tt * 512],
                            gt4[fc % 4][:, :]), 16) == gt_sp[gi]
                if t < NTT:
                    if t >= 2:
                        s.wait_ge(s_mm.h, mm0 + (t - 1) * 32)
                    assert s_ldp[t % 2].inc(s.dma_start(
                        ap2(p.abf, xslab[t % 2].col,
                            [[ABF_W, 128], [512, 8], [1, 512]]),
                        ap2(io["xnfm"], 1 + HALO - 512 + t * 512,
                            [[1 + EXT, 128], [128 * (1 + EXT), 8], [1, 512]])),
                        16) == ld_mk[t]

        @blk.tensor
        def _(t_):
            for t in range(NTT):
                t_.wait_ge(s_ldp[t % 2].h, ld_mk[t])
                for fc in range(32):
                    k = t * 32 + fc
                    if k >= 2:
                        t_.wait_ge(s_sc.h, sc0 + k - 1)
                    ps = p.pf[fc % 2]
                    for c in range(8):
                        mi = t_.matmul(ps[:, :],
                                       ap2(p.abf, w_in.col + c * 4096 + fc * 128,
                                           [[ABF_W, 128], [1, 128]]),
                                       ap2(p.abf, xslab[t % 2].col + c * 512,
                                           [[ABF_W, 128], [1, 512]]),
                                       start=(c == 0), stop=(c == 7))
                    s_mm.inc(mi)

        @blk.scalar
        def _(s):
            s = Drained(s)
            for t in range(NTT):
                for fc in range(16):
                    s.wait_ge(s_mm.h, mm0 + t * 32 + fc + 1)
                    if t >= 1:
                        s.wait_ge(s_cv.h, cv0 + ((t - 1) * 16 + fc) * 2 + 2)
                    s_sc.inc(s.activation(
                        work[fc][:, 2:514], p.pf[fc % 2][:, :], AF.Identity,
                        bias=p.bias[:, BO["b_in"] + fc:BO["b_in"] + fc + 1],
                        scale=1.0))
                    if fc >= 1:
                        fp = fc - 1
                        s.wait_ge(s_cv.h, cv0 + (t * 16 + fp) * 2 + 1)
                        if fp == 0 and (t - 1) in xm_mk:
                            s.wait_ge(s_xmp[(t - 1) % 2].h, xm_mk[t - 1])
                        s_xm.inc(s.activation(
                            xmn[fp][:, :], cvb[fp % 2][:, :], AF.Silu,
                            bias=p.bias[:, BO["conv_b"] + fp:BO["conv_b"] + fp + 1],
                            scale=1.0))
                fp = 15
                s.wait_ge(s_cv.h, cv0 + (t * 16 + fp) * 2 + 1)
                s_xm.inc(s.activation(
                    xmn[fp][:, :], cvb[fp % 2][:, :], AF.Silu,
                    bias=p.bias[:, BO["conv_b"] + fp:BO["conv_b"] + fp + 1],
                    scale=1.0))
                for fc in range(16):
                    s.wait_ge(s_mm.h, mm0 + t * 32 + 16 + fc + 1)
                    gi = ((t - 1) * 16 + fc - 4) if fc >= 4 \
                        else ((t - 2) * 16 + fc + 12)
                    if gi in gt_sp:
                        s.wait_ge(s_gtp[gi % 2].h, gt_sp[gi])
                    s_sc.inc(s.activation(
                        gt4[fc % 4][:, :], p.pf[fc % 2][:, :], AF.Sigmoid,
                        bias=p.bias[:, BO["b_in"] + 16 + fc:BO["b_in"] + 17 + fc],
                        scale=1.0))

        @blk.vector
        def _(v):
            v = Drained(v)
            for t in range(NTT):
                for fc in range(16):
                    v.wait_ge(s_sc.h, sc0 + t * 32 + fc + 1)
                    wk = work[fc]
                    c0 = BO["conv_w0"] + fc
                    c1 = BO["conv_w1"] + fc
                    c2 = BO["conv_w2"] + fc
                    v.tensor_scalar(cva[:, :], wk[:, 1:513],
                                    p.bias[:, c1:c1 + 1], None, ALU.mult)
                    v.scalar_tensor_tensor(cvb[fc % 2][:, :], wk[:, 0:512],
                                           p.bias[:, c0:c0 + 1], cva[:, :],
                                           ALU.mult, ALU.add)
                    s_cv.inc(v.scalar_tensor_tensor(
                        cvb[fc % 2][:, :], wk[:, 2:514],
                        p.bias[:, c2:c2 + 1], cvb[fc % 2][:, :],
                        ALU.mult, ALU.add))
                    s_cv.inc(v.tensor_copy(wk[:, 0:2], wk[:, 512:514]))

    mm1, sc1, d20, d30 = s_mm.v, s_sc.v, s_d2.v, s_d3.v
    ld2_mk = {}
    for t in range(8):
        for fc in range(16):
            k = t * 16 + fc
            ld2_mk[k] = s_ld2[k % 2].v + 16 * (k // 2 + 1)
    with nc.Block("p1a2") as blk:
        @blk.sync
        def _(s):
            for sp in (s_xmp[0], s_xmp[1], s_gtp[0], s_gtp[1]):
                s.wait_ge(sp.h, sp.v)
            for t in range(8):
                for fc in range(16):
                    k = t * 16 + fc
                    if k >= 2:
                        s.wait_ge(s_mm.h, mm1 + k - 1)
                    assert s_ld2[k % 2].inc(s.dma_start(
                        xm2[fc % 2][:, :],
                        io["xmain"][fc * 128:(fc + 1) * 128,
                                    t * 512:(t + 1) * 512]), 16) == ld2_mk[k]

        @blk.tensor
        def _(t_):
            for t in range(8):
                for fc in range(16):
                    k = t * 16 + fc
                    t_.wait_ge(s_ld2[k % 2].h, ld2_mk[k])
                    if fc == 0 and t >= 1:
                        t_.wait_ge(s_sc.h, sc1 + t)
                    t_.matmul(p.pf[2][:, :],
                              ap2(p.abf, wdtB.col + fc * 128,
                                  [[ABF_W, 128], [1, 128]]),
                              xm2[fc % 2][:, :],
                              start=(fc == 0), stop=(fc == 15))
                    mi = t_.matmul(p.pf[3][0:64, :],
                                   ap2(p.abf, wC.col + fc * 64,
                                       [[ABF_W, 128], [1, 64]]),
                                   xm2[fc % 2][:, :],
                                   start=(fc == 0), stop=(fc == 15))
                    s_mm.inc(mi)

        @blk.scalar
        def _(s):
            s = Drained(s)
            for t in range(8):
                s.wait_ge(s_mm.h, mm1 + (t + 1) * 16)
                if t >= 1:
                    s.wait_ge(s_d3.h, d30 + t * 3)
                s.activation(tmp_ev[0:64, :], p.pf[2][0:64, :], AF.Sigmoid,
                             bias=p.bias[0:64, BO["b_dt"]:BO["b_dt"] + 1],
                             scale=1.0)
                s.activation(tmp_ev[64:128, :], p.pf[2][64:128, :], AF.Identity,
                             bias=p.bias[64:128, BO["b_B"]:BO["b_B"] + 1],
                             scale=1.0)
                s_sc.inc(s.activation(cva[0:64, :], p.pf[3][0:64, :], AF.Identity,
                                      bias=p.bias[0:64, BO["b_C"]:BO["b_C"] + 1],
                                      scale=1.0))

        @blk.vector
        def _(v):
            v = Drained(v)
            for t in range(8):
                v.wait_ge(s_sc.h, sc1 + t + 1)
                half, c0 = t // 4, (t * 512) % HT
                b = 64 * half
                s_d3.inc(v.tensor_copy(dt_s[b:b + 64, c0:c0 + 512],
                                       tmp_ev[0:64, :]))
                s_d3.inc(v.tensor_copy(B_s[b:b + 64, c0:c0 + 512],
                                       tmp_ev[64:128, :]))
                s_d3.inc(v.tensor_copy(C_bf[b:b + 64, c0:c0 + 512],
                                       cva[0:64, :]))

    with nc.Block("p1a3") as blk:
        @blk.vector
        def _(v):
            v = Drained(v)
            v.tensor_tensor(B_s[:, :], dt_s[:, :], B_s[:, :], ALU.mult)
            v.tensor_scalar(dt_s[:, :], dt_s[:, :], -1.0, 1.0, ALU.mult, ALU.add)
            v.tensor_tensor_scan(st_s[:, :], dt_s[:, :], B_s[:, :], 0.0,
                                 ALU.mult, ALU.add)
            v.tensor_tensor_scan(pp_bf[:, :], dt_s[:, :], zz_bf[:, :], 1.0,
                                 ALU.mult, ALU.add)
            v.tensor_copy(p.small[64:128, 61:62], st_s[0:64, HT - 1:HT])
            v.tensor_copy(p.small[64:128, 62:63], pp_bf[0:64, HT - 1:HT])
            v.scalar_tensor_tensor(st_s[64:128, :], pp_bf[64:128, :],
                                   p.small[64:128, 61:62], st_s[64:128, :],
                                   ALU.mult, ALU.add)
            v.tensor_scalar(pp_bf[64:128, :], pp_bf[64:128, :],
                            p.small[64:128, 62:63], None, ALU.mult)
            s_cv.inc(v.tensor_tensor(p.small[64:128, 63:64],
                                     st_s[64:128, HT - 1:HT],
                                     p.flags[64:128, 0:1], ALU.mult))

        @blk.sync
        def _(s):
            s.wait_ge(s_cv.h, s_cv.v)
            s_w.inc(s.dma_start(io["st_scr"][:, :], st_s[:, :]), 16)
            s_w.inc(s.dma_start(io["pp_scr"][:, :], pp_bf[:, :]), 16)
            s_w.inc(s.dma_start(io["c_scr"][:, :], C_bf[:, :]), 16)
            s_w.inc(s.dma_start(ap2(io["cc_in"], 0, [[1, 64], [1, 1]]),
                                p.small[64:128, 63:64]), 16)
            s.wait_ge(s_w.h, s_w.v)

        @blk.gpsimd
        def _(g):
            g = Drained(g)
            g.wait_ge(s_w.h, s_w.v)
            s_cc_i = p.s_cc
            s_cc_i.inc(g.collective_compute(
                "AllReduce", ALU.add,
                replica_groups=[[0, 1], [2, 3], [4, 5], [6, 7]],
                ins=[io["cc_in"].ap().opt()],
                outs=[io["cc_out"].ap().opt()]))




def phase_qkv(p, l):
    nc, io = p.nc, p.io
    ab = Alloc(p.abf, ABF_W)
    af = Alloc(p.af, AF_W)
    wq = ab.tile(8 * 1536)
    xnsl = [ab.tile(8 * 513) for _ in range(2)]
    dsl = [ab.tile(8 * 513) for _ in range(2)] if l == 1 else None
    sqb = ab.tile(512)
    qhb2 = [ab.tile(512) for _ in range(2)]
    vcf = ab.tile(512)
    vsb = [ab.tile(768) for _ in range(4)]
    tabs = [[af.tile(512) for _ in range(4)] for _ in range(2)]
    nn = af.tile(512)
    rstd = af.tile(512)
    qn = af.tile(512)
    qrot = af.tile(512)
    (s_mm, s_sq, s_m2, s_fr2, s_rs, s_fr, s_sp, s_t, s_vev) = (
        p.s_mm, p.s_sc, p.s_a4, p.s_a5, p.s_a6, p.s_a0, p.s_a1,
        p.s_a2, p.s_a3)
    s_hs = p.s_ev
    s_ldp = [p.s_d0, p.s_d1]
    s_gp = [p.s_g0, p.s_g2]
    s_tabp = [p.s_d6, p.s_d7]
    s_qsp = [p.s_d8, p.s_d9]
    s_vwp = [p.s_d2, p.s_d3]

    if l == 0:
        tlist = [(-HALO + 512 * t, 512) for t in range(10)] + [(3968, 128)]
    else:
        tlist = [(-640, 128)] + [(-512 + 512 * t, 512) for t in range(9)]
    TABN = ["cs1q", "cs2q", "cs1k", "cs2k"]
    NTI = len(tlist)

    with nc.Block(f"qkv{l}_w") as blk:
        @blk.gpsimd
        def _(g):
            g = Drained(g)
            for c in range(8):
                p.s_wl.inc(g.dma_start(
                    ap2(p.abf, wq.col + c * 1536, [[ABF_W, 128], [1, 1536]]),
                    ap2(io["L0_ffn_wqkv"], l * DMODEL * 1536 + c * 128 * 1536,
                        [[1536, 128], [1, 1536]])), 16)
            for j in range(4):
                for hp in range(4):
                    g.memset(vsb[j][:, 192 * hp + 64:192 * hp + 128], 1.0)
            g.wait_ge(p.s_wl.h, p.s_wl.v)

    mm0, sq0, m20, fr20, rs0, fr0, sp0 = (s_mm.v, s_sq.v, s_m2.v,
                                          s_fr2.v, s_rs.v, s_fr.v, s_sp.v)
    t0v, vev0, hs0 = s_t.v, s_vev.v, s_hs.v
    mk_mm, mk_sq, mk_m2, mk_fr2, mk_rs, mk_fr, mk_sp = {}, {}, {}, {}, {}, {}, {}
    mk_t, mk_vev, mk_vw, mk_spw, mk_hs, mk_ld, mk_tab = {}, {}, {}, {}, {}, {}, {}
    tp_order = []
    mk_g0 = {}
    mk_qs = {}
    for ti, (tau0, n) in enumerate(tlist):
        nj = n // 128
        mk_ld[ti] = s_ldp[ti % 2].v + 16 * (ti // 2 + 1)
        mk_g0[ti] = s_gp[ti % 2].v + 32 * (ti // 2 + 1)
        mk_hs[ti] = hs0 + 8 * (ti + 1)
        mk_tab[ti] = s_tabp[ti % 2].v + 128 * (ti // 2 + 1)
        for fc in range(8):
            kq = ti * 8 + fc
            mk_qs[(ti, fc)] = s_qsp[kq % 2].v + 16 * (kq // 2 + 1)
        for fc in range(12):
            mk_mm[(ti, fc)] = mm0 + ti * 12 + fc + 1
            mk_fr[(ti, fc)] = fr0 + ti * 12 + fc + 1
        for fc in range(8):
            mk_sq[(ti, fc)] = sq0 + ti * 8 + fc + 1
            mk_m2[(ti, fc)] = m20 + ti * 8 + fc + 1
            mk_fr2[(ti, fc)] = fr20 + ti * 8 + fc + 1
            mk_rs[(ti, fc)] = rs0 + ti * 8 + fc + 1
            mk_sp[(ti, fc)] = sp0 + ti * 8 + fc + 1
        for hp in range(4):
            for j in range(nj):
                tp_order.append((ti, hp, j))
    for k, key in enumerate(tp_order):
        mk_t[key] = t0v + k + 1
        mk_vev[key] = vev0 + k + 1
    vw_par = [s_vwp[0].v, s_vwp[1].v]
    for ti, (tau0, n) in enumerate(tlist):
        vw_par[ti % 2] += 16 * (n // 128)
        mk_vw[ti] = vw_par[ti % 2]

    with nc.Block(f"qkv{l}") as blk:
        @blk.vector
        def _(v):
            v = Drained(v)
            for ti, (tau0, n) in enumerate(tlist):
                nj = n // 128
                if l == 1:
                    v.wait_ge(s_ldp[ti % 2].h, mk_ld[ti])
                    v.wait_ge(s_gp[ti % 2].h, mk_g0[ti])
                    for c in range(8):
                        ins = v.tensor_tensor(
                            xnsl[ti % 2][:, c * 513:c * 513 + n + 1],
                            xnsl[ti % 2][:, c * 513:c * 513 + n + 1],
                            dsl[ti % 2][:, c * 513:c * 513 + n + 1], ALU.add)
                    assert s_hs.inc(ins, 8) == mk_hs[ti]
                for fc in range(8):
                    cc = l * 2 + (0 if fc < 4 else 1)
                    v.wait_ge(s_m2.h, mk_m2[(ti, fc)])
                    assert s_fr2.inc(
                        v.tensor_scalar(nn[:, 0:n], p.pf[3 + fc % 2][:, 0:n],
                                        1e-24, None, ALU.max)) == mk_fr2[(ti, fc)]
                    v.wait_ge(s_rs.h, mk_rs[(ti, fc)])
                    v.tensor_scalar(rstd[:, 0:n], rstd[:, 0:n],
                                    p.small[:, cc:cc + 1], None, ALU.mult)
                    assert s_fr.inc(
                        v.tensor_tensor(qn[:, 0:n], p.pf[fc % 3][:, 0:n],
                                        rstd[:, 0:n], ALU.mult)) == mk_fr[(ti, fc)]
                    if p.dev and l == 0 and ti == 2 and fc == 0:
                        v.tensor_copy(p.abf[:, 85024:85536], rstd[:, 0:512])
                        v.tensor_copy(p.abf[:, 84512:85024], qn[:, 0:512])
                        v.tensor_copy(p.abf[:, 85536:86048], nn[:, 0:512])
                    for bb in range(4):
                        src = [32, 0, 96, 64][bb]
                        eng = v if bb == 3 else v._e
                        eng.tensor_copy(qrot[32 * bb:32 * bb + 32, 0:n],
                                        qn[src:src + 32, 0:n])
                    tb1 = tabs[ti % 2][0 if fc < 4 else 2].ap
                    tb2 = tabs[ti % 2][1 if fc < 4 else 3].ap
                    v.wait_ge(s_tabp[ti % 2].h, mk_tab[ti])
                    v.tensor_tensor(qn[:, 0:n], qn[:, 0:n], tb1[:, 0:n], ALU.mult)
                    v.tensor_tensor(qrot[:, 0:n], qrot[:, 0:n], tb2[:, 0:n],
                                    ALU.mult)
                    kq = ti * 8 + fc
                    prev = (ti, fc - 2) if fc >= 2 else (ti - 1, fc + 6)
                    if prev in mk_qs:
                        v.wait_ge(s_qsp[kq % 2].h, mk_qs[prev])
                    qhb = qhb2[kq % 2]
                    if p.dev and l == 0 and ti == 4 and fc == 0:
                        v.tensor_copy(p.abf[:, 80000:80512], rstd[:, 0:n])
                        v.tensor_copy(p.abf[:, 80512:81024], qn[:, 0:n])
                        v.tensor_copy(p.abf[:, 81024:81536], qrot[:, 0:n])
                        v.tensor_copy(p.abf[:, 81536:82048], nn[:, 0:n])
                    assert s_sp.inc(
                        v.tensor_tensor(qhb[:, 0:n], qn[:, 0:n], qrot[:, 0:n],
                                        ALU.add)) == mk_sp[(ti, fc)]
                for hp in range(4):
                    fc = 8 + hp
                    v.wait_ge(s_mm.h, mk_mm[(ti, fc)])
                    if hp >= 1:
                        v.wait_ge(s_t.h, mk_t[(ti, hp - 1, nj - 1)])
                    assert s_fr.inc(
                        v.tensor_copy(vcf[:, 0:n], p.pf[fc % 3][:, 0:n])
                    ) == mk_fr[(ti, fc)]
                    if p.dev and l == 0 and ti == 2 and hp == 0:
                        v.tensor_copy(p.abf[:, 84000:84512], vcf[:, 0:512])
                    for j in range(nj):
                        v.wait_ge(s_t.h, mk_t[(ti, hp, j)])
                        if hp == 0 and j == 0 and (ti - 1) in mk_vw:
                            v.wait_ge(s_vwp[(ti - 1) % 2].h, mk_vw[ti - 1])
                        v._e.tensor_copy(vsb[j][:, 192 * hp:192 * hp + 64],
                                         p.pb[j % 2][:, 0:64])
                        assert s_vev.inc(v.tensor_copy(
                            vsb[j][:, 192 * hp + 128:192 * hp + 192],
                            p.pb[j % 2][:, 64:128])) == mk_vev[(ti, hp, j)]

        @blk.scalar
        def _(s):
            s = Drained(s)
            for ti, (tau0, n) in enumerate(tlist):
                for fc in range(8):
                    s.wait_ge(s_mm.h, mk_mm[(ti, fc)])
                    assert s_sq.inc(
                        s.activation(sqb[:, 0:n], p.pf[fc % 3][:, 0:n],
                                     AF.Square, bias=0.0, scale=1.0)
                    ) == mk_sq[(ti, fc)]
                    s.wait_ge(s_fr2.h, mk_fr2[(ti, fc)])
                    s.activation(nn[:, 0:n], nn[:, 0:n], AF.Ln,
                                 bias=0.0, scale=1.0)
                    assert s_rs.inc(
                        s.activation(rstd[:, 0:n], nn[:, 0:n], AF.Exp,
                                     bias=0.0, scale=-0.5)) == mk_rs[(ti, fc)]

        @blk.tensor
        def _(t_):
            tcnt = 0
            for ti, (tau0, n) in enumerate(tlist):
                nj = n // 128
                if l == 1:
                    t_.wait_ge(s_hs.h, mk_hs[ti])
                else:
                    t_.wait_ge(s_ldp[ti % 2].h, mk_ld[ti])
                for fc in range(12):
                    prev = (ti, fc - 3) if fc >= 3 else (ti - 1, fc + 9)
                    if prev in mk_fr:
                        t_.wait_ge(s_fr.h, mk_fr[prev])
                    ps = p.pf[fc % 3]
                    for c in range(8):
                        off = 1 if c < 4 else 0
                        mi = t_.matmul(
                            ps[:, 0:n],
                            ap2(p.abf, wq.col + c * 1536 + fc * 128,
                                [[ABF_W, 128], [1, 128]]),
                            ap2(p.abf, xnsl[ti % 2].col + c * 513 + off,
                                [[ABF_W, 128], [1, n]]),
                            start=(c == 0), stop=(c == 7))
                    assert s_mm.inc(mi) == mk_mm[(ti, fc)]
                    if fc < 8:
                        t_.wait_ge(s_sq.h, mk_sq[(ti, fc)])
                        prev2 = (ti, fc - 2) if fc >= 2 else (ti - 1, fc + 6)
                        if prev2 in mk_fr2:
                            t_.wait_ge(s_fr2.h, mk_fr2[prev2])
                        assert s_m2.inc(
                            t_.matmul(p.pf[3 + fc % 2][:, 0:n],
                                      p.blkdiag[:, :], sqb[:, 0:n],
                                      start=True, stop=True)) == mk_m2[(ti, fc)]
                    else:
                        hp = fc - 8
                        t_.wait_ge(s_fr.h, mk_fr[(ti, fc)])
                        for j in range(nj):
                            if tcnt >= 2:
                                t_.wait_ge(s_vev.h,
                                           mk_vev[tp_order[tcnt - 2]])
                            assert s_t.inc(
                                t_.transpose(p.pb[j % 2][:, 0:128],
                                             vcf[:, j * 128:(j + 1) * 128],
                                             p.eye_bf[:, :])
                            ) == mk_t[(ti, hp, j)]
                            tcnt += 1

        @blk.sync
        def _(s):
            for ti, (tau0, n) in enumerate(tlist):
                nj = n // 128
                if ti >= 2:
                    s.wait_ge(s_mm.h, mk_mm[(ti - 2, 11)])
                assert s_ldp[ti % 2].inc(s.dma_start(
                    ap2(p.abf, xnsl[ti % 2].col,
                        [[ABF_W, 128], [513, 8], [1, n + 1]]),
                    ap2(io["xnfm"], tau0 + HALO,
                        [[1 + EXT, 128], [128 * (1 + EXT), 8], [1, n + 1]])),
                    16) == mk_ld[ti]
                if ti >= 2:
                    s.wait_ge(s_sp.h, mk_sp[(ti - 2, 7)])
                for tb in range(4):
                    for half in range(2):
                        s_tabp[ti % 2].inc(s.dma_start(
                            tabs[ti % 2][tb].ap[64 * half:64 * half + 64, 0:n],
                            ap2(io[TABN[tb]], tau0 + HALO, [[EXT, 64], [1, n]])),
                            16)
                assert s_tabp[ti % 2].v == mk_tab[ti]
                for fc in range(8):
                    kq = ti * 8 + fc
                    s.wait_ge(s_sp.h, mk_sp[(ti, fc)])
                    if fc < 4:
                        dst = ap2(io["qh"], fc * 128 * EXT + tau0 + HALO,
                                  [[EXT, 128], [1, n]])
                    else:
                        dst = ap2(io["kh"], (fc - 4) * 128 * KH_W + tau0 + 1536,
                                  [[KH_W, 128], [1, n]])
                    assert s_qsp[kq % 2].inc(
                        s.dma_start(dst, qhb2[kq % 2][:, 0:n]), 16) \
                        == mk_qs[(ti, fc)]
                for j in range(nj):
                    s.wait_ge(s_vev.h, mk_vev[(ti, 3, j)])
                    bidx = (tau0 + 128 * j + 1536) // 128
                    mkv = s_vwp[ti % 2].inc(s.dma_start(
                        ap2(io["vstat"], bidx * 128 * 768, [[768, 128], [1, 768]]),
                        vsb[j][:, :]), 16)
                    if j == nj - 1:
                        assert mkv == mk_vw[ti]
                if p.dev and l == 0 and ti == 2:
                    p.s_ev.inc(s.dma_start(
                        ap2(io["dbg"], 0, [[2048, 128], [1, 2048]]),
                        p.abf[:, 84000:86048]), 16)
            for sp in (s_qsp[0], s_qsp[1], s_vwp[0], s_vwp[1]):
                s.wait_ge(sp.h, sp.v)

        if l == 1:
            @blk.gpsimd
            def _(g):
                g = Drained(g)
                for ti, (tau0, n) in enumerate(tlist):
                    if ti >= 2:
                        g.wait_ge(s_hs.h, mk_hs[ti - 2])
                    for hf in range(2):
                        mk = s_gp[ti % 2].inc(g.dma_start(
                            ap2(p.abf, dsl[ti % 2].col + hf * 4 * 513,
                                [[ABF_W, 128], [513, 4], [1, n + 1]]),
                            ap2(io["dacc_b"],
                                tau0 - 1 + DH0 + hf * 4 * 128 * DACC_W,
                                [[DACC_W, 128], [128 * DACC_W, 4], [1, n + 1]])),
                            16)
                    assert mk == mk_g0[ti]


def phase_p1b(p):
    nc, io = p.nc, p.io
    BO = p.BOFF
    ab = Alloc(p.abf, ABF_W)
    af = Alloc(p.af, AF_W)
    wsi = ab.tile(2048)
    wout = ab.tile(16 * 1024)
    C_bf = ab.tile(HT)
    pp_bf = ab.tile(HT)
    ybf = ab.tile(HT)
    y2b = [ab.tile(512) for _ in range(2)]
    dxb = ab.tile(512)
    ysig = [ab.tile(512) for _ in range(16)]
    xm2 = [ab.tile(512) for _ in range(2)]
    sg2 = [Tile(p.af16[:, 0:512], 0), Tile(p.af16[:, 512:1024], 512)]
    st = af.tile(HT)
    muT = af.tile(512)
    msT = af.tile(512)
    tmpT = af.tile(512)
    ST = af.tile(512)
    e32 = af.tile(512)
    dmt = [af.tile(512) for _ in range(2)]
    (s_mm, s_sc, s_ev, s_fr, s_mm2, s_dev) = (
        p.s_mm, p.s_sc, p.s_ev, p.s_a0, p.s_a1, p.s_a2)
    s_ldp = [p.s_d0, p.s_d1]
    s_stp = [p.s_d2, p.s_d3]

    with nc.Block("p1b_w") as blk:
        @blk.gpsimd
        def _(g):
            g = Drained(g)
            for half in range(2):
                p.s_wl.inc(g.dma_start(
                    p.abf[64 * half:64 * half + 64, wsi.col:wsi.col + 2048],
                    io["L0_attn_W_si"][0:64, :]), 16)
            for c in range(16):
                p.s_wl.inc(g.dma_start(
                    ap2(p.abf, wout.col + c * 1024, [[ABF_W, 128], [1, 1024]]),
                    io["L0_attn_W_out"][c * 128:(c + 1) * 128, :]), 16)
            p.s_g0.inc(g.dma_start(st[:, :], io["st_scr"][:, :]), 16)
            p.s_g0.inc(g.dma_start(pp_bf[:, :], io["pp_scr"][:, :]), 16)
            p.s_g0.inc(g.dma_start(C_bf[:, :], io["c_scr"][:, :]), 16)
            g.wait_ge(p.s_cc.h, 1)
            for half in range(2):
                p.s_g0.inc(g.dma_start(
                    p.small[64 * half:64 * half + 64, 60:61],
                    ap2(io["cc_out"], 0, [[1, 64], [1, 1]]),
                    allow_slow_non_contiguous=True), 16)
            g.wait_ge(p.s_wl.h, p.s_wl.v)
            g.wait_ge(p.s_g0.h, p.s_g0.v)

    mm0, sc0, ev0, fr0, mm20, dev0 = (s_mm.v, s_sc.v, s_ev.v, s_fr.v,
                                      s_mm2.v, s_dev.v)
    stats_mm = {g: mm0 + 2 * (g + 1) for g in range(8)}
    sq_mk = {g: sc0 + 2 * g + 1 for g in range(8)}
    rstd_mk = {g: sc0 + 2 * g + 2 for g in range(8)}
    var_mk = {g: ev0 + 1 + 2 * g + 1 for g in range(8)}
    apply_mk = {g: ev0 + 1 + 2 * g + 2 for g in range(8)}
    ysi_mm = {}
    fr_mk = {}
    v = mm0 + 16
    for tt in range(8):
        for fc in range(16):
            ysi_mm[(tt, fc)] = mm0 + 16 + tt * 16 + fc + 1
            fr_mk[(tt, fc)] = fr0 + tt * 16 + fc + 1
    wout_mm = {(tt, fo): mm20 + tt * 8 + fo + 1 for tt in range(8)
               for fo in range(8)}
    dev_mk = {(tt, fo): dev0 + tt * 8 + fo + 1 for tt in range(8)
              for fo in range(8)}
    st_mk = {(tt, fo): s_stp[(tt * 8 + fo) % 2].v
             + 16 * ((tt * 8 + fo) // 2 + 1)
             for tt in range(8) for fo in range(8)}
    ld_mk = {(tt, fc): s_ldp[(tt * 16 + fc) % 2].v
             + 32 * ((tt * 16 + fc) // 2 + 1)
             for tt in range(8) for fc in range(16)}

    with nc.Block("p1b") as blk:
        @blk.vector
        def _(v_):
            v_ = Drained(v_)
            v_.tensor_tensor(p.small[:, 60:61], p.small[:, 60:61],
                             p.flags[:, 1:2], ALU.mult)
            v_.scalar_tensor_tensor(st[:, :], pp_bf[:, :], p.small[:, 60:61],
                                    st[:, :], ALU.mult, ALU.add)
            v_.tensor_tensor(st[:, :], C_bf[:, :], st[:, :], ALU.mult)
            s_ev.inc(v_.tensor_copy(ybf[:, :], st[:, :]))
            for g in range(8):
                half, ct = g // 4, g % 4
                b = 64 * half
                cols = slice(ct * 512 + (0 if half == 0 else 0), 0)
                c0 = ct * 512
                v_.wait_ge(s_mm.h, stats_mm[g])
                if g >= 1:
                    v_.wait_ge(s_sc.h, rstd_mk[g - 1])
                v_.tensor_scalar(muT[b:b + 64, :], p.pf[4][0:64, :],
                                 1.0 / DSTATE, None, ALU.mult)
                v_.tensor_scalar(msT[b:b + 64, :], p.pf[5][0:64, :],
                                 1.0 / DSTATE, None, ALU.mult)
                v_.tensor_tensor(tmpT[b:b + 64, :], muT[b:b + 64, :],
                                 muT[b:b + 64, :], ALU.mult)
                assert s_ev.inc(v_.tensor_tensor(
                    msT[b:b + 64, :], msT[b:b + 64, :], tmpT[b:b + 64, :],
                    ALU.subtract)) == var_mk[g]
                v_.wait_ge(s_sc.h, rstd_mk[g])
                v_.tensor_tensor(tmpT[b:b + 64, :], muT[b:b + 64, :],
                                 ST[b:b + 64, :], ALU.mult)
                v_.tensor_tensor(st[b:b + 64, c0:c0 + 512],
                                 st[b:b + 64, c0:c0 + 512],
                                 ST[b:b + 64, :], ALU.mult)
                assert s_ev.inc(v_.tensor_tensor(
                    ybf[b:b + 64, c0:c0 + 512], st[b:b + 64, c0:c0 + 512],
                    tmpT[b:b + 64, :], ALU.subtract)) == apply_mk[g]
            for tt in range(8):
                for fc in range(16):
                    v_.wait_ge(s_mm.h, ysi_mm[(tt, fc)])
                    v_.wait_ge(s_ldp[(tt * 16 + fc) % 2].h, ld_mk[(tt, fc)])
                    if tt >= 1:
                        v_.wait_ge(s_mm2.h, wout_mm[(tt - 1, 7)])
                    v_.tensor_scalar(dxb[:, :], xm2[fc % 2][:, :],
                                     p.bias[:, BO["D"] + fc:BO["D"] + fc + 1],
                                     None, ALU.mult)
                    v_.scalar_tensor_tensor(
                        e32[:, :], p.pf[fc % 2][:, :],
                        p.bias[:, BO["b_si"] + fc:BO["b_si"] + fc + 1],
                        dxb[:, :], ALU.add, ALU.add)
                    assert s_fr.inc(v_.tensor_tensor(
                        ysig[fc][:, :], e32[:, :], sg2[fc % 2][:, :],
                        ALU.mult)) == fr_mk[(tt, fc)]
                for fo in range(8):
                    v_.wait_ge(s_mm2.h, wout_mm[(tt, fo)])
                    if (tt * 8 + fo) >= 2:
                        pk = (tt * 8 + fo - 2)
                        v_.wait_ge(s_stp[pk % 2].h, st_mk[(pk // 8, pk % 8)])
                    assert s_dev.inc(v_.tensor_scalar(
                        dmt[fo % 2][:, :], p.pf[2 + fo % 2][:, :],
                        p.bias[:, BO["b_out"] + fo:BO["b_out"] + fo + 1],
                        None, ALU.add)) == dev_mk[(tt, fo)]

        @blk.scalar
        def _(s):
            s = Drained(s)
            for g in range(8):
                half, ct = g // 4, g % 4
                b = 64 * half
                s.wait_ge(s_ev.h, ev0 + 1)
                if g >= 2:
                    s.wait_ge(s_mm.h, stats_mm[g - 2])
                assert s_sc.inc(s.activation(
                    y2b[g % 2][b:b + 64, :],
                    st[b:b + 64, ct * 512:(ct + 1) * 512], AF.Square,
                    bias=0.0, scale=1.0)) == sq_mk[g]
                s.wait_ge(s_ev.h, var_mk[g])
                s.activation(tmpT[b:b + 64, :], msT[b:b + 64, :], AF.Ln,
                             bias=p.flags[b:b + 64, 3:4], scale=1.0)
                assert s_sc.inc(s.activation(
                    ST[b:b + 64, :], tmpT[b:b + 64, :], AF.Exp,
                    bias=0.0, scale=-0.5)) == rstd_mk[g]

        @blk.tensor
        def _(t_):
            for g in range(8):
                half, ct = g // 4, g % 4
                b = 64 * half
                t_.wait_ge(s_sc.h, sq_mk[g])
                if g >= 1:
                    t_.wait_ge(s_ev.h, var_mk[g - 1])
                t_.matmul(p.pf[4][0:64, :], p.ones_bf[b:b + 64, 0:64],
                          ybf[b:b + 64, ct * 512:(ct + 1) * 512],
                          start=True, stop=True)
                assert s_mm.inc(t_.matmul(
                    p.pf[5][0:64, :], p.ones_bf[b:b + 64, 0:64],
                    y2b[g % 2][b:b + 64, :], start=True, stop=True), 2) \
                    == stats_mm[g]
            for tt in range(8):
                half, ct = tt // 4, tt % 4
                b = 64 * half
                t_.wait_ge(s_ev.h, apply_mk[tt // 4 * 4 + tt % 4])
                for fc in range(16):
                    if (tt * 16 + fc) >= 2:
                        pk = tt * 16 + fc - 2
                        t_.wait_ge(s_fr.h, fr_mk[(pk // 16, pk % 16)])
                    assert s_mm.inc(t_.matmul(
                        p.pf[fc % 2][:, :],
                        p.abf[b:b + 64, wsi.col + fc * 128:wsi.col + fc * 128 + 128],
                        ybf[b:b + 64, ct * 512:(ct + 1) * 512],
                        start=True, stop=True)) == ysi_mm[(tt, fc)]
                t_.wait_ge(s_fr.h, fr_mk[(tt, 15)])
                for fo in range(8):
                    if (tt * 8 + fo) >= 2:
                        pk = tt * 8 + fo - 2
                        t_.wait_ge(s_dev.h, dev_mk[(pk // 8, pk % 8)])
                    for fc in range(16):
                        mi = t_.matmul(
                            p.pf[2 + fo % 2][:, :],
                            ap2(p.abf, wout.col + fc * 1024 + fo * 128,
                                [[ABF_W, 128], [1, 128]]),
                            ysig[fc][:, :], start=(fc == 0), stop=(fc == 15))
                    assert s_mm2.inc(mi) == wout_mm[(tt, fo)]

        @blk.sync
        def _(s):
            for tt in range(8):
                for fc in range(16):
                    if (tt * 16 + fc) >= 2:
                        pk = tt * 16 + fc - 2
                        s.wait_ge(s_fr.h, fr_mk[(pk // 16, pk % 16)])
                    sl = s_ldp[(tt * 16 + fc) % 2]
                    sl.inc(s.dma_start(
                        xm2[fc % 2][:, :],
                        io["xmain"][fc * 128:(fc + 1) * 128,
                                    tt * 512:(tt + 1) * 512]), 16)
                    sl.inc(s.dma_start(
                        sg2[fc % 2].ap,
                        io["sgate"][fc * 128:(fc + 1) * 128,
                                    tt * 512:(tt + 1) * 512]), 16)
                    assert sl.v == ld_mk[(tt, fc)]
                for fo in range(8):
                    s.wait_ge(s_dev.h, dev_mk[(tt, fo)])
                    assert s_stp[(tt * 8 + fo) % 2].inc(s.dma_start(
                        ap2(io["dm"], fo * 128 * T + tt * 512,
                            [[T, 128], [1, 512]]),
                        dmt[fo % 2][:, :]), 16) == st_mk[(tt, fo)]
            for sp in (s_stp[0], s_stp[1]):
                s.wait_ge(sp.h, sp.v)


def phase_attn(p, l):
    nc, io = p.nc, p.io
    ab = Alloc(p.abf, ABF_W)
    af = Alloc(p.af, AF_W)
    wo_sb = ab.tile(4 * 1024)
    qsl = [ab.tile(4 * 512) for _ in range(2)]
    ksl = [ab.tile(4 * 512) for _ in range(2)]
    vt = [ab.tile(768) for _ in range(8)]
    attE = [ab.tile(512) for _ in range(2)]
    attO = [ab.tile(512) for _ in range(2)]
    o_all = ab.tile(4 * 512)
    recT = af.tile(512)
    dwt = [af.tile(512) for _ in range(2)]
    dbt = [af.tile(512) for _ in range(2)]
    (s_mm, s_msk, s_exp, s_oev, s_dev) = (
        p.s_mm, p.s_ev, p.s_sc, p.s_a0, p.s_a1)
    s_ldp = [p.s_d0, p.s_d1]
    s_stp = [p.s_d2, p.s_d3]
    s_dldp = [p.s_g1, p.s_g3]

    wlist = list(range(-2, 8)) if l == 0 else list(range(-1, 8))
    NW = len(wlist)
    dacc_rd = "dacc_b"
    M_OWN = p.masks[:, 0:128]
    M_LOOK = p.masks[:, 128:256]
    M0D = p.masks[:, 256:384]
    M0R = p.masks[:, 384:384 + W]

    with nc.Block(f"attn{l}_w") as blk:
        @blk.gpsimd
        def _(g):
            g = Drained(g)
            for c in range(4):
                p.s_wl.inc(g.dma_start(
                    ap2(p.abf, wo_sb.col + c * 1024, [[ABF_W, 128], [1, 1024]]),
                    ap2(io["L0_ffn_wo"], l * INNER_A * DMODEL + c * 128 * DMODEL,
                        [[DMODEL, 128], [1, 1024]])), 16)
            g.wait_ge(p.s_wl.h, p.s_wl.v)

    mm0, msk0, exp0, oev0, dev0 = (s_mm.v, s_msk.v, s_exp.v,
                                   s_oev.v, s_dev.v)
    sides = ["own", "look"]
    sim_mk, msk_mk, exp_mk, pv_mk = {}, {}, {}, {}
    oev_mk, wo_mk, dev_mk, stw_mk, ld_mk, dld_mk = {}, {}, {}, {}, {}, {}
    mmv = mm0
    for wi in range(NW):
        for hp in range(4):
            for si in range(2):
                for kc in range(4):
                    mmv += 1
                    sim_mk[(wi, hp, si, kc)] = mmv
                    mmv += 1
                    pv_mk[(wi, hp, si, kc)] = mmv
        for fo in range(8):
            mmv += 1
            wo_mk[(wi, fo)] = mmv
    k = 0
    for wi in range(NW):
        for hp in range(4):
            for si in range(2):
                for kc in range(4):
                    k += 1
                    msk_mk[(wi, hp, si, kc)] = msk0 + k
                    exp_mk[(wi, hp, si, kc)] = exp0 + k
    for wi in range(NW):
        for hp in range(4):
            oev_mk[(wi, hp)] = oev0 + wi * 4 + hp + 1
        for fo in range(8):
            dev_mk[(wi, fo)] = dev0 + wi * 8 + fo + 1
    for wi in range(NW):
        for fo in range(8):
            k2 = wi * 8 + fo
            stw_mk[(wi, fo)] = s_stp[k2 % 2].v + 16 * (k2 // 2 + 1)
            dld_mk[(wi, fo)] = s_dldp[k2 % 2].v + 16 * (k2 // 2 + 1)
    lvp = [s_ldp[0].v, s_ldp[1].v]
    for wi in range(NW):
        lvp[wi % 2] += 32
        if wi == 0:
            lvp[wi % 2] += 16
        lvp[wi % 2] += 16 * (8 if wi == 0 else 4)
        ld_mk[wi] = lvp[wi % 2]

    def vslot(wi, side, kc):
        w = wlist[wi]
        bidx = 4 * w + (12 if side == 0 else 8) + kc
        return bidx % 8, bidx

    with nc.Block(f"attn{l}") as blk:
        @blk.sync
        def _(s):
            for wi in range(NW):
                w = wlist[wi]
                tau0 = w * 512
                if wi >= 2:
                    s.wait_ge(s_mm.h, wo_mk[(wi - 2, 7)])
                s_ldp[wi % 2].inc(s.dma_start(
                    ap2(p.abf, qsl[wi % 2].col, [[ABF_W, 128], [512, 4], [1, 512]]),
                    ap2(io["qh"], tau0 + HALO,
                        [[EXT, 128], [128 * EXT, 4], [1, 512]])), 16)
                if wi >= 2:
                    s.wait_ge(s_mm.h, wo_mk[(wi - 1, 7)])
                s_ldp[wi % 2].inc(s.dma_start(
                    ap2(p.abf, ksl[wi % 2].col, [[ABF_W, 128], [512, 4], [1, 512]]),
                    ap2(io["kh"], tau0 + 1536,
                        [[KH_W, 128], [128 * KH_W, 4], [1, 512]])), 16)
                if wi == 0:
                    s_ldp[wi % 2].inc(s.dma_start(
                        ap2(p.abf, ksl[1].col, [[ABF_W, 128], [512, 4], [1, 512]]),
                        ap2(io["kh"], tau0 + 1024,
                            [[KH_W, 128], [128 * KH_W, 4], [1, 512]])), 16)
                rng = range(8) if wi == 0 else range(4)
                for j in rng:
                    bidx = 4 * w + 8 + j if wi == 0 else 4 * w + 12 + j
                    slot = bidx % 8
                    if wi >= 2:
                        s.wait_ge(s_mm.h, wo_mk[(wi - 1, 7)])
                    s_ldp[wi % 2].inc(s.dma_start(
                        vt[slot][:, :],
                        ap2(io["vstat"], bidx * 128 * 768,
                            [[768, 128], [1, 768]])), 16)
                assert s_ldp[wi % 2].v == ld_mk[wi]
                for fo in range(8):
                    k2 = wi * 8 + fo
                    s.wait_ge(s_dev.h, dev_mk[(wi, fo)])
                    w_lo = max(0, -tau0 - DH0)
                    assert s_stp[k2 % 2].inc(s.dma_start(
                        ap2(io["dacc_a"], fo * 128 * DACC_W + tau0 + DH0 + w_lo,
                            [[DACC_W, 128], [1, 512 - w_lo]]),
                        dwt[fo % 2][:, w_lo:512]), 16) == stw_mk[(wi, fo)]
                if p.dev and l == 2 and w == 3:
                    s.wait_ge(s_dev.h, dev_mk[(wi, 7)])
                    p.s_a7.inc(s.dma_start(
                        ap2(io["dbg"], 0, [[2048, 128], [1, 2048]]),
                        p.abf[:, 84000:86048]), 16)
                    s.wait_ge(p.s_a7.h, p.s_a7.v)
            for sp in (s_stp[0], s_stp[1]):
                s.wait_ge(sp.h, sp.v)

        if l == 1:
            @blk.gpsimd
            def _(g):
                g = Drained(g)
                for wi in range(NW):
                    tau0 = wlist[wi] * 512
                    for fo in range(8):
                        if (wi * 8 + fo) >= 2:
                            pk = wi * 8 + fo - 2
                            g.wait_ge(s_dev.h, dev_mk[(pk // 8, pk % 8)])
                        w_lo = max(0, -tau0 - DH0)
                        assert s_dldp[(wi * 8 + fo) % 2].inc(g.dma_start(
                            dbt[fo % 2][:, w_lo:512],
                            ap2(io["dacc_b"],
                                fo * 128 * DACC_W + tau0 + DH0 + w_lo,
                                [[DACC_W, 128], [1, 512 - w_lo]])),
                            16) == dld_mk[(wi, fo)]

        @blk.tensor
        def _(t_):
            for wi in range(NW):
                w = wlist[wi]
                t_.wait_ge(s_ldp[wi % 2].h, ld_mk[wi])
                for hp in range(4):
                    if hp >= 1:
                        t_.wait_ge(s_oev.h, oev_mk[(wi, hp - 1)])
                    elif wi >= 1:
                        t_.wait_ge(s_oev.h, oev_mk[(wi - 1, 3)])
                    for si in range(2):
                        kslab = ksl[wi % 2] if si == 0 else \
                            (ksl[(wi - 1) % 2] if wi >= 1 else ksl[1])
                        for kc in range(4):
                            if si == 0:
                                u0, u1 = kc * 128, 512
                            else:
                                u0, u1 = 0, kc * 128 + 128
                            pkey = (wi, hp, si, kc - 1) if kc >= 1 else \
                                ((wi, hp, si - 1, 3) if si == 1 else
                                 ((wi, hp - 1, 1, 3) if hp >= 1 else
                                  ((wi - 1, 3, 1, 3) if wi >= 1 else None)))
                            if pkey is not None and pkey in exp_mk:
                                t_.wait_ge(s_exp.h, exp_mk[pkey])
                            for half, pspsum in ((0, p.pf[0]), (1, p.pf[1])):
                                b = 64 * half
                                mi = t_.matmul(
                                    pspsum[:, u0:u1],
                                    p.abf[b:b + 64,
                                          kslab.col + hp * 512 + kc * 128:
                                          kslab.col + hp * 512 + kc * 128 + 128],
                                    p.abf[b:b + 64,
                                          qsl[wi % 2].col + hp * 512 + u0:
                                          qsl[wi % 2].col + hp * 512 + u1],
                                    start=True, stop=True)
                            assert s_mm.inc(mi) == sim_mk[(wi, hp, si, kc)]
                            t_.wait_ge(s_exp.h, exp_mk[(wi, hp, si, kc)])
                            slot, _b = vslot(wi, si, kc)
                            first = (si == 0 and kc == 0)
                            last = (si == 1 and kc == 3)
                            t_.matmul(p.pf[2][:, u0:u1],
                                      vt[slot][:, 192 * hp:192 * hp + 128],
                                      attE[kc % 2][:, u0:u1],
                                      start=first, stop=last)
                            mi = t_.matmul(p.pf[3][:, u0:u1],
                                           vt[slot][:, 192 * hp + 64:
                                                    192 * hp + 192],
                                           attO[kc % 2][:, u0:u1],
                                           start=first, stop=last)
                            assert s_mm.inc(mi) == pv_mk[(wi, hp, si, kc)]
                t_.wait_ge(s_oev.h, oev_mk[(wi, 3)])
                for fo in range(8):
                    if (wi * 8 + fo) >= 2:
                        pk = wi * 8 + fo - 2
                        t_.wait_ge(s_dev.h, dev_mk[(pk // 8, pk % 8)])
                    for c in range(4):
                        mi = t_.matmul(
                            p.pf[4 + fo % 2][:, :],
                            ap2(p.abf, wo_sb.col + c * 1024 + fo * 128,
                                [[ABF_W, 128], [1, 128]]),
                            o_all[:, c * 512:(c + 1) * 512],
                            start=(c == 0), stop=(c == 3))
                    assert s_mm.inc(mi) == wo_mk[(wi, fo)]

        @blk.vector
        def _(v_):
            v_ = Drained(v_)
            for wi in range(NW):
                w = wlist[wi]
                for hp in range(4):
                    for si in range(2):
                        for kc in range(4):
                            if si == 0:
                                u0, u1 = kc * 128, 512
                                dmask = M_OWN
                            else:
                                u0, u1 = 0, kc * 128 + 128
                                dmask = M0D if w == 0 else M_LOOK
                            d0 = kc * 128
                            v_.wait_ge(s_mm.h, sim_mk[(wi, hp, si, kc)])
                            v_._e.tensor_tensor(p.pf[0][:, d0:d0 + 128],
                                                p.pf[0][:, d0:d0 + 128],
                                                dmask, ALU.add)
                            if si == 1 and w == 0 and kc >= 1:
                                v_._e.tensor_tensor(p.pf[0][:, 0:d0],
                                                    p.pf[0][:, 0:d0],
                                                    M0R[:, 0:d0], ALU.add)
                                v_._e.tensor_tensor(p.pf[1][:, 0:d0],
                                                    p.pf[1][:, 0:d0],
                                                    M0R[:, 0:d0], ALU.add)
                            ins = v_.tensor_tensor(p.pf[1][:, d0:d0 + 128],
                                                   p.pf[1][:, d0:d0 + 128],
                                                   dmask, ALU.add)
                            assert s_msk.inc(ins) == msk_mk[(wi, hp, si, kc)]
                    v_.wait_ge(s_mm.h, pv_mk[(wi, hp, 1, 3)])
                    if wi >= 1:
                        v_.wait_ge(s_mm.h, wo_mk[(wi - 1, 7)])
                    v_._e.reciprocal(recT[0:64, :], p.pf[2][64:128, :])
                    v_.reciprocal(recT[64:128, :], p.pf[3][0:64, :])
                    v_._e.tensor_tensor(o_all[0:64, hp * 512:hp * 512 + 512],
                                        p.pf[2][0:64, :], recT[0:64, :],
                                        ALU.mult)
                    assert s_oev.inc(v_.tensor_tensor(
                        o_all[64:128, hp * 512:hp * 512 + 512],
                        p.pf[3][64:128, :], recT[64:128, :],
                        ALU.mult)) == oev_mk[(wi, hp)]
                if p.dev and l == 2 and w == 3:
                    v_.tensor_copy(p.abf[:, 84000:86048], o_all[:, :])
                tau0 = w * 512
                w_lo = max(0, -tau0 - DH0)
                for fo in range(8):
                    v_.wait_ge(s_mm.h, wo_mk[(wi, fo)])
                    if (wi * 8 + fo) >= 2:
                        pk = wi * 8 + fo - 2
                        v_.wait_ge(s_stp[pk % 2].h, stw_mk[(pk // 8, pk % 8)])
                    if l == 0:
                        ins = v_.tensor_copy(dwt[fo % 2][:, w_lo:512],
                                             p.pf[4 + fo % 2][:, w_lo:512])
                    else:
                        v_.wait_ge(s_dldp[(wi * 8 + fo) % 2].h,
                                   dld_mk[(wi, fo)])
                        ins = v_.tensor_tensor(dwt[fo % 2][:, w_lo:512],
                                               p.pf[4 + fo % 2][:, w_lo:512],
                                               dbt[fo % 2][:, w_lo:512], ALU.add)
                    assert s_dev.inc(ins) == dev_mk[(wi, fo)]

        @blk.scalar
        def _(s):
            s = Drained(s)
            negb = p.small[:, 6 + l:7 + l]
            for wi in range(NW):
                for hp in range(4):
                    for si in range(2):
                        for kc in range(4):
                            if si == 0:
                                u0, u1 = kc * 128, 512
                            else:
                                u0, u1 = 0, kc * 128 + 128
                            s.wait_ge(s_msk.h, msk_mk[(wi, hp, si, kc)])
                            kk = hp * 8 + si * 4 + kc
                            if kk >= 2:
                                pk2 = kk - 2
                                key2 = (wi, pk2 // 8, (pk2 % 8) // 4, pk2 % 4)
                                s.wait_ge(s_mm.h, pv_mk[key2])
                            elif wi >= 1:
                                key2 = (wi - 1, 3, 1, 2 + kk)
                                s.wait_ge(s_mm.h, pv_mk[key2])
                            s._e.activation(attE[kc % 2][:, u0:u1],
                                            p.pf[0][:, u0:u1], AF.Exp,
                                            bias=negb, scale=8.0)
                            ins = s.activation(attO[kc % 2][:, u0:u1],
                                               p.pf[1][:, u0:u1], AF.Exp,
                                               bias=negb, scale=8.0)
                            assert s_exp.inc(ins) == exp_mk[(wi, hp, si, kc)]


def phase_mlp(p, l):
    nc, io = p.nc, p.io
    BO = p.BOFF
    ab = Alloc(p.abf, ABF_W)
    af = Alloc(p.af, AF_W)
    w1sb = ab.tile(8 * 5460)
    w2sb = ab.tile(22 * 1024)
    hslab = ab.tile(8 * 513)
    xln = ab.tile(8 * 512)
    h2b = [ab.tile(512) for _ in range(2)]
    tqb = h2b[0]
    h1act = [ab.tile(512) for _ in range(22)]
    dslab = af.tile(8 * 513)
    muT = af.tile(512)
    msT = af.tile(512)
    tmpT = af.tile(512)
    ST = af.tile(512)
    gg = muT
    dnt = [msT, tmpT]
    (s_ld, s_mm, s_sc, s_ev, s_fr, s_mm2, s_dev) = (
        p.s_dd, p.s_mm, p.s_sc, p.s_ev, p.s_a0, p.s_a1, p.s_a2)
    s_stp = [p.s_d2, p.s_d3]

    if l == 0:
        tlist = [(-640, 128)] + [(-512 + 512 * t, 512) for t in range(9)]
    else:
        tlist = [(512 * t, 512) for t in range(8)]
    NTI = len(tlist)
    JW = [128] * 21 + [HREM]

    with nc.Block(f"mlp{l}_w") as blk:
        @blk.gpsimd
        def _(g):
            g = Drained(g)
            for c in range(8):
                p.s_wl.inc(g.dma_start(
                    ap2(p.abf, w1sb.col + c * 5460, [[ABF_W, 128], [1, 5460]]),
                    ap2(io["L0_ffn_w1"], l * DMODEL * 2 * HID + c * 128 * 2 * HID,
                        [[2 * HID, 128], [1, 2 * HID]])), 16)
            for j in range(22):
                wj = JW[j]
                p.s_wl.inc(g.dma_start(
                    ap2(p.abf, w2sb.col + j * 1024, [[ABF_W, wj], [1, 1024]]),
                    ap2(io["L0_ffn_w2"], l * HID * DMODEL + j * 128 * DMODEL,
                        [[DMODEL, wj], [1, 1024]])), 16)
            g.wait_ge(p.s_wl.h, p.s_wl.v)

    mm0, sc0, ev0, fr0, mm20, dev0, ld0 = (s_mm.v, s_sc.v, s_ev.v, s_fr.v,
                                           s_mm2.v, s_dev.v, s_ld.v)
    hs_mk = {ti: ev0 + 9 * ti + 8 for ti in range(NTI)}
    var_mk = {ti: ev0 + 9 * ti + 9 for ti in range(NTI)}
    sq_mk = {(ti, c): sc0 + 10 * ti + c + 1 for ti in range(NTI) for c in range(8)}
    rstd_mk = {ti: sc0 + 10 * ti + 9 for ti in range(NTI)}
    stat_mm = {(ti, c): mm0 + (22 * 8 * 2 + 16) * ti + 2 * (c + 1)
               for ti in range(NTI) for c in range(8)}
    apl_mk = {}
    gel_mk = {}
    act_mk = {}
    w1_mm = {}
    w2_mm = {}
    dev_mk = {}
    stw_mk = {}
    ld_mk = {ti: ld0 + 32 * (ti + 1) for ti in range(NTI)}
    for ti in range(NTI):
        base = mm0 + (22 * 8 * 2 + 16) * ti + 16
        for j in range(22):
            w1_mm[(ti, j)] = base + 16 * (j + 1)
    mv = mm20
    for ti in range(NTI):
        for fo in range(8):
            mv += 1
            w2_mm[(ti, fo)] = mv
    for ti in range(NTI):
        for fo in range(8):
            k2 = ti * 8 + fo
            dev_mk[(ti, fo)] = dev0 + k2 + 1
            stw_mk[(ti, fo)] = s_stp[k2 % 2].v + 16 * (k2 // 2 + 1)
        for j in range(22):
            gel_mk[(ti, j)] = sc0 + 10 * NTI + ti * 22 + j + 1
            act_mk[(ti, j)] = fr0 + ti * 22 + j + 1
    sqv = sc0
    sq_mk, rstd_mk, gel_mk = {}, {}, {}
    for ti in range(NTI):
        for c in range(8):
            sqv += 1
            sq_mk[(ti, c)] = sqv
        sqv += 1
        rstd_mk[ti] = sqv
        for j in range(22):
            sqv += 1
            gel_mk[(ti, j)] = sqv
    evv, frv, devv = ev0, fr0, dev0
    hs_mk, var_mk, apl_mk = {}, {}, {}
    for ti in range(NTI):
        evv += 1
        hs_mk[ti] = evv
        evv += 1
        var_mk[ti] = evv
        evv += 1
        apl_mk[ti] = evv

    with nc.Block(f"mlp{l}") as blk:
        @blk.sync
        def _(s):
            for ti, (tau0, n) in enumerate(tlist):
                if ti >= 1:
                    s.wait_ge(s_mm.h, w1_mm[(ti - 1, 21)])
                if ti >= 1:
                    s.wait_ge(s_dev.h, dev_mk[(ti - 1, 7)])
                s_ld.inc(s.dma_start(
                    ap2(p.abf, hslab.col, [[ABF_W, 128], [513, 8], [1, n + 1]]),
                    ap2(io["xnfm"], tau0 + HALO,
                        [[1 + EXT, 128], [128 * (1 + EXT), 8], [1, n + 1]])), 16)
                s_ld.inc(s.dma_start(
                    ap2(p.af, dslab.col, [[AF_W, 128], [513, 8], [1, n + 1]]),
                    ap2(io["dacc_a"], tau0 - 1 + DH0,
                        [[DACC_W, 128], [128 * DACC_W, 8], [1, n + 1]])), 16)
                assert s_ld.v == ld_mk[ti]
                for fo in range(8):
                    s.wait_ge(s_dev.h, dev_mk[(ti, fo)])
                    assert s_stp[(ti * 8 + fo) % 2].inc(s.dma_start(
                        ap2(io["dacc_b"], fo * 128 * DACC_W + tau0 + DH0,
                            [[DACC_W, 128], [1, n]]),
                        dnt[fo % 2][:, 0:n]), 16) == stw_mk[(ti, fo)]
                if p.dev and l == 0 and ti == 2:
                    p.s_a7.inc(s.dma_start(
                        ap2(io["dbg"], 0, [[2048, 128], [1, 512]]),
                        xln[:, 0:512]), 16)
                    p.s_a7.inc(s.dma_start(
                        ap2(io["dbg"], 512, [[2048, 128], [1, 512]]),
                        xln[:, 4 * 512:5 * 512]), 16)
                    p.s_a7.inc(s.dma_start(
                        ap2(io["dbg"], 1024, [[2048, 128], [1, 512]]),
                        hslab[:, 4 * 513:4 * 513 + 512]), 16)
                    s.wait_ge(p.s_a7.h, p.s_a7.v)
                    s.wait_ge(p.s_g1.h, p.s_g1.v)
            for sp in (s_stp[0], s_stp[1]):
                s.wait_ge(sp.h, sp.v)

        if p.dev and l == 0:
            @blk.gpsimd
            def _(g):
                g.wait_ge(s_dev.h, dev_mk[(2, 7)])
                p.s_g1.inc(g.dma_start(
                    ap2(io["dbg"], 1536, [[2048, 128], [1, 512]]),
                    ST[:, 0:512]), 16)

        @blk.vector
        def _(v_):
            v_ = Drained(v_)
            for ti, (tau0, n) in enumerate(tlist):
                v_.wait_ge(s_ld.h, ld_mk[ti])
                for c in range(8):
                    eng = v_ if c == 7 else v_._e
                    ins = eng.tensor_tensor(
                        hslab[:, c * 513:c * 513 + n + 1],
                        p.af[:, dslab.col + c * 513:dslab.col + c * 513 + n + 1],
                        p.abf[:, hslab.col + c * 513:hslab.col + c * 513 + n + 1],
                        ALU.add)
                assert s_ev.inc(ins) == hs_mk[ti]
                v_.wait_ge(s_mm.h, stat_mm[(ti, 7)])
                if ti >= 1:
                    v_.wait_ge(s_stp[0].h, stw_mk[(ti - 1, 6)])
                    v_.wait_ge(s_stp[1].h, stw_mk[(ti - 1, 7)])
                v_.tensor_scalar(muT[:, 0:n], p.pf[0][:, 0:n], 1.0 / DMODEL,
                                 None, ALU.mult)
                v_.tensor_scalar(msT[:, 0:n], p.pf[1][:, 0:n], 1.0 / DMODEL,
                                 None, ALU.mult)
                v_.tensor_tensor(tmpT[:, 0:n], muT[:, 0:n], muT[:, 0:n],
                                 ALU.mult)
                assert s_ev.inc(v_.tensor_tensor(
                    msT[:, 0:n], msT[:, 0:n], tmpT[:, 0:n],
                    ALU.subtract)) == var_mk[ti]
                v_.wait_ge(s_sc.h, rstd_mk[ti])
                for c in range(8):
                    off = 1 if c < 4 else 0
                    hsv = hslab[:, c * 513 + off:c * 513 + off + n]
                    eng = v_ if c == 7 else v_._e
                    eng.tensor_tensor(xln[:, c * 512:c * 512 + n], hsv,
                                      muT[:, 0:n], ALU.subtract)
                for c in range(8):
                    eng = v_ if c == 7 else v_._e
                    eng.tensor_tensor(xln[:, c * 512:c * 512 + n],
                                      xln[:, c * 512:c * 512 + n],
                                      ST[:, 0:n], ALU.mult)
                for c in range(8):
                    eng = v_ if c == 7 else v_._e
                    ins = eng.tensor_scalar(
                        xln[:, c * 512:c * 512 + n],
                        xln[:, c * 512:c * 512 + n],
                        p.bias[:, BO[f"ln_g{l}"] + c:BO[f"ln_g{l}"] + c + 1],
                        p.bias[:, BO[f"ln_b{l}"] + c:BO[f"ln_b{l}"] + c + 1],
                        ALU.mult, ALU.add)
                assert s_ev.inc(ins) == apl_mk[ti]

                for j in range(22):
                    wj = JW[j]
                    v_.wait_ge(s_sc.h, gel_mk[(ti, j)])
                    if ti >= 1 or j >= 1:
                        pass
                    if ti >= 1:
                        v_.wait_ge(s_mm2.h, w2_mm[(ti - 1, 7)])
                    ins = v_.scalar_tensor_tensor(
                        h1act[j][0:wj, 0:n],
                        p.pf[2 + 2 * (j % 2)][0:wj, 0:n],
                        p.bias[0:wj, BO[f"b1a{l}"] + j:BO[f"b1a{l}"] + j + 1],
                        gg[0:wj, 0:n], ALU.add, ALU.mult)
                    assert s_fr.inc(ins) == act_mk[(ti, j)]
                for fo in range(8):
                    v_.wait_ge(s_mm2.h, w2_mm[(ti, fo)])
                    if (ti * 8 + fo) >= 2:
                        pk = ti * 8 + fo - 2
                        v_.wait_ge(s_stp[pk % 2].h, stw_mk[(pk // 8, pk % 8)])
                    ins = v_.scalar_tensor_tensor(
                        dnt[fo % 2][:, 0:n], p.pf[fo % 2][:, 0:n],
                        p.bias[:, BO[f"b2{l}"] + fo:BO[f"b2{l}"] + fo + 1],
                        p.af[:, dslab.col + fo * 513 + 1:
                              dslab.col + fo * 513 + 1 + n], ALU.add, ALU.add)
                    if l == 0 and tau0 < 0:
                        nh = min(n, -tau0)
                        ins = v_.tensor_scalar(dnt[fo % 2][:, 0:nh],
                                               dnt[fo % 2][:, 0:nh],
                                               p.flags[:, 2:3], None, ALU.mult)
                    assert s_dev.inc(ins) == dev_mk[(ti, fo)]

        @blk.scalar
        def _(s):
            s = Drained(s)
            for ti, (tau0, n) in enumerate(tlist):
                for c in range(8):
                    off = 1 if c < 4 else 0
                    s.wait_ge(s_ev.h, hs_mk[ti])
                    if c == 0 and ti >= 1:
                        s.wait_ge(s_ev.h, apl_mk[ti - 1])
                    if c >= 2:
                        s.wait_ge(s_mm.h, stat_mm[(ti, c - 2)])
                    elif ti >= 1:
                        s.wait_ge(s_mm.h, stat_mm[(ti - 1, 6 + c)])
                    hsv = hslab[:, c * 513 + off:c * 513 + off + n]
                    assert s_sc.inc(s.activation(
                        h2b[c % 2][:, 0:n], hsv, AF.Square,
                        bias=0.0, scale=1.0)) == sq_mk[(ti, c)]
                s.wait_ge(s_ev.h, var_mk[ti])
                s.activation(tmpT[:, 0:n], msT[:, 0:n], AF.Ln, bias=p.flags[:, 3:4],
                             scale=1.0)
                assert s_sc.inc(s.activation(ST[:, 0:n], tmpT[:, 0:n], AF.Exp,
                                             bias=0.0, scale=-0.5)) == rstd_mk[ti]
                for j in range(22):
                    wj = JW[j]
                    s.wait_ge(s_mm.h, w1_mm[(ti, j)])
                    if j >= 1:
                        s.wait_ge(s_fr.h, act_mk[(ti, j - 1)])
                    elif ti >= 1:
                        s.wait_ge(s_fr.h, act_mk[(ti - 1, 21)])
                    ins = s.activation(
                        gg[0:wj, 0:n], p.pf[3 + 2 * (j % 2)][0:wj, 0:n],
                        AF.Gelu,
                        bias=p.bias[0:wj, BO[f"b1g{l}"] + j:BO[f"b1g{l}"] + j + 1],
                        scale=1.0)
                    assert s_sc.inc(ins) == gel_mk[(ti, j)]

        @blk.tensor
        def _(t_):
            for ti, (tau0, n) in enumerate(tlist):
                for c in range(8):
                    off = 1 if c < 4 else 0
                    hsv = p.abf[:, hslab.col + c * 513 + off:
                                hslab.col + c * 513 + off + n]
                    if c == 0:
                        t_.wait_ge(s_ev.h, hs_mk[ti])
                        if ti >= 1:
                            t_.wait_ge(s_dev.h, dev_mk[(ti - 1, 7)])
                    t_.matmul(p.pf[0][:, 0:n], p.ones_bf[:, :], hsv,
                              start=(c == 0), stop=(c == 7))
                    t_.wait_ge(s_sc.h, sq_mk[(ti, c)])
                    mi = t_.matmul(p.pf[1][:, 0:n], p.ones_bf[:, :],
                                   h2b[c % 2][:, 0:n],
                                   start=(c == 0), stop=(c == 7))
                    assert s_mm.inc(mi, 2) == stat_mm[(ti, c)]
                t_.wait_ge(s_ev.h, apl_mk[ti])
                for j in range(22):
                    wj = JW[j]
                    if j >= 2:
                        t_.wait_ge(s_fr.h, act_mk[(ti, j - 2)])
                    elif ti >= 1:
                        t_.wait_ge(s_fr.h, act_mk[(ti - 1, 20 + j)])
                    if j >= 1:
                        t_.wait_ge(s_sc.h, gel_mk[(ti, j - 1)])
                    elif ti >= 1:
                        t_.wait_ge(s_sc.h, gel_mk[(ti - 1, 21)])
                    for c in range(8):
                        t_.matmul(p.pf[2 + 2 * (j % 2)][0:wj, 0:n],
                                  ap2(p.abf, w1sb.col + c * 5460 + j * 128,
                                      [[ABF_W, 128], [1, wj]]),
                                  p.abf[:, xln.col + c * 512:
                                        xln.col + c * 512 + n],
                                  start=(c == 0), stop=(c == 7))
                    for c in range(8):
                        mi = t_.matmul(p.pf[3 + 2 * (j % 2)][0:wj, 0:n],
                                       ap2(p.abf,
                                           w1sb.col + c * 5460 + HID + j * 128,
                                           [[ABF_W, 128], [1, wj]]),
                                       p.abf[:, xln.col + c * 512:
                                             xln.col + c * 512 + n],
                                       start=(c == 0), stop=(c == 7))
                    assert s_mm.inc(mi, 16) == w1_mm[(ti, j)]
                for fo in range(8):
                    if (ti * 8 + fo) >= 2:
                        pk = ti * 8 + fo - 2
                        t_.wait_ge(s_dev.h, dev_mk[(pk // 8, pk % 8)])
                    for j in range(22):
                        wj = JW[j]
                        if fo == 0:
                            t_.wait_ge(s_fr.h, act_mk[(ti, j)])
                        mi = t_.matmul(
                            p.pf[fo % 2][:, 0:n],
                            ap2(p.abf, w2sb.col + j * 1024 + fo * 128,
                                [[ABF_W, wj], [1, 128]]),
                            h1act[j][0:wj, 0:n],
                            start=(j == 0), stop=(j == 21))
                    assert s_mm2.inc(mi) == w2_mm[(ti, fo)]


def phase_fin(p):
    nc, io = p.nc, p.io
    ab = Alloc(p.abf, ABF_W)
    af = Alloc(p.af, AF_W)
    ut = [af.tile(1024) for _ in range(2)]
    ot = [af.tile(1024) for _ in range(2)]
    dmsl = af.tile(1024)
    dasl = af.tile(1024)
    s8 = dmsl
    (s_ld, s_mm, s_ev) = (p.s_dd, p.s_mm, p.s_ev)
    s_op = [p.s_d4, p.s_d5]
    ld0, mm0, ev0 = s_ld.v, s_mm.v, s_ev.v
    NG = 32
    ow_mk = {i: s_op[i % 2].v + 16 * (i // 2 + 1) for i in range(NG)}

    with nc.Block("fin") as blk:
        @blk.sync
        def _(s):
            for i in range(NG):
                if i >= 1:
                    s.wait_ge(s_ev.h, ev0 + i * 9 - 8)
                    s.wait_ge(s_mm.h, mm0 + 8 * i)
                s_ld.inc(s.dma_start(
                    ap2(p.af, dmsl.col, [[AF_W, 128], [128, 8], [1, 128]]),
                    ap2(io["dm"], i * 128,
                        [[T, 128], [128 * T, 8], [1, 128]])), 16)
                s_ld.inc(s.dma_start(
                    ap2(p.af, dasl.col, [[AF_W, 128], [128, 8], [1, 128]]),
                    ap2(io["dacc_b"], DH0 + i * 128,
                        [[DACC_W, 128], [128 * DACC_W, 8], [1, 128]])), 16)
                if i >= 2:
                    s.wait_ge(s_ev.h, ev0 + (i - 1) * 9 + 9)
                s_ld.inc(s.dma_start(ut[i % 2][:, :],
                                     io["u_scr"][i * 128:(i + 1) * 128, :]), 16)
                s.wait_ge(s_ev.h, ev0 + i * 9 + 9)
                assert s_op[i % 2].inc(
                    s.dma_start(io["out"][i * 128:(i + 1) * 128, :],
                                ot[i % 2][:, :]), 16) == ow_mk[i]
            for sp in (s_op[0], s_op[1]):
                s.wait_ge(sp.h, sp.v)

        @blk.vector
        def _(v_):
            v_ = Drained(v_)
            for i in range(NG):
                v_.wait_ge(s_ld.h, ld0 + 48 * (i + 1))
                if i >= 1:
                    v_.wait_ge(s_mm.h, mm0 + 8 * i)
                s_ev.inc(v_.tensor_add(s8[:, :], dmsl.ap, dasl.ap))
                for c in range(8):
                    v_.wait_ge(s_mm.h, mm0 + i * 8 + c + 1)
                    if c == 0 and i >= 2:
                        v_.wait_ge(s_op[i % 2].h, ow_mk[i - 2])
                    ins = v_.tensor_tensor(ot[i % 2][:, c * 128:(c + 1) * 128],
                                           ut[i % 2][:, c * 128:(c + 1) * 128],
                                           p.pf[4 + c % 2][:, 0:128], ALU.add)
                    s_ev.inc(ins)

        @blk.tensor
        def _(t_):
            for i in range(NG):
                t_.wait_ge(s_ev.h, ev0 + i * 9 + 1)
                for c in range(8):
                    k = i * 8 + c
                    if k >= 2:
                        t_.wait_ge(s_ev.h, ev0 + (k - 2) // 8 * 9 + (k - 2) % 8 + 2)
                    mi = t_.transpose(p.pf[4 + c % 2][:, 0:128],
                                      p.af[:, s8.col + c * 128:
                                           s8.col + c * 128 + 128],
                                      p.eye_f32[:, :])
                    s_mm.inc(mi)


_CACHE = {}


def kernel(**inputs):
    if "nc" not in _CACHE:
        _CACHE["nc"], _CACHE["io"] = build_program()
    nc = _CACHE["nc"]
    in_maps = [make_core_inputs(c, inputs) for c in range(8)]
    res = run_bass_kernel_spmd(nc, in_maps, core_ids=list(range(8)))
    out = np.zeros((B, N, DMODEL), np.float32)
    for c in range(8):
        b, h = c // 2, c % 2
        out[b, h * T:(h + 1) * T] = np.asarray(res.results[c]["out"])
    return out


# version 11
# speedup vs baseline: 1.0734x; 1.0089x over previous
from contextlib import ExitStack

import numpy as np
import ml_dtypes
import concourse.bass as bass
import concourse.mybir as mybir
from concourse.bass_utils import run_bass_kernel_spmd

F32 = mybir.dt.float32
BF16 = mybir.dt.bfloat16
FP16 = mybir.dt.float16
AF = mybir.ActivationFunctionType
ALU = mybir.AluOpType
AX = mybir.AxisListType

B, N, DMODEL = 4, 8192, 1024
T = 4096
HALO = 1152
EXT = HALO + T
W = 512
INNER_M = 2048
DSTATE = 64
HEADS, DH = 8, 64
INNER_A = 512
HID = 2730
NEG = -1.0e30
EPS = 1e-5
DH0 = 644
DACC_W = DH0 + T
HT = T // 2
HREM = HID - 21 * 128
ABF_W = 87040
AF_W = 6400
CONSTS_LIMIT = None
KH_W = 384 + EXT


def make_core_inputs(core, inputs):
    b, h = core // 2, core % 2
    s0 = h * T
    x = np.asarray(inputs["L0_attn_x"])
    lo = s0 - HALO
    xs = np.zeros((EXT, DMODEL), np.float32)
    src_lo = max(lo, 0)
    xs[src_lo - lo:] = np.asarray(x[b, src_lo:s0 + T], np.float32)

    pos = np.clip(np.arange(lo, s0 + T, dtype=np.float64), 0, None)
    half = np.arange(0, DH, 2, dtype=np.float64)
    inv_freq = 1.0 / 10000.0 ** (half / DH)
    fr = pos[None, :] * inv_freq[:, None]
    freqs = np.concatenate([fr, fr], 0)
    base = (half + 0.4 * DH) / (1.4 * DH)
    power = (pos - N // 2) / float(W // 2)
    scale = np.concatenate([base[:, None] ** power[None, :]] * 2, 0)
    cos, sin = np.cos(freqs), np.sin(freqs)
    sgn = np.concatenate([-np.ones(32), np.ones(32)])[:, None]

    r = np.arange(128)
    m_own = np.where(r[:, None] > r[None, :], NEG, 0.0)
    m_look = np.where(r[None, :] > r[:, None], NEG, 0.0)
    if h == 0:
        m0_d = np.full((128, 128), NEG, np.float32)
        m0_rect = np.full((128, W), NEG, np.float32)
    else:
        m0_d = m_look.astype(np.float32)
        m0_rect = np.zeros((128, W), np.float32)

    blkdiag = np.zeros((128, 128))
    blkdiag[:64, :64] = 1.0
    blkdiag[64:, 64:] = 1.0

    def f32(a):
        return np.ascontiguousarray(a, np.float32)

    d = {
        "xs": xs,
        "cs1q": f32(cos * scale), "cs2q": f32(sin * scale * sgn),
        "cs1k": f32(cos / scale), "cs2k": f32(sin / scale * sgn),
        "m_own": m_own.astype(ml_dtypes.bfloat16),
        "m_look": m_look.astype(ml_dtypes.bfloat16),
        "m0_d": m0_d.astype(ml_dtypes.bfloat16),
        "m0_rect": m0_rect.astype(ml_dtypes.bfloat16),
        "eye_bf": np.eye(128, dtype=ml_dtypes.bfloat16),
        "eye_f32": np.eye(128, dtype=np.float32),
        "ones_bf": np.ones((128, 128), ml_dtypes.bfloat16),
        "blkdiag_bf": blkdiag.astype(ml_dtypes.bfloat16),
        "evenm": np.full((128, 1), 1.0 if h == 0 else 0.0, np.float32),
        "oddm": np.full((128, 1), 1.0 if h == 1 else 0.0, np.float32),
        "haloflag": np.full((128, 1), 0.0 if h == 0 else 1.0, np.float32),
    }
    for k, v in inputs.items():
        if k != "L0_attn_x":
            d[k] = np.ascontiguousarray(np.asarray(v))
    cw = np.asarray(inputs["L0_attn_conv_w"], np.float32)
    for k in range(3):
        d[f"conv_w{k}"] = np.ascontiguousarray(cw[:, k])
    return d


def ap2(handle, offset, dims):
    return bass.AP(handle, offset, [list(x) for x in dims])


class Sem:
    def __init__(self, ctx, nc, name):
        self.h = ctx.enter_context(nc.semaphore(name))
        self.v = 0

    def inc(self, instr, n=1):
        instr.then_inc(self.h, n)
        self.v += n
        return self.v


class Tile:
    def __init__(self, ap, col):
        self.ap = ap
        self.col = col

    def __getitem__(self, idx):
        return self.ap[idx]


class Alloc:
    def __init__(self, arena, width):
        self.a, self.w, self.top = arena, width, 0

    def tile(self, n, parts=128, base=0):
        c0 = self.top
        self.top += n
        assert self.top <= self.w, f"arena overflow {self.top} > {self.w}"
        return Tile(self.a[base:base + parts, c0:c0 + n], c0)


class Drained:

    _PASS = ("wait_ge", "dma_start", "drain", "engine_nop", "nop",
             "collective_compute", "indirect_dma_start", "then_inc")

    def __init__(self, eng):
        self._e = eng

    def __getattr__(self, nm):
        f = getattr(self._e, nm)
        if nm in self._PASS or not callable(f):
            return f
        e = self._e

        def w(*a, **k):
            f(*a, **k)
            return e.drain()
        return w


class P:
    pass


def build_program(dev=False, upto=None):
    nc = bass.Bass("TRN2", target_bir_lowering=False)
    io = {}

    def din(name, shape, dtype=F32):
        io[name] = nc.dram_tensor(name, list(shape), dtype, kind="ExternalInput")

    def dscr(name, shape, dtype):
        kind = "ExternalOutput" if dev else "Internal"
        io[name] = nc.dram_tensor(name, list(shape), dtype, kind=kind)

    din("xs", [EXT, DMODEL])
    for nm in ["cs1q", "cs2q", "cs1k", "cs2k"]:
        din(nm, [DH, EXT])
    for nm in ["m_own", "m_look", "m0_d"]:
        din(nm, [128, 128], BF16)
    din("m0_rect", [128, W], BF16)
    din("eye_bf", [128, 128], BF16)
    din("eye_f32", [128, 128], F32)
    din("ones_bf", [128, 128], BF16)
    din("blkdiag_bf", [128, 128], BF16)
    din("evenm", [128, 1])
    din("oddm", [128, 1])
    din("haloflag", [128, 1])
    din("L0_attn_W_in", [DMODEL, 2 * INNER_M])
    din("L0_attn_b_in", [2 * INNER_M])
    din("L0_attn_conv_w", [INNER_M, 3])
    for k in range(3):
        din(f"conv_w{k}", [INNER_M])
    din("L0_attn_conv_b", [INNER_M])
    din("L0_attn_W_dt", [INNER_M, DSTATE])
    din("L0_attn_b_dt", [DSTATE])
    din("L0_attn_W_B", [INNER_M, DSTATE])
    din("L0_attn_b_B", [DSTATE])
    din("L0_attn_W_C", [INNER_M, DSTATE])
    din("L0_attn_b_C", [DSTATE])
    din("L0_attn_W_si", [DSTATE, INNER_M])
    din("L0_attn_b_si", [INNER_M])
    din("L0_attn_D", [INNER_M])
    din("L0_attn_W_out", [INNER_M, DMODEL])
    din("L0_attn_b_out", [DMODEL])
    din("L0_ffn_wqkv", [2, DMODEL, 3 * INNER_A])
    din("L0_ffn_q_scale", [2, DH])
    din("L0_ffn_k_scale", [2, DH])
    din("L0_ffn_wo", [2, INNER_A, DMODEL])
    din("L0_ffn_ln_g", [2, DMODEL])
    din("L0_ffn_ln_b", [2, DMODEL])
    din("L0_ffn_w1", [2, DMODEL, 2 * HID])
    din("L0_ffn_b1", [2, 2 * HID])
    din("L0_ffn_w2", [2, HID, DMODEL])
    din("L0_ffn_b2", [2, DMODEL])
    io["out"] = nc.dram_tensor("out", [T, DMODEL], F32, kind="ExternalOutput")
    dscr("u_scr", [T, DMODEL], F32)
    dscr("xnfm", [DMODEL, 1 + EXT], BF16)
    dscr("xmain", [INNER_M, T], BF16)
    dscr("sgate", [INNER_M, T], FP16)
    dscr("dm", [DMODEL, T], F32)
    dscr("st_scr", [128, HT], F32)
    dscr("pp_scr", [128, HT], BF16)
    dscr("c_scr", [128, HT], BF16)
    dscr("qh", [INNER_A, EXT], BF16)
    dscr("kh", [INNER_A, KH_W], BF16)
    dscr("vstat", [44 * 128, 768], BF16)
    dscr("dacc_a", [DMODEL, DACC_W], F32)
    dscr("dacc_b", [DMODEL, DACC_W], F32)
    dscr("dbg", [128, 2048], BF16)
    io["cc_in"] = nc.dram_tensor("cc_in", [DSTATE, 1], F32)
    io["cc_out"] = nc.dram_tensor("cc_out", [DSTATE, 1], F32)

    with ExitStack() as ctx:
        p = P()
        p.nc, p.ctx, p.io, p.dev = nc, ctx, io, dev
        p.abf = ctx.enter_context(nc.sbuf_tensor("abf", [128, ABF_W], BF16))
        p.af = ctx.enter_context(nc.sbuf_tensor("af", [128, AF_W], F32))
        p.af16 = ctx.enter_context(nc.sbuf_tensor("af16", [128, 1024], FP16))
        p.af16b = ctx.enter_context(nc.sbuf_tensor("af16b", [128, 1024], FP16))
        p.eye_bf = ctx.enter_context(nc.sbuf_tensor("sb_eye_bf", [128, 128], BF16))
        p.eye_f32 = ctx.enter_context(nc.sbuf_tensor("sb_eye_f32", [128, 128], F32))
        p.ones_bf = ctx.enter_context(nc.sbuf_tensor("sb_ones_bf", [128, 128], BF16))
        p.blkdiag = ctx.enter_context(nc.sbuf_tensor("sb_blkdiag", [128, 128], BF16))
        p.masks = ctx.enter_context(nc.sbuf_tensor("masks", [128, 384 + W], BF16))
        p.bias = ctx.enter_context(nc.sbuf_tensor("bias", [128, 288], F32))
        p.flags = ctx.enter_context(nc.sbuf_tensor("flags", [128, 8], F32))
        p.small = ctx.enter_context(nc.sbuf_tensor("small", [128, 80], F32))
        p.pf = [ctx.enter_context(nc.psum_tensor(f"pf{i}", [128, 512], F32))
                for i in range(6)]
        p.pb = [ctx.enter_context(nc.psum_tensor(f"pb{i}", [128, 1024], BF16))
                for i in range(2)]
        for nm in ["ld", "wl", "mm", "ev", "sc", "dd", "st", "cc", "g0", "g1",
                   "g2", "g3", "a0", "a1", "a2", "a3", "a4", "a5", "a6", "a7",
                   "d0", "d1", "d2", "d3", "d4", "d5", "d6", "d7", "d8", "d9"]:
            setattr(p, "s_" + nm, Sem(ctx, nc, nm))
        p.BOFF = {}
        steps = [lambda: load_consts(p),
                 lambda: phase_p0(p),
                 lambda: phase_p1a(p),
                 lambda: phase_qkv(p, 0),
                 lambda: phase_p1b(p),
                 lambda: phase_attn(p, 0),
                 lambda: phase_mlp(p, 0),
                 lambda: phase_qkv(p, 1),
                 lambda: phase_attn(p, 1),
                 lambda: phase_mlp(p, 1),
                 lambda: phase_fin(p)]
        if upto is None:
            upto = len(steps)
        for f in steps[:upto]:
            f()
    return nc, io


def load_consts(p):
    nc, io = p.nc, p.io
    s_ld, s_mm, s_ev = p.s_ld, p.s_mm, p.s_ev
    vtmp = p.af[:, 0:128]
    vps = p.pf[0][:, 0:128]
    cs = [0]

    with nc.Block("consts") as blk:
        @blk.gpsimd
        def _(g):
            g = Drained(g)
            for dst, src in [(p.eye_bf, "eye_bf"), (p.eye_f32, "eye_f32"),
                             (p.ones_bf, "ones_bf"), (p.blkdiag, "blkdiag_bf")]:
                s_ld.inc(g.dma_start(dst[:, :], io[src][:, :]), 16)
            s_ld.inc(g.dma_start(p.masks[:, 0:128], io["m_own"][:, :]), 16)
            s_ld.inc(g.dma_start(p.masks[:, 128:256], io["m_look"][:, :]), 16)
            s_ld.inc(g.dma_start(p.masks[:, 256:384], io["m0_d"][:, :]), 16)
            s_ld.inc(g.dma_start(p.masks[:, 384:384 + W], io["m0_rect"][:, :]), 16)
            s_ld.inc(g.dma_start(p.flags[:, 0:1], io["evenm"][:, :]), 16)
            s_ld.inc(g.dma_start(p.flags[:, 1:2], io["oddm"][:, :]), 16)
            s_ld.inc(g.dma_start(p.flags[:, 2:3], io["haloflag"][:, :]), 16)
            for l in range(2):
                s_ld.inc(g.dma_start(
                    p.af[0:1, 512 + 128 * l:576 + 128 * l],
                    ap2(io["L0_ffn_q_scale"], l * DH, [[64, 1], [1, 64]])), 16)
                s_ld.inc(g.dma_start(
                    p.af[0:1, 576 + 128 * l:640 + 128 * l],
                    ap2(io["L0_ffn_k_scale"], l * DH, [[64, 1], [1, 64]])), 16)
            g.memset(p.abf[:, 0:768], 0.0)
            g.memset(p.af[:, 1408:1440], 0.0)
            g.memset(p.flags[:, 3:4], EPS)
            g.memset(p.af[0:1, 1280:1408], 1.0)
            p.s_a5.inc(g.memset(p.flags[:, 4:5], 0.0))
            g.wait_ge(s_ld.h, s_ld.v)

        @blk.sync
        def _(sy):
            if CONSTS_LIMIT == -4:
                return
            sy.wait_ge(p.s_a5.h, p.s_a5.v)
            p.s_st.inc(sy.dma_start(
                ap2(io["dacc_b"], 0,
                    [[DACC_W, 128], [128 * DACC_W, 8], [1, 4]]),
                ap2(p.af, 1408, [[AF_W, 128], [4, 8], [1, 4]])), 16)
            p.s_st.inc(sy.dma_start(
                ap2(io["xnfm"], 0, [[1 + EXT, 128], [128 * (1 + EXT), 8], [1, 1]]),
                ap2(p.abf, 0, [[ABF_W, 128], [1, 8], [1, 1]]),
                allow_slow_non_contiguous=True), 16)
            for c in range(4):
                p.s_st.inc(sy.dma_start(
                    ap2(io["kh"], c * 128 * KH_W, [[KH_W, 128], [1, 384]]),
                    ap2(p.abf, 0, [[ABF_W, 128], [1, 384]])), 16)
            for blki in range(3):
                p.s_st.inc(sy.dma_start(
                    ap2(io["vstat"], blki * 128 * 768, [[768, 128], [1, 768]]),
                    ap2(p.abf, 0, [[ABF_W, 128], [1, 768]])), 16)
            sy.wait_ge(p.s_st.h, p.s_st.v)

    def vec(name, dram, length, offset=0):
        nchunk = (length + 127) // 128
        col = cs[0]
        cs[0] += nchunk
        p.BOFF[name] = col
        with nc.Block("v_" + name) as blk:
            @blk.gpsimd
            def _(g):
                g = Drained(g)
                p.s_a5.inc(g.memset(vtmp[:, :], 0.0))
                g.wait_ge(p.s_a5.h, p.s_a5.v)
                full, rem = divmod(length, 128)
                if full:
                    s_ld.inc(g.dma_start(
                        vtmp[0:full, 0:128],
                        ap2(dram, offset, [[128, full], [1, 128]])), 16)
                if rem:
                    s_ld.inc(g.dma_start(
                        vtmp[full:full + 1, 0:rem],
                        ap2(dram, offset + full * 128, [[rem, 1], [1, rem]])), 16)

            @blk.tensor
            def _(t):
                t.wait_ge(s_ld.h, s_ld.v)
                s_mm.inc(t.transpose(vps, vtmp[:, :], p.eye_f32[:, :]))

            @blk.vector
            def _(v):
                v = Drained(v)
                v.wait_ge(s_mm.h, s_mm.v)
                s_ev.inc(v.tensor_copy(p.bias[:, col:col + nchunk],
                                       vps[:, 0:nchunk]))

    veclist = [(f"conv_w{k}", io[f"conv_w{k}"], INNER_M, 0) for k in range(3)]
    veclist += [("b_in", io["L0_attn_b_in"], 2 * INNER_M, 0),
               ("conv_b", io["L0_attn_conv_b"], INNER_M, 0),
               ("b_si", io["L0_attn_b_si"], INNER_M, 0),
               ("D", io["L0_attn_D"], INNER_M, 0),
               ("b_out", io["L0_attn_b_out"], DMODEL, 0)]
    for l in range(2):
        veclist += [(f"ln_g{l}", io["L0_ffn_ln_g"], DMODEL, l * DMODEL),
                    (f"ln_b{l}", io["L0_ffn_ln_b"], DMODEL, l * DMODEL),
                    (f"b1a{l}", io["L0_ffn_b1"], HID, l * 2 * HID),
                    (f"b1g{l}", io["L0_ffn_b1"], HID, l * 2 * HID + HID),
                    (f"b2{l}", io["L0_ffn_b2"], DMODEL, l * DMODEL)]
    if CONSTS_LIMIT is not None:
        veclist = veclist[:max(0, CONSTS_LIMIT)]
    for nm_, dram_, ln_, off_ in veclist:
        vec(nm_, dram_, ln_, off_)
    if CONSTS_LIMIT == -4:
        pass
    if CONSTS_LIMIT is not None and CONSTS_LIMIT == -1:
        return
    SKIP_TAIL = CONSTS_LIMIT == -2
    SKIP_BOUND = CONSTS_LIMIT == -3
    if CONSTS_LIMIT in (-4, -5, -6):
        SKIP_TAIL = SKIP_BOUND = False

    if SKIP_TAIL:
        with nc.Block("bound") as blk:
            @blk.gpsimd
            def _(g):
                g = Drained(g)
                for l in range(2):
                    s_ld.inc(g.dma_start(
                        p.af[0:1, 0 + 128 * l:64 + 128 * l],
                        ap2(io["L0_ffn_q_scale"], l * DH, [[64, 1], [1, 64]])), 16)
                    s_ld.inc(g.dma_start(
                        p.af[0:1, 64 + 128 * l:128 + 128 * l],
                        ap2(io["L0_ffn_k_scale"], l * DH, [[64, 1], [1, 64]])), 16)
            @blk.vector
            def _(v):
                v = Drained(v)
                v.wait_ge(s_ld.h, s_ld.v)
                s_ev.inc(v.tensor_copy(p.af[0:1, 192:256], p.af[0:1, 0:64]))
        return
    with nc.Block("v_tail") as blk:
        @blk.gpsimd
        def _(g):
            g = Drained(g)
            for nm, dram, base in [("b_dt", io["L0_attn_b_dt"], 0),
                                   ("b_B", io["L0_attn_b_B"], 64),
                                   ("b_C", io["L0_attn_b_C"], 0)]:
                col = cs[0]
                cs[0] += 1
                p.BOFF[nm] = col
                s_ld.inc(g.dma_start(p.bias[base:base + 64, col:col + 1],
                                     ap2(dram, 0, [[1, 64], [1, 1]]),
                                     allow_slow_non_contiguous=True), 16)
            for l in range(2):
                for j, nm in [(0, "L0_ffn_q_scale"), (1, "L0_ffn_k_scale")]:
                    cc = l * 2 + j
                    for half in range(2):
                        s_ld.inc(g.dma_start(
                            p.small[64 * half:64 * half + 64, cc:cc + 1],
                            ap2(io[nm], l * DH, [[1, 64], [1, 1]]),
                            allow_slow_non_contiguous=True), 16)
            g.wait_ge(s_ld.h, s_ld.v)
    assert cs[0] <= 288, cs[0]

    if SKIP_BOUND:
        return
    with nc.Block("bound") as blk:
        @blk.vector
        def _(v):
            v = Drained(v)
            v.wait_ge(s_ld.h, s_ld.v)
            for l in range(2):
                v.tensor_tensor(p.af[0:1, 1024 + 64 * l:1024 + 64 * l + 64],
                                p.af[0:1, 512 + 128 * l:512 + 128 * l + 64],
                                p.af[0:1, 512 + 128 * l + 64:512 + 128 * l + 128],
                                ALU.mult)
                v.reduce_max(p.small[0:1, 56 + l:57 + l],
                             p.af[0:1, 1024 + 64 * l:1024 + 64 * l + 64],
                             axis=AX.X, apply_absolute_value=True)
                s_ev.inc(v.tensor_scalar(p.small[0:1, 56 + l:57 + l],
                                         p.small[0:1, 56 + l:57 + l],
                                         -8.0, None, ALU.mult))

        @blk.tensor
        def _(t):
            t.wait_ge(s_ev.h, s_ev.v)
            s_mm.inc(t.matmul(p.pf[1][:, 0:2], p.af[0:1, 1280:1408],
                              p.small[0:1, 56:58], start=True, stop=True))

        @blk.scalar
        def _(s):
            s = Drained(s)
            s.wait_ge(s_mm.h, s_mm.v)
            s_ev.inc(s.copy(p.small[:, 6:8], p.pf[1][:, 0:2]))


def phase_p0(p):
    nc, io = p.nc, p.io
    s_a, s_b, s_t, s_e2, s_e3 = (p.s_a0, p.s_a1, p.s_a2, p.s_a3, p.s_a4)
    s_ldp = [p.s_d0, p.s_d1]
    s_fmp = [p.s_d2, p.s_d3]
    s_up = [p.s_d4, p.s_d5]
    ab = Alloc(p.abf, ABF_W)
    af = Alloc(p.af, AF_W)
    xnbf = [ab.tile(1024) for _ in range(2)]
    sqj = ab.tile(1024)
    fmst = [ab.tile(1024) for _ in range(2)]
    xt = [af.tile(1024) for _ in range(2)]
    xn32 = af.tile(1024)
    ut = [af.tile(1024) for _ in range(2)]
    stt = af.tile(16)
    NT = EXT // 128
    OWN0 = HALO // 128
    e20, e30, a0, b0, t0 = s_e2.v, s_e3.v, s_a.v, s_b.v, s_t.v
    ld_mk, fm_mk, u_mk = {}, {}, {}
    for i in range(NT):
        ld_mk[i] = s_ldp[i % 2].v + 16 * (i // 2 + 1)
        fm_mk[i] = s_fmp[i % 2].v + 16 * (i // 2 + 1)
    ucnt = [0, 0]
    for i in range(OWN0, NT):
        ucnt[i % 2] += 1
        u_mk[i] = s_up[i % 2].v + 16 * ucnt[i % 2]

    with nc.Block("p0") as blk:
        @blk.sync
        def _(s):
            for i in range(NT):
                if i >= 2:
                    s.wait_ge(s_e3.h, e30 + i - 1)
                assert s_ldp[i % 2].inc(
                    s.dma_start(xt[i % 2][:, :],
                                io["xs"][i * 128:(i + 1) * 128, :]),
                    16) == ld_mk[i]
                s.wait_ge(s_e2.h, e20 + 8 * (i + 1))
                assert s_fmp[i % 2].inc(s.dma_start(
                    ap2(io["xnfm"], 1 + i * 128,
                        [[1 + EXT, 128], [128 * (1 + EXT), 8], [1, 128]]),
                    ap2(p.abf, fmst[i % 2].col,
                        [[ABF_W, 128], [128, 8], [1, 128]])), 16) == fm_mk[i]
                if i >= OWN0:
                    s.wait_ge(s_e3.h, e30 + i + 1)
                    assert s_up[i % 2].inc(s.dma_start(
                        io["u_scr"][(i - OWN0) * 128:(i - OWN0 + 1) * 128, :],
                        ut[i % 2][:, :]), 16) == u_mk[i]
            for sp in (s_fmp[0], s_fmp[1], s_up[0], s_up[1]):
                s.wait_ge(sp.h, sp.v)

        @blk.vector
        def _(v):
            v = Drained(v)
            for i in range(NT):
                x = xt[i % 2]
                v.wait_ge(s_ldp[i % 2].h, ld_mk[i])
                if i >= 1:
                    v.wait_ge(s_b.h, b0 + i)
                v.scalar_tensor_tensor(sqj[:, :], x[:, :], 1.0, x[:, :],
                                       ALU.mult, ALU.mult, accum_out=stt[:, 1:2])
                v.reduce_sum(stt[:, 0:1], x[:, :], axis=AX.X)
                v.tensor_scalar(stt[:, 2:3], stt[:, 0:1], 1.0 / DMODEL, None, ALU.mult)
                v.tensor_scalar(stt[:, 3:4], stt[:, 1:2], 1.0 / DMODEL, None, ALU.mult)
                v.tensor_tensor(stt[:, 4:5], stt[:, 2:3], stt[:, 2:3], ALU.mult)
                s_a.inc(v.tensor_tensor(stt[:, 3:4], stt[:, 3:4], stt[:, 4:5],
                                        ALU.subtract))
                for j in range(8):
                    v.wait_ge(s_t.h, t0 + i * 8 + j + 1)
                    s_e2.inc(v.tensor_copy(fmst[i % 2][:, j * 128:(j + 1) * 128],
                                           p.pb[j % 2][:, 0:128]))
                if i >= OWN0:
                    v.wait_ge(s_b.h, b0 + i + 1)
                    if i - 2 in u_mk:
                        v.wait_ge(s_up[i % 2].h, u_mk[i - 2])
                    s_e3.inc(v.tensor_add(ut[i % 2][:, :], x[:, :],
                                          xn32[:, :]))
                else:
                    s_e3.inc(v.engine_nop())

        @blk.scalar
        def _(s):
            s = Drained(s)
            for i in range(NT):
                x = xt[i % 2]
                s.wait_ge(s_a.h, a0 + i + 1)
                if i >= 1:
                    s.wait_ge(s_e3.h, e30 + i)
                s.activation(stt[:, 5:6], stt[:, 3:4], AF.Ln, bias=p.flags[:, 3:4], scale=1.0)
                s.activation(stt[:, 6:7], stt[:, 5:6], AF.Exp, bias=0.0, scale=-0.5)
                s.mul(stt[:, 7:8], stt[:, 2:3], stt[:, 6:7])
                s.mul(stt[:, 7:8], stt[:, 7:8], -1.0)
                last = s.activation(xnbf[i % 2][:, :], x[:, :], AF.Identity,
                                    bias=stt[:, 7:8], scale=stt[:, 6:7])
                if i >= OWN0:
                    last = s.activation(xn32[:, :], x[:, :], AF.Identity,
                                        bias=stt[:, 7:8], scale=stt[:, 6:7])
                s_b.inc(last)

        @blk.tensor
        def _(t):
            for i in range(NT):
                t.wait_ge(s_b.h, b0 + i + 1)
                if i >= 2:
                    t.wait_ge(s_fmp[i % 2].h, fm_mk[i - 2])
                for j in range(8):
                    k = i * 8 + j
                    if k >= 2:
                        t.wait_ge(s_e2.h, e20 + k - 1)
                    s_t.inc(t.transpose(p.pb[j % 2][:, 0:128],
                                        xnbf[i % 2][:, j * 128:(j + 1) * 128],
                                        p.eye_bf[:, :]))


def phase_p1a(p):
    nc, io = p.nc, p.io
    BO = p.BOFF
    ab = Alloc(p.abf, ABF_W)
    af = Alloc(p.af, AF_W)
    w_in = ab.tile(8 * 4096)
    wdtB = ab.tile(16 * 128)
    wC = ab.tile(16 * 64)
    xslab = [ab.tile(8 * 512) for _ in range(2)]
    work = [ab.tile(514) for _ in range(16)]
    xmn = [ab.tile(512) for _ in range(16)]
    C_bf = ab.tile(HT)
    pp_bf = ab.tile(HT)
    zz_bf = ab.tile(HT)
    xm2 = [ab.tile(512) for _ in range(2)]
    gt4 = [Tile(p.af16[:, 0:512], 0), Tile(p.af16[:, 512:1024], 512),
           Tile(p.af16b[:, 0:512], 0), Tile(p.af16b[:, 512:1024], 512)]
    cva = af.tile(512)
    cvb = [af.tile(512) for _ in range(2)]
    tmp_ev = af.tile(512)
    dt_s = af.tile(HT)
    B_s = af.tile(HT)
    st_s = Tile(p.af[:, 0:HT], 0)
    NTT = 9
    s_wl, s_mm, s_sc, s_cv, s_xm, s_w = (p.s_wl, p.s_mm, p.s_sc,
                                         p.s_ev, p.s_a0, p.s_st)
    s_d2, s_d3 = p.s_a1, p.s_a2
    s_ldp = [p.s_d0, p.s_d1]
    s_xmp = [p.s_d2b, p.s_d3b] if False else [p.s_d6, p.s_d7]
    s_gtp = [p.s_d8, p.s_d9]
    s_ld2 = [p.s_d4, p.s_d5]

    with nc.Block("p1a_w") as blk:
        @blk.gpsimd
        def _(g):
            g = Drained(g)
            for c in range(8):
                s_wl.inc(g.dma_start(
                    ap2(p.abf, w_in.col + c * 4096, [[ABF_W, 128], [1, 4096]]),
                    io["L0_attn_W_in"][c * 128:(c + 1) * 128, :]), 16)
            for c in range(16):
                s_wl.inc(g.dma_start(
                    ap2(p.abf, wdtB.col + c * 128, [[ABF_W, 128], [1, 64]]),
                    io["L0_attn_W_dt"][c * 128:(c + 1) * 128, :]), 16)
                s_wl.inc(g.dma_start(
                    ap2(p.abf, wdtB.col + c * 128 + 64, [[ABF_W, 128], [1, 64]]),
                    io["L0_attn_W_B"][c * 128:(c + 1) * 128, :]), 16)
                s_wl.inc(g.dma_start(
                    ap2(p.abf, wC.col + c * 64, [[ABF_W, 128], [1, 64]]),
                    io["L0_attn_W_C"][c * 128:(c + 1) * 128, :]), 16)
            g.memset(zz_bf[:, :], 0.0)
            for fc in range(16):
                g.memset(work[fc][:, 0:2], 0.0)
            g.wait_ge(s_wl.h, s_wl.v)

    mm0, sc0, cv0, xm0 = s_mm.v, s_sc.v, s_cv.v, s_xm.v
    ld_mk = {t: s_ldp[t % 2].v + 16 * (t // 2 + 1) for t in range(NTT)}
    xm_mk = {}
    gt_sp = {}
    gb = [s_gtp[0].v, s_gtp[1].v]
    for tt in range(1, NTT):
        xm_mk[tt] = s_xmp[tt % 2].v + 16 * 16 * ((tt - 1) // 2 + 1)
        for fc in range(16):
            gi = (tt - 1) * 16 + fc
            gt_sp[gi] = gb[gi % 2] + 16 * (gi // 2 + 1)

    with nc.Block("p1a1") as blk:
        @blk.sync
        def _(s):
            for t in range(NTT + 1):
                if t >= 2:
                    tt = t - 1
                    for fc in range(16):
                        s.wait_ge(s_xm.h, xm0 + tt * 16 + fc + 1)
                        mk = s_xmp[tt % 2].inc(s.dma_start(
                            io["xmain"][fc * 128:(fc + 1) * 128,
                                        (tt - 1) * 512:tt * 512],
                            xmn[fc][:, :]), 16)
                        if fc == 15:
                            assert mk == xm_mk[tt]
                    for fc in range(16):
                        gi = (tt - 1) * 16 + fc
                        s.wait_ge(s_sc.h, sc0 + tt * 32 + 16 + fc + 1)
                        if gi >= 2:
                            s.wait_ge(s_gtp[gi % 2].h, gt_sp[gi - 2])
                        assert s_gtp[gi % 2].inc(s.dma_start(
                            io["sgate"][fc * 128:(fc + 1) * 128,
                                        (tt - 1) * 512:tt * 512],
                            gt4[fc % 4][:, :]), 16) == gt_sp[gi]
                if t < NTT:
                    if t >= 2:
                        s.wait_ge(s_mm.h, mm0 + (t - 1) * 32)
                    assert s_ldp[t % 2].inc(s.dma_start(
                        ap2(p.abf, xslab[t % 2].col,
                            [[ABF_W, 128], [512, 8], [1, 512]]),
                        ap2(io["xnfm"], 1 + HALO - 512 + t * 512,
                            [[1 + EXT, 128], [128 * (1 + EXT), 8], [1, 512]])),
                        16) == ld_mk[t]

        @blk.tensor
        def _(t_):
            for t in range(NTT):
                t_.wait_ge(s_ldp[t % 2].h, ld_mk[t])
                for fc in range(32):
                    k = t * 32 + fc
                    if k >= 2:
                        t_.wait_ge(s_sc.h, sc0 + k - 1)
                    ps = p.pf[fc % 2]
                    for c in range(8):
                        mi = t_.matmul(ps[:, :],
                                       ap2(p.abf, w_in.col + c * 4096 + fc * 128,
                                           [[ABF_W, 128], [1, 128]]),
                                       ap2(p.abf, xslab[t % 2].col + c * 512,
                                           [[ABF_W, 128], [1, 512]]),
                                       start=(c == 0), stop=(c == 7))
                    s_mm.inc(mi)

        @blk.scalar
        def _(s):
            s = Drained(s)
            for t in range(NTT):
                for fc in range(16):
                    s.wait_ge(s_mm.h, mm0 + t * 32 + fc + 1)
                    if t >= 1:
                        s.wait_ge(s_cv.h, cv0 + ((t - 1) * 16 + fc) * 2 + 2)
                    s_sc.inc(s.activation(
                        work[fc][:, 2:514], p.pf[fc % 2][:, :], AF.Identity,
                        bias=p.bias[:, BO["b_in"] + fc:BO["b_in"] + fc + 1],
                        scale=1.0))
                    if fc >= 1:
                        fp = fc - 1
                        s.wait_ge(s_cv.h, cv0 + (t * 16 + fp) * 2 + 1)
                        if fp == 0 and (t - 1) in xm_mk:
                            s.wait_ge(s_xmp[(t - 1) % 2].h, xm_mk[t - 1])
                        s_xm.inc(s.activation(
                            xmn[fp][:, :], cvb[fp % 2][:, :], AF.Silu,
                            bias=p.bias[:, BO["conv_b"] + fp:BO["conv_b"] + fp + 1],
                            scale=1.0))
                fp = 15
                s.wait_ge(s_cv.h, cv0 + (t * 16 + fp) * 2 + 1)
                s_xm.inc(s.activation(
                    xmn[fp][:, :], cvb[fp % 2][:, :], AF.Silu,
                    bias=p.bias[:, BO["conv_b"] + fp:BO["conv_b"] + fp + 1],
                    scale=1.0))
                for fc in range(16):
                    s.wait_ge(s_mm.h, mm0 + t * 32 + 16 + fc + 1)
                    gi = ((t - 1) * 16 + fc - 4) if fc >= 4 \
                        else ((t - 2) * 16 + fc + 12)
                    if gi in gt_sp:
                        s.wait_ge(s_gtp[gi % 2].h, gt_sp[gi])
                    s_sc.inc(s.activation(
                        gt4[fc % 4][:, :], p.pf[fc % 2][:, :], AF.Sigmoid,
                        bias=p.bias[:, BO["b_in"] + 16 + fc:BO["b_in"] + 17 + fc],
                        scale=1.0))

        @blk.vector
        def _(v):
            v = Drained(v)
            for t in range(NTT):
                for fc in range(16):
                    v.wait_ge(s_sc.h, sc0 + t * 32 + fc + 1)
                    wk = work[fc]
                    c0 = BO["conv_w0"] + fc
                    c1 = BO["conv_w1"] + fc
                    c2 = BO["conv_w2"] + fc
                    v.tensor_scalar(cva[:, :], wk[:, 1:513],
                                    p.bias[:, c1:c1 + 1], None, ALU.mult)
                    v.scalar_tensor_tensor(cvb[fc % 2][:, :], wk[:, 0:512],
                                           p.bias[:, c0:c0 + 1], cva[:, :],
                                           ALU.mult, ALU.add)
                    s_cv.inc(v.scalar_tensor_tensor(
                        cvb[fc % 2][:, :], wk[:, 2:514],
                        p.bias[:, c2:c2 + 1], cvb[fc % 2][:, :],
                        ALU.mult, ALU.add))
                    s_cv.inc(v.tensor_copy(wk[:, 0:2], wk[:, 512:514]))

    mm1, sc1, d20, d30 = s_mm.v, s_sc.v, s_d2.v, s_d3.v
    ld2_mk = {}
    for t in range(8):
        for fc in range(16):
            k = t * 16 + fc
            ld2_mk[k] = s_ld2[k % 2].v + 16 * (k // 2 + 1)
    with nc.Block("p1a2") as blk:
        @blk.sync
        def _(s):
            for sp in (s_xmp[0], s_xmp[1], s_gtp[0], s_gtp[1]):
                s.wait_ge(sp.h, sp.v)
            for t in range(8):
                for fc in range(16):
                    k = t * 16 + fc
                    if k >= 2:
                        s.wait_ge(s_mm.h, mm1 + k - 1)
                    assert s_ld2[k % 2].inc(s.dma_start(
                        xm2[fc % 2][:, :],
                        io["xmain"][fc * 128:(fc + 1) * 128,
                                    t * 512:(t + 1) * 512]), 16) == ld2_mk[k]

        @blk.tensor
        def _(t_):
            for t in range(8):
                for fc in range(16):
                    k = t * 16 + fc
                    t_.wait_ge(s_ld2[k % 2].h, ld2_mk[k])
                    if fc == 0 and t >= 1:
                        t_.wait_ge(s_sc.h, sc1 + t)
                    t_.matmul(p.pf[2][:, :],
                              ap2(p.abf, wdtB.col + fc * 128,
                                  [[ABF_W, 128], [1, 128]]),
                              xm2[fc % 2][:, :],
                              start=(fc == 0), stop=(fc == 15))
                    mi = t_.matmul(p.pf[3][0:64, :],
                                   ap2(p.abf, wC.col + fc * 64,
                                       [[ABF_W, 128], [1, 64]]),
                                   xm2[fc % 2][:, :],
                                   start=(fc == 0), stop=(fc == 15))
                    s_mm.inc(mi)

        @blk.scalar
        def _(s):
            s = Drained(s)
            for t in range(8):
                s.wait_ge(s_mm.h, mm1 + (t + 1) * 16)
                if t >= 1:
                    s.wait_ge(s_d3.h, d30 + t * 3)
                s.activation(tmp_ev[0:64, :], p.pf[2][0:64, :], AF.Sigmoid,
                             bias=p.bias[0:64, BO["b_dt"]:BO["b_dt"] + 1],
                             scale=1.0)
                s.activation(tmp_ev[64:128, :], p.pf[2][64:128, :], AF.Identity,
                             bias=p.bias[64:128, BO["b_B"]:BO["b_B"] + 1],
                             scale=1.0)
                s_sc.inc(s.activation(cva[0:64, :], p.pf[3][0:64, :], AF.Identity,
                                      bias=p.bias[0:64, BO["b_C"]:BO["b_C"] + 1],
                                      scale=1.0))

        @blk.vector
        def _(v):
            v = Drained(v)
            for t in range(8):
                v.wait_ge(s_sc.h, sc1 + t + 1)
                half, c0 = t // 4, (t * 512) % HT
                b = 64 * half
                s_d3.inc(v.tensor_copy(dt_s[b:b + 64, c0:c0 + 512],
                                       tmp_ev[0:64, :]))
                s_d3.inc(v.tensor_copy(B_s[b:b + 64, c0:c0 + 512],
                                       tmp_ev[64:128, :]))
                s_d3.inc(v.tensor_copy(C_bf[b:b + 64, c0:c0 + 512],
                                       cva[0:64, :]))

    with nc.Block("p1a3") as blk:
        @blk.vector
        def _(v):
            v = Drained(v)
            v.tensor_tensor(B_s[:, :], dt_s[:, :], B_s[:, :], ALU.mult)
            v.tensor_scalar(dt_s[:, :], dt_s[:, :], -1.0, 1.0, ALU.mult, ALU.add)
            v.tensor_tensor_scan(st_s[:, :], dt_s[:, :], B_s[:, :], 0.0,
                                 ALU.mult, ALU.add)
            v.tensor_tensor_scan(pp_bf[:, :], dt_s[:, :], zz_bf[:, :], 1.0,
                                 ALU.mult, ALU.add)
            v.tensor_copy(p.small[64:128, 61:62], st_s[0:64, HT - 1:HT])
            v.tensor_copy(p.small[64:128, 62:63], pp_bf[0:64, HT - 1:HT])
            v.scalar_tensor_tensor(st_s[64:128, :], pp_bf[64:128, :],
                                   p.small[64:128, 61:62], st_s[64:128, :],
                                   ALU.mult, ALU.add)
            v.tensor_scalar(pp_bf[64:128, :], pp_bf[64:128, :],
                            p.small[64:128, 62:63], None, ALU.mult)
            s_cv.inc(v.tensor_tensor(p.small[64:128, 63:64],
                                     st_s[64:128, HT - 1:HT],
                                     p.flags[64:128, 0:1], ALU.mult))

        @blk.sync
        def _(s):
            s.wait_ge(s_cv.h, s_cv.v)
            s_w.inc(s.dma_start(io["st_scr"][:, :], st_s[:, :]), 16)
            s_w.inc(s.dma_start(io["pp_scr"][:, :], pp_bf[:, :]), 16)
            s_w.inc(s.dma_start(io["c_scr"][:, :], C_bf[:, :]), 16)
            s_w.inc(s.dma_start(ap2(io["cc_in"], 0, [[1, 64], [1, 1]]),
                                p.small[64:128, 63:64]), 16)
            s.wait_ge(s_w.h, s_w.v)

        @blk.gpsimd
        def _(g):
            g = Drained(g)
            g.wait_ge(s_w.h, s_w.v)
            s_cc_i = p.s_cc
            s_cc_i.inc(g.collective_compute(
                "AllReduce", ALU.add,
                replica_groups=[[0, 1], [2, 3], [4, 5], [6, 7]],
                ins=[io["cc_in"].ap().opt()],
                outs=[io["cc_out"].ap().opt()]))




def phase_qkv(p, l):
    nc, io = p.nc, p.io
    ab = Alloc(p.abf, ABF_W)
    af = Alloc(p.af, AF_W)
    wq = ab.tile(8 * 1536)
    xnsl = [ab.tile(8 * 513) for _ in range(2)]
    dsl = [ab.tile(8 * 513) for _ in range(2)] if l == 1 else None
    sqb = ab.tile(512)
    qhb2 = [ab.tile(512) for _ in range(2)]
    vcf = ab.tile(512)
    vsb = [ab.tile(768) for _ in range(4)]
    tabs = [[af.tile(512) for _ in range(4)] for _ in range(2)]
    nn = af.tile(512)
    rstd = af.tile(512)
    qn = af.tile(512)
    qrot = af.tile(512)
    (s_mm, s_sq, s_m2, s_fr2, s_rs, s_fr, s_sp, s_t, s_vev) = (
        p.s_mm, p.s_sc, p.s_a4, p.s_a5, p.s_a6, p.s_a0, p.s_a1,
        p.s_a2, p.s_a3)
    s_hs = p.s_ev
    s_ldp = [p.s_d0, p.s_d1]
    s_gp = [p.s_g0, p.s_g2]
    s_tabp = [p.s_d6, p.s_d7]
    s_qsp = [p.s_d8, p.s_d9]
    s_vwp = [p.s_d2, p.s_d3]

    if l == 0:
        tlist = [(-HALO + 512 * t, 512) for t in range(10)] + [(3968, 128)]
    else:
        tlist = [(-640, 128)] + [(-512 + 512 * t, 512) for t in range(9)]
    TABN = ["cs1q", "cs2q", "cs1k", "cs2k"]
    NTI = len(tlist)

    with nc.Block(f"qkv{l}_w") as blk:
        @blk.gpsimd
        def _(g):
            g = Drained(g)
            for c in range(8):
                p.s_wl.inc(g.dma_start(
                    ap2(p.abf, wq.col + c * 1536, [[ABF_W, 128], [1, 1536]]),
                    ap2(io["L0_ffn_wqkv"], l * DMODEL * 1536 + c * 128 * 1536,
                        [[1536, 128], [1, 1536]])), 16)
            for j in range(4):
                for hp in range(4):
                    g.memset(vsb[j][:, 192 * hp + 64:192 * hp + 128], 1.0)
            g.wait_ge(p.s_wl.h, p.s_wl.v)

    mm0, sq0, m20, fr20, rs0, fr0, sp0 = (s_mm.v, s_sq.v, s_m2.v,
                                          s_fr2.v, s_rs.v, s_fr.v, s_sp.v)
    t0v, vev0, hs0 = s_t.v, s_vev.v, s_hs.v
    mk_mm, mk_sq, mk_m2, mk_fr2, mk_rs, mk_fr, mk_sp = {}, {}, {}, {}, {}, {}, {}
    mk_t, mk_vev, mk_vw, mk_spw, mk_hs, mk_ld, mk_tab = {}, {}, {}, {}, {}, {}, {}
    tp_order = []
    mk_g0 = {}
    mk_qs = {}
    for ti, (tau0, n) in enumerate(tlist):
        nj = n // 128
        mk_ld[ti] = s_ldp[ti % 2].v + 16 * (ti // 2 + 1)
        mk_g0[ti] = s_gp[ti % 2].v + 32 * (ti // 2 + 1)
        mk_hs[ti] = hs0 + 8 * (ti + 1)
        mk_tab[ti] = s_tabp[ti % 2].v + 128 * (ti // 2 + 1)
        for fc in range(8):
            kq = ti * 8 + fc
            mk_qs[(ti, fc)] = s_qsp[kq % 2].v + 16 * (kq // 2 + 1)
        for fc in range(12):
            mk_mm[(ti, fc)] = mm0 + ti * 12 + fc + 1
            mk_fr[(ti, fc)] = fr0 + ti * 12 + fc + 1
        for fc in range(8):
            mk_sq[(ti, fc)] = sq0 + ti * 8 + fc + 1
            mk_m2[(ti, fc)] = m20 + ti * 8 + fc + 1
            mk_fr2[(ti, fc)] = fr20 + ti * 8 + fc + 1
            mk_rs[(ti, fc)] = rs0 + ti * 8 + fc + 1
            mk_sp[(ti, fc)] = sp0 + ti * 8 + fc + 1
        for hp in range(4):
            for j in range(nj):
                tp_order.append((ti, hp, j))
    for k, key in enumerate(tp_order):
        mk_t[key] = t0v + k + 1
        mk_vev[key] = vev0 + k + 1
    vw_par = [s_vwp[0].v, s_vwp[1].v]
    for ti, (tau0, n) in enumerate(tlist):
        vw_par[ti % 2] += 16 * (n // 128)
        mk_vw[ti] = vw_par[ti % 2]

    with nc.Block(f"qkv{l}") as blk:
        @blk.vector
        def _(v):
            v = Drained(v)
            for ti, (tau0, n) in enumerate(tlist):
                nj = n // 128
                if l == 1:
                    v.wait_ge(s_ldp[ti % 2].h, mk_ld[ti])
                    v.wait_ge(s_gp[ti % 2].h, mk_g0[ti])
                    for c in range(8):
                        ins = v.tensor_tensor(
                            xnsl[ti % 2][:, c * 513:c * 513 + n + 1],
                            xnsl[ti % 2][:, c * 513:c * 513 + n + 1],
                            dsl[ti % 2][:, c * 513:c * 513 + n + 1], ALU.add)
                    assert s_hs.inc(ins, 8) == mk_hs[ti]
                for fc in range(8):
                    cc = l * 2 + (0 if fc < 4 else 1)
                    v.wait_ge(s_m2.h, mk_m2[(ti, fc)])
                    assert s_fr2.inc(
                        v.tensor_scalar(nn[:, 0:n], p.pf[3 + fc % 2][:, 0:n],
                                        1e-24, None, ALU.max)) == mk_fr2[(ti, fc)]
                    v.wait_ge(s_rs.h, mk_rs[(ti, fc)])
                    v.tensor_scalar(rstd[:, 0:n], rstd[:, 0:n],
                                    p.small[:, cc:cc + 1], None, ALU.mult)
                    assert s_fr.inc(
                        v.tensor_tensor(qn[:, 0:n], p.pf[fc % 3][:, 0:n],
                                        rstd[:, 0:n], ALU.mult)) == mk_fr[(ti, fc)]
                    if p.dev and l == 0 and ti == 2 and fc == 0:
                        v.tensor_copy(p.abf[:, 85024:85536], rstd[:, 0:512])
                        v.tensor_copy(p.abf[:, 84512:85024], qn[:, 0:512])
                        v.tensor_copy(p.abf[:, 85536:86048], nn[:, 0:512])
                    for bb in range(4):
                        src = [32, 0, 96, 64][bb]
                        eng = v if bb == 3 else v._e
                        eng.tensor_copy(qrot[32 * bb:32 * bb + 32, 0:n],
                                        qn[src:src + 32, 0:n])
                    tb1 = tabs[ti % 2][0 if fc < 4 else 2].ap
                    tb2 = tabs[ti % 2][1 if fc < 4 else 3].ap
                    v.wait_ge(s_tabp[ti % 2].h, mk_tab[ti])
                    v.tensor_tensor(qn[:, 0:n], qn[:, 0:n], tb1[:, 0:n], ALU.mult)
                    v.tensor_tensor(qrot[:, 0:n], qrot[:, 0:n], tb2[:, 0:n],
                                    ALU.mult)
                    kq = ti * 8 + fc
                    prev = (ti, fc - 2) if fc >= 2 else (ti - 1, fc + 6)
                    if prev in mk_qs:
                        v.wait_ge(s_qsp[kq % 2].h, mk_qs[prev])
                    qhb = qhb2[kq % 2]
                    if p.dev and l == 0 and ti == 4 and fc == 0:
                        v.tensor_copy(p.abf[:, 80000:80512], rstd[:, 0:n])
                        v.tensor_copy(p.abf[:, 80512:81024], qn[:, 0:n])
                        v.tensor_copy(p.abf[:, 81024:81536], qrot[:, 0:n])
                        v.tensor_copy(p.abf[:, 81536:82048], nn[:, 0:n])
                    assert s_sp.inc(
                        v.tensor_tensor(qhb[:, 0:n], qn[:, 0:n], qrot[:, 0:n],
                                        ALU.add)) == mk_sp[(ti, fc)]
                for hp in range(4):
                    fc = 8 + hp
                    v.wait_ge(s_mm.h, mk_mm[(ti, fc)])
                    if hp >= 1:
                        v.wait_ge(s_t.h, mk_t[(ti, hp - 1, nj - 1)])
                    assert s_fr.inc(
                        v.tensor_copy(vcf[:, 0:n], p.pf[fc % 3][:, 0:n])
                    ) == mk_fr[(ti, fc)]
                    if p.dev and l == 0 and ti == 2 and hp == 0:
                        v.tensor_copy(p.abf[:, 84000:84512], vcf[:, 0:512])
                    for j in range(nj):
                        v.wait_ge(s_t.h, mk_t[(ti, hp, j)])
                        if hp == 0 and j == 0 and (ti - 1) in mk_vw:
                            v.wait_ge(s_vwp[(ti - 1) % 2].h, mk_vw[ti - 1])
                        v._e.tensor_copy(vsb[j][:, 192 * hp:192 * hp + 64],
                                         p.pb[j % 2][:, 0:64])
                        assert s_vev.inc(v.tensor_copy(
                            vsb[j][:, 192 * hp + 128:192 * hp + 192],
                            p.pb[j % 2][:, 64:128])) == mk_vev[(ti, hp, j)]

        @blk.scalar
        def _(s):
            s = Drained(s)
            for ti, (tau0, n) in enumerate(tlist):
                for fc in range(8):
                    s.wait_ge(s_mm.h, mk_mm[(ti, fc)])
                    assert s_sq.inc(
                        s.activation(sqb[:, 0:n], p.pf[fc % 3][:, 0:n],
                                     AF.Square, bias=0.0, scale=1.0)
                    ) == mk_sq[(ti, fc)]
                    s.wait_ge(s_fr2.h, mk_fr2[(ti, fc)])
                    s.activation(nn[:, 0:n], nn[:, 0:n], AF.Ln,
                                 bias=0.0, scale=1.0)
                    assert s_rs.inc(
                        s.activation(rstd[:, 0:n], nn[:, 0:n], AF.Exp,
                                     bias=0.0, scale=-0.5)) == mk_rs[(ti, fc)]

        @blk.tensor
        def _(t_):
            tcnt = 0
            for ti, (tau0, n) in enumerate(tlist):
                nj = n // 128
                if l == 1:
                    t_.wait_ge(s_hs.h, mk_hs[ti])
                else:
                    t_.wait_ge(s_ldp[ti % 2].h, mk_ld[ti])
                for fc in range(12):
                    prev = (ti, fc - 3) if fc >= 3 else (ti - 1, fc + 9)
                    if prev in mk_fr:
                        t_.wait_ge(s_fr.h, mk_fr[prev])
                    ps = p.pf[fc % 3]
                    for c in range(8):
                        off = 1 if c < 4 else 0
                        mi = t_.matmul(
                            ps[:, 0:n],
                            ap2(p.abf, wq.col + c * 1536 + fc * 128,
                                [[ABF_W, 128], [1, 128]]),
                            ap2(p.abf, xnsl[ti % 2].col + c * 513 + off,
                                [[ABF_W, 128], [1, n]]),
                            start=(c == 0), stop=(c == 7))
                    assert s_mm.inc(mi) == mk_mm[(ti, fc)]
                    if fc < 8:
                        t_.wait_ge(s_sq.h, mk_sq[(ti, fc)])
                        prev2 = (ti, fc - 2) if fc >= 2 else (ti - 1, fc + 6)
                        if prev2 in mk_fr2:
                            t_.wait_ge(s_fr2.h, mk_fr2[prev2])
                        assert s_m2.inc(
                            t_.matmul(p.pf[3 + fc % 2][:, 0:n],
                                      p.blkdiag[:, :], sqb[:, 0:n],
                                      start=True, stop=True)) == mk_m2[(ti, fc)]
                    else:
                        hp = fc - 8
                        t_.wait_ge(s_fr.h, mk_fr[(ti, fc)])
                        for j in range(nj):
                            if tcnt >= 2:
                                t_.wait_ge(s_vev.h,
                                           mk_vev[tp_order[tcnt - 2]])
                            assert s_t.inc(
                                t_.transpose(p.pb[j % 2][:, 0:128],
                                             vcf[:, j * 128:(j + 1) * 128],
                                             p.eye_bf[:, :])
                            ) == mk_t[(ti, hp, j)]
                            tcnt += 1

        @blk.sync
        def _(s):
            for ti, (tau0, n) in enumerate(tlist):
                nj = n // 128
                if ti >= 2:
                    s.wait_ge(s_mm.h, mk_mm[(ti - 2, 11)])
                assert s_ldp[ti % 2].inc(s.dma_start(
                    ap2(p.abf, xnsl[ti % 2].col,
                        [[ABF_W, 128], [513, 8], [1, n + 1]]),
                    ap2(io["xnfm"], tau0 + HALO,
                        [[1 + EXT, 128], [128 * (1 + EXT), 8], [1, n + 1]])),
                    16) == mk_ld[ti]
                if ti >= 2:
                    s.wait_ge(s_sp.h, mk_sp[(ti - 2, 7)])
                for tb in range(4):
                    for half in range(2):
                        s_tabp[ti % 2].inc(s.dma_start(
                            tabs[ti % 2][tb].ap[64 * half:64 * half + 64, 0:n],
                            ap2(io[TABN[tb]], tau0 + HALO, [[EXT, 64], [1, n]])),
                            16)
                assert s_tabp[ti % 2].v == mk_tab[ti]
                for fc in range(8):
                    kq = ti * 8 + fc
                    s.wait_ge(s_sp.h, mk_sp[(ti, fc)])
                    if fc < 4:
                        dst = ap2(io["qh"], fc * 128 * EXT + tau0 + HALO,
                                  [[EXT, 128], [1, n]])
                    else:
                        dst = ap2(io["kh"], (fc - 4) * 128 * KH_W + tau0 + 1536,
                                  [[KH_W, 128], [1, n]])
                    assert s_qsp[kq % 2].inc(
                        s.dma_start(dst, qhb2[kq % 2][:, 0:n]), 16) \
                        == mk_qs[(ti, fc)]
                for j in range(nj):
                    s.wait_ge(s_vev.h, mk_vev[(ti, 3, j)])
                    bidx = (tau0 + 128 * j + 1536) // 128
                    mkv = s_vwp[ti % 2].inc(s.dma_start(
                        ap2(io["vstat"], bidx * 128 * 768, [[768, 128], [1, 768]]),
                        vsb[j][:, :]), 16)
                    if j == nj - 1:
                        assert mkv == mk_vw[ti]
                if p.dev and l == 0 and ti == 2:
                    p.s_ev.inc(s.dma_start(
                        ap2(io["dbg"], 0, [[2048, 128], [1, 2048]]),
                        p.abf[:, 84000:86048]), 16)
            for sp in (s_qsp[0], s_qsp[1], s_vwp[0], s_vwp[1]):
                s.wait_ge(sp.h, sp.v)

        if l == 1:
            @blk.gpsimd
            def _(g):
                g = Drained(g)
                for ti, (tau0, n) in enumerate(tlist):
                    if ti >= 2:
                        g.wait_ge(s_hs.h, mk_hs[ti - 2])
                    for hf in range(2):
                        mk = s_gp[ti % 2].inc(g.dma_start(
                            ap2(p.abf, dsl[ti % 2].col + hf * 4 * 513,
                                [[ABF_W, 128], [513, 4], [1, n + 1]]),
                            ap2(io["dacc_b"],
                                tau0 - 1 + DH0 + hf * 4 * 128 * DACC_W,
                                [[DACC_W, 128], [128 * DACC_W, 4], [1, n + 1]])),
                            16)
                    assert mk == mk_g0[ti]


def phase_p1b(p):
    nc, io = p.nc, p.io
    BO = p.BOFF
    ab = Alloc(p.abf, ABF_W)
    af = Alloc(p.af, AF_W)
    wsi = ab.tile(2048)
    wout = ab.tile(16 * 1024)
    C_bf = ab.tile(HT)
    pp_bf = ab.tile(HT)
    ybf = ab.tile(HT)
    y2b = [ab.tile(512) for _ in range(2)]
    dxb = ab.tile(512)
    ysig = [ab.tile(512) for _ in range(16)]
    xm2 = [ab.tile(512) for _ in range(2)]
    sg2 = [Tile(p.af16[:, 0:512], 0), Tile(p.af16[:, 512:1024], 512)]
    st = af.tile(HT)
    muT = af.tile(512)
    msT = af.tile(512)
    tmpT = af.tile(512)
    ST = af.tile(512)
    e32 = af.tile(512)
    dmt = [af.tile(512) for _ in range(2)]
    (s_mm, s_sc, s_ev, s_fr, s_mm2, s_dev) = (
        p.s_mm, p.s_sc, p.s_ev, p.s_a0, p.s_a1, p.s_a2)
    s_ldp = [p.s_d0, p.s_d1]
    s_stp = [p.s_d2, p.s_d3]

    with nc.Block("p1b_w") as blk:
        @blk.gpsimd
        def _(g):
            g = Drained(g)
            for half in range(2):
                p.s_wl.inc(g.dma_start(
                    p.abf[64 * half:64 * half + 64, wsi.col:wsi.col + 2048],
                    io["L0_attn_W_si"][0:64, :]), 16)
            for c in range(16):
                p.s_wl.inc(g.dma_start(
                    ap2(p.abf, wout.col + c * 1024, [[ABF_W, 128], [1, 1024]]),
                    io["L0_attn_W_out"][c * 128:(c + 1) * 128, :]), 16)
            p.s_g0.inc(g.dma_start(st[:, :], io["st_scr"][:, :]), 16)
            p.s_g0.inc(g.dma_start(pp_bf[:, :], io["pp_scr"][:, :]), 16)
            p.s_g0.inc(g.dma_start(C_bf[:, :], io["c_scr"][:, :]), 16)
            g.wait_ge(p.s_cc.h, 1)
            for half in range(2):
                p.s_g0.inc(g.dma_start(
                    p.small[64 * half:64 * half + 64, 60:61],
                    ap2(io["cc_out"], 0, [[1, 64], [1, 1]]),
                    allow_slow_non_contiguous=True), 16)
            g.wait_ge(p.s_wl.h, p.s_wl.v)
            g.wait_ge(p.s_g0.h, p.s_g0.v)

    mm0, sc0, ev0, fr0, mm20, dev0 = (s_mm.v, s_sc.v, s_ev.v, s_fr.v,
                                      s_mm2.v, s_dev.v)
    stats_mm = {g: mm0 + 2 * (g + 1) for g in range(8)}
    sq_mk = {g: sc0 + 2 * g + 1 for g in range(8)}
    rstd_mk = {g: sc0 + 2 * g + 2 for g in range(8)}
    var_mk = {g: ev0 + 1 + 2 * g + 1 for g in range(8)}
    apply_mk = {g: ev0 + 1 + 2 * g + 2 for g in range(8)}
    ysi_mm = {}
    fr_mk = {}
    v = mm0 + 16
    for tt in range(8):
        for fc in range(16):
            ysi_mm[(tt, fc)] = mm0 + 16 + tt * 16 + fc + 1
            fr_mk[(tt, fc)] = fr0 + tt * 16 + fc + 1
    wout_mm = {(tt, fo): mm20 + tt * 8 + fo + 1 for tt in range(8)
               for fo in range(8)}
    dev_mk = {(tt, fo): dev0 + tt * 8 + fo + 1 for tt in range(8)
              for fo in range(8)}
    st_mk = {(tt, fo): s_stp[(tt * 8 + fo) % 2].v
             + 16 * ((tt * 8 + fo) // 2 + 1)
             for tt in range(8) for fo in range(8)}
    ld_mk = {(tt, fc): s_ldp[(tt * 16 + fc) % 2].v
             + 32 * ((tt * 16 + fc) // 2 + 1)
             for tt in range(8) for fc in range(16)}

    with nc.Block("p1b") as blk:
        @blk.vector
        def _(v_):
            v_ = Drained(v_)
            v_.tensor_tensor(p.small[:, 60:61], p.small[:, 60:61],
                             p.flags[:, 1:2], ALU.mult)
            v_.scalar_tensor_tensor(st[:, :], pp_bf[:, :], p.small[:, 60:61],
                                    st[:, :], ALU.mult, ALU.add)
            v_.tensor_tensor(st[:, :], C_bf[:, :], st[:, :], ALU.mult)
            s_ev.inc(v_.tensor_copy(ybf[:, :], st[:, :]))
            for g in range(8):
                half, ct = g // 4, g % 4
                b = 64 * half
                cols = slice(ct * 512 + (0 if half == 0 else 0), 0)
                c0 = ct * 512
                v_.wait_ge(s_mm.h, stats_mm[g])
                if g >= 1:
                    v_.wait_ge(s_sc.h, rstd_mk[g - 1])
                v_.tensor_scalar(muT[b:b + 64, :], p.pf[4][0:64, :],
                                 1.0 / DSTATE, None, ALU.mult)
                v_.tensor_scalar(msT[b:b + 64, :], p.pf[5][0:64, :],
                                 1.0 / DSTATE, None, ALU.mult)
                v_.tensor_tensor(tmpT[b:b + 64, :], muT[b:b + 64, :],
                                 muT[b:b + 64, :], ALU.mult)
                assert s_ev.inc(v_.tensor_tensor(
                    msT[b:b + 64, :], msT[b:b + 64, :], tmpT[b:b + 64, :],
                    ALU.subtract)) == var_mk[g]
                v_.wait_ge(s_sc.h, rstd_mk[g])
                v_.tensor_tensor(tmpT[b:b + 64, :], muT[b:b + 64, :],
                                 ST[b:b + 64, :], ALU.mult)
                v_.tensor_tensor(st[b:b + 64, c0:c0 + 512],
                                 st[b:b + 64, c0:c0 + 512],
                                 ST[b:b + 64, :], ALU.mult)
                assert s_ev.inc(v_.tensor_tensor(
                    ybf[b:b + 64, c0:c0 + 512], st[b:b + 64, c0:c0 + 512],
                    tmpT[b:b + 64, :], ALU.subtract)) == apply_mk[g]
            for tt in range(8):
                for fc in range(16):
                    v_.wait_ge(s_mm.h, ysi_mm[(tt, fc)])
                    v_.wait_ge(s_ldp[(tt * 16 + fc) % 2].h, ld_mk[(tt, fc)])
                    if tt >= 1:
                        v_.wait_ge(s_mm2.h, wout_mm[(tt - 1, 7)])
                    v_.tensor_scalar(dxb[:, :], xm2[fc % 2][:, :],
                                     p.bias[:, BO["D"] + fc:BO["D"] + fc + 1],
                                     None, ALU.mult)
                    v_.scalar_tensor_tensor(
                        e32[:, :], p.pf[fc % 2][:, :],
                        p.bias[:, BO["b_si"] + fc:BO["b_si"] + fc + 1],
                        dxb[:, :], ALU.add, ALU.add)
                    assert s_fr.inc(v_.tensor_tensor(
                        ysig[fc][:, :], e32[:, :], sg2[fc % 2][:, :],
                        ALU.mult)) == fr_mk[(tt, fc)]
                for fo in range(8):
                    v_.wait_ge(s_mm2.h, wout_mm[(tt, fo)])
                    if (tt * 8 + fo) >= 2:
                        pk = (tt * 8 + fo - 2)
                        v_.wait_ge(s_stp[pk % 2].h, st_mk[(pk // 8, pk % 8)])
                    assert s_dev.inc(v_.tensor_scalar(
                        dmt[fo % 2][:, :], p.pf[2 + fo % 2][:, :],
                        p.bias[:, BO["b_out"] + fo:BO["b_out"] + fo + 1],
                        None, ALU.add)) == dev_mk[(tt, fo)]

        @blk.scalar
        def _(s):
            s = Drained(s)
            for g in range(8):
                half, ct = g // 4, g % 4
                b = 64 * half
                s.wait_ge(s_ev.h, ev0 + 1)
                if g >= 2:
                    s.wait_ge(s_mm.h, stats_mm[g - 2])
                assert s_sc.inc(s.activation(
                    y2b[g % 2][b:b + 64, :],
                    st[b:b + 64, ct * 512:(ct + 1) * 512], AF.Square,
                    bias=0.0, scale=1.0)) == sq_mk[g]
                s.wait_ge(s_ev.h, var_mk[g])
                s.activation(tmpT[b:b + 64, :], msT[b:b + 64, :], AF.Ln,
                             bias=p.flags[b:b + 64, 3:4], scale=1.0)
                assert s_sc.inc(s.activation(
                    ST[b:b + 64, :], tmpT[b:b + 64, :], AF.Exp,
                    bias=0.0, scale=-0.5)) == rstd_mk[g]

        @blk.tensor
        def _(t_):
            for g in range(8):
                half, ct = g // 4, g % 4
                b = 64 * half
                t_.wait_ge(s_sc.h, sq_mk[g])
                if g >= 1:
                    t_.wait_ge(s_ev.h, var_mk[g - 1])
                t_.matmul(p.pf[4][0:64, :], p.ones_bf[b:b + 64, 0:64],
                          ybf[b:b + 64, ct * 512:(ct + 1) * 512],
                          start=True, stop=True)
                assert s_mm.inc(t_.matmul(
                    p.pf[5][0:64, :], p.ones_bf[b:b + 64, 0:64],
                    y2b[g % 2][b:b + 64, :], start=True, stop=True), 2) \
                    == stats_mm[g]
            for tt in range(8):
                half, ct = tt // 4, tt % 4
                b = 64 * half
                t_.wait_ge(s_ev.h, apply_mk[tt // 4 * 4 + tt % 4])
                for fc in range(16):
                    if (tt * 16 + fc) >= 2:
                        pk = tt * 16 + fc - 2
                        t_.wait_ge(s_fr.h, fr_mk[(pk // 16, pk % 16)])
                    assert s_mm.inc(t_.matmul(
                        p.pf[fc % 2][:, :],
                        p.abf[b:b + 64, wsi.col + fc * 128:wsi.col + fc * 128 + 128],
                        ybf[b:b + 64, ct * 512:(ct + 1) * 512],
                        start=True, stop=True)) == ysi_mm[(tt, fc)]
                t_.wait_ge(s_fr.h, fr_mk[(tt, 15)])
                for fo in range(8):
                    if (tt * 8 + fo) >= 2:
                        pk = tt * 8 + fo - 2
                        t_.wait_ge(s_dev.h, dev_mk[(pk // 8, pk % 8)])
                    for fc in range(16):
                        mi = t_.matmul(
                            p.pf[2 + fo % 2][:, :],
                            ap2(p.abf, wout.col + fc * 1024 + fo * 128,
                                [[ABF_W, 128], [1, 128]]),
                            ysig[fc][:, :], start=(fc == 0), stop=(fc == 15))
                    assert s_mm2.inc(mi) == wout_mm[(tt, fo)]

        @blk.sync
        def _(s):
            for tt in range(8):
                for fc in range(16):
                    if (tt * 16 + fc) >= 2:
                        pk = tt * 16 + fc - 2
                        s.wait_ge(s_fr.h, fr_mk[(pk // 16, pk % 16)])
                    sl = s_ldp[(tt * 16 + fc) % 2]
                    sl.inc(s.dma_start(
                        xm2[fc % 2][:, :],
                        io["xmain"][fc * 128:(fc + 1) * 128,
                                    tt * 512:(tt + 1) * 512]), 16)
                    sl.inc(s.dma_start(
                        sg2[fc % 2].ap,
                        io["sgate"][fc * 128:(fc + 1) * 128,
                                    tt * 512:(tt + 1) * 512]), 16)
                    assert sl.v == ld_mk[(tt, fc)]
                for fo in range(8):
                    s.wait_ge(s_dev.h, dev_mk[(tt, fo)])
                    assert s_stp[(tt * 8 + fo) % 2].inc(s.dma_start(
                        ap2(io["dm"], fo * 128 * T + tt * 512,
                            [[T, 128], [1, 512]]),
                        dmt[fo % 2][:, :]), 16) == st_mk[(tt, fo)]
            for sp in (s_stp[0], s_stp[1]):
                s.wait_ge(sp.h, sp.v)


def phase_attn(p, l):
    nc, io = p.nc, p.io
    ab = Alloc(p.abf, ABF_W)
    af = Alloc(p.af, AF_W)
    wo_sb = ab.tile(4 * 1024)
    qsl = [ab.tile(4 * 512) for _ in range(2)]
    ksl = [ab.tile(4 * 512) for _ in range(2)]
    vt = [ab.tile(768) for _ in range(8)]
    attE = [ab.tile(512) for _ in range(2)]
    attO = [ab.tile(512) for _ in range(2)]
    o_all = ab.tile(4 * 512)
    recT = af.tile(512)
    dwt = [af.tile(512) for _ in range(2)]
    dbt = [af.tile(512) for _ in range(2)]
    (s_mm, s_msk, s_exp, s_oev, s_dev) = (
        p.s_mm, p.s_ev, p.s_sc, p.s_a0, p.s_a1)
    s_ldp = [p.s_d0, p.s_d1]
    s_stp = [p.s_d2, p.s_d3]
    s_dldp = [p.s_g1, p.s_g3]

    wlist = list(range(-2, 8)) if l == 0 else list(range(-1, 8))
    NW = len(wlist)
    dacc_rd = "dacc_b"
    M_OWN = p.masks[:, 0:128]
    M_LOOK = p.masks[:, 128:256]
    M0D = p.masks[:, 256:384]
    M0R = p.masks[:, 384:384 + W]

    with nc.Block(f"attn{l}_w") as blk:
        @blk.gpsimd
        def _(g):
            g = Drained(g)
            for c in range(4):
                p.s_wl.inc(g.dma_start(
                    ap2(p.abf, wo_sb.col + c * 1024, [[ABF_W, 128], [1, 1024]]),
                    ap2(io["L0_ffn_wo"], l * INNER_A * DMODEL + c * 128 * DMODEL,
                        [[DMODEL, 128], [1, 1024]])), 16)
            g.wait_ge(p.s_wl.h, p.s_wl.v)

    mm0, msk0, exp0, oev0, dev0 = (s_mm.v, s_msk.v, s_exp.v,
                                   s_oev.v, s_dev.v)
    sides = ["own", "look"]
    sim_mk, msk_mk, exp_mk, pv_mk = {}, {}, {}, {}
    oev_mk, wo_mk, dev_mk, stw_mk, ld_mk, dld_mk = {}, {}, {}, {}, {}, {}
    mmv = mm0
    for wi in range(NW):
        for hp in range(4):
            for si in range(2):
                for kc in range(4):
                    mmv += 1
                    sim_mk[(wi, hp, si, kc)] = mmv
                    mmv += 1
                    pv_mk[(wi, hp, si, kc)] = mmv
        for fo in range(8):
            mmv += 1
            wo_mk[(wi, fo)] = mmv
    k = 0
    for wi in range(NW):
        for hp in range(4):
            for si in range(2):
                for kc in range(4):
                    k += 1
                    msk_mk[(wi, hp, si, kc)] = msk0 + k
                    exp_mk[(wi, hp, si, kc)] = exp0 + k
    for wi in range(NW):
        for hp in range(4):
            oev_mk[(wi, hp)] = oev0 + wi * 4 + hp + 1
        for fo in range(8):
            dev_mk[(wi, fo)] = dev0 + wi * 8 + fo + 1
    for wi in range(NW):
        for fo in range(8):
            k2 = wi * 8 + fo
            stw_mk[(wi, fo)] = s_stp[k2 % 2].v + 16 * (k2 // 2 + 1)
            dld_mk[(wi, fo)] = s_dldp[k2 % 2].v + 16 * (k2 // 2 + 1)
    lvp = [s_ldp[0].v, s_ldp[1].v]
    for wi in range(NW):
        lvp[wi % 2] += 32
        if wi == 0:
            lvp[wi % 2] += 16
        lvp[wi % 2] += 16 * (8 if wi == 0 else 4)
        ld_mk[wi] = lvp[wi % 2]

    def vslot(wi, side, kc):
        w = wlist[wi]
        bidx = 4 * w + (12 if side == 0 else 8) + kc
        return bidx % 8, bidx

    with nc.Block(f"attn{l}") as blk:
        @blk.sync
        def _(s):
            for wi in range(NW):
                w = wlist[wi]
                tau0 = w * 512
                if wi >= 2:
                    s.wait_ge(s_mm.h, wo_mk[(wi - 2, 7)])
                s_ldp[wi % 2].inc(s.dma_start(
                    ap2(p.abf, qsl[wi % 2].col, [[ABF_W, 128], [512, 4], [1, 512]]),
                    ap2(io["qh"], tau0 + HALO,
                        [[EXT, 128], [128 * EXT, 4], [1, 512]])), 16)
                if wi >= 2:
                    s.wait_ge(s_mm.h, wo_mk[(wi - 1, 7)])
                s_ldp[wi % 2].inc(s.dma_start(
                    ap2(p.abf, ksl[wi % 2].col, [[ABF_W, 128], [512, 4], [1, 512]]),
                    ap2(io["kh"], tau0 + 1536,
                        [[KH_W, 128], [128 * KH_W, 4], [1, 512]])), 16)
                if wi == 0:
                    s_ldp[wi % 2].inc(s.dma_start(
                        ap2(p.abf, ksl[1].col, [[ABF_W, 128], [512, 4], [1, 512]]),
                        ap2(io["kh"], tau0 + 1024,
                            [[KH_W, 128], [128 * KH_W, 4], [1, 512]])), 16)
                rng = range(8) if wi == 0 else range(4)
                for j in rng:
                    bidx = 4 * w + 8 + j if wi == 0 else 4 * w + 12 + j
                    slot = bidx % 8
                    if wi >= 2:
                        s.wait_ge(s_mm.h, wo_mk[(wi - 1, 7)])
                    s_ldp[wi % 2].inc(s.dma_start(
                        vt[slot][:, :],
                        ap2(io["vstat"], bidx * 128 * 768,
                            [[768, 128], [1, 768]])), 16)
                assert s_ldp[wi % 2].v == ld_mk[wi]
                for fo in range(8):
                    k2 = wi * 8 + fo
                    s.wait_ge(s_dev.h, dev_mk[(wi, fo)])
                    w_lo = max(0, -tau0 - DH0)
                    assert s_stp[k2 % 2].inc(s.dma_start(
                        ap2(io["dacc_a"], fo * 128 * DACC_W + tau0 + DH0 + w_lo,
                            [[DACC_W, 128], [1, 512 - w_lo]]),
                        dwt[fo % 2][:, w_lo:512]), 16) == stw_mk[(wi, fo)]
                if p.dev and l == 2 and w == 3:
                    s.wait_ge(s_dev.h, dev_mk[(wi, 7)])
                    p.s_a7.inc(s.dma_start(
                        ap2(io["dbg"], 0, [[2048, 128], [1, 2048]]),
                        p.abf[:, 84000:86048]), 16)
                    s.wait_ge(p.s_a7.h, p.s_a7.v)
            for sp in (s_stp[0], s_stp[1]):
                s.wait_ge(sp.h, sp.v)

        if l == 1:
            @blk.gpsimd
            def _(g):
                g = Drained(g)
                for wi in range(NW):
                    tau0 = wlist[wi] * 512
                    for fo in range(8):
                        if (wi * 8 + fo) >= 2:
                            pk = wi * 8 + fo - 2
                            g.wait_ge(s_dev.h, dev_mk[(pk // 8, pk % 8)])
                        w_lo = max(0, -tau0 - DH0)
                        assert s_dldp[(wi * 8 + fo) % 2].inc(g.dma_start(
                            dbt[fo % 2][:, w_lo:512],
                            ap2(io["dacc_b"],
                                fo * 128 * DACC_W + tau0 + DH0 + w_lo,
                                [[DACC_W, 128], [1, 512 - w_lo]])),
                            16) == dld_mk[(wi, fo)]

        @blk.tensor
        def _(t_):
            for wi in range(NW):
                w = wlist[wi]
                t_.wait_ge(s_ldp[wi % 2].h, ld_mk[wi])
                for hp in range(4):
                    if hp >= 1:
                        t_.wait_ge(s_oev.h, oev_mk[(wi, hp - 1)])
                    elif wi >= 1:
                        t_.wait_ge(s_oev.h, oev_mk[(wi - 1, 3)])
                    for si in range(2):
                        kslab = ksl[wi % 2] if si == 0 else \
                            (ksl[(wi - 1) % 2] if wi >= 1 else ksl[1])
                        for kc in range(4):
                            if si == 0:
                                u0, u1 = kc * 128, 512
                            else:
                                u0, u1 = 0, kc * 128 + 128
                            st2 = hp * 8 + si * 4 + kc
                            pk2 = st2 - 2
                            pkey = (wi, pk2 // 8, (pk2 % 8) // 4, pk2 % 4) \
                                if pk2 >= 0 else \
                                ((wi - 1, (pk2 + 32) // 8,
                                  ((pk2 + 32) % 8) // 4, (pk2 + 32) % 4)
                                 if wi >= 1 else None)
                            if pkey is not None and pkey in exp_mk:
                                t_.wait_ge(s_exp.h, exp_mk[pkey])
                            if st2 == 1 and wi >= 1:
                                t_.wait_ge(s_dev.h, dev_mk[(wi - 1, 7)])
                            pj = 0 if st2 % 2 == 0 else 4
                            for half, pspsum in ((0, p.pf[pj]),
                                                 (1, p.pf[pj + 1])):
                                b = 64 * half
                                mi = t_.matmul(
                                    pspsum[:, u0:u1],
                                    p.abf[b:b + 64,
                                          kslab.col + hp * 512 + kc * 128:
                                          kslab.col + hp * 512 + kc * 128 + 128],
                                    p.abf[b:b + 64,
                                          qsl[wi % 2].col + hp * 512 + u0:
                                          qsl[wi % 2].col + hp * 512 + u1],
                                    start=True, stop=True)
                            assert s_mm.inc(mi) == sim_mk[(wi, hp, si, kc)]
                            t_.wait_ge(s_exp.h, exp_mk[(wi, hp, si, kc)])
                            slot, _b = vslot(wi, si, kc)
                            first = (si == 0 and kc == 0)
                            last = (si == 1 and kc == 3)
                            t_.matmul(p.pf[2][:, u0:u1],
                                      vt[slot][:, 192 * hp:192 * hp + 128],
                                      attE[kc % 2][:, u0:u1],
                                      start=first, stop=last)
                            mi = t_.matmul(p.pf[3][:, u0:u1],
                                           vt[slot][:, 192 * hp + 64:
                                                    192 * hp + 192],
                                           attO[kc % 2][:, u0:u1],
                                           start=first, stop=last)
                            assert s_mm.inc(mi) == pv_mk[(wi, hp, si, kc)]
                t_.wait_ge(s_oev.h, oev_mk[(wi, 3)])
                for fo in range(8):
                    if (wi * 8 + fo) >= 2:
                        pk = wi * 8 + fo - 2
                        t_.wait_ge(s_dev.h, dev_mk[(pk // 8, pk % 8)])
                    for c in range(4):
                        mi = t_.matmul(
                            p.pf[4 + fo % 2][:, :],
                            ap2(p.abf, wo_sb.col + c * 1024 + fo * 128,
                                [[ABF_W, 128], [1, 128]]),
                            o_all[:, c * 512:(c + 1) * 512],
                            start=(c == 0), stop=(c == 3))
                    assert s_mm.inc(mi) == wo_mk[(wi, fo)]

        @blk.vector
        def _(v_):
            v_ = Drained(v_)
            for wi in range(NW):
                w = wlist[wi]
                for hp in range(4):
                    for si in range(2):
                        for kc in range(4):
                            if si == 0:
                                u0, u1 = kc * 128, 512
                                dmask = M_OWN
                            else:
                                u0, u1 = 0, kc * 128 + 128
                                dmask = M0D if w == 0 else M_LOOK
                            d0 = kc * 128
                            pj = 0 if (hp * 8 + si * 4 + kc) % 2 == 0 else 4
                            v_.wait_ge(s_mm.h, sim_mk[(wi, hp, si, kc)])
                            v_._e.tensor_tensor(p.pf[pj][:, d0:d0 + 128],
                                                p.pf[pj][:, d0:d0 + 128],
                                                dmask, ALU.add)
                            if si == 1 and w == 0 and kc >= 1:
                                v_._e.tensor_tensor(p.pf[pj][:, 0:d0],
                                                    p.pf[pj][:, 0:d0],
                                                    M0R[:, 0:d0], ALU.add)
                                v_._e.tensor_tensor(p.pf[pj + 1][:, 0:d0],
                                                    p.pf[pj + 1][:, 0:d0],
                                                    M0R[:, 0:d0], ALU.add)
                            ins = v_.tensor_tensor(p.pf[pj + 1][:, d0:d0 + 128],
                                                   p.pf[pj + 1][:, d0:d0 + 128],
                                                   dmask, ALU.add)
                            assert s_msk.inc(ins) == msk_mk[(wi, hp, si, kc)]
                    v_.wait_ge(s_mm.h, pv_mk[(wi, hp, 1, 3)])
                    if wi >= 1:
                        v_.wait_ge(s_mm.h, wo_mk[(wi - 1, 7)])
                    v_._e.reciprocal(recT[0:64, :], p.pf[2][64:128, :])
                    v_.reciprocal(recT[64:128, :], p.pf[3][0:64, :])
                    v_._e.tensor_tensor(o_all[0:64, hp * 512:hp * 512 + 512],
                                        p.pf[2][0:64, :], recT[0:64, :],
                                        ALU.mult)
                    assert s_oev.inc(v_.tensor_tensor(
                        o_all[64:128, hp * 512:hp * 512 + 512],
                        p.pf[3][64:128, :], recT[64:128, :],
                        ALU.mult)) == oev_mk[(wi, hp)]
                if p.dev and l == 2 and w == 3:
                    v_.tensor_copy(p.abf[:, 84000:86048], o_all[:, :])
                tau0 = w * 512
                w_lo = max(0, -tau0 - DH0)
                for fo in range(8):
                    v_.wait_ge(s_mm.h, wo_mk[(wi, fo)])
                    if (wi * 8 + fo) >= 2:
                        pk = wi * 8 + fo - 2
                        v_.wait_ge(s_stp[pk % 2].h, stw_mk[(pk // 8, pk % 8)])
                    if l == 0:
                        ins = v_.tensor_copy(dwt[fo % 2][:, w_lo:512],
                                             p.pf[4 + fo % 2][:, w_lo:512])
                    else:
                        v_.wait_ge(s_dldp[(wi * 8 + fo) % 2].h,
                                   dld_mk[(wi, fo)])
                        ins = v_.tensor_tensor(dwt[fo % 2][:, w_lo:512],
                                               p.pf[4 + fo % 2][:, w_lo:512],
                                               dbt[fo % 2][:, w_lo:512], ALU.add)
                    assert s_dev.inc(ins) == dev_mk[(wi, fo)]

        @blk.scalar
        def _(s):
            s = Drained(s)
            negb = p.small[:, 6 + l:7 + l]
            for wi in range(NW):
                for hp in range(4):
                    for si in range(2):
                        for kc in range(4):
                            if si == 0:
                                u0, u1 = kc * 128, 512
                            else:
                                u0, u1 = 0, kc * 128 + 128
                            s.wait_ge(s_msk.h, msk_mk[(wi, hp, si, kc)])
                            kk = hp * 8 + si * 4 + kc
                            if kk >= 2:
                                pk2 = kk - 2
                                key2 = (wi, pk2 // 8, (pk2 % 8) // 4, pk2 % 4)
                                s.wait_ge(s_mm.h, pv_mk[key2])
                            elif wi >= 1:
                                key2 = (wi - 1, 3, 1, 2 + kk)
                                s.wait_ge(s_mm.h, pv_mk[key2])
                            pj = 0 if (hp * 8 + si * 4 + kc) % 2 == 0 else 4
                            s._e.activation(attE[kc % 2][:, u0:u1],
                                            p.pf[pj][:, u0:u1], AF.Exp,
                                            bias=negb, scale=8.0)
                            ins = s.activation(attO[kc % 2][:, u0:u1],
                                               p.pf[pj + 1][:, u0:u1], AF.Exp,
                                               bias=negb, scale=8.0)
                            assert s_exp.inc(ins) == exp_mk[(wi, hp, si, kc)]


def phase_mlp(p, l):
    nc, io = p.nc, p.io
    BO = p.BOFF
    ab = Alloc(p.abf, ABF_W)
    af = Alloc(p.af, AF_W)
    w1sb = ab.tile(8 * 5460)
    w2sb = ab.tile(22 * 1024)
    hslab = ab.tile(8 * 513)
    xln = ab.tile(8 * 512)
    h2b = [ab.tile(512) for _ in range(2)]
    tqb = h2b[0]
    h1act = [ab.tile(512) for _ in range(22)]
    dslab = af.tile(8 * 513)
    muT = af.tile(512)
    msT = af.tile(512)
    tmpT = af.tile(512)
    ST = af.tile(512)
    gg = muT
    dnt = [msT, tmpT]
    (s_ld, s_mm, s_sc, s_ev, s_fr, s_mm2, s_dev) = (
        p.s_dd, p.s_mm, p.s_sc, p.s_ev, p.s_a0, p.s_a1, p.s_a2)
    s_stp = [p.s_d2, p.s_d3]

    if l == 0:
        tlist = [(-640, 128)] + [(-512 + 512 * t, 512) for t in range(9)]
    else:
        tlist = [(512 * t, 512) for t in range(8)]
    NTI = len(tlist)
    JW = [128] * 21 + [HREM]

    with nc.Block(f"mlp{l}_w") as blk:
        @blk.gpsimd
        def _(g):
            g = Drained(g)
            for c in range(8):
                p.s_wl.inc(g.dma_start(
                    ap2(p.abf, w1sb.col + c * 5460, [[ABF_W, 128], [1, 5460]]),
                    ap2(io["L0_ffn_w1"], l * DMODEL * 2 * HID + c * 128 * 2 * HID,
                        [[2 * HID, 128], [1, 2 * HID]])), 16)
            for j in range(22):
                wj = JW[j]
                p.s_wl.inc(g.dma_start(
                    ap2(p.abf, w2sb.col + j * 1024, [[ABF_W, wj], [1, 1024]]),
                    ap2(io["L0_ffn_w2"], l * HID * DMODEL + j * 128 * DMODEL,
                        [[DMODEL, wj], [1, 1024]])), 16)
            g.wait_ge(p.s_wl.h, p.s_wl.v)

    mm0, sc0, ev0, fr0, mm20, dev0, ld0 = (s_mm.v, s_sc.v, s_ev.v, s_fr.v,
                                           s_mm2.v, s_dev.v, s_ld.v)
    hs_mk = {ti: ev0 + 9 * ti + 8 for ti in range(NTI)}
    var_mk = {ti: ev0 + 9 * ti + 9 for ti in range(NTI)}
    sq_mk = {(ti, c): sc0 + 10 * ti + c + 1 for ti in range(NTI) for c in range(8)}
    rstd_mk = {ti: sc0 + 10 * ti + 9 for ti in range(NTI)}
    stat_mm = {(ti, c): mm0 + (22 * 8 * 2 + 16) * ti + 2 * (c + 1)
               for ti in range(NTI) for c in range(8)}
    apl_mk = {}
    gel_mk = {}
    act_mk = {}
    w1_mm = {}
    w2_mm = {}
    dev_mk = {}
    stw_mk = {}
    ld_mk = {ti: ld0 + 32 * (ti + 1) for ti in range(NTI)}
    for ti in range(NTI):
        base = mm0 + (22 * 8 * 2 + 16) * ti + 16
        for j in range(22):
            w1_mm[(ti, j)] = base + 16 * (j + 1)
    mv = mm20
    for ti in range(NTI):
        for fo in range(8):
            mv += 1
            w2_mm[(ti, fo)] = mv
    for ti in range(NTI):
        for fo in range(8):
            k2 = ti * 8 + fo
            dev_mk[(ti, fo)] = dev0 + k2 + 1
            stw_mk[(ti, fo)] = s_stp[k2 % 2].v + 16 * (k2 // 2 + 1)
        for j in range(22):
            gel_mk[(ti, j)] = sc0 + 10 * NTI + ti * 22 + j + 1
            act_mk[(ti, j)] = fr0 + ti * 22 + j + 1
    sqv = sc0
    sq_mk, rstd_mk, gel_mk = {}, {}, {}
    for ti in range(NTI):
        for c in range(8):
            sqv += 1
            sq_mk[(ti, c)] = sqv
        sqv += 1
        rstd_mk[ti] = sqv
        for j in range(22):
            sqv += 1
            gel_mk[(ti, j)] = sqv
    evv, frv, devv = ev0, fr0, dev0
    hs_mk, var_mk, apl_mk = {}, {}, {}
    for ti in range(NTI):
        evv += 1
        hs_mk[ti] = evv
        evv += 1
        var_mk[ti] = evv
        evv += 1
        apl_mk[ti] = evv

    with nc.Block(f"mlp{l}") as blk:
        @blk.sync
        def _(s):
            for ti, (tau0, n) in enumerate(tlist):
                if ti >= 1:
                    s.wait_ge(s_mm.h, w1_mm[(ti - 1, 21)])
                if ti >= 1:
                    s.wait_ge(s_dev.h, dev_mk[(ti - 1, 7)])
                s_ld.inc(s.dma_start(
                    ap2(p.abf, hslab.col, [[ABF_W, 128], [513, 8], [1, n + 1]]),
                    ap2(io["xnfm"], tau0 + HALO,
                        [[1 + EXT, 128], [128 * (1 + EXT), 8], [1, n + 1]])), 16)
                s_ld.inc(s.dma_start(
                    ap2(p.af, dslab.col, [[AF_W, 128], [513, 8], [1, n + 1]]),
                    ap2(io["dacc_a"], tau0 - 1 + DH0,
                        [[DACC_W, 128], [128 * DACC_W, 8], [1, n + 1]])), 16)
                assert s_ld.v == ld_mk[ti]
                for fo in range(8):
                    s.wait_ge(s_dev.h, dev_mk[(ti, fo)])
                    assert s_stp[(ti * 8 + fo) % 2].inc(s.dma_start(
                        ap2(io["dacc_b"], fo * 128 * DACC_W + tau0 + DH0,
                            [[DACC_W, 128], [1, n]]),
                        dnt[fo % 2][:, 0:n]), 16) == stw_mk[(ti, fo)]
                if p.dev and l == 0 and ti == 2:
                    p.s_a7.inc(s.dma_start(
                        ap2(io["dbg"], 0, [[2048, 128], [1, 512]]),
                        xln[:, 0:512]), 16)
                    p.s_a7.inc(s.dma_start(
                        ap2(io["dbg"], 512, [[2048, 128], [1, 512]]),
                        xln[:, 4 * 512:5 * 512]), 16)
                    p.s_a7.inc(s.dma_start(
                        ap2(io["dbg"], 1024, [[2048, 128], [1, 512]]),
                        hslab[:, 4 * 513:4 * 513 + 512]), 16)
                    s.wait_ge(p.s_a7.h, p.s_a7.v)
                    s.wait_ge(p.s_g1.h, p.s_g1.v)
            for sp in (s_stp[0], s_stp[1]):
                s.wait_ge(sp.h, sp.v)

        if p.dev and l == 0:
            @blk.gpsimd
            def _(g):
                g.wait_ge(s_dev.h, dev_mk[(2, 7)])
                p.s_g1.inc(g.dma_start(
                    ap2(io["dbg"], 1536, [[2048, 128], [1, 512]]),
                    ST[:, 0:512]), 16)

        @blk.vector
        def _(v_):
            v_ = Drained(v_)
            for ti, (tau0, n) in enumerate(tlist):
                v_.wait_ge(s_ld.h, ld_mk[ti])
                for c in range(8):
                    eng = v_ if c == 7 else v_._e
                    ins = eng.tensor_tensor(
                        hslab[:, c * 513:c * 513 + n + 1],
                        p.af[:, dslab.col + c * 513:dslab.col + c * 513 + n + 1],
                        p.abf[:, hslab.col + c * 513:hslab.col + c * 513 + n + 1],
                        ALU.add)
                assert s_ev.inc(ins) == hs_mk[ti]
                v_.wait_ge(s_mm.h, stat_mm[(ti, 7)])
                if ti >= 1:
                    v_.wait_ge(s_stp[0].h, stw_mk[(ti - 1, 6)])
                    v_.wait_ge(s_stp[1].h, stw_mk[(ti - 1, 7)])
                v_.tensor_scalar(muT[:, 0:n], p.pf[0][:, 0:n], 1.0 / DMODEL,
                                 None, ALU.mult)
                v_.tensor_scalar(msT[:, 0:n], p.pf[1][:, 0:n], 1.0 / DMODEL,
                                 None, ALU.mult)
                v_.tensor_tensor(tmpT[:, 0:n], muT[:, 0:n], muT[:, 0:n],
                                 ALU.mult)
                assert s_ev.inc(v_.tensor_tensor(
                    msT[:, 0:n], msT[:, 0:n], tmpT[:, 0:n],
                    ALU.subtract)) == var_mk[ti]
                v_.wait_ge(s_sc.h, rstd_mk[ti])
                for c in range(8):
                    off = 1 if c < 4 else 0
                    hsv = hslab[:, c * 513 + off:c * 513 + off + n]
                    eng = v_ if c == 7 else v_._e
                    eng.tensor_tensor(xln[:, c * 512:c * 512 + n], hsv,
                                      muT[:, 0:n], ALU.subtract)
                for c in range(8):
                    eng = v_ if c == 7 else v_._e
                    eng.tensor_tensor(xln[:, c * 512:c * 512 + n],
                                      xln[:, c * 512:c * 512 + n],
                                      ST[:, 0:n], ALU.mult)
                for c in range(8):
                    eng = v_ if c == 7 else v_._e
                    ins = eng.tensor_scalar(
                        xln[:, c * 512:c * 512 + n],
                        xln[:, c * 512:c * 512 + n],
                        p.bias[:, BO[f"ln_g{l}"] + c:BO[f"ln_g{l}"] + c + 1],
                        p.bias[:, BO[f"ln_b{l}"] + c:BO[f"ln_b{l}"] + c + 1],
                        ALU.mult, ALU.add)
                assert s_ev.inc(ins) == apl_mk[ti]

                for j in range(22):
                    wj = JW[j]
                    v_.wait_ge(s_sc.h, gel_mk[(ti, j)])
                    if ti >= 1 or j >= 1:
                        pass
                    if ti >= 1:
                        v_.wait_ge(s_mm2.h, w2_mm[(ti - 1, 7)])
                    ins = v_.scalar_tensor_tensor(
                        h1act[j][0:wj, 0:n],
                        p.pf[2 + 2 * (j % 2)][0:wj, 0:n],
                        p.bias[0:wj, BO[f"b1a{l}"] + j:BO[f"b1a{l}"] + j + 1],
                        gg[0:wj, 0:n], ALU.add, ALU.mult)
                    assert s_fr.inc(ins) == act_mk[(ti, j)]
                for fo in range(8):
                    v_.wait_ge(s_mm2.h, w2_mm[(ti, fo)])
                    if (ti * 8 + fo) >= 2:
                        pk = ti * 8 + fo - 2
                        v_.wait_ge(s_stp[pk % 2].h, stw_mk[(pk // 8, pk % 8)])
                    ins = v_.scalar_tensor_tensor(
                        dnt[fo % 2][:, 0:n], p.pf[fo % 2][:, 0:n],
                        p.bias[:, BO[f"b2{l}"] + fo:BO[f"b2{l}"] + fo + 1],
                        p.af[:, dslab.col + fo * 513 + 1:
                              dslab.col + fo * 513 + 1 + n], ALU.add, ALU.add)
                    if l == 0 and tau0 < 0:
                        nh = min(n, -tau0)
                        ins = v_.tensor_scalar(dnt[fo % 2][:, 0:nh],
                                               dnt[fo % 2][:, 0:nh],
                                               p.flags[:, 2:3], None, ALU.mult)
                    assert s_dev.inc(ins) == dev_mk[(ti, fo)]

        @blk.scalar
        def _(s):
            s = Drained(s)
            for ti, (tau0, n) in enumerate(tlist):
                for c in range(8):
                    off = 1 if c < 4 else 0
                    s.wait_ge(s_ev.h, hs_mk[ti])
                    if c == 0 and ti >= 1:
                        s.wait_ge(s_ev.h, apl_mk[ti - 1])
                    if c >= 2:
                        s.wait_ge(s_mm.h, stat_mm[(ti, c - 2)])
                    elif ti >= 1:
                        s.wait_ge(s_mm.h, stat_mm[(ti - 1, 6 + c)])
                    hsv = hslab[:, c * 513 + off:c * 513 + off + n]
                    assert s_sc.inc(s.activation(
                        h2b[c % 2][:, 0:n], hsv, AF.Square,
                        bias=0.0, scale=1.0)) == sq_mk[(ti, c)]
                s.wait_ge(s_ev.h, var_mk[ti])
                s.activation(tmpT[:, 0:n], msT[:, 0:n], AF.Ln, bias=p.flags[:, 3:4],
                             scale=1.0)
                assert s_sc.inc(s.activation(ST[:, 0:n], tmpT[:, 0:n], AF.Exp,
                                             bias=0.0, scale=-0.5)) == rstd_mk[ti]
                for j in range(22):
                    wj = JW[j]
                    s.wait_ge(s_mm.h, w1_mm[(ti, j)])
                    if j >= 1:
                        s.wait_ge(s_fr.h, act_mk[(ti, j - 1)])
                    elif ti >= 1:
                        s.wait_ge(s_fr.h, act_mk[(ti - 1, 21)])
                    ins = s.activation(
                        gg[0:wj, 0:n], p.pf[3 + 2 * (j % 2)][0:wj, 0:n],
                        AF.Gelu,
                        bias=p.bias[0:wj, BO[f"b1g{l}"] + j:BO[f"b1g{l}"] + j + 1],
                        scale=1.0)
                    assert s_sc.inc(ins) == gel_mk[(ti, j)]

        @blk.tensor
        def _(t_):
            for ti, (tau0, n) in enumerate(tlist):
                for c in range(8):
                    off = 1 if c < 4 else 0
                    hsv = p.abf[:, hslab.col + c * 513 + off:
                                hslab.col + c * 513 + off + n]
                    if c == 0:
                        t_.wait_ge(s_ev.h, hs_mk[ti])
                        if ti >= 1:
                            t_.wait_ge(s_dev.h, dev_mk[(ti - 1, 7)])
                    t_.matmul(p.pf[0][:, 0:n], p.ones_bf[:, :], hsv,
                              start=(c == 0), stop=(c == 7))
                    t_.wait_ge(s_sc.h, sq_mk[(ti, c)])
                    mi = t_.matmul(p.pf[1][:, 0:n], p.ones_bf[:, :],
                                   h2b[c % 2][:, 0:n],
                                   start=(c == 0), stop=(c == 7))
                    assert s_mm.inc(mi, 2) == stat_mm[(ti, c)]
                t_.wait_ge(s_ev.h, apl_mk[ti])
                for j in range(22):
                    wj = JW[j]
                    if j >= 2:
                        t_.wait_ge(s_fr.h, act_mk[(ti, j - 2)])
                    elif ti >= 1:
                        t_.wait_ge(s_fr.h, act_mk[(ti - 1, 20 + j)])
                    if j >= 1:
                        t_.wait_ge(s_sc.h, gel_mk[(ti, j - 1)])
                    elif ti >= 1:
                        t_.wait_ge(s_sc.h, gel_mk[(ti - 1, 21)])
                    for c in range(8):
                        t_.matmul(p.pf[2 + 2 * (j % 2)][0:wj, 0:n],
                                  ap2(p.abf, w1sb.col + c * 5460 + j * 128,
                                      [[ABF_W, 128], [1, wj]]),
                                  p.abf[:, xln.col + c * 512:
                                        xln.col + c * 512 + n],
                                  start=(c == 0), stop=(c == 7))
                    for c in range(8):
                        mi = t_.matmul(p.pf[3 + 2 * (j % 2)][0:wj, 0:n],
                                       ap2(p.abf,
                                           w1sb.col + c * 5460 + HID + j * 128,
                                           [[ABF_W, 128], [1, wj]]),
                                       p.abf[:, xln.col + c * 512:
                                             xln.col + c * 512 + n],
                                       start=(c == 0), stop=(c == 7))
                    assert s_mm.inc(mi, 16) == w1_mm[(ti, j)]
                for fo in range(8):
                    if (ti * 8 + fo) >= 2:
                        pk = ti * 8 + fo - 2
                        t_.wait_ge(s_dev.h, dev_mk[(pk // 8, pk % 8)])
                    for j in range(22):
                        wj = JW[j]
                        if fo == 0:
                            t_.wait_ge(s_fr.h, act_mk[(ti, j)])
                        mi = t_.matmul(
                            p.pf[fo % 2][:, 0:n],
                            ap2(p.abf, w2sb.col + j * 1024 + fo * 128,
                                [[ABF_W, wj], [1, 128]]),
                            h1act[j][0:wj, 0:n],
                            start=(j == 0), stop=(j == 21))
                    assert s_mm2.inc(mi) == w2_mm[(ti, fo)]


def phase_fin(p):
    nc, io = p.nc, p.io
    ab = Alloc(p.abf, ABF_W)
    af = Alloc(p.af, AF_W)
    ut = [af.tile(1024) for _ in range(2)]
    ot = [af.tile(1024) for _ in range(2)]
    dmsl = af.tile(1024)
    dasl = af.tile(1024)
    s8 = dmsl
    (s_ld, s_mm, s_ev) = (p.s_dd, p.s_mm, p.s_ev)
    s_op = [p.s_d4, p.s_d5]
    ld0, mm0, ev0 = s_ld.v, s_mm.v, s_ev.v
    NG = 32
    ow_mk = {i: s_op[i % 2].v + 16 * (i // 2 + 1) for i in range(NG)}

    with nc.Block("fin") as blk:
        @blk.sync
        def _(s):
            for i in range(NG):
                if i >= 1:
                    s.wait_ge(s_ev.h, ev0 + i * 9 - 8)
                    s.wait_ge(s_mm.h, mm0 + 8 * i)
                s_ld.inc(s.dma_start(
                    ap2(p.af, dmsl.col, [[AF_W, 128], [128, 8], [1, 128]]),
                    ap2(io["dm"], i * 128,
                        [[T, 128], [128 * T, 8], [1, 128]])), 16)
                s_ld.inc(s.dma_start(
                    ap2(p.af, dasl.col, [[AF_W, 128], [128, 8], [1, 128]]),
                    ap2(io["dacc_b"], DH0 + i * 128,
                        [[DACC_W, 128], [128 * DACC_W, 8], [1, 128]])), 16)
                if i >= 2:
                    s.wait_ge(s_ev.h, ev0 + (i - 1) * 9 + 9)
                s_ld.inc(s.dma_start(ut[i % 2][:, :],
                                     io["u_scr"][i * 128:(i + 1) * 128, :]), 16)
                s.wait_ge(s_ev.h, ev0 + i * 9 + 9)
                assert s_op[i % 2].inc(
                    s.dma_start(io["out"][i * 128:(i + 1) * 128, :],
                                ot[i % 2][:, :]), 16) == ow_mk[i]
            for sp in (s_op[0], s_op[1]):
                s.wait_ge(sp.h, sp.v)

        @blk.vector
        def _(v_):
            v_ = Drained(v_)
            for i in range(NG):
                v_.wait_ge(s_ld.h, ld0 + 48 * (i + 1))
                if i >= 1:
                    v_.wait_ge(s_mm.h, mm0 + 8 * i)
                s_ev.inc(v_.tensor_add(s8[:, :], dmsl.ap, dasl.ap))
                for c in range(8):
                    v_.wait_ge(s_mm.h, mm0 + i * 8 + c + 1)
                    if c == 0 and i >= 2:
                        v_.wait_ge(s_op[i % 2].h, ow_mk[i - 2])
                    ins = v_.tensor_tensor(ot[i % 2][:, c * 128:(c + 1) * 128],
                                           ut[i % 2][:, c * 128:(c + 1) * 128],
                                           p.pf[4 + c % 2][:, 0:128], ALU.add)
                    s_ev.inc(ins)

        @blk.tensor
        def _(t_):
            for i in range(NG):
                t_.wait_ge(s_ev.h, ev0 + i * 9 + 1)
                for c in range(8):
                    k = i * 8 + c
                    if k >= 2:
                        t_.wait_ge(s_ev.h, ev0 + (k - 2) // 8 * 9 + (k - 2) % 8 + 2)
                    mi = t_.transpose(p.pf[4 + c % 2][:, 0:128],
                                      p.af[:, s8.col + c * 128:
                                           s8.col + c * 128 + 128],
                                      p.eye_f32[:, :])
                    s_mm.inc(mi)


_CACHE = {}


def kernel(**inputs):
    if "nc" not in _CACHE:
        _CACHE["nc"], _CACHE["io"] = build_program()
    nc = _CACHE["nc"]
    in_maps = [make_core_inputs(c, inputs) for c in range(8)]
    res = run_bass_kernel_spmd(nc, in_maps, core_ids=list(range(8)))
    out = np.zeros((B, N, DMODEL), np.float32)
    for c in range(8):
        b, h = c // 2, c % 2
        out[b, h * T:(h + 1) * T] = np.asarray(res.results[c]["out"])
    return out
